# version 03
# baseline (speedup 1.0000x reference)
import numpy as np
from contextlib import ExitStack

import concourse.bass as bass
import concourse.mybir as mybir
from concourse.bass import IndirectOffsetOnAxis
from concourse.bass_utils import run_bass_kernel_spmd

F32 = mybir.dt.float32
I32 = mybir.dt.int32
AF = mybir.ActivationFunctionType
ALU = mybir.AluOpType

D = 1024
DI = 2048
DSTATE = 64
NH = 32
HD = 64
DCONV = 4
CONVD = DI + 2 * DSTATE
DIP = 2 * DI + 2 * DSTATE + NH
NZX = (DIP + 127) // 128
NXBC = CONVD // 128
EPS = 1e-5
E = 64
CAP = 512
TOPK = 6
MOE_I = 704
SH_I = 1408
MLP_H = 4096
NCORES = 8
EPC = E // NCORES
TSTRIDE = 16448
SC = 128
CH = 512
BIGNEG = -1e30

_CACHE = {}


class SQ:
    ENGS = ("sync", "scalar", "vector", "tensor", "gpsimd")

    def __init__(self, nc, stack):
        self.nc = nc
        self.stack = stack
        self.ops = {e: [] for e in self.ENGS}
        self.csem = {e: self._mk(f"c_{e}") for e in self.ENGS}
        self.dsem = {e: self._mk(f"d_{e}") for e in self.ENGS}
        self.cval = {e: 0 for e in self.ENGS}
        self.dval = {e: 0 for e in self.ENGS}
        self.seen_c = {e: {f: 0 for f in self.ENGS} for e in self.ENGS}
        self.seen_d = {e: {f: 0 for f in self.ENGS} for e in self.ENGS}
        self.ccsem = self._mk("cc")
        self.ccval = 0
        self.seen_cc = {e: 0 for e in self.ENGS}

    def _mk(self, name):
        return self.stack.enter_context(self.nc.semaphore(name))

    def _barrier(self, eng):
        for f in self.ENGS:
            if f == eng:
                continue
            v = self.cval[f]
            if v > self.seen_c[eng][f]:
                self.ops[eng].append(
                    lambda e, h=self.csem[f], v=v: e.wait_ge(h, v))
                self.seen_c[eng][f] = v
            dv = self.dval[f]
            if dv > self.seen_d[eng][f]:
                self.ops[eng].append(
                    lambda e, h=self.dsem[f], v=dv: e.wait_ge(h, v))
                self.seen_d[eng][f] = dv
        dv = self.dval[eng]
        if dv > self.seen_d[eng][eng]:
            self.ops[eng].append(
                lambda e, h=self.dsem[eng], v=dv: e.wait_ge(h, v))
            self.seen_d[eng][eng] = dv

    def step(self, eng, fns):
        if not isinstance(fns, (list, tuple)):
            fns = [fns]
        self._barrier(eng)
        if eng in ("scalar", "vector", "gpsimd"):
            self.ops[eng].append(lambda e: e.drain())
        for fn in fns[:-1]:
            self.ops[eng].append(lambda e, fn=fn: fn(e))
        self.cval[eng] += 1
        self.ops[eng].append(
            lambda e, fn=fns[-1], h=self.csem[eng]:
            fn(e).then_inc(h, 1))

    def dma(self, eng, pairs):
        self._barrier(eng)
        for dst, src in pairs:
            self.dval[eng] += 16
            self.ops[eng].append(
                lambda e, d=dst, s=src, h=self.dsem[eng]:
                e.dma_start(out=d, in_=s, allow_slow_non_contiguous=True).then_inc(h, 16))

    def indirect(self, pairs):
        self._barrier("gpsimd")
        for o, oo, i, io in pairs:
            self.dval["gpsimd"] += 16
            self.ops["gpsimd"].append(
                lambda e, o=o, oo=oo, i=i, io=io, h=self.dsem["gpsimd"]:
                e.indirect_dma_start(out=o, out_offset=oo, in_=i,
                                     in_offset=io).then_inc(h, 16))

    def collective(self, kind, op, groups, ins, outs):
        self._barrier("gpsimd")
        self.ccval += 1
        self.ops["gpsimd"].append(
            lambda e, k=kind, o=op, g=groups, i=ins, u=outs, h=self.ccsem:
            e.collective_compute(k, o, replica_groups=g, ins=i,
                                 outs=u).then_inc(h, 1))

    def wait_cc(self, eng, upto=None):
        v = self.ccval if upto is None else upto
        if v > self.seen_cc[eng]:
            self.ops[eng].append(lambda e, h=self.ccsem, v=v: e.wait_ge(h, v))
            self.seen_cc[eng] = v

    def raw(self, eng, fn):
        self.ops[eng].append(fn)

    def emit(self):
        ops = self.ops
        with self.nc.Block() as block:
            @block.sync
            def _(e):
                for f in ops["sync"]:
                    f(e)

            @block.scalar
            def _(e):
                for f in ops["scalar"]:
                    f(e)

            @block.vector
            def _(e):
                for f in ops["vector"]:
                    f(e)

            @block.tensor
            def _(e):
                for f in ops["tensor"]:
                    f(e)

            @block.gpsimd
            def _(e):
                for f in ops["gpsimd"]:
                    f(e)


class Pool:
    def __init__(self, t, n):
        self.t = t
        self.n = n
        self.off = 0
        self.hi = 0
        self.marks = []

    def tile(self, w):
        assert self.off + w <= self.n, f"pool overflow {self.off}+{w}"
        ap = self.t[:, self.off:self.off + w]
        self.off += w
        self.hi = max(self.hi, self.off)
        return ap

    def tiles(self, k, w):
        return [self.tile(w) for _ in range(k)]

    def push(self):
        self.marks.append(self.off)

    def pop(self):
        self.off = self.marks.pop()


def _finish_trunc(q, S, out, zerot, T):
    q.dma("gpsimd", [(out[tt * 128:(tt + 1) * 128, :], zerot[:, 0:1024])
                     for tt in range(T // 128)])
    S("gpsimd", lambda e: e.engine_nop())


def build_program(T=2048, TP=2048, n_ranks=NCORES, stop=99):
    nc = bass.Bass("TRN2", target_bir_lowering=False)
    st = ExitStack()
    q = SQ(nc, st)

    NCH = T // CH
    NCHP = TP // CH
    TALL = T * n_ranks
    NTILE_ALL = TALL // 128
    epc = E // n_ranks

    def din(name, shape, dt=F32):
        return nc.dram_tensor(name, shape, dt, kind="ExternalInput").ap()

    x_own = din("x_own", [T, D])
    x_prev = din("x_prev", [TP, D])
    flag = din("flag", [1, 1])
    c_row = din("c_row", [1, D])
    wmod = din("wmod", [D, 6 * D])
    ln1w = din("ln1w", [1, D]); ln1b = din("ln1b", [1, D])
    ln2w = din("ln2w", [1, D]); ln2b = din("ln2b", [1, D])
    normw = din("normw", [1, DI])
    mlpb1 = din("mlpb1", [1, MLP_H]); mlpb2 = din("mlpb2", [1, D])
    gateb = din("gateb", [1, E])
    a_log = din("a_log", [1, NH]); d_vec = din("d_vec", [1, NH])
    dtbias = din("dtbias", [1, NH])
    convw = din("convw", [CONVD, DCONV]); convb = din("convb", [CONVD, 1])
    w_ip = din("w_ip", [D, DIP])
    w_op = din("w_op", [DI, D])
    w_m1 = din("w_m1", [D, MLP_H])
    w_m2 = din("w_m2", [MLP_H, D])
    w_gt = din("w_gt", [D, E])
    w_s1 = din("w_s1", [D, SH_I]); w_s3 = din("w_s3", [D, SH_I])
    w_s2 = din("w_s2", [SH_I, D])
    w_e1 = din("w_e1", [epc, D, MOE_I])
    w_e3 = din("w_e3", [epc, D, MOE_I])
    w_e2 = din("w_e2", [epc, MOE_I, D])
    rank_i = din("rank_i", [1, 1], I32)

    out = nc.dram_tensor("out", [T, D], F32, kind="ExternalOutput").ap()

    def dint(name, shape, dt=F32):
        return nc.dram_tensor(name, shape, dt).ap()

    x_hbm = dint("x_hbm", [T, D])
    ag_xf_in = dint("ag_xf_in", [T, D])
    xf_all = dint("xf_all", [TALL + 128, D])
    ag_pay_in = dint("ag_pay_in", [T, 16])
    pay_all = dint("pay_all", [TALL, 16])
    tblh = dint("tblh", [E * TSTRIDE, 1])
    slotwf = dint("slotwf", [TALL, 16])
    cnts_h = dint("cnts_h", [1, E])
    tbl_own = dint("tbl_own", [E, 512])
    ag_yb_in = dint("ag_yb_in", [epc * CAP, D])
    yb_all = dint("yb_all", [E * CAP, D])
    z_hbm = dint("z_hbm", [T, D])

    POOLW = 53100
    poolt = st.enter_context(nc.sbuf_tensor("pool", [128, POOLW], F32))
    pool = Pool(poolt, POOLW)
    pooli = st.enter_context(nc.sbuf_tensor("pooli", [128, 64], I32))

    ones_row = pool.tile(128)
    ones_col = pool.tile(1)
    iot128 = pool.tile(128)
    iotc = pool.tile(1)
    ident = pool.tile(128)
    sltm = pool.tile(128)
    maskbig = pool.tile(128)
    modb = pool.tiles(6, D)
    lncol = pool.tiles(4, 8)
    b2b = pool.tile(D)
    gbB = pool.tile(E)
    nrmc = pool.tile(16)
    b1c = pool.tile(32)
    a128 = pool.tile(NH)
    d128 = pool.tile(NH)
    acol = pool.tile(1)
    nacol = pool.tile(1)
    dtbc = pool.tile(1)
    flagc = pool.tile(1)
    cw = pool.tiles(NXBC, DCONV)
    cb = pool.tiles(NXBC, 1)
    c_fm = pool.tile(8)
    state = pool.tile(NH * HD)
    tails = pool.tile(NXBC * 4)
    smallA = pool.tile(128)
    smallB = pool.tile(16)
    smallC = pool.tile(512)
    etimes = pool.tile(1)
    constc = pool.tile(4)
    baser = pool.tile(E)
    basec = pool.tile(1)

    persist_end = pool.off

    pool.push()
    srow = pool.tile(6 * D)
    zerot = pool.tile(2056)
    pool.pop()

    pool.push()
    xchunk = pool.tiles(4, D)
    htok = pool.tiles(4, D)
    x1tok = pool.tiles(4, D)
    hfm_big = pool.tile(8 * CH)
    hfm = [hfm_big[:, i * CH:(i + 1) * CH] for i in range(8)]
    zx = pool.tiles(NZX, CH)
    cvs = pool.tile(CH)
    ccopy = pool.tile(CH)
    wbufA = pool.tile(8 * 256)
    wbufB = wbufA
    wbufC = pool.tile(D)
    wbufD = wbufC
    xs_tok = hfm_big[:, 0:DI]
    y_tok = hfm_big[:, DI:2 * DI]
    bct = pool.tile(128)
    sdt = pool.tile(SC)
    sdtT = pool.tile(96)
    row0 = pool.tile(512)
    wfm = pool.tile(SC)
    dkc = pool.tile(1)
    dkrow = pool.tile(NH)
    asig = pool.tile(SC)
    aST = pool.tile(NH)
    easT = pool.tile(NH)
    dkerb = pool.tile(NH)
    scoresT = pool.tile(128)
    lpre = pool.tiles(4, 128)
    mtb = pool.tiles(4, 128)
    dtt = pool.tile(CH)
    onescN = pool.tile(SC)
    payb = pool.tiles(T // 128, 16)
    scorekeep = pool.tile(E)
    attn_end = pool.off
    assert attn_end < POOLW, attn_end

    print('pool hi', pool.hi, 'off', pool.off)
    pb = [st.enter_context(nc.psum_tensor(f"pb{i}", [128, 512], F32))
          for i in range(8)]

    def S(eng, *fns):
        q.step(eng, list(fns))

    def MM(psum, lhsT, rhs, start, stop):
        return lambda e, o=psum, l=lhsT, r=rhs, a=start, b=stop: \
            e.matmul(o, l, r, start=a, stop=b)

    def TR(psum, src, npart):
        return lambda e, o=psum, i=src, n=npart: \
            e.transpose(o, i, ident[0:n, 0:n])

    def CP(dst, src):
        return lambda e, d=dst, s=src: e.tensor_copy(d, s)

    def TT(dst, a, b, op):
        return lambda e, d=dst, a=a, b=b, o=op: e.tensor_tensor(d, a, b, o)

    def TS(dst, a, s1, s2, op0, op1=ALU.bypass):
        return lambda e, d=dst, a=a, s1=s1, s2=s2, o0=op0, o1=op1: \
            e.tensor_scalar(d, a, s1, s2, o0, o1)

    def STT(dst, in0, sc, in1, op0, op1):
        return lambda e, d=dst, i0=in0, s=sc, i1=in1, o0=op0, o1=op1: \
            e.scalar_tensor_tensor(d, i0, s, i1, o0, o1)

    def ACT(dst, src, f, bias=0.0, scale=1.0):
        return lambda e, d=dst, s=src, f=f, b=bias, sc=scale: \
            e.activation(d, s, f, bias=b, scale=sc)

    def RED(dst, src, op):
        return lambda e, d=dst, s=src, o=op: \
            e.tensor_reduce(d, s, mybir.AxisListType.X, o)

    def MS(dst, v):
        return lambda e, d=dst, v=v: e.memset(d, v)

    def bcast(psum, row, nparts):
        w = row.shape[-1]
        return MM(psum[0:nparts, 0:w], ones_row[0:1, 0:nparts], row,
                  True, True)

    q.dma("sync", [
        (c_fm, c_row.rearrange("o (q f) -> (o f) q", f=128)),
        (smallA[0:1, 0:NH], a_log[:, :]),
        (smallA[0:1, NH:2 * NH], d_vec[:, :]),
        (smallA[0:1, 2 * NH:3 * NH], dtbias[:, :]),
        (flagc[0:1, :], flag[:, :]),
        (nrmc, normw.rearrange("o (q f) -> (o f) q", f=128)),
        (b1c, mlpb1.rearrange("o (q f) -> (o f) q", f=128)),
        (smallC[0:1, 0:E], gateb[:, :]),
        (lncol[0], ln1w.rearrange("o (q f) -> (o f) q", f=128)),
        (lncol[1], ln1b.rearrange("o (q f) -> (o f) q", f=128)),
        (lncol[2], ln2w.rearrange("o (q f) -> (o f) q", f=128)),
        (lncol[3], ln2b.rearrange("o (q f) -> (o f) q", f=128)),
        (srow[0:1, 4 * D:5 * D], mlpb2[:, :]),
    ] + [(cw[i], convw[i * 128:(i + 1) * 128, :]) for i in range(NXBC)]
      + [(cb[i], convb[i * 128:(i + 1) * 128, :]) for i in range(NXBC)])

    S("vector", MS(ident, 1.0))
    S("vector", lambda e: e.tensor_tensor_scan(iot128, ident, ident, 0.0,
                                               ALU.mult, ALU.add))
    S("vector", TS(iot128, iot128, 1.0, None, ALU.subtract))
    S("tensor", TR(pb[0][0:128, 0:1], iot128[0:1, 0:128], 1))
    S("vector", CP(iotc, pb[0][0:128, 0:1]))
    S("vector",
      MS(ones_row[0:1, :], 1.0), MS(ones_col, 1.0), MS(onescN[0:NH, :], 1.0),
      MS(zerot, 0.0), MS(state[0:64, :], 0.0), MS(tails, 0.0),
      MS(baser[0:1, :], 0.0), MS(basec[0:64, :], 0.0),
      TS(ident, iot128, iotc, None, ALU.is_equal),
      TS(sltm, iot128, iotc, None, ALU.is_gt),
      TS(maskbig, iot128, iotc, None, ALU.is_ge),
      TS(etimes[0:64, :], iotc[0:64, :], float(TSTRIDE), None, ALU.mult),
      MS(constc[:, 0:1], float(EPS)), MS(constc[:, 1:2], 1.0),
      MS(constc[:, 2:3], -1.0), MS(constc[:, 3:4], 1.0 / DI))
    S("vector", TS(maskbig, maskbig, 1.0, 1e30, ALU.subtract, ALU.mult))
    S("scalar", ACT(smallA[0:1, 0:NH], smallA[0:1, 0:NH], AF.Exp))
    S("scalar", ACT(smallA[0:1, 0:NH], smallA[0:1, 0:NH], AF.Copy, scale=constc[0:1, 2:3]),
      ACT(c_fm, c_fm, AF.Silu))
    ztb = tblh.rearrange("(a b) o -> a (b o)", b=8224)
    q.dma("gpsimd", [(ztb[i * 128:(i + 1) * 128, j * 1028:(j + 1) * 1028],
                      zerot[:, 0:1028]) for i in range(1) for j in range(8)])
    q.dma("gpsimd", [(xf_all[TALL:TALL + 128, :], zerot[:, 0:1024])])

    bc_jobs = [(b2b, srow[0:1, 4 * D:5 * D]), (gbB, smallC[0:1, 0:E]),
               (a128, smallA[0:1, 0:NH]), (d128, smallA[0:1, NH:2 * NH])]
    for i, (dst, row) in enumerate(bc_jobs):
        w = row.shape[-1]
        for si in range((w + 511) // 512):
            lo, hi = si * 512, min(w, (si + 1) * 512)
            S("tensor", bcast(pb[0], row[:, lo:hi], 128))
            S("vector", CP(dst[:, lo:hi], pb[0][0:128, 0:hi - lo]))
    S("tensor", TR(pb[1][0:NH, 0:1], smallA[0:1, 0:NH], 1))
    S("vector", CP(acol[0:NH, :], pb[1][0:NH, 0:1]))
    S("vector", TS(nacol[0:NH, :], acol[0:NH, :], -1.0, None, ALU.mult))
    S("tensor", TR(pb[1][0:NH, 0:1], smallA[0:1, 2 * NH:3 * NH], 1))
    S("vector", CP(dtbc[0:NH, :], pb[1][0:NH, 0:1]))
    S("tensor", bcast(pb[1], flagc[0:1, 0:1], 128))
    S("vector", CP(flagc, pb[1][0:128, 0:1]))

    if stop < 1:
        _finish_trunc(q, S, out, zerot, T)
        q.emit(); st.close(); return nc
    NW = 6 * D // 512
    for nwi in range(NW):
        for kc in range(8):
            wb = wbufC if kc % 2 == 0 else wbufD
            q.dma("sync", [(wb[:, 0:512],
                            wmod[kc * 128:(kc + 1) * 128,
                                 nwi * 512:(nwi + 1) * 512])])
            S("tensor", MM(pb[2][0:1, 0:512], c_fm[:, kc:kc + 1],
                           wb[:, 0:512], kc == 0, kc == 7))
        S("vector", CP(srow[0:1, nwi * 512:(nwi + 1) * 512],
                       pb[2][0:1, 0:512]))
    for gi in (2, 5):
        sl = srow[0:1, gi * D:(gi + 1) * D]
        s0 = zerot[0:1, 0:D]
        s1 = zerot[0:1, D:2 * D]
        S("vector", TS(s0, sl, 0.0, 2.0, ALU.is_ge, ALU.mult),
          TS(s1, sl, -1.0, None, ALU.mult))
        S("vector", TS(s0, s0, 1.0, None, ALU.subtract),
          TT(s1, s1, sl, ALU.max))
        S("vector", TS(s1, s1, 0.25, None, ALU.max))
        S("vector", TT(sl, s0, s1, ALU.mult))
    S("vector", TS(srow[0:1, D:2 * D], srow[0:1, D:2 * D], 1.0, None, ALU.add),
      TS(srow[0:1, 4 * D:5 * D], srow[0:1, 4 * D:5 * D], 1.0, None, ALU.add))
    for i in range(6):
        row = srow[0:1, i * D:(i + 1) * D]
        for si in range(2):
            lo, hi = si * 512, (si + 1) * 512
            S("tensor", bcast(pb[0], row[:, lo:hi], 128))
            S("vector", CP(modb[i][:, lo:hi], pb[0][0:128, 0:512]))

    def ln_tiles(src_tiles, dst_tiles, mul_tile, add_tile, nt):
        for tt in range(nt):
            S("vector", lambda e, s=src_tiles[tt]: e.bn_stats(
                smallB[0:128, 0:6], s[:, 0:512]),
              lambda e, s=src_tiles[tt]: e.bn_stats(
                smallB[0:128, 6:12], s[:, 512:1024]))
            S("vector", lambda e: e.bn_aggr(smallB[0:128, 8:10] if False else
                                            smallB[0:128, 12:14],
                                            smallB[0:128, 0:12]))
            S("scalar", ACT(smallB[0:128, 10:11], smallB[0:128, 13:14],
                            AF.Sqrt, bias=constc[:, 0:1]))
            S("vector", lambda e: e.reciprocal(smallB[0:128, 10:11],
                                               smallB[0:128, 10:11]))
            S("vector",
              TS(dst_tiles[tt], src_tiles[tt], smallB[0:128, 12:13],
                 smallB[0:128, 10:11], ALU.subtract, ALU.mult))
            if mul_tile is not None:
                S("vector", TT(dst_tiles[tt], dst_tiles[tt], mul_tile,
                               ALU.mult))
                S("vector", TT(dst_tiles[tt], dst_tiles[tt], add_tile,
                               ALU.add))

    def to_fm(src_tiles, dst_fm_tiles, nf, nt, wcol=None, bcol=None):
        jobs = []
        for fc in range(nf):
            for tt in range(nt):
                jobs.append((fc, tt))
        for j0 in range(0, len(jobs), 4):
            grp = jobs[j0:j0 + 4]
            S("tensor", *[TR(pb[4 + i][0:128, 0:128],
                             src_tiles[tt][:, fc * 128:(fc + 1) * 128], 128)
                          for i, (fc, tt) in enumerate(grp)])
            if wcol is None:
                S("vector", *[CP(dst_fm_tiles[fc][:, tt * 128:(tt + 1) * 128],
                                 pb[4 + i][0:128, 0:128])
                              for i, (fc, tt) in enumerate(grp)])
            else:
                S("vector", *[TS(dst_fm_tiles[fc][:, tt * 128:(tt + 1) * 128],
                                 pb[4 + i][0:128, 0:128],
                                 wcol[:, fc:fc + 1], bcol[:, fc:fc + 1],
                                 ALU.mult, ALU.add)
                              for i, (fc, tt) in enumerate(grp)])

    def patternA(wdram, m0t, m1t, out_cb, Tn, kparts=8, mtot=None):
        if mtot is None:
            mtot = m1t * 128
        mcs = list(range(m0t, m1t))
        wid = {mc: min(128, mtot - mc * 128) for mc in mcs}
        for bi in range(0, len(mcs), 2):
            blk = mcs[bi:bi + 2]
            wb = wbufA if (bi // 2) % 2 == 0 else wbufB
            mw = sum(wid[mc] for mc in blk)
            q.dma("sync", [(wb[:, kc * 256:kc * 256 + mw],
                            wdram[kc * 128:(kc + 1) * 128,
                                  blk[0] * 128:blk[0] * 128 + mw])
                           for kc in range(kparts)])
            for mi, mc in enumerate(blk):
                w = wid[mc]
                o0 = mi * 128
                bank = pb[mc % 4]
                S("tensor", *[MM(bank[0:w, 0:Tn],
                                 wb[:, kc * 256 + o0:kc * 256 + o0 + w],
                                 hfm[kc][:, 0:Tn], kc == 0, kc == kparts - 1)
                              for kc in range(kparts)])
                out_cb(mc, bank[0:w, 0:Tn])

    def conv_chunk(Tn):
        xbc = [cvs] + [zx[15 + i] for i in range(1, NXBC)]
        S("vector", *[CP(smallC[0:128, i * 3:(i + 1) * 3],
                         zx[16 + i][:, Tn - 3:Tn]) for i in range(NXBC)])
        for i in range(NXBC):
            srct = zx[16 + i]
            dstt = xbc[i]
            ops1 = [TS(dstt[:, 0:Tn], srct[:, 0:Tn], cw[i][:, 3:4], None,
                       ALU.mult)]
            S("vector", *ops1)
            for k in range(3):
                S("vector", STT(dstt[:, 3 - k:Tn], srct[:, 0:Tn - 3 + k],
                                cw[i][:, k:k + 1], dstt[:, 3 - k:Tn],
                                ALU.mult, ALU.add))
                tops = []
                for t in range(0, 3 - k):
                    tops.append(STT(dstt[:, t:t + 1],
                                    tails[:, i * 4 + t + k:i * 4 + t + k + 1],
                                    cw[i][:, k:k + 1], dstt[:, t:t + 1],
                                    ALU.mult, ALU.add))
                if tops:
                    S("vector", *tops)
        S("vector", *[CP(tails[:, i * 4:i * 4 + 3],
                         smallC[0:128, i * 3:(i + 1) * 3])
                      for i in range(NXBC)])
        S("scalar", *[ACT(xbc[i][:, 0:Tn], xbc[i][:, 0:Tn], AF.Silu,
                          bias=cb[i][:, :]) for i in range(NXBC)])
        return xbc

    def dt_chunk(Tn):
        dts = smallC
        S("scalar", ACT(dtt[0:NH, 0:Tn], zx[33][0:NH, 0:Tn], AF.Abs,
                        bias=dtbc[0:NH, :]))
        S("scalar", ACT(dts[0:NH, 0:Tn], dtt[0:NH, 0:Tn], AF.Exp, scale=constc[0:NH, 2:3]))
        S("scalar", ACT(dts[0:NH, 0:Tn], dts[0:NH, 0:Tn], AF.Ln, bias=constc[0:NH, 1:2]))
        S("scalar", ACT(dtt[0:NH, 0:Tn], zx[33][0:NH, 0:Tn], AF.Relu,
                        bias=dtbc[0:NH, :]))
        S("vector", TT(dtt[0:NH, 0:Tn], dtt[0:NH, 0:Tn], dts[0:NH, 0:Tn],
                       ALU.add))

    def ssd_subchunk(xbc, o, partial):
        bc_tile = xbc[16]
        q.dma("gpsimd", [(ccopy[0:64, 0:SC], bc_tile[64:128, o:o + SC])])
        S("vector",
          lambda e: e.tensor_tensor_scan(sdt[0:NH, 0:SC], onescN[0:NH, 0:SC],
                                         dtt[0:NH, o:o + SC], 0.0,
                                         ALU.mult, ALU.add))
        S("vector", TS(asig[0:NH, 0:SC], sdt[0:NH, 0:SC], acol[0:NH, :],
                       None, ALU.mult),
          TS(wfm[0:NH, 0:SC], sdt[0:NH, 0:SC],
             sdt[0:NH, SC - 1:SC], nacol[0:NH, :], ALU.subtract, ALU.mult))
        S("scalar", ACT(wfm[0:NH, 0:SC], wfm[0:NH, 0:SC], AF.Exp))
        S("vector", TT(wfm[0:NH, 0:SC], wfm[0:NH, 0:SC],
                       dtt[0:NH, o:o + SC], ALU.mult),
          TS(dkc[0:NH, :], sdt[0:NH, SC - 1:SC], acol[0:NH, :], None,
             ALU.mult))
        S("scalar", ACT(dkc[0:NH, :], dkc[0:NH, :], AF.Exp))
        S("tensor", TR(pb[4][0:SC, 0:NH], sdt[0:NH, 0:SC], NH),
          TR(pb[5][0:SC, 0:NH], dtt[0:NH, o:o + SC], NH),
          TR(pb[6][0:SC, 0:NH], wfm[0:NH, 0:SC], NH),
          TR(pb[7][0:1, 0:NH], dkc[0:NH, :], NH))
        S("vector", CP(sdtT[:, 0:NH], pb[4][0:SC, 0:NH]),
          CP(sdtT[:, NH:2 * NH], pb[5][0:SC, 0:NH]),
          CP(sdtT[:, 2 * NH:3 * NH], pb[6][0:SC, 0:NH]),
          CP(dkrow[0:1, 0:NH], pb[7][0:1, 0:NH]))
        S("vector", TT(aST[:, 0:NH], sdtT[:, 0:NH], a128, ALU.mult))
        S("scalar", ACT(easT[:, 0:NH], aST[:, 0:NH], AF.Exp))
        S("tensor", bcast(pb[6], dkrow[0:1, 0:NH], 64))
        S("vector", CP(dkerb[0:64, 0:NH], pb[6][0:64, 0:NH]))
        S("tensor", TR(pb[7][0:128, 0:128], bc_tile[:, o:o + SC], 128))
        S("vector", CP(bct[:, 0:128], pb[7][0:128, 0:128]))
        for f0 in range(0, 16, 4):
            S("tensor", *[TR(pb[4 + i][0:128, 0:128],
                             xbc[f0 + i][:, o:o + SC], 128) for i in range(4)])
            S("vector", *[CP(xs_tok[:, (f0 + i) * 128:(f0 + i + 1) * 128],
                             pb[4 + i][0:128, 0:128]) for i in range(4)])
        if not partial:
            S("tensor", MM(pb[6][0:128, 0:128], bc_tile[0:64, o:o + SC],
                           ccopy[0:64, 0:SC], True, True))
            S("vector", CP(scoresT[:, 0:128], pb[6][0:128, 0:128]))

        for hg in range(NH // 4):
            hs = hg * 4
            if not partial:
                S("tensor", *[TR(pb[4 + j][0:1, 0:128],
                                 aST[:, hs + j:hs + j + 1], 128)
                              for j in range(4)])
                S("vector", *[CP(row0[0:1, j * 128:(j + 1) * 128],
                                 pb[4 + j][0:1, 0:128]) for j in range(4)])
                S("tensor", *[MM(pb[0][0:128, j * 128:(j + 1) * 128],
                                 ones_row[0:1, 0:128],
                                 row0[0:1, j * 128:(j + 1) * 128],
                                 j == 0, j == 3) for j in range(4)])
                S("vector", *[TS(lpre[j], pb[0][0:128, j * 128:(j + 1) * 128],
                                 aST[:, hs + j:hs + j + 1], None,
                                 ALU.subtract) for j in range(4)])
                S("vector", *[TT(lpre[j], lpre[j], maskbig, ALU.min)
                              for j in range(4)])
                S("scalar", *[ACT(lpre[j], lpre[j], AF.Exp) for j in range(4)])
                S("vector", *[STT(mtb[j], lpre[j],
                                  sdtT[:, NH + hs + j:NH + hs + j + 1],
                                  scoresT, ALU.mult, ALU.mult)
                              for j in range(4)])
                S("tensor",
                  *[MM(pb[2][0:128, j * 64:(j + 1) * 64], mtb[j],
                       xs_tok[:, (hs + j) * 64:(hs + j + 1) * 64],
                       j == 0, j == 3) for j in range(4)],
                  *[MM(pb[3][0:128, j * 64:(j + 1) * 64],
                       ccopy[0:64, 0:SC],
                       state[0:64, (hs + j) * 64:(hs + j + 1) * 64],
                       j == 0, j == 3) for j in range(4)])
                S("vector", *[CP(y_tok[:, (hs + j) * 64:(hs + j + 1) * 64],
                                 pb[2][0:128, j * 64:(j + 1) * 64])
                              for j in range(4)])
                S("vector", *[STT(y_tok[:, (hs + j) * 64:(hs + j + 1) * 64],
                                  pb[3][0:128, j * 64:(j + 1) * 64],
                                  easT[:, hs + j:hs + j + 1],
                                  y_tok[:, (hs + j) * 64:(hs + j + 1) * 64],
                                  ALU.mult, ALU.add) for j in range(4)])
                S("vector", *[STT(y_tok[:, (hs + j) * 64:(hs + j + 1) * 64],
                                  xs_tok[:, (hs + j) * 64:(hs + j + 1) * 64],
                                  d128[:, hs + j:hs + j + 1],
                                  y_tok[:, (hs + j) * 64:(hs + j + 1) * 64],
                                  ALU.mult, ALU.add) for j in range(4)])
            S("vector", *[TS(xs_tok[:, (hs + j) * 64:(hs + j + 1) * 64],
                             xs_tok[:, (hs + j) * 64:(hs + j + 1) * 64],
                             sdtT[:, 2 * NH + hs + j:2 * NH + hs + j + 1],
                             None, ALU.mult)
                          for j in range(4)])
            S("tensor", *[MM(pb[1][0:64, j * 64:(j + 1) * 64], bct[:, 0:64],
                             xs_tok[:, (hs + j) * 64:(hs + j + 1) * 64],
                             j == 0, j == 3) for j in range(4)])
            S("vector", *[STT(state[0:64, (hs + j) * 64:(hs + j + 1) * 64],
                              state[0:64, (hs + j) * 64:(hs + j + 1) * 64],
                              dkerb[0:64, hs + j:hs + j + 1],
                              pb[1][0:64, j * 64:(j + 1) * 64],
                              ALU.mult, ALU.add) for j in range(4)])

    def attn_chunk(c, partial, xsrc):
        nt = CH // 128
        q.dma("sync", [(xchunk[tt], xsrc[c * CH + tt * 128:
                                         c * CH + (tt + 1) * 128, :])
                       for tt in range(nt)])
        ln_tiles(xchunk, htok, modb[1], modb[0], nt)
        ln_tiles(htok, x1tok, None, None, nt)
        to_fm(x1tok, hfm, 8, nt, lncol[0], lncol[1])
        m0 = 16 if partial else 0
        patternA(w_ip, m0, NZX,
                 lambda mc, ps: S("vector", CP(
                     zx[mc][0:(32 if mc == NZX - 1 else 128), 0:CH], ps)),
                 CH, mtot=DIP)
        xbc = conv_chunk(CH)
        dt_chunk(CH)
        for sci in range(CH // SC):
            ssd_subchunk(xbc, sci * SC, partial)
            if not partial:
                post_ssd_subchunk(c, sci, xbc)
        if not partial:
            post_chunk(c, nt)

    def post_ssd_subchunk(c, sci, xbc):
        o = sci * SC
        S("scalar", *[ACT(zx[fc][:, o:o + SC], zx[fc][:, o:o + SC], AF.Silu)
                      for fc in range(16)])
        for f0 in range(0, 16, 4):
            S("tensor", *[TR(pb[4 + i][0:128, 0:128],
                             y_tok[:, (f0 + i) * 128:(f0 + i + 1) * 128], 128)
                          for i in range(4)])
            S("vector", *[TT(zx[f0 + i][:, o:o + SC],
                             pb[4 + i][0:128, 0:128],
                             zx[f0 + i][:, o:o + SC], ALU.mult)
                          for i in range(4)])
        sqops = []
        mmops = []
        for fc in range(16):
            sqops.append(TT(lpre[fc % 4], zx[fc][:, o:o + SC],
                            zx[fc][:, o:o + SC], ALU.mult))
            mmops.append(MM(pb[6][0:1, 0:SC], ones_col[0:128, :],
                            lpre[fc % 4], fc == 0, fc == 15))
        for f0 in range(0, 16, 4):
            S("vector", *sqops[f0:f0 + 4])
            S("tensor", *mmops[f0:f0 + 4])
        S("scalar", ACT(smallC[0:1, 0:SC], pb[6][0:1, 0:SC], AF.Sqrt,
                        bias=constc[0:1, 0:1], scale=constc[0:1, 3:4]))
        S("vector", lambda e: e.reciprocal(smallC[0:1, 0:SC],
                                           smallC[0:1, 0:SC]))
        S("tensor", bcast(pb[7], smallC[0:1, 0:SC], 128))
        S("vector", *[STT(zx[fc][:, o:o + SC], pb[7][0:128, 0:SC],
                          nrmc[:, fc:fc + 1], zx[fc][:, o:o + SC],
                          ALU.mult, ALU.mult) for fc in range(16)])

    def post_chunk(c, nt):
        for kc in range(16):
            wb = wbufC if kc % 2 == 0 else wbufD
            q.dma("sync", [(wb[:, 0:D], w_op[kc * 128:(kc + 1) * 128, :])])
            S("tensor", *[MM(pb[tt * 2 + si][0:128, 0:512],
                             zx[kc][:, tt * 128:(tt + 1) * 128],
                             wb[:, si * 512:(si + 1) * 512],
                             kc == 0, kc == 15)
                          for tt in range(nt) for si in range(2)])
        S("vector", *[TT(x1tok[tt][:, si * 512:(si + 1) * 512],
                         pb[tt * 2 + si][0:128, 0:512],
                         htok[tt][:, si * 512:(si + 1) * 512], ALU.add)
                      for tt in range(nt) for si in range(2)])
        ln_tiles(x1tok, htok, None, None, nt)
        to_fm(htok, hfm, 8, nt, lncol[2], lncol[3])
        patternA(w_m1, 0, 32,
                 lambda mc, ps: S("scalar",
                                  ACT(zx[mc][:, 0:CH], ps, AF.Gelu,
                                      bias=b1c[:, mc:mc + 1])), CH)
        for kc in range(32):
            wb = wbufC if kc % 2 == 0 else wbufD
            q.dma("sync", [(wb[:, 0:D], w_m2[kc * 128:(kc + 1) * 128, :])])
            S("tensor", *[MM(pb[tt * 2 + si][0:128, 0:512],
                             zx[kc][:, tt * 128:(tt + 1) * 128],
                             wb[:, si * 512:(si + 1) * 512],
                             kc == 0, kc == 31)
                          for tt in range(nt) for si in range(2)])
        S("vector", *[TT(htok[tt][:, si * 512:(si + 1) * 512],
                         pb[tt * 2 + si][0:128, 0:512],
                         b2b[:, si * 512:(si + 1) * 512], ALU.add)
                      for tt in range(nt) for si in range(2)])
        S("vector", *[TT(htok[tt], htok[tt], x1tok[tt], ALU.add)
                      for tt in range(nt)])
        S("vector", *[TT(htok[tt], htok[tt], modb[2], ALU.mult)
                      for tt in range(nt)])
        S("vector", *[TT(xchunk[tt], xchunk[tt], htok[tt], ALU.add)
                      for tt in range(nt)])
        q.dma("gpsimd", [(x_hbm[c * CH + tt * 128:c * CH + (tt + 1) * 128, :],
                          xchunk[tt]) for tt in range(nt)])
        ln_tiles(xchunk, htok, modb[4], modb[3], nt)
        q.dma("gpsimd", [(ag_xf_in[c * CH + tt * 128:c * CH + (tt + 1) * 128, :],
                          htok[tt]) for tt in range(nt)])
        to_fm(htok, hfm, 8, nt)
        gate_topk(c, nt)

    def gate_topk(c, nt):
        for tt in range(nt):
            wb = wbufC
            if tt == 0:
                q.dma("sync", [(wbufC[:, kc * 64:(kc + 1) * 64],
                                w_gt[kc * 128:(kc + 1) * 128, :])
                               for kc in range(8)])
            S("tensor", *[MM(pb[4][0:128, 0:E],
                             hfm[kc][:, tt * 128:(tt + 1) * 128],
                             wbufC[:, kc * 64:(kc + 1) * 64],
                             kc == 0, kc == 7) for kc in range(8)])
            S("vector", RED(smallB[0:128, 0:1], pb[4][0:128, 0:E], ALU.max))
            S("vector", TS(smallB[0:128, 1:2], smallB[0:128, 0:1], -1.0,
                           None, ALU.mult))
            S("scalar", lambda e: e.activation(lpre[0][:, 0:E], pb[4][0:128, 0:E],
                                               AF.Exp, bias=smallB[0:128, 1:2]))
            S("vector", RED(smallB[0:128, 2:3], lpre[0][:, 0:E], ALU.add))
            S("vector", lambda e: e.reciprocal(smallB[0:128, 3:4],
                                               smallB[0:128, 2:3]))
            S("vector", TS(scorekeep[:, 0:E], lpre[0][:, 0:E],
                           smallB[0:128, 3:4], None, ALU.mult))
            S("vector", TT(lpre[1][:, 0:E], scorekeep[:, 0:E], gbB, ALU.add))
            pt = payb[c * nt + tt]
            S("vector", MS(pt, 0.0))
            for k in range(TOPK):
                S("vector", RED(smallB[0:128, 4:5], lpre[1][:, 0:E], ALU.max))
                S("vector", TS(lpre[2][:, 0:E], lpre[1][:, 0:E],
                               smallB[0:128, 4:5], None, ALU.is_equal))
                S("vector", TT(lpre[3][:, 0:E], lpre[2][:, 0:E], iot128[:, 0:64],
                               ALU.mult),
                  TS(mtb[0][:, 0:E], lpre[2][:, 0:E], 1.0, -64.0,
                     ALU.subtract, ALU.mult))
                S("vector", TT(lpre[3][:, 0:E], lpre[3][:, 0:E],
                               mtb[0][:, 0:E], ALU.add))
                S("vector", RED(pt[:, k:k + 1], lpre[3][:, 0:E], ALU.min))
                S("vector", TS(lpre[2][:, 0:E], iot128[:, 0:64], pt[:, k:k + 1], None,
                               ALU.is_equal))
                S("vector", TT(lpre[3][:, 0:E], lpre[2][:, 0:E],
                               scorekeep[:, 0:E], ALU.mult))
                S("vector", RED(pt[:, 8 + k:9 + k], lpre[3][:, 0:E], ALU.add))
                S("vector", STT(lpre[1][:, 0:E], lpre[2][:, 0:E], -1e30,
                                lpre[1][:, 0:E], ALU.mult, ALU.add))
        q.dma("gpsimd", [(ag_pay_in[c * CH + tt * 128:c * CH + (tt + 1) * 128, :],
                          payb[c * nt + tt]) for tt in range(nt)])

    if stop < 2:
        _finish_trunc(q, S, out, zerot, T)
        q.emit(); st.close(); return nc
    for c in range(NCHP):
        attn_chunk(c, True, x_prev)
    S("vector", TS(state[0:64, :], state[0:64, :], flagc[0:64, :], None,
                   ALU.mult),
      TS(tails, tails, flagc, None, ALU.mult))
    if stop < 3:
        _finish_trunc(q, S, out, zerot, T)
        q.emit(); st.close(); return nc
    for c in range(NCH):
        attn_chunk(c, False, x_own)

    if stop < 4:
        _finish_trunc(q, S, out, zerot, T)
        q.emit(); st.close(); return nc
    q.collective("AllGather", ALU.bypass, [list(range(n_ranks))],
                 [ag_pay_in.opt()], [pay_all.opt()])
    cc_pay = q.ccval
    q.collective("AllGather", ALU.bypass, [list(range(n_ranks))],
                 [ag_xf_in.opt()], [xf_all[0:TALL, :].opt()])
    cc_xf = q.ccval

    pool.pop()
    pool.push()
    ptile = pool.tile(16)
    otile = pool.tile(E)
    cex = pool.tile(E)
    ptab = pool.tile(E)
    msk = [pool.tile(E) for _ in range(TOPK)]
    colk = pool.tile(16)
    swf = pool.tile(16)
    ctile = pool.tile(E)
    tmpe = [pool.tile(E) for _ in range(2)]
    ofs = pool.tile(1)
    ofs_i = pooli[:, 0:1]

    if stop < 5:
        _finish_trunc(q, S, out, zerot[0:128, 0:1024] if False else zerot, T)
        q.emit(); st.close(); return nc
    q.wait_cc("sync", cc_pay)
    q.wait_cc("gpsimd", cc_pay)
    S("vector", MS(swf, 0.0))
    for ti in range(NTILE_ALL):
        q.dma("sync", [(ptile, pay_all[ti * 128:(ti + 1) * 128, :])])
        S("vector", *[TS(msk[k], iot128[:, 0:64], ptile[:, k:k + 1], None, ALU.is_equal)
                      for k in range(TOPK)],
          TS(colk[:, 15:16], iotc, float(ti * 128), None, ALU.add))
        S("vector", TT(otile, msk[0], msk[1], ALU.add),
          TT(tmpe[0], msk[2], msk[3], ALU.add),
          TT(tmpe[1], msk[4], msk[5], ALU.add))
        S("vector", TT(otile, otile, tmpe[0], ALU.add))
        S("vector", TT(otile, otile, tmpe[1], ALU.add))
        S("vector", TT(ofs[0:64, :], basec[0:64, :], etimes[0:64, :], ALU.add))
        S("vector", CP(ofs_i[0:64, :], ofs[0:64, :]))
        S("tensor", MM(pb[0][0:128, 0:E], sltm, otile, True, False),
          MM(pb[0][0:128, 0:E], ones_row[0:1, 0:128].rearrange("a b -> a b"),
             baser[0:1, 0:E], False, True))
        S("vector", CP(ptab, pb[0][0:128, 0:E]))
        S("tensor", MM(pb[0][0:128, 0:E], sltm, otile, True, True))
        S("vector", CP(cex, pb[0][0:128, 0:E]))
        S("tensor", MM(pb[1][0:1, 0:E], ones_col[0:128, :], otile,
                       True, True),
          MM(pb[2][0:64, 0:1], otile[:, 0:64], ones_col[0:128, :],
             True, True))
        for k in range(TOPK):
            S("vector", TT(tmpe[0], msk[k], cex, ALU.mult),
              TT(tmpe[1], msk[k], ptab, ALU.mult))
            S("vector", RED(colk[:, k:k + 1], tmpe[0], ALU.add),
              RED(colk[:, 8 + k:9 + k], tmpe[1], ALU.add))
        for k in range(TOPK):
            S("vector", TS(tmpe[0], iot128[:, 0:64], colk[:, k:k + 1], None,
                           ALU.is_equal))
            S("vector", TS(tmpe[0], tmpe[0], colk[:, 15:16], None, ALU.mult))
            S("tensor", MM(pb[3][0:64, 0:E], msk[k], tmpe[0],
                           k == 0, k == TOPK - 1))
        S("vector", CP(ctile[0:64, :], pb[3][0:64, 0:E]))
        q.indirect([(tblh[:, :], IndirectOffsetOnAxis(ofs_i[0:64, :], 0),
                     ctile[0:64, :], None)])
        S("vector", TT(baser[0:1, 0:E], baser[0:1, 0:E], pb[1][0:1, 0:E],
                       ALU.add),
          TT(basec[0:64, :], basec[0:64, :], pb[2][0:64, 0:1], ALU.add))
        S("vector", *[TS(swf[:, 8 + k:9 + k], colk[:, 8 + k:9 + k], 511.0,
                         None, ALU.min) for k in range(TOPK)])
        S("vector", *[STT(swf[:, k:k + 1], ptile[:, k:k + 1], 512.0,
                          swf[:, 8 + k:9 + k], ALU.mult, ALU.add)
                      for k in range(TOPK)])
        S("vector", *[TS(tmpe[0][:, k:k + 1], colk[:, 8 + k:9 + k], 512.0,
                         None, ALU.is_lt) for k in range(TOPK)])
        S("vector", *[TT(swf[:, 8 + k:9 + k], ptile[:, 8 + k:9 + k],
                         tmpe[0][:, k:k + 1], ALU.mult) for k in range(TOPK)])
        q.dma("gpsimd", [(slotwf[ti * 128:(ti + 1) * 128, :], swf)])
    q.dma("gpsimd", [(cnts_h[:, :], baser[0:1, 0:E])])

    if stop < 6:
        _finish_trunc(q, S, out, zerot, T)
        q.emit(); st.close(); return nc
    pool.pop()
    pool.push()
    xbt = pool.tiles(4, D)
    xbfm = pool.tiles(8, 512)
    he = pool.tiles(11, 512)
    ybt = pool.tiles(4, D)
    ewA = pool.tile(8 * 256)
    ewB = ewA
    ewC = pool.tile(D)
    ewD = ewC
    idxv = pool.tile(8)
    cnt8 = pool.tile(E)
    vld = pool.tile(4)
    swft = pool.tile(16)
    rankc = pool.tile(1)
    ofs = pool.tile(1)
    tbl_own_sb = pool.tile(512)
    xmt = pool.tile(D)
    zt = pool.tile(D)
    gath = pool.tile(D)
    acc = pool.tile(D)
    moe_end = pool.off
    idx_i4 = pooli[:, 4:8]
    slot_i = pooli[:, 8:9]

    rank_s = pooli[:, 12:13]
    q.dma("gpsimd", [(rank_s[0:1, :], rank_i[:, :])])
    S("vector", CP(smallB[0:1, 14:15], rank_s[0:1, :]))
    S("tensor", bcast(pb[0], smallB[0:1, 14:15], 128))
    S("vector", CP(rankc, pb[0][0:128, 0:1]))
    S("vector", TS(ofs[0:epc, :], rankc[0:epc, :], float(epc), None,
                   ALU.mult))
    S("vector", TT(ofs[0:epc, :], ofs[0:epc, :], iotc[0:epc, :], ALU.add))
    S("vector", CP(pooli[0:epc, 13:14], ofs[0:epc, :]))
    cnts_flat = cnts_h.rearrange("o e -> (o e)")[:, None]
    q.indirect([(smallC[0:epc, 0:1], None, cnts_flat,
                 IndirectOffsetOnAxis(pooli[0:epc, 13:14], 0))])
    S("tensor", TR(pb[0][0:1, 0:epc], smallC[0:epc, 0:1], epc))
    S("vector", CP(smallC[0:1, 4:4 + epc], pb[0][0:1, 0:epc]))
    S("tensor", bcast(pb[0], smallC[0:1, 4:4 + epc], 128))
    S("vector", CP(cnt8[:, 0:epc], pb[0][0:128, 0:epc]))
    S("vector", TS(ofs[0:epc, :], rankc[0:epc, :], float(epc), None,
                   ALU.mult))
    S("vector", TT(ofs[0:epc, :], ofs[0:epc, :], iotc[0:epc, :], ALU.add))
    S("vector", TS(ofs[0:epc, :], ofs[0:epc, :], float(TSTRIDE), None,
                   ALU.mult))
    S("vector", CP(pooli[0:epc, 14:15], ofs[0:epc, :]))
    q.indirect([(tbl_own_sb[0:epc, 0:512], None, tblh[:, :],
                 IndirectOffsetOnAxis(pooli[0:epc, 14:15], 0))])
    q.dma("gpsimd", [(tbl_own[0:epc, :], tbl_own_sb[0:epc, 0:512])])
    for j in range(epc):
        q.dma("gpsimd", [(idxv[:, (j % 2) * 4:(j % 2) * 4 + 4],
                          tbl_own[j, 0:512].rearrange("(f q) -> q f", q=128))])
        jj = j % 2
        S("vector", *[TS(vld[:, col:col + 1],
                         iotc, float(col * 128), None, ALU.add)
                      for col in range(4)])
        S("vector", *[TS(vld[:, col:col + 1], vld[:, col:col + 1],
                         cnt8[:, j:j + 1], None, ALU.is_lt)
                      for col in range(4)])
        S("vector", TS(idxv[:, jj * 4:jj * 4 + 4],
                       idxv[:, jj * 4:jj * 4 + 4], float(TALL), None,
                       ALU.subtract))
        S("vector", TT(idxv[:, jj * 4:jj * 4 + 4],
                       idxv[:, jj * 4:jj * 4 + 4], vld, ALU.mult))
        S("vector", TS(idxv[:, jj * 4:jj * 4 + 4],
                       idxv[:, jj * 4:jj * 4 + 4], float(TALL), None,
                       ALU.add))
        S("vector", CP(idx_i4, idxv[:, jj * 4:jj * 4 + 4]))
        if j == 0:
            q.wait_cc("gpsimd", cc_xf)
        q.indirect([(xbt[col], None, xf_all[:, :],
                     IndirectOffsetOnAxis(idx_i4[:, col:col + 1], 0))
                    for col in range(4)])
        for f0 in range(0, 8, 2):
            S("tensor", *[TR(pb[4 * i + tt][0:128, 0:128],
                             xbt[tt][:, (f0 + i) * 128:(f0 + i + 1) * 128], 128)
                          for i in range(2) for tt in range(4)])
            S("vector", *[CP(xbfm[f0 + i][:, tt * 128:(tt + 1) * 128],
                             pb[4 * i + tt][0:128, 0:128])
                          for i in range(2) for tt in range(4)])
        mws = [128, 128, 128, 128, 128, 64]
        def eA(wdram, j, cb_):
            for bi in range(0, 6, 2):
                wb = ewA if (bi // 2) % 2 == 0 else ewB
                mw = mws[bi] + (mws[bi + 1] if bi + 1 < 6 else 0)
                q.dma("sync", [(wb[:, kc * 256:kc * 256 + mw],
                                wdram[j, kc * 128:(kc + 1) * 128,
                                      bi * 128:bi * 128 + mw])
                               for kc in range(8)])
                for mi in range(2):
                    if bi + mi >= 6:
                        continue
                    mwid = mws[bi + mi]
                    bank = pb[(bi + mi) % 4]
                    S("tensor", *[MM(bank[0:mwid, 0:512],
                                     wb[:, kc * 256 + mi * 128:
                                        kc * 256 + mi * 128 + mwid],
                                     xbfm[kc][:, 0:512], kc == 0, kc == 7)
                                  for kc in range(8)])
                    cb_(bi + mi, bank[0:mwid, 0:512], mwid)
        eA(w_e1, j, lambda ic, ps, mwid:
           S("scalar", ACT(he[ic][0:mwid, :], ps, AF.Silu)))
        eA(w_e3, j, lambda ic, ps, mwid:
           S("vector", TT(he[ic][0:mwid, :], he[ic][0:mwid, :], ps,
                          ALU.mult)))
        for kc2 in range(6):
            mwid = mws[kc2]
            wb = ewC if kc2 % 2 == 0 else ewD
            q.dma("sync", [(wb[0:mwid, 0:D],
                            w_e2[j, kc2 * 128:kc2 * 128 + mwid, :])])
            S("tensor", *[MM(pb[sl * 2 + si][0:128, 0:512],
                             he[kc2][0:mwid, sl * 128:(sl + 1) * 128],
                             wb[0:mwid, si * 512:(si + 1) * 512],
                             kc2 == 0, kc2 == 5)
                          for sl in range(4) for si in range(2)])
        S("vector", *[CP(ybt[sl][:, si * 512:(si + 1) * 512],
                         pb[sl * 2 + si][0:128, 0:512])
                      for sl in range(4) for si in range(2)])
        q.dma("gpsimd", [(ag_yb_in[j * CAP + sl * 128:j * CAP + (sl + 1) * 128,
                                   :], ybt[sl]) for sl in range(4)])

    q.collective("AllGather", ALU.bypass, [list(range(n_ranks))],
                 [ag_yb_in.opt()], [yb_all.opt()])
    cc_yb = q.ccval

    if stop < 7:
        _finish_trunc(q, S, out, zerot, T)
        q.emit(); st.close(); return nc
    for c in range(NCH):
        nt = CH // 128
        q.dma("sync", [(xbt[tt], ag_xf_in[c * CH + tt * 128:
                                          c * CH + (tt + 1) * 128, :])
                       for tt in range(nt)])
        for f0 in range(0, 8, 2):
            S("tensor", *[TR(pb[4 * i + tt][0:128, 0:128],
                             xbt[tt][:, (f0 + i) * 128:(f0 + i + 1) * 128], 128)
                          for i in range(2) for tt in range(4)])
            S("vector", *[CP(xbfm[f0 + i][:, tt * 128:(tt + 1) * 128],
                             pb[4 * i + tt][0:128, 0:128])
                          for i in range(2) for tt in range(4)])
        def sA(wdram, cb_):
            for bi in range(0, 11, 2):
                nb = min(2, 11 - bi)
                wb = ewA if (bi // 2) % 2 == 0 else ewB
                mw = nb * 128
                q.dma("sync", [(wb[:, kc * 256:kc * 256 + mw],
                                wdram[kc * 128:(kc + 1) * 128,
                                      bi * 128:bi * 128 + mw])
                               for kc in range(8)])
                for mi in range(nb):
                    bank = pb[(bi + mi) % 4]
                    S("tensor", *[MM(bank[0:128, 0:512],
                                     wb[:, kc * 256 + mi * 128:
                                        kc * 256 + (mi + 1) * 128],
                                     xbfm[kc][:, 0:512], kc == 0, kc == 7)
                                  for kc in range(8)])
                    cb_(bi + mi, bank[0:128, 0:512])
        sA(w_s1, lambda ic, ps: S("scalar", ACT(he[ic], ps, AF.Silu)))
        sA(w_s3, lambda ic, ps: S("vector", TT(he[ic], he[ic], ps, ALU.mult)))
        for kc2 in range(11):
            wb = ewC if kc2 % 2 == 0 else ewD
            q.dma("sync", [(wb[:, 0:D], w_s2[kc2 * 128:(kc2 + 1) * 128, :])])
            S("tensor", *[MM(pb[sl * 2 + si][0:128, 0:512],
                             he[kc2][:, sl * 128:(sl + 1) * 128],
                             wb[:, si * 512:(si + 1) * 512],
                             kc2 == 0, kc2 == 10)
                          for sl in range(4) for si in range(2)])
        S("vector", *[CP(ybt[sl][:, si * 512:(si + 1) * 512],
                         pb[sl * 2 + si][0:128, 0:512])
                      for sl in range(4) for si in range(2)])
        q.dma("gpsimd", [(z_hbm[c * CH + sl * 128:c * CH + (sl + 1) * 128, :],
                          ybt[sl]) for sl in range(4)])

    if stop < 8:
        _finish_trunc(q, S, out, zerot, T)
        q.emit(); st.close(); return nc
    q.wait_cc("gpsimd", cc_yb)
    for tt in range(T // 128):
        q.dma("sync", [(xmt, x_hbm[tt * 128:(tt + 1) * 128, :]),
                       (zt, z_hbm[tt * 128:(tt + 1) * 128, :])])
        S("vector", TS(ofs, rankc, float(T), None, ALU.mult))
        S("vector", TT(ofs, ofs, iotc, ALU.add))
        S("vector", TS(ofs, ofs, 1.0, float(tt * 128), ALU.mult, ALU.add))
        S("vector", CP(pooli[:, 15:16], ofs))
        q.indirect([(swft, None, slotwf[:, :],
                     IndirectOffsetOnAxis(pooli[:, 15:16], 0))])
        for k in range(TOPK):
            S("vector", CP(slot_i, swft[:, k:k + 1]))
            q.indirect([(gath, None, yb_all[:, :],
                         IndirectOffsetOnAxis(slot_i, 0))])
            if k == 0:
                S("vector", TS(acc, gath, swft[:, 8:9], None, ALU.mult))
            else:
                S("vector", STT(acc, gath, swft[:, 8 + k:9 + k], acc,
                                ALU.mult, ALU.add))
        S("vector", TT(acc, acc, zt, ALU.add))
        S("vector", TT(acc, acc, modb[5], ALU.mult))
        S("vector", TT(acc, acc, xmt, ALU.add))
        q.dma("gpsimd", [(out[tt * 128:(tt + 1) * 128, :], acc)])
    S("gpsimd", lambda e: e.engine_nop())

    q.emit()
    st.close()
    return nc


def prep_inmaps(inputs, T, n_ranks):
    def t(a):
        return np.ascontiguousarray(np.transpose(np.asarray(a, np.float32)))

    x = np.ascontiguousarray(inputs["L0_attn_x"], dtype=np.float32)
    epc = E // n_ranks
    shared = {
        "wmod": np.ascontiguousarray(inputs["L0_wmod"], np.float32),
        "ln1w": inputs["L0_attn_ln1_w"].reshape(1, D).astype(np.float32),
        "ln1b": inputs["L0_attn_ln1_b"].reshape(1, D).astype(np.float32),
        "ln2w": inputs["L0_attn_ln2_w"].reshape(1, D).astype(np.float32),
        "ln2b": inputs["L0_attn_ln2_b"].reshape(1, D).astype(np.float32),
        "normw": inputs["L0_attn_norm_w"].reshape(1, DI).astype(np.float32),
        "mlpb1": inputs["L0_attn_mlp_b1"].reshape(1, MLP_H).astype(np.float32),
        "mlpb2": inputs["L0_attn_mlp_b2"].reshape(1, D).astype(np.float32),
        "gateb": inputs["L0_ffn_gate_b"].reshape(1, E).astype(np.float32),
        "a_log": inputs["L0_attn_A_log"].reshape(1, NH).astype(np.float32),
        "d_vec": inputs["L0_attn_D"].reshape(1, NH).astype(np.float32),
        "dtbias": inputs["L0_attn_dt_bias"].reshape(1, NH).astype(np.float32),
        "convw": np.ascontiguousarray(inputs["L0_attn_conv_w"], np.float32),
        "convb": inputs["L0_attn_conv_b"].reshape(CONVD, 1).astype(np.float32),
        "w_ip": t(inputs["L0_attn_in_proj_w"]),
        "w_op": t(inputs["L0_attn_out_proj_w"]),
        "w_m1": t(inputs["L0_attn_mlp_w1"]),
        "w_m2": t(inputs["L0_attn_mlp_w2"]),
        "w_gt": t(inputs["L0_ffn_gate_w"]),
        "w_s1": t(inputs["L0_ffn_ws1"]),
        "w_s3": t(inputs["L0_ffn_ws3"]),
        "w_s2": t(inputs["L0_ffn_ws2"]),
    }
    we1 = np.ascontiguousarray(np.transpose(
        np.asarray(inputs["L0_ffn_w1"], dtype=np.float32), (0, 2, 1)))
    we3 = np.ascontiguousarray(np.transpose(
        np.asarray(inputs["L0_ffn_w3"], dtype=np.float32), (0, 2, 1)))
    we2 = np.ascontiguousarray(np.transpose(
        np.asarray(inputs["L0_ffn_w2"], dtype=np.float32), (0, 2, 1)))
    c_all = np.asarray(inputs["L0_c"], dtype=np.float32)

    in_maps = []
    zeros_prev = np.zeros((T, D), np.float32)
    for r in range(n_ranks):
        b, h = r // 2, r % 2
        m = dict(shared)
        m["c_row"] = np.ascontiguousarray(c_all[b:b + 1])
        m["x_own"] = np.ascontiguousarray(x[b, h * T:(h + 1) * T])
        m["x_prev"] = (np.ascontiguousarray(x[b, 0:T]) if h == 1
                       else zeros_prev)
        m["flag"] = np.array([[1.0 if h == 1 else 0.0]], np.float32)
        m["rank_i"] = np.array([[r]], np.int32)
        m["w_e1"] = np.ascontiguousarray(we1[r * epc:(r + 1) * epc])
        m["w_e3"] = np.ascontiguousarray(we3[r * epc:(r + 1) * epc])
        m["w_e2"] = np.ascontiguousarray(we2[r * epc:(r + 1) * epc])
        in_maps.append(m)
    return in_maps


def kernel(**inputs):
    T = 2048
    key = (T,)
    if key not in _CACHE:
        _CACHE[key] = build_program(T=T, TP=T, n_ranks=NCORES)
    nc = _CACHE[key]
    in_maps = prep_inmaps(inputs, T, NCORES)
    res = run_bass_kernel_spmd(nc, in_maps, core_ids=list(range(NCORES)))
    outs = [np.asarray(res.results[r]["out"]) for r in range(NCORES)]
    x = inputs["L0_attn_x"]
    B, L, _ = x.shape
    full = np.empty((B, L, D), np.float32)
    for r in range(NCORES):
        b, h = r // 2, r % 2
        full[b, h * T:(h + 1) * T] = outs[r]
    return full


# version 04
# speedup vs baseline: 1.0334x; 1.0334x over previous
import numpy as np
from contextlib import ExitStack

import concourse.bass as bass
import concourse.mybir as mybir
from concourse.bass import IndirectOffsetOnAxis
from concourse.bass_utils import run_bass_kernel_spmd

F32 = mybir.dt.float32
I32 = mybir.dt.int32
AF = mybir.ActivationFunctionType
ALU = mybir.AluOpType

D = 1024
DI = 2048
DSTATE = 64
NH = 32
HD = 64
DCONV = 4
CONVD = DI + 2 * DSTATE
DIP = 2 * DI + 2 * DSTATE + NH
NZX = (DIP + 127) // 128
NXBC = CONVD // 128
EPS = 1e-5
E = 64
CAP = 512
TOPK = 6
MOE_I = 704
SH_I = 1408
MLP_H = 4096
NCORES = 8
EPC = E // NCORES
TSTRIDE = 16448
SC = 128
CH = 512
BIGNEG = -1e30

_CACHE = {}


class SQ:
    ENGS = ("sync", "scalar", "vector", "tensor", "gpsimd")

    def __init__(self, nc, stack):
        self.nc = nc
        self.stack = stack
        self.ops = {e: [] for e in self.ENGS}
        self.csem = {e: self._mk(f"c_{e}") for e in self.ENGS}
        self.dsem = {e: self._mk(f"d_{e}") for e in self.ENGS}
        self.cval = {e: 0 for e in self.ENGS}
        self.dval = {e: 0 for e in self.ENGS}
        self.seen_c = {e: {f: 0 for f in self.ENGS} for e in self.ENGS}
        self.seen_d = {e: {f: 0 for f in self.ENGS} for e in self.ENGS}
        self.ccsem = self._mk("cc")
        self.ccval = 0
        self.seen_cc = {e: 0 for e in self.ENGS}

    def _mk(self, name):
        return self.stack.enter_context(self.nc.semaphore(name))

    def _barrier(self, eng, lag=None):
        for f in self.ENGS:
            if f == eng:
                continue
            v = self.cval[f]
            if lag and f in lag:
                v = max(0, v - lag[f])
            if v > self.seen_c[eng][f]:
                self.ops[eng].append(
                    lambda e, h=self.csem[f], v=v: e.wait_ge(h, v))
                self.seen_c[eng][f] = v
            dv = self.dval[f]
            if dv > self.seen_d[eng][f]:
                self.ops[eng].append(
                    lambda e, h=self.dsem[f], v=dv: e.wait_ge(h, v))
                self.seen_d[eng][f] = dv
        dv = self.dval[eng]
        if dv > self.seen_d[eng][eng]:
            self.ops[eng].append(
                lambda e, h=self.dsem[eng], v=dv: e.wait_ge(h, v))
            self.seen_d[eng][eng] = dv

    def step(self, eng, fns, lag=None):
        if not isinstance(fns, (list, tuple)):
            fns = [fns]
        self._barrier(eng, lag)
        if eng in ("scalar", "vector", "gpsimd"):
            self.ops[eng].append(lambda e: e.drain())
        for fn in fns[:-1]:
            self.ops[eng].append(lambda e, fn=fn: fn(e))
        self.cval[eng] += 1
        self.ops[eng].append(
            lambda e, fn=fns[-1], h=self.csem[eng]:
            fn(e).then_inc(h, 1))

    def dma(self, eng, pairs):
        self._barrier(eng)
        for dst, src in pairs:
            self.dval[eng] += 16
            self.ops[eng].append(
                lambda e, d=dst, s=src, h=self.dsem[eng]:
                e.dma_start(out=d, in_=s, allow_slow_non_contiguous=True).then_inc(h, 16))

    def indirect(self, pairs):
        self._barrier("gpsimd")
        for o, oo, i, io in pairs:
            self.dval["gpsimd"] += 16
            self.ops["gpsimd"].append(
                lambda e, o=o, oo=oo, i=i, io=io, h=self.dsem["gpsimd"]:
                e.indirect_dma_start(out=o, out_offset=oo, in_=i,
                                     in_offset=io).then_inc(h, 16))

    def collective(self, kind, op, groups, ins, outs):
        self._barrier("gpsimd")
        self.ccval += 1
        self.ops["gpsimd"].append(
            lambda e, k=kind, o=op, g=groups, i=ins, u=outs, h=self.ccsem:
            e.collective_compute(k, o, replica_groups=g, ins=i,
                                 outs=u).then_inc(h, 1))

    def wait_cc(self, eng, upto=None):
        v = self.ccval if upto is None else upto
        if v > self.seen_cc[eng]:
            self.ops[eng].append(lambda e, h=self.ccsem, v=v: e.wait_ge(h, v))
            self.seen_cc[eng] = v

    def raw(self, eng, fn):
        self.ops[eng].append(fn)

    def emit(self):
        ops = self.ops
        with self.nc.Block() as block:
            @block.sync
            def _(e):
                for f in ops["sync"]:
                    f(e)

            @block.scalar
            def _(e):
                for f in ops["scalar"]:
                    f(e)

            @block.vector
            def _(e):
                for f in ops["vector"]:
                    f(e)

            @block.tensor
            def _(e):
                for f in ops["tensor"]:
                    f(e)

            @block.gpsimd
            def _(e):
                for f in ops["gpsimd"]:
                    f(e)


class Pool:
    def __init__(self, t, n):
        self.t = t
        self.n = n
        self.off = 0
        self.hi = 0
        self.marks = []

    def tile(self, w):
        assert self.off + w <= self.n, f"pool overflow {self.off}+{w}"
        ap = self.t[:, self.off:self.off + w]
        self.off += w
        self.hi = max(self.hi, self.off)
        return ap

    def tiles(self, k, w):
        return [self.tile(w) for _ in range(k)]

    def push(self):
        self.marks.append(self.off)

    def pop(self):
        self.off = self.marks.pop()


def _finish_trunc(q, S, out, zerot, T):
    q.dma("gpsimd", [(out[tt * 128:(tt + 1) * 128, :], zerot[:, 0:1024])
                     for tt in range(T // 128)])
    S("gpsimd", lambda e: e.engine_nop())


def build_program(T=2048, TP=2048, n_ranks=NCORES, stop=99):
    nc = bass.Bass("TRN2", target_bir_lowering=False)
    st = ExitStack()
    q = SQ(nc, st)

    NCH = T // CH
    NCHP = TP // CH
    TALL = T * n_ranks
    NTILE_ALL = TALL // 128
    epc = E // n_ranks

    def din(name, shape, dt=F32):
        return nc.dram_tensor(name, shape, dt, kind="ExternalInput").ap()

    x_own = din("x_own", [T, D])
    x_prev = din("x_prev", [TP, D])
    flag = din("flag", [1, 1])
    c_row = din("c_row", [1, D])
    wmod = din("wmod", [D, 6 * D])
    ln1w = din("ln1w", [1, D]); ln1b = din("ln1b", [1, D])
    ln2w = din("ln2w", [1, D]); ln2b = din("ln2b", [1, D])
    normw = din("normw", [1, DI])
    mlpb1 = din("mlpb1", [1, MLP_H]); mlpb2 = din("mlpb2", [1, D])
    gateb = din("gateb", [1, E])
    a_log = din("a_log", [1, NH]); d_vec = din("d_vec", [1, NH])
    dtbias = din("dtbias", [1, NH])
    convw = din("convw", [CONVD, DCONV]); convb = din("convb", [CONVD, 1])
    w_ip = din("w_ip", [D, DIP])
    w_op = din("w_op", [DI, D])
    w_m1 = din("w_m1", [D, MLP_H])
    w_m2 = din("w_m2", [MLP_H, D])
    w_gt = din("w_gt", [D, E])
    w_s1 = din("w_s1", [D, SH_I]); w_s3 = din("w_s3", [D, SH_I])
    w_s2 = din("w_s2", [SH_I, D])
    w_e1 = din("w_e1", [epc, D, MOE_I])
    w_e3 = din("w_e3", [epc, D, MOE_I])
    w_e2 = din("w_e2", [epc, MOE_I, D])
    rank_i = din("rank_i", [1, 1], I32)

    out = nc.dram_tensor("out", [T, D], F32, kind="ExternalOutput").ap()

    def dint(name, shape, dt=F32):
        return nc.dram_tensor(name, shape, dt).ap()

    x_hbm = dint("x_hbm", [T, D])
    ag_xf_in = dint("ag_xf_in", [T, D])
    xf_all = dint("xf_all", [TALL + 128, D])
    ag_pay_in = dint("ag_pay_in", [T, 16])
    pay_all = dint("pay_all", [TALL, 16])
    tblh = dint("tblh", [E * TSTRIDE, 1])
    slotwf = dint("slotwf", [TALL, 16])
    cnts_h = dint("cnts_h", [1, E])
    tbl_own = dint("tbl_own", [E, 512])
    ag_yb_in = dint("ag_yb_in", [epc * CAP, D])
    yb_all = dint("yb_all", [E * CAP, D])
    z_hbm = dint("z_hbm", [T, D])

    POOLW = 53100
    poolt = st.enter_context(nc.sbuf_tensor("pool", [128, POOLW], F32))
    pool = Pool(poolt, POOLW)
    pooli = st.enter_context(nc.sbuf_tensor("pooli", [128, 64], I32))

    ones_row = pool.tile(128)
    ones_col = pool.tile(1)
    iot128 = pool.tile(128)
    iotc = pool.tile(1)
    ident = pool.tile(128)
    sltm = pool.tile(128)
    maskbig = pool.tile(128)
    modb = pool.tiles(6, D)
    lncol = pool.tiles(4, 8)
    b2b = pool.tile(D)
    gbB = pool.tile(E)
    nrmc = pool.tile(16)
    b1c = pool.tile(32)
    a128 = pool.tile(NH)
    d128 = pool.tile(NH)
    acol = pool.tile(1)
    nacol = pool.tile(1)
    dtbc = pool.tile(1)
    flagc = pool.tile(1)
    cw = pool.tiles(NXBC, DCONV)
    cb = pool.tiles(NXBC, 1)
    c_fm = pool.tile(8)
    state = pool.tile(NH * HD)
    tails = pool.tile(NXBC * 4)
    smallA = pool.tile(128)
    smallB = pool.tile(16)
    smallC = pool.tile(512)
    etimes = pool.tile(1)
    constc = pool.tile(4)
    baser = pool.tile(E)
    basec = pool.tile(1)

    persist_end = pool.off

    pool.push()
    srow = pool.tile(6 * D)
    zerot = pool.tile(2056)
    pool.pop()

    pool.push()
    xchunk = pool.tiles(4, D)
    htok = pool.tiles(4, D)
    x1tok = pool.tiles(4, D)
    hfm_big = pool.tile(8 * CH)
    hfm = [hfm_big[:, i * CH:(i + 1) * CH] for i in range(8)]
    zx = pool.tiles(NZX, CH)
    cvs = pool.tile(CH)
    ccopy = pool.tile(CH)
    wbufA = pool.tile(8 * 256)
    wbufB = wbufA
    wbufC = pool.tile(D)
    wbufD = wbufC
    xs_tok = hfm_big[:, 0:DI]
    y_tok = hfm_big[:, DI:2 * DI]
    bct = pool.tile(128)
    sdt = pool.tile(SC)
    sdtT = pool.tile(96)
    row0 = pool.tile(512)
    wfm = pool.tile(SC)
    dkc = pool.tile(1)
    dkrow = pool.tile(NH)
    asig = pool.tile(SC)
    aST = pool.tile(NH)
    easT = pool.tile(NH)
    dkerb = pool.tile(NH)
    scoresT = pool.tile(128)
    lpre = pool.tiles(4, 128)
    mtb = pool.tiles(4, 128)
    dtt = pool.tile(CH)
    onescN = pool.tile(SC)
    payb = pool.tiles(T // 128, 16)
    scorekeep = pool.tile(E)
    attn_end = pool.off
    assert attn_end < POOLW, attn_end

    print('pool hi', pool.hi, 'off', pool.off)
    pb = [st.enter_context(nc.psum_tensor(f"pb{i}", [128, 512], F32))
          for i in range(8)]

    def S(eng, *fns):
        q.step(eng, list(fns))

    def SL(eng, lag, *fns):
        q.step(eng, list(fns), lag=lag)

    def MM(psum, lhsT, rhs, start, stop):
        return lambda e, o=psum, l=lhsT, r=rhs, a=start, b=stop: \
            e.matmul(o, l, r, start=a, stop=b)

    def TR(psum, src, npart):
        return lambda e, o=psum, i=src, n=npart: \
            e.transpose(o, i, ident[0:n, 0:n])

    def CP(dst, src):
        return lambda e, d=dst, s=src: e.tensor_copy(d, s)

    def TT(dst, a, b, op):
        return lambda e, d=dst, a=a, b=b, o=op: e.tensor_tensor(d, a, b, o)

    def TS(dst, a, s1, s2, op0, op1=ALU.bypass):
        return lambda e, d=dst, a=a, s1=s1, s2=s2, o0=op0, o1=op1: \
            e.tensor_scalar(d, a, s1, s2, o0, o1)

    def STT(dst, in0, sc, in1, op0, op1):
        return lambda e, d=dst, i0=in0, s=sc, i1=in1, o0=op0, o1=op1: \
            e.scalar_tensor_tensor(d, i0, s, i1, o0, o1)

    def ACT(dst, src, f, bias=0.0, scale=1.0):
        return lambda e, d=dst, s=src, f=f, b=bias, sc=scale: \
            e.activation(d, s, f, bias=b, scale=sc)

    def RED(dst, src, op):
        return lambda e, d=dst, s=src, o=op: \
            e.tensor_reduce(d, s, mybir.AxisListType.X, o)

    def MS(dst, v):
        return lambda e, d=dst, v=v: e.memset(d, v)

    def bcast(psum, row, nparts):
        w = row.shape[-1]
        return MM(psum[0:nparts, 0:w], ones_row[0:1, 0:nparts], row,
                  True, True)

    q.dma("sync", [
        (c_fm, c_row.rearrange("o (q f) -> (o f) q", f=128)),
        (smallA[0:1, 0:NH], a_log[:, :]),
        (smallA[0:1, NH:2 * NH], d_vec[:, :]),
        (smallA[0:1, 2 * NH:3 * NH], dtbias[:, :]),
        (flagc[0:1, :], flag[:, :]),
        (nrmc, normw.rearrange("o (q f) -> (o f) q", f=128)),
        (b1c, mlpb1.rearrange("o (q f) -> (o f) q", f=128)),
        (smallC[0:1, 0:E], gateb[:, :]),
        (lncol[0], ln1w.rearrange("o (q f) -> (o f) q", f=128)),
        (lncol[1], ln1b.rearrange("o (q f) -> (o f) q", f=128)),
        (lncol[2], ln2w.rearrange("o (q f) -> (o f) q", f=128)),
        (lncol[3], ln2b.rearrange("o (q f) -> (o f) q", f=128)),
        (srow[0:1, 4 * D:5 * D], mlpb2[:, :]),
    ] + [(cw[i], convw[i * 128:(i + 1) * 128, :]) for i in range(NXBC)]
      + [(cb[i], convb[i * 128:(i + 1) * 128, :]) for i in range(NXBC)])

    S("vector", MS(ident, 1.0))
    S("vector", lambda e: e.tensor_tensor_scan(iot128, ident, ident, 0.0,
                                               ALU.mult, ALU.add))
    S("vector", TS(iot128, iot128, 1.0, None, ALU.subtract))
    S("tensor", TR(pb[0][0:128, 0:1], iot128[0:1, 0:128], 1))
    S("vector", CP(iotc, pb[0][0:128, 0:1]))
    S("vector",
      MS(ones_row[0:1, :], 1.0), MS(ones_col, 1.0), MS(onescN[0:NH, :], 1.0),
      MS(zerot, 0.0), MS(state[0:64, :], 0.0), MS(tails, 0.0),
      MS(baser[0:1, :], 0.0), MS(basec[0:64, :], 0.0),
      TS(ident, iot128, iotc, None, ALU.is_equal),
      TS(sltm, iot128, iotc, None, ALU.is_gt),
      TS(maskbig, iot128, iotc, None, ALU.is_ge),
      TS(etimes[0:64, :], iotc[0:64, :], float(TSTRIDE), None, ALU.mult),
      MS(constc[:, 0:1], float(EPS)), MS(constc[:, 1:2], 1.0),
      MS(constc[:, 2:3], -1.0), MS(constc[:, 3:4], 1.0 / DI))
    S("vector", TS(maskbig, maskbig, 1.0, 1e30, ALU.subtract, ALU.mult))
    S("scalar", ACT(smallA[0:1, 0:NH], smallA[0:1, 0:NH], AF.Exp))
    S("scalar", ACT(smallA[0:1, 0:NH], smallA[0:1, 0:NH], AF.Copy, scale=constc[0:1, 2:3]),
      ACT(c_fm, c_fm, AF.Silu))
    ztb = tblh.rearrange("(a b) o -> a (b o)", b=8224)
    q.dma("gpsimd", [(ztb[i * 128:(i + 1) * 128, j * 1028:(j + 1) * 1028],
                      zerot[:, 0:1028]) for i in range(1) for j in range(8)])
    q.dma("gpsimd", [(xf_all[TALL:TALL + 128, :], zerot[:, 0:1024])])

    bc_jobs = [(b2b, srow[0:1, 4 * D:5 * D]), (gbB, smallC[0:1, 0:E]),
               (a128, smallA[0:1, 0:NH]), (d128, smallA[0:1, NH:2 * NH])]
    for i, (dst, row) in enumerate(bc_jobs):
        w = row.shape[-1]
        for si in range((w + 511) // 512):
            lo, hi = si * 512, min(w, (si + 1) * 512)
            S("tensor", bcast(pb[0], row[:, lo:hi], 128))
            S("vector", CP(dst[:, lo:hi], pb[0][0:128, 0:hi - lo]))
    S("tensor", TR(pb[1][0:NH, 0:1], smallA[0:1, 0:NH], 1))
    S("vector", CP(acol[0:NH, :], pb[1][0:NH, 0:1]))
    S("vector", TS(nacol[0:NH, :], acol[0:NH, :], -1.0, None, ALU.mult))
    S("tensor", TR(pb[1][0:NH, 0:1], smallA[0:1, 2 * NH:3 * NH], 1))
    S("vector", CP(dtbc[0:NH, :], pb[1][0:NH, 0:1]))
    S("tensor", bcast(pb[1], flagc[0:1, 0:1], 128))
    S("vector", CP(flagc, pb[1][0:128, 0:1]))

    if stop < 1:
        _finish_trunc(q, S, out, zerot, T)
        q.emit(); st.close(); return nc
    NW = 6 * D // 512
    for nwi in range(NW):
        for kc in range(8):
            wb = wbufC if kc % 2 == 0 else wbufD
            q.dma("sync", [(wb[:, 0:512],
                            wmod[kc * 128:(kc + 1) * 128,
                                 nwi * 512:(nwi + 1) * 512])])
            S("tensor", MM(pb[2][0:1, 0:512], c_fm[:, kc:kc + 1],
                           wb[:, 0:512], kc == 0, kc == 7))
        S("vector", CP(srow[0:1, nwi * 512:(nwi + 1) * 512],
                       pb[2][0:1, 0:512]))
    for gi in (2, 5):
        sl = srow[0:1, gi * D:(gi + 1) * D]
        s0 = zerot[0:1, 0:D]
        s1 = zerot[0:1, D:2 * D]
        S("vector", TS(s0, sl, 0.0, 2.0, ALU.is_ge, ALU.mult),
          TS(s1, sl, -1.0, None, ALU.mult))
        S("vector", TS(s0, s0, 1.0, None, ALU.subtract),
          TT(s1, s1, sl, ALU.max))
        S("vector", TS(s1, s1, 0.25, None, ALU.max))
        S("vector", TT(sl, s0, s1, ALU.mult))
    S("vector", TS(srow[0:1, D:2 * D], srow[0:1, D:2 * D], 1.0, None, ALU.add),
      TS(srow[0:1, 4 * D:5 * D], srow[0:1, 4 * D:5 * D], 1.0, None, ALU.add))
    for i in range(6):
        row = srow[0:1, i * D:(i + 1) * D]
        for si in range(2):
            lo, hi = si * 512, (si + 1) * 512
            S("tensor", bcast(pb[0], row[:, lo:hi], 128))
            S("vector", CP(modb[i][:, lo:hi], pb[0][0:128, 0:512]))

    def ln_tiles(src_tiles, dst_tiles, mul_tile, add_tile, nt):
        for tt in range(nt):
            S("vector", lambda e, s=src_tiles[tt]: e.bn_stats(
                smallB[0:128, 0:6], s[:, 0:512]),
              lambda e, s=src_tiles[tt]: e.bn_stats(
                smallB[0:128, 6:12], s[:, 512:1024]))
            S("vector", lambda e: e.bn_aggr(smallB[0:128, 8:10] if False else
                                            smallB[0:128, 12:14],
                                            smallB[0:128, 0:12]))
            S("scalar", ACT(smallB[0:128, 10:11], smallB[0:128, 13:14],
                            AF.Sqrt, bias=constc[:, 0:1]))
            S("vector", lambda e: e.reciprocal(smallB[0:128, 10:11],
                                               smallB[0:128, 10:11]))
            S("vector",
              TS(dst_tiles[tt], src_tiles[tt], smallB[0:128, 12:13],
                 smallB[0:128, 10:11], ALU.subtract, ALU.mult))
            if mul_tile is not None:
                S("vector", TT(dst_tiles[tt], dst_tiles[tt], mul_tile,
                               ALU.mult))
                S("vector", TT(dst_tiles[tt], dst_tiles[tt], add_tile,
                               ALU.add))

    def to_fm(src_tiles, dst_fm_tiles, nf, nt, wcol=None, bcol=None):
        jobs = []
        for fc in range(nf):
            for tt in range(nt):
                jobs.append((fc, tt))
        for j0 in range(0, len(jobs), 4):
            grp = jobs[j0:j0 + 4]
            S("tensor", *[TR(pb[4 + i][0:128, 0:128],
                             src_tiles[tt][:, fc * 128:(fc + 1) * 128], 128)
                          for i, (fc, tt) in enumerate(grp)])
            if wcol is None:
                S("vector", *[CP(dst_fm_tiles[fc][:, tt * 128:(tt + 1) * 128],
                                 pb[4 + i][0:128, 0:128])
                              for i, (fc, tt) in enumerate(grp)])
            else:
                S("vector", *[TS(dst_fm_tiles[fc][:, tt * 128:(tt + 1) * 128],
                                 pb[4 + i][0:128, 0:128],
                                 wcol[:, fc:fc + 1], bcol[:, fc:fc + 1],
                                 ALU.mult, ALU.add)
                              for i, (fc, tt) in enumerate(grp)])

    def patternA(wdram, m0t, m1t, out_cb, Tn, kparts=8, mtot=None):
        if mtot is None:
            mtot = m1t * 128
        mcs = list(range(m0t, m1t))
        wid = {mc: min(128, mtot - mc * 128) for mc in mcs}
        wsrc = wdram.rearrange("(kc p) m -> p kc m", p=128)
        for bi in range(0, len(mcs), 2):
            blk = mcs[bi:bi + 2]
            wb = wbufA if (bi // 2) % 2 == 0 else wbufB
            mw = sum(wid[mc] for mc in blk)
            q.dma("sync", [(
                wb[:, 0:kparts * mw].rearrange("p (kc m) -> p kc m", kc=kparts),
                wsrc[:, 0:kparts, blk[0] * 128:blk[0] * 128 + mw])])
            for mi, mc in enumerate(blk):
                w = wid[mc]
                o0 = mi * mw // len(blk) if False else mi * 128
                bank = pb[mc % 4]
                q.step("tensor",
                       [MM(bank[0:w, 0:Tn],
                           wb[:, kc * mw + o0:kc * mw + o0 + w],
                           hfm[kc][:, 0:Tn], kc == 0, kc == kparts - 1)
                        for kc in range(kparts)],
                       lag=None if mc == mcs[0] else {"vector": 3,
                                                      "scalar": 3})
                out_cb(mc, bank[0:w, 0:Tn])

    def conv_chunk(Tn):
        xbc = [cvs] + [zx[15 + i] for i in range(1, NXBC)]
        S("vector", *[CP(smallC[0:128, i * 3:(i + 1) * 3],
                         zx[16 + i][:, Tn - 3:Tn]) for i in range(NXBC)])
        for i in range(NXBC):
            srct = zx[16 + i]
            dstt = xbc[i]
            ops1 = [TS(dstt[:, 0:Tn], srct[:, 0:Tn], cw[i][:, 3:4], None,
                       ALU.mult)]
            S("vector", *ops1)
            for k in range(3):
                S("vector", STT(dstt[:, 3 - k:Tn], srct[:, 0:Tn - 3 + k],
                                cw[i][:, k:k + 1], dstt[:, 3 - k:Tn],
                                ALU.mult, ALU.add))
                tops = []
                for t in range(0, 3 - k):
                    tops.append(STT(dstt[:, t:t + 1],
                                    tails[:, i * 4 + t + k:i * 4 + t + k + 1],
                                    cw[i][:, k:k + 1], dstt[:, t:t + 1],
                                    ALU.mult, ALU.add))
                if tops:
                    S("vector", *tops)
        S("vector", *[CP(tails[:, i * 4:i * 4 + 3],
                         smallC[0:128, i * 3:(i + 1) * 3])
                      for i in range(NXBC)])
        S("scalar", *[ACT(xbc[i][:, 0:Tn], xbc[i][:, 0:Tn], AF.Silu,
                          bias=cb[i][:, :]) for i in range(NXBC)])
        return xbc

    def dt_chunk(Tn):
        dts = smallC
        S("scalar", ACT(dtt[0:NH, 0:Tn], zx[33][0:NH, 0:Tn], AF.Abs,
                        bias=dtbc[0:NH, :]))
        S("scalar", ACT(dts[0:NH, 0:Tn], dtt[0:NH, 0:Tn], AF.Exp, scale=constc[0:NH, 2:3]))
        S("scalar", ACT(dts[0:NH, 0:Tn], dts[0:NH, 0:Tn], AF.Ln, bias=constc[0:NH, 1:2]))
        S("scalar", ACT(dtt[0:NH, 0:Tn], zx[33][0:NH, 0:Tn], AF.Relu,
                        bias=dtbc[0:NH, :]))
        S("vector", TT(dtt[0:NH, 0:Tn], dtt[0:NH, 0:Tn], dts[0:NH, 0:Tn],
                       ALU.add))

    def ssd_subchunk(xbc, o, partial):
        bc_tile = xbc[16]
        q.dma("gpsimd", [(ccopy[0:64, 0:SC], bc_tile[64:128, o:o + SC])])
        S("vector",
          lambda e: e.tensor_tensor_scan(sdt[0:NH, 0:SC], onescN[0:NH, 0:SC],
                                         dtt[0:NH, o:o + SC], 0.0,
                                         ALU.mult, ALU.add))
        S("vector", TS(asig[0:NH, 0:SC], sdt[0:NH, 0:SC], acol[0:NH, :],
                       None, ALU.mult),
          TS(wfm[0:NH, 0:SC], sdt[0:NH, 0:SC],
             sdt[0:NH, SC - 1:SC], nacol[0:NH, :], ALU.subtract, ALU.mult))
        S("scalar", ACT(wfm[0:NH, 0:SC], wfm[0:NH, 0:SC], AF.Exp))
        S("vector", TT(wfm[0:NH, 0:SC], wfm[0:NH, 0:SC],
                       dtt[0:NH, o:o + SC], ALU.mult),
          TS(dkc[0:NH, :], sdt[0:NH, SC - 1:SC], acol[0:NH, :], None,
             ALU.mult))
        S("scalar", ACT(dkc[0:NH, :], dkc[0:NH, :], AF.Exp))
        S("tensor", TR(pb[4][0:SC, 0:NH], sdt[0:NH, 0:SC], NH),
          TR(pb[5][0:SC, 0:NH], dtt[0:NH, o:o + SC], NH),
          TR(pb[6][0:SC, 0:NH], wfm[0:NH, 0:SC], NH),
          TR(pb[7][0:1, 0:NH], dkc[0:NH, :], NH))
        S("vector", CP(sdtT[:, 0:NH], pb[4][0:SC, 0:NH]),
          CP(sdtT[:, NH:2 * NH], pb[5][0:SC, 0:NH]),
          CP(sdtT[:, 2 * NH:3 * NH], pb[6][0:SC, 0:NH]),
          CP(dkrow[0:1, 0:NH], pb[7][0:1, 0:NH]))
        S("vector", TT(aST[:, 0:NH], sdtT[:, 0:NH], a128, ALU.mult))
        S("scalar", ACT(easT[:, 0:NH], aST[:, 0:NH], AF.Exp))
        S("tensor", bcast(pb[6], dkrow[0:1, 0:NH], 64))
        S("vector", CP(dkerb[0:64, 0:NH], pb[6][0:64, 0:NH]))
        S("tensor", TR(pb[7][0:128, 0:128], bc_tile[:, o:o + SC], 128))
        S("vector", CP(bct[:, 0:128], pb[7][0:128, 0:128]))
        for f0 in range(0, 16, 4):
            S("tensor", *[TR(pb[4 + i][0:128, 0:128],
                             xbc[f0 + i][:, o:o + SC], 128) for i in range(4)])
            S("vector", *[CP(xs_tok[:, (f0 + i) * 128:(f0 + i + 1) * 128],
                             pb[4 + i][0:128, 0:128]) for i in range(4)])
        if not partial:
            S("tensor", MM(pb[6][0:128, 0:128], bc_tile[0:64, o:o + SC],
                           ccopy[0:64, 0:SC], True, True))
            S("vector", CP(scoresT[:, 0:128], pb[6][0:128, 0:128]))

        for hg in range(NH // 4):
            hs = hg * 4
            if not partial:
                S("tensor", *[TR(pb[4 + j][0:1, 0:128],
                                 aST[:, hs + j:hs + j + 1], 128)
                              for j in range(4)])
                S("vector", *[CP(row0[0:1, j * 128:(j + 1) * 128],
                                 pb[4 + j][0:1, 0:128]) for j in range(4)])
                S("tensor", *[MM(pb[0][0:128, j * 128:(j + 1) * 128],
                                 ones_row[0:1, 0:128],
                                 row0[0:1, j * 128:(j + 1) * 128],
                                 j == 0, j == 3) for j in range(4)])
                S("vector", *[TS(lpre[j], pb[0][0:128, j * 128:(j + 1) * 128],
                                 aST[:, hs + j:hs + j + 1], None,
                                 ALU.subtract) for j in range(4)])
                S("vector", *[TT(lpre[j], lpre[j], maskbig, ALU.min)
                              for j in range(4)])
                S("scalar", *[ACT(lpre[j], lpre[j], AF.Exp) for j in range(4)])
                S("vector", *[STT(mtb[j], lpre[j],
                                  sdtT[:, NH + hs + j:NH + hs + j + 1],
                                  scoresT, ALU.mult, ALU.mult)
                              for j in range(4)])
                S("tensor",
                  *[MM(pb[2][0:128, j * 64:(j + 1) * 64], mtb[j],
                       xs_tok[:, (hs + j) * 64:(hs + j + 1) * 64],
                       j == 0, j == 3) for j in range(4)],
                  *[MM(pb[3][0:128, j * 64:(j + 1) * 64],
                       ccopy[0:64, 0:SC],
                       state[0:64, (hs + j) * 64:(hs + j + 1) * 64],
                       j == 0, j == 3) for j in range(4)])
                S("vector", *[CP(y_tok[:, (hs + j) * 64:(hs + j + 1) * 64],
                                 pb[2][0:128, j * 64:(j + 1) * 64])
                              for j in range(4)])
                S("vector", *[STT(y_tok[:, (hs + j) * 64:(hs + j + 1) * 64],
                                  pb[3][0:128, j * 64:(j + 1) * 64],
                                  easT[:, hs + j:hs + j + 1],
                                  y_tok[:, (hs + j) * 64:(hs + j + 1) * 64],
                                  ALU.mult, ALU.add) for j in range(4)])
                S("vector", *[STT(y_tok[:, (hs + j) * 64:(hs + j + 1) * 64],
                                  xs_tok[:, (hs + j) * 64:(hs + j + 1) * 64],
                                  d128[:, hs + j:hs + j + 1],
                                  y_tok[:, (hs + j) * 64:(hs + j + 1) * 64],
                                  ALU.mult, ALU.add) for j in range(4)])
            S("vector", *[TS(xs_tok[:, (hs + j) * 64:(hs + j + 1) * 64],
                             xs_tok[:, (hs + j) * 64:(hs + j + 1) * 64],
                             sdtT[:, 2 * NH + hs + j:2 * NH + hs + j + 1],
                             None, ALU.mult)
                          for j in range(4)])
            S("tensor", *[MM(pb[1][0:64, j * 64:(j + 1) * 64], bct[:, 0:64],
                             xs_tok[:, (hs + j) * 64:(hs + j + 1) * 64],
                             j == 0, j == 3) for j in range(4)])
            S("vector", *[STT(state[0:64, (hs + j) * 64:(hs + j + 1) * 64],
                              state[0:64, (hs + j) * 64:(hs + j + 1) * 64],
                              dkerb[0:64, hs + j:hs + j + 1],
                              pb[1][0:64, j * 64:(j + 1) * 64],
                              ALU.mult, ALU.add) for j in range(4)])

    def attn_chunk(c, partial, xsrc):
        nt = CH // 128
        q.dma("sync", [(xchunk[tt], xsrc[c * CH + tt * 128:
                                         c * CH + (tt + 1) * 128, :])
                       for tt in range(nt)])
        ln_tiles(xchunk, htok, modb[1], modb[0], nt)
        ln_tiles(htok, x1tok, None, None, nt)
        to_fm(x1tok, hfm, 8, nt, lncol[0], lncol[1])
        m0 = 16 if partial else 0
        patternA(w_ip, m0, NZX,
                 lambda mc, ps: S("vector", CP(
                     zx[mc][0:(32 if mc == NZX - 1 else 128), 0:CH], ps)),
                 CH, mtot=DIP)
        xbc = conv_chunk(CH)
        dt_chunk(CH)
        for sci in range(CH // SC):
            ssd_subchunk(xbc, sci * SC, partial)
            if not partial:
                post_ssd_subchunk(c, sci, xbc)
        if not partial:
            post_chunk(c, nt)

    def post_ssd_subchunk(c, sci, xbc):
        o = sci * SC
        S("scalar", *[ACT(zx[fc][:, o:o + SC], zx[fc][:, o:o + SC], AF.Silu)
                      for fc in range(16)])
        for f0 in range(0, 16, 4):
            S("tensor", *[TR(pb[4 + i][0:128, 0:128],
                             y_tok[:, (f0 + i) * 128:(f0 + i + 1) * 128], 128)
                          for i in range(4)])
            S("vector", *[TT(zx[f0 + i][:, o:o + SC],
                             pb[4 + i][0:128, 0:128],
                             zx[f0 + i][:, o:o + SC], ALU.mult)
                          for i in range(4)])
        sqops = []
        mmops = []
        for fc in range(16):
            sqops.append(TT(lpre[fc % 4], zx[fc][:, o:o + SC],
                            zx[fc][:, o:o + SC], ALU.mult))
            mmops.append(MM(pb[6][0:1, 0:SC], ones_col[0:128, :],
                            lpre[fc % 4], fc == 0, fc == 15))
        for f0 in range(0, 16, 4):
            S("vector", *sqops[f0:f0 + 4])
            S("tensor", *mmops[f0:f0 + 4])
        S("scalar", ACT(smallC[0:1, 0:SC], pb[6][0:1, 0:SC], AF.Sqrt,
                        bias=constc[0:1, 0:1], scale=constc[0:1, 3:4]))
        S("vector", lambda e: e.reciprocal(smallC[0:1, 0:SC],
                                           smallC[0:1, 0:SC]))
        S("tensor", bcast(pb[7], smallC[0:1, 0:SC], 128))
        S("vector", *[STT(zx[fc][:, o:o + SC], pb[7][0:128, 0:SC],
                          nrmc[:, fc:fc + 1], zx[fc][:, o:o + SC],
                          ALU.mult, ALU.mult) for fc in range(16)])

    def post_chunk(c, nt):
        for kc in range(16):
            wb = wbufC if kc % 2 == 0 else wbufD
            q.dma("sync", [(wb[:, 0:D], w_op[kc * 128:(kc + 1) * 128, :])])
            q.step("tensor", [MM(pb[tt * 2 + si][0:128, 0:512],
                                 zx[kc][:, tt * 128:(tt + 1) * 128],
                                 wb[:, si * 512:(si + 1) * 512],
                                 kc == 0, kc == 15)
                              for tt in range(nt) for si in range(2)],
                   lag=None if kc == 0 else {"vector": 99, "scalar": 99})
        S("vector", *[TT(x1tok[tt][:, si * 512:(si + 1) * 512],
                         pb[tt * 2 + si][0:128, 0:512],
                         htok[tt][:, si * 512:(si + 1) * 512], ALU.add)
                      for tt in range(nt) for si in range(2)])
        ln_tiles(x1tok, htok, None, None, nt)
        to_fm(htok, hfm, 8, nt, lncol[2], lncol[3])
        patternA(w_m1, 0, 32,
                 lambda mc, ps: S("scalar",
                                  ACT(zx[mc][:, 0:CH], ps, AF.Gelu,
                                      bias=b1c[:, mc:mc + 1])), CH)
        for kc in range(32):
            wb = wbufC if kc % 2 == 0 else wbufD
            q.dma("sync", [(wb[:, 0:D], w_m2[kc * 128:(kc + 1) * 128, :])])
            q.step("tensor", [MM(pb[tt * 2 + si][0:128, 0:512],
                                 zx[kc][:, tt * 128:(tt + 1) * 128],
                                 wb[:, si * 512:(si + 1) * 512],
                                 kc == 0, kc == 31)
                              for tt in range(nt) for si in range(2)],
                   lag=None if kc == 0 else {"vector": 99, "scalar": 99})
        S("vector", *[TT(htok[tt][:, si * 512:(si + 1) * 512],
                         pb[tt * 2 + si][0:128, 0:512],
                         b2b[:, si * 512:(si + 1) * 512], ALU.add)
                      for tt in range(nt) for si in range(2)])
        S("vector", *[TT(htok[tt], htok[tt], x1tok[tt], ALU.add)
                      for tt in range(nt)])
        S("vector", *[TT(htok[tt], htok[tt], modb[2], ALU.mult)
                      for tt in range(nt)])
        S("vector", *[TT(xchunk[tt], xchunk[tt], htok[tt], ALU.add)
                      for tt in range(nt)])
        q.dma("gpsimd", [(x_hbm[c * CH + tt * 128:c * CH + (tt + 1) * 128, :],
                          xchunk[tt]) for tt in range(nt)])
        ln_tiles(xchunk, htok, modb[4], modb[3], nt)
        q.dma("gpsimd", [(ag_xf_in[c * CH + tt * 128:c * CH + (tt + 1) * 128, :],
                          htok[tt]) for tt in range(nt)])
        to_fm(htok, hfm, 8, nt)
        gate_topk(c, nt)

    def gate_topk(c, nt):
        for tt in range(nt):
            wb = wbufC
            if tt == 0:
                q.dma("sync", [(wbufC[:, kc * 64:(kc + 1) * 64],
                                w_gt[kc * 128:(kc + 1) * 128, :])
                               for kc in range(8)])
            S("tensor", *[MM(pb[4][0:128, 0:E],
                             hfm[kc][:, tt * 128:(tt + 1) * 128],
                             wbufC[:, kc * 64:(kc + 1) * 64],
                             kc == 0, kc == 7) for kc in range(8)])
            S("vector", RED(smallB[0:128, 0:1], pb[4][0:128, 0:E], ALU.max))
            S("vector", TS(smallB[0:128, 1:2], smallB[0:128, 0:1], -1.0,
                           None, ALU.mult))
            S("scalar", lambda e: e.activation(lpre[0][:, 0:E], pb[4][0:128, 0:E],
                                               AF.Exp, bias=smallB[0:128, 1:2]))
            S("vector", RED(smallB[0:128, 2:3], lpre[0][:, 0:E], ALU.add))
            S("vector", lambda e: e.reciprocal(smallB[0:128, 3:4],
                                               smallB[0:128, 2:3]))
            S("vector", TS(scorekeep[:, 0:E], lpre[0][:, 0:E],
                           smallB[0:128, 3:4], None, ALU.mult))
            S("vector", TT(lpre[1][:, 0:E], scorekeep[:, 0:E], gbB, ALU.add))
            pt = payb[c * nt + tt]
            S("vector", MS(pt, 0.0))
            for k in range(TOPK):
                S("vector", RED(smallB[0:128, 4:5], lpre[1][:, 0:E], ALU.max))
                S("vector", TS(lpre[2][:, 0:E], lpre[1][:, 0:E],
                               smallB[0:128, 4:5], None, ALU.is_equal))
                S("vector", TT(lpre[3][:, 0:E], lpre[2][:, 0:E], iot128[:, 0:64],
                               ALU.mult),
                  TS(mtb[0][:, 0:E], lpre[2][:, 0:E], 1.0, -64.0,
                     ALU.subtract, ALU.mult))
                S("vector", TT(lpre[3][:, 0:E], lpre[3][:, 0:E],
                               mtb[0][:, 0:E], ALU.add))
                S("vector", RED(pt[:, k:k + 1], lpre[3][:, 0:E], ALU.min))
                S("vector", TS(lpre[2][:, 0:E], iot128[:, 0:64], pt[:, k:k + 1], None,
                               ALU.is_equal))
                S("vector", TT(lpre[3][:, 0:E], lpre[2][:, 0:E],
                               scorekeep[:, 0:E], ALU.mult))
                S("vector", RED(pt[:, 8 + k:9 + k], lpre[3][:, 0:E], ALU.add))
                S("vector", STT(lpre[1][:, 0:E], lpre[2][:, 0:E], -1e30,
                                lpre[1][:, 0:E], ALU.mult, ALU.add))
        q.dma("gpsimd", [(ag_pay_in[c * CH + tt * 128:c * CH + (tt + 1) * 128, :],
                          payb[c * nt + tt]) for tt in range(nt)])

    if stop < 2:
        _finish_trunc(q, S, out, zerot, T)
        q.emit(); st.close(); return nc
    for c in range(NCHP):
        attn_chunk(c, True, x_prev)
    S("vector", TS(state[0:64, :], state[0:64, :], flagc[0:64, :], None,
                   ALU.mult),
      TS(tails, tails, flagc, None, ALU.mult))
    if stop < 3:
        _finish_trunc(q, S, out, zerot, T)
        q.emit(); st.close(); return nc
    for c in range(NCH):
        attn_chunk(c, False, x_own)

    if stop < 4:
        _finish_trunc(q, S, out, zerot, T)
        q.emit(); st.close(); return nc
    q.collective("AllGather", ALU.bypass, [list(range(n_ranks))],
                 [ag_pay_in.opt()], [pay_all.opt()])
    cc_pay = q.ccval
    q.collective("AllGather", ALU.bypass, [list(range(n_ranks))],
                 [ag_xf_in.opt()], [xf_all[0:TALL, :].opt()])
    cc_xf = q.ccval

    pool.pop()
    pool.push()
    ptile = pool.tile(16)
    otile = pool.tile(E)
    cex = pool.tile(E)
    ptab = pool.tile(E)
    msk = [pool.tile(E) for _ in range(TOPK)]
    colk = pool.tile(16)
    swf = pool.tile(16)
    ctile = pool.tile(E)
    tmpe = [pool.tile(E) for _ in range(2)]
    ofs = pool.tile(1)
    ofs_i = pooli[:, 0:1]

    if stop < 5:
        _finish_trunc(q, S, out, zerot[0:128, 0:1024] if False else zerot, T)
        q.emit(); st.close(); return nc
    q.wait_cc("sync", cc_pay)
    q.wait_cc("gpsimd", cc_pay)
    S("vector", MS(swf, 0.0))
    for ti in range(NTILE_ALL):
        q.dma("sync", [(ptile, pay_all[ti * 128:(ti + 1) * 128, :])])
        S("vector", *[TS(msk[k], iot128[:, 0:64], ptile[:, k:k + 1], None, ALU.is_equal)
                      for k in range(TOPK)],
          TS(colk[:, 15:16], iotc, float(ti * 128), None, ALU.add))
        S("vector", TT(otile, msk[0], msk[1], ALU.add),
          TT(tmpe[0], msk[2], msk[3], ALU.add),
          TT(tmpe[1], msk[4], msk[5], ALU.add))
        S("vector", TT(otile, otile, tmpe[0], ALU.add))
        S("vector", TT(otile, otile, tmpe[1], ALU.add))
        S("vector", TT(ofs[0:64, :], basec[0:64, :], etimes[0:64, :], ALU.add))
        S("vector", CP(ofs_i[0:64, :], ofs[0:64, :]))
        S("tensor", MM(pb[0][0:128, 0:E], sltm, otile, True, False),
          MM(pb[0][0:128, 0:E], ones_row[0:1, 0:128].rearrange("a b -> a b"),
             baser[0:1, 0:E], False, True))
        S("vector", CP(ptab, pb[0][0:128, 0:E]))
        S("tensor", MM(pb[0][0:128, 0:E], sltm, otile, True, True))
        S("vector", CP(cex, pb[0][0:128, 0:E]))
        S("tensor", MM(pb[1][0:1, 0:E], ones_col[0:128, :], otile,
                       True, True),
          MM(pb[2][0:64, 0:1], otile[:, 0:64], ones_col[0:128, :],
             True, True))
        for k in range(TOPK):
            S("vector", TT(tmpe[0], msk[k], cex, ALU.mult),
              TT(tmpe[1], msk[k], ptab, ALU.mult))
            S("vector", RED(colk[:, k:k + 1], tmpe[0], ALU.add),
              RED(colk[:, 8 + k:9 + k], tmpe[1], ALU.add))
        for k in range(TOPK):
            S("vector", TS(tmpe[0], iot128[:, 0:64], colk[:, k:k + 1], None,
                           ALU.is_equal))
            S("vector", TS(tmpe[0], tmpe[0], colk[:, 15:16], None, ALU.mult))
            S("tensor", MM(pb[3][0:64, 0:E], msk[k], tmpe[0],
                           k == 0, k == TOPK - 1))
        S("vector", CP(ctile[0:64, :], pb[3][0:64, 0:E]))
        q.indirect([(tblh[:, :], IndirectOffsetOnAxis(ofs_i[0:64, :], 0),
                     ctile[0:64, :], None)])
        S("vector", TT(baser[0:1, 0:E], baser[0:1, 0:E], pb[1][0:1, 0:E],
                       ALU.add),
          TT(basec[0:64, :], basec[0:64, :], pb[2][0:64, 0:1], ALU.add))
        S("vector", *[TS(swf[:, 8 + k:9 + k], colk[:, 8 + k:9 + k], 511.0,
                         None, ALU.min) for k in range(TOPK)])
        S("vector", *[STT(swf[:, k:k + 1], ptile[:, k:k + 1], 512.0,
                          swf[:, 8 + k:9 + k], ALU.mult, ALU.add)
                      for k in range(TOPK)])
        S("vector", *[TS(tmpe[0][:, k:k + 1], colk[:, 8 + k:9 + k], 512.0,
                         None, ALU.is_lt) for k in range(TOPK)])
        S("vector", *[TT(swf[:, 8 + k:9 + k], ptile[:, 8 + k:9 + k],
                         tmpe[0][:, k:k + 1], ALU.mult) for k in range(TOPK)])
        q.dma("gpsimd", [(slotwf[ti * 128:(ti + 1) * 128, :], swf)])
    q.dma("gpsimd", [(cnts_h[:, :], baser[0:1, 0:E])])

    if stop < 6:
        _finish_trunc(q, S, out, zerot, T)
        q.emit(); st.close(); return nc
    pool.pop()
    pool.push()
    xbt = pool.tiles(4, D)
    xbfm = pool.tiles(8, 512)
    he = pool.tiles(11, 512)
    ybt = pool.tiles(4, D)
    ewA = pool.tile(8 * 256)
    ewB = ewA
    ewC = pool.tile(D)
    ewD = ewC
    idxv = pool.tile(8)
    cnt8 = pool.tile(E)
    vld = pool.tile(4)
    swft = pool.tile(16)
    rankc = pool.tile(1)
    ofs = pool.tile(1)
    tbl_own_sb = pool.tile(512)
    xmt = pool.tile(D)
    zt = pool.tile(D)
    gath = pool.tile(D)
    acc = pool.tile(D)
    moe_end = pool.off
    idx_i4 = pooli[:, 4:8]
    slot_i = pooli[:, 8:9]

    rank_s = pooli[:, 12:13]
    q.dma("gpsimd", [(rank_s[0:1, :], rank_i[:, :])])
    S("vector", CP(smallB[0:1, 14:15], rank_s[0:1, :]))
    S("tensor", bcast(pb[0], smallB[0:1, 14:15], 128))
    S("vector", CP(rankc, pb[0][0:128, 0:1]))
    S("vector", TS(ofs[0:epc, :], rankc[0:epc, :], float(epc), None,
                   ALU.mult))
    S("vector", TT(ofs[0:epc, :], ofs[0:epc, :], iotc[0:epc, :], ALU.add))
    S("vector", CP(pooli[0:epc, 13:14], ofs[0:epc, :]))
    cnts_flat = cnts_h.rearrange("o e -> (o e)")[:, None]
    q.indirect([(smallC[0:epc, 0:1], None, cnts_flat,
                 IndirectOffsetOnAxis(pooli[0:epc, 13:14], 0))])
    S("tensor", TR(pb[0][0:1, 0:epc], smallC[0:epc, 0:1], epc))
    S("vector", CP(smallC[0:1, 4:4 + epc], pb[0][0:1, 0:epc]))
    S("tensor", bcast(pb[0], smallC[0:1, 4:4 + epc], 128))
    S("vector", CP(cnt8[:, 0:epc], pb[0][0:128, 0:epc]))
    S("vector", TS(ofs[0:epc, :], rankc[0:epc, :], float(epc), None,
                   ALU.mult))
    S("vector", TT(ofs[0:epc, :], ofs[0:epc, :], iotc[0:epc, :], ALU.add))
    S("vector", TS(ofs[0:epc, :], ofs[0:epc, :], float(TSTRIDE), None,
                   ALU.mult))
    S("vector", CP(pooli[0:epc, 14:15], ofs[0:epc, :]))
    q.indirect([(tbl_own_sb[0:epc, 0:512], None, tblh[:, :],
                 IndirectOffsetOnAxis(pooli[0:epc, 14:15], 0))])
    q.dma("gpsimd", [(tbl_own[0:epc, :], tbl_own_sb[0:epc, 0:512])])
    for j in range(epc):
        q.dma("gpsimd", [(idxv[:, (j % 2) * 4:(j % 2) * 4 + 4],
                          tbl_own[j, 0:512].rearrange("(f q) -> q f", q=128))])
        jj = j % 2
        S("vector", *[TS(vld[:, col:col + 1],
                         iotc, float(col * 128), None, ALU.add)
                      for col in range(4)])
        S("vector", *[TS(vld[:, col:col + 1], vld[:, col:col + 1],
                         cnt8[:, j:j + 1], None, ALU.is_lt)
                      for col in range(4)])
        S("vector", TS(idxv[:, jj * 4:jj * 4 + 4],
                       idxv[:, jj * 4:jj * 4 + 4], float(TALL), None,
                       ALU.subtract))
        S("vector", TT(idxv[:, jj * 4:jj * 4 + 4],
                       idxv[:, jj * 4:jj * 4 + 4], vld, ALU.mult))
        S("vector", TS(idxv[:, jj * 4:jj * 4 + 4],
                       idxv[:, jj * 4:jj * 4 + 4], float(TALL), None,
                       ALU.add))
        S("vector", CP(idx_i4, idxv[:, jj * 4:jj * 4 + 4]))
        if j == 0:
            q.wait_cc("gpsimd", cc_xf)
        q.indirect([(xbt[col], None, xf_all[:, :],
                     IndirectOffsetOnAxis(idx_i4[:, col:col + 1], 0))
                    for col in range(4)])
        for f0 in range(0, 8, 2):
            S("tensor", *[TR(pb[4 * i + tt][0:128, 0:128],
                             xbt[tt][:, (f0 + i) * 128:(f0 + i + 1) * 128], 128)
                          for i in range(2) for tt in range(4)])
            S("vector", *[CP(xbfm[f0 + i][:, tt * 128:(tt + 1) * 128],
                             pb[4 * i + tt][0:128, 0:128])
                          for i in range(2) for tt in range(4)])
        mws = [128, 128, 128, 128, 128, 64]
        def eA(wdram, j, cb_):
            wsrc = wdram.rearrange("e (kc p) m -> e p kc m", p=128)
            for bi in range(0, 6, 2):
                wb = ewA if (bi // 2) % 2 == 0 else ewB
                mw = mws[bi] + (mws[bi + 1] if bi + 1 < 6 else 0)
                q.dma("sync", [(
                    wb[:, 0:8 * mw].rearrange("p (kc m) -> p kc m", kc=8),
                    wsrc[j, :, 0:8, bi * 128:bi * 128 + mw])])
                for mi in range(2):
                    if bi + mi >= 6:
                        continue
                    mwid = mws[bi + mi]
                    bank = pb[(bi + mi) % 4]
                    q.step("tensor",
                           [MM(bank[0:mwid, 0:512],
                               wb[:, kc * mw + mi * 128:
                                  kc * mw + mi * 128 + mwid],
                               xbfm[kc][:, 0:512], kc == 0, kc == 7)
                            for kc in range(8)],
                           lag=None if bi + mi == 0 else {"vector": 3,
                                                          "scalar": 3})
                    cb_(bi + mi, bank[0:mwid, 0:512], mwid)
        eA(w_e1, j, lambda ic, ps, mwid:
           S("scalar", ACT(he[ic][0:mwid, :], ps, AF.Silu)))
        eA(w_e3, j, lambda ic, ps, mwid:
           S("vector", TT(he[ic][0:mwid, :], he[ic][0:mwid, :], ps,
                          ALU.mult)))
        for kc2 in range(6):
            mwid = mws[kc2]
            wb = ewC if kc2 % 2 == 0 else ewD
            q.dma("sync", [(wb[0:mwid, 0:D],
                            w_e2[j, kc2 * 128:kc2 * 128 + mwid, :])])
            q.step("tensor", [MM(pb[sl * 2 + si][0:128, 0:512],
                                 he[kc2][0:mwid, sl * 128:(sl + 1) * 128],
                                 wb[0:mwid, si * 512:(si + 1) * 512],
                                 kc2 == 0, kc2 == 5)
                              for sl in range(4) for si in range(2)],
                   lag=None if kc2 == 0 else {"vector": 99, "scalar": 99})
        S("vector", *[CP(ybt[sl][:, si * 512:(si + 1) * 512],
                         pb[sl * 2 + si][0:128, 0:512])
                      for sl in range(4) for si in range(2)])
        q.dma("gpsimd", [(ag_yb_in[j * CAP + sl * 128:j * CAP + (sl + 1) * 128,
                                   :], ybt[sl]) for sl in range(4)])

    q.collective("AllGather", ALU.bypass, [list(range(n_ranks))],
                 [ag_yb_in.opt()], [yb_all.opt()])
    cc_yb = q.ccval

    if stop < 7:
        _finish_trunc(q, S, out, zerot, T)
        q.emit(); st.close(); return nc
    for c in range(NCH):
        nt = CH // 128
        q.dma("sync", [(xbt[tt], ag_xf_in[c * CH + tt * 128:
                                          c * CH + (tt + 1) * 128, :])
                       for tt in range(nt)])
        for f0 in range(0, 8, 2):
            S("tensor", *[TR(pb[4 * i + tt][0:128, 0:128],
                             xbt[tt][:, (f0 + i) * 128:(f0 + i + 1) * 128], 128)
                          for i in range(2) for tt in range(4)])
            S("vector", *[CP(xbfm[f0 + i][:, tt * 128:(tt + 1) * 128],
                             pb[4 * i + tt][0:128, 0:128])
                          for i in range(2) for tt in range(4)])
        def sA(wdram, cb_):
            wsrc = wdram.rearrange("(kc p) m -> p kc m", p=128)
            for bi in range(0, 11, 2):
                nb = min(2, 11 - bi)
                wb = ewA if (bi // 2) % 2 == 0 else ewB
                mw = nb * 128
                q.dma("sync", [(
                    wb[:, 0:8 * mw].rearrange("p (kc m) -> p kc m", kc=8),
                    wsrc[:, 0:8, bi * 128:bi * 128 + mw])])
                for mi in range(nb):
                    bank = pb[(bi + mi) % 4]
                    q.step("tensor",
                           [MM(bank[0:128, 0:512],
                               wb[:, kc * mw + mi * 128:
                                  kc * mw + (mi + 1) * 128],
                               xbfm[kc][:, 0:512], kc == 0, kc == 7)
                            for kc in range(8)],
                           lag=None if bi + mi == 0 else {"vector": 3,
                                                          "scalar": 3})
                    cb_(bi + mi, bank[0:128, 0:512])
        sA(w_s1, lambda ic, ps: S("scalar", ACT(he[ic], ps, AF.Silu)))
        sA(w_s3, lambda ic, ps: S("vector", TT(he[ic], he[ic], ps, ALU.mult)))
        for kc2 in range(11):
            wb = ewC if kc2 % 2 == 0 else ewD
            q.dma("sync", [(wb[:, 0:D], w_s2[kc2 * 128:(kc2 + 1) * 128, :])])
            q.step("tensor", [MM(pb[sl * 2 + si][0:128, 0:512],
                                 he[kc2][:, sl * 128:(sl + 1) * 128],
                                 wb[:, si * 512:(si + 1) * 512],
                                 kc2 == 0, kc2 == 10)
                              for sl in range(4) for si in range(2)],
                   lag=None if kc2 == 0 else {"vector": 99, "scalar": 99})
        S("vector", *[CP(ybt[sl][:, si * 512:(si + 1) * 512],
                         pb[sl * 2 + si][0:128, 0:512])
                      for sl in range(4) for si in range(2)])
        q.dma("gpsimd", [(z_hbm[c * CH + sl * 128:c * CH + (sl + 1) * 128, :],
                          ybt[sl]) for sl in range(4)])

    if stop < 8:
        _finish_trunc(q, S, out, zerot, T)
        q.emit(); st.close(); return nc
    q.wait_cc("gpsimd", cc_yb)
    for tt in range(T // 128):
        q.dma("sync", [(xmt, x_hbm[tt * 128:(tt + 1) * 128, :]),
                       (zt, z_hbm[tt * 128:(tt + 1) * 128, :])])
        S("vector", TS(ofs, rankc, float(T), None, ALU.mult))
        S("vector", TT(ofs, ofs, iotc, ALU.add))
        S("vector", TS(ofs, ofs, 1.0, float(tt * 128), ALU.mult, ALU.add))
        S("vector", CP(pooli[:, 15:16], ofs))
        q.indirect([(swft, None, slotwf[:, :],
                     IndirectOffsetOnAxis(pooli[:, 15:16], 0))])
        for k in range(TOPK):
            S("vector", CP(slot_i, swft[:, k:k + 1]))
            q.indirect([(gath, None, yb_all[:, :],
                         IndirectOffsetOnAxis(slot_i, 0))])
            if k == 0:
                S("vector", TS(acc, gath, swft[:, 8:9], None, ALU.mult))
            else:
                S("vector", STT(acc, gath, swft[:, 8 + k:9 + k], acc,
                                ALU.mult, ALU.add))
        S("vector", TT(acc, acc, zt, ALU.add))
        S("vector", TT(acc, acc, modb[5], ALU.mult))
        S("vector", TT(acc, acc, xmt, ALU.add))
        q.dma("gpsimd", [(out[tt * 128:(tt + 1) * 128, :], acc)])
    S("gpsimd", lambda e: e.engine_nop())

    q.emit()
    st.close()
    return nc


def prep_inmaps(inputs, T, n_ranks):
    def t(a):
        return np.ascontiguousarray(np.transpose(np.asarray(a, np.float32)))

    x = np.ascontiguousarray(inputs["L0_attn_x"], dtype=np.float32)
    epc = E // n_ranks
    shared = {
        "wmod": np.ascontiguousarray(inputs["L0_wmod"], np.float32),
        "ln1w": inputs["L0_attn_ln1_w"].reshape(1, D).astype(np.float32),
        "ln1b": inputs["L0_attn_ln1_b"].reshape(1, D).astype(np.float32),
        "ln2w": inputs["L0_attn_ln2_w"].reshape(1, D).astype(np.float32),
        "ln2b": inputs["L0_attn_ln2_b"].reshape(1, D).astype(np.float32),
        "normw": inputs["L0_attn_norm_w"].reshape(1, DI).astype(np.float32),
        "mlpb1": inputs["L0_attn_mlp_b1"].reshape(1, MLP_H).astype(np.float32),
        "mlpb2": inputs["L0_attn_mlp_b2"].reshape(1, D).astype(np.float32),
        "gateb": inputs["L0_ffn_gate_b"].reshape(1, E).astype(np.float32),
        "a_log": inputs["L0_attn_A_log"].reshape(1, NH).astype(np.float32),
        "d_vec": inputs["L0_attn_D"].reshape(1, NH).astype(np.float32),
        "dtbias": inputs["L0_attn_dt_bias"].reshape(1, NH).astype(np.float32),
        "convw": np.ascontiguousarray(inputs["L0_attn_conv_w"], np.float32),
        "convb": inputs["L0_attn_conv_b"].reshape(CONVD, 1).astype(np.float32),
        "w_ip": t(inputs["L0_attn_in_proj_w"]),
        "w_op": t(inputs["L0_attn_out_proj_w"]),
        "w_m1": t(inputs["L0_attn_mlp_w1"]),
        "w_m2": t(inputs["L0_attn_mlp_w2"]),
        "w_gt": t(inputs["L0_ffn_gate_w"]),
        "w_s1": t(inputs["L0_ffn_ws1"]),
        "w_s3": t(inputs["L0_ffn_ws3"]),
        "w_s2": t(inputs["L0_ffn_ws2"]),
    }
    we1 = np.ascontiguousarray(np.transpose(
        np.asarray(inputs["L0_ffn_w1"], dtype=np.float32), (0, 2, 1)))
    we3 = np.ascontiguousarray(np.transpose(
        np.asarray(inputs["L0_ffn_w3"], dtype=np.float32), (0, 2, 1)))
    we2 = np.ascontiguousarray(np.transpose(
        np.asarray(inputs["L0_ffn_w2"], dtype=np.float32), (0, 2, 1)))
    c_all = np.asarray(inputs["L0_c"], dtype=np.float32)

    in_maps = []
    zeros_prev = np.zeros((T, D), np.float32)
    for r in range(n_ranks):
        b, h = r // 2, r % 2
        m = dict(shared)
        m["c_row"] = np.ascontiguousarray(c_all[b:b + 1])
        m["x_own"] = np.ascontiguousarray(x[b, h * T:(h + 1) * T])
        m["x_prev"] = (np.ascontiguousarray(x[b, 0:T]) if h == 1
                       else zeros_prev)
        m["flag"] = np.array([[1.0 if h == 1 else 0.0]], np.float32)
        m["rank_i"] = np.array([[r]], np.int32)
        m["w_e1"] = np.ascontiguousarray(we1[r * epc:(r + 1) * epc])
        m["w_e3"] = np.ascontiguousarray(we3[r * epc:(r + 1) * epc])
        m["w_e2"] = np.ascontiguousarray(we2[r * epc:(r + 1) * epc])
        in_maps.append(m)
    return in_maps


def kernel(**inputs):
    T = 2048
    key = (T,)
    if key not in _CACHE:
        _CACHE[key] = build_program(T=T, TP=T, n_ranks=NCORES)
    nc = _CACHE[key]
    in_maps = prep_inmaps(inputs, T, NCORES)
    res = run_bass_kernel_spmd(nc, in_maps, core_ids=list(range(NCORES)))
    outs = [np.asarray(res.results[r]["out"]) for r in range(NCORES)]
    x = inputs["L0_attn_x"]
    B, L, _ = x.shape
    full = np.empty((B, L, D), np.float32)
    for r in range(NCORES):
        b, h = r // 2, r % 2
        full[b, h * T:(h + 1) * T] = outs[r]
    return full


# version 10
# speedup vs baseline: 1.1859x; 1.1475x over previous
import numpy as np
from contextlib import ExitStack

import concourse.bass as bass
import concourse.mybir as mybir
from concourse.bass import IndirectOffsetOnAxis
from concourse.bass_utils import run_bass_kernel_spmd

F32 = mybir.dt.float32
I32 = mybir.dt.int32
AF = mybir.ActivationFunctionType
ALU = mybir.AluOpType

D = 1024
DI = 2048
DSTATE = 64
NH = 32
HD = 64
DCONV = 4
CONVD = DI + 2 * DSTATE
DIP = 2 * DI + 2 * DSTATE + NH
NZX = (DIP + 127) // 128
NXBC = CONVD // 128
EPS = 1e-5
E = 64
CAP = 512
TOPK = 6
MOE_I = 704
SH_I = 1408
MLP_H = 4096
NCORES = 8
EPC = E // NCORES
TSTRIDE = 16448
SC = 128
CH = 512
BIGNEG = -1e30

_CACHE = {}


class SQ:
    ENGS = ("sync", "scalar", "vector", "tensor", "gpsimd")

    def __init__(self, nc, stack):
        self.nc = nc
        self.stack = stack
        self.ops = {e: [] for e in self.ENGS}
        self.csem = {e: self._mk(f"c_{e}") for e in self.ENGS}
        self.dsem = {e: self._mk(f"d_{e}") for e in self.ENGS}
        self.cval = {e: 0 for e in self.ENGS}
        self.dval = {e: 0 for e in self.ENGS}
        self.seen_c = {e: {f: 0 for f in self.ENGS} for e in self.ENGS}
        self.seen_d = {e: {f: 0 for f in self.ENGS} for e in self.ENGS}
        self.ccsem = self._mk("cc")
        self.ccval = 0
        self.seen_cc = {e: 0 for e in self.ENGS}

    def _mk(self, name):
        return self.stack.enter_context(self.nc.semaphore(name))

    def _barrier(self, eng, lag=None):
        for f in self.ENGS:
            if f == eng:
                continue
            v = self.cval[f]
            if lag and f in lag:
                v = max(0, v - lag[f])
            if v > self.seen_c[eng][f]:
                self.ops[eng].append(
                    lambda e, h=self.csem[f], v=v: e.wait_ge(h, v))
                self.seen_c[eng][f] = v
            dv = self.dval[f]
            if dv > self.seen_d[eng][f]:
                self.ops[eng].append(
                    lambda e, h=self.dsem[f], v=dv: e.wait_ge(h, v))
                self.seen_d[eng][f] = dv
        dv = self.dval[eng]
        if dv > self.seen_d[eng][eng]:
            self.ops[eng].append(
                lambda e, h=self.dsem[eng], v=dv: e.wait_ge(h, v))
            self.seen_d[eng][eng] = dv

    def step(self, eng, fns, lag=None):
        if not isinstance(fns, (list, tuple)):
            fns = [fns]
        self._barrier(eng, lag)
        if eng in ("scalar", "vector", "gpsimd"):
            self.ops[eng].append(lambda e: e.drain())
        for fn in fns[:-1]:
            self.ops[eng].append(lambda e, fn=fn: fn(e))
        self.cval[eng] += 1
        self.ops[eng].append(
            lambda e, fn=fns[-1], h=self.csem[eng]:
            fn(e).then_inc(h, 1))

    def dma(self, eng, pairs, lag=None):
        self._barrier(eng, lag)
        for dst, src in pairs:
            self.dval[eng] += 16
            self.ops[eng].append(
                lambda e, d=dst, s=src, h=self.dsem[eng]:
                e.dma_start(out=d, in_=s, allow_slow_non_contiguous=True).then_inc(h, 16))

    def indirect(self, pairs):
        self._barrier("gpsimd")
        for o, oo, i, io in pairs:
            self.dval["gpsimd"] += 16
            self.ops["gpsimd"].append(
                lambda e, o=o, oo=oo, i=i, io=io, h=self.dsem["gpsimd"]:
                e.indirect_dma_start(out=o, out_offset=oo, in_=i,
                                     in_offset=io).then_inc(h, 16))

    def collective(self, kind, op, groups, ins, outs):
        self._barrier("gpsimd")
        self.ccval += 1
        self.ops["gpsimd"].append(
            lambda e, k=kind, o=op, g=groups, i=ins, u=outs, h=self.ccsem:
            e.collective_compute(k, o, replica_groups=g, ins=i,
                                 outs=u).then_inc(h, 1))

    def wait_cc(self, eng, upto=None):
        v = self.ccval if upto is None else upto
        if v > self.seen_cc[eng]:
            self.ops[eng].append(lambda e, h=self.ccsem, v=v: e.wait_ge(h, v))
            self.seen_cc[eng] = v

    def raw(self, eng, fn):
        self.ops[eng].append(fn)

    def emit(self):
        ops = self.ops
        with self.nc.Block() as block:
            @block.sync
            def _(e):
                for f in ops["sync"]:
                    f(e)

            @block.scalar
            def _(e):
                for f in ops["scalar"]:
                    f(e)

            @block.vector
            def _(e):
                for f in ops["vector"]:
                    f(e)

            @block.tensor
            def _(e):
                for f in ops["tensor"]:
                    f(e)

            @block.gpsimd
            def _(e):
                for f in ops["gpsimd"]:
                    f(e)


class Pool:
    def __init__(self, t, n):
        self.t = t
        self.n = n
        self.off = 0
        self.hi = 0
        self.marks = []

    def tile(self, w):
        assert self.off + w <= self.n, f"pool overflow {self.off}+{w}"
        ap = self.t[:, self.off:self.off + w]
        self.off += w
        self.hi = max(self.hi, self.off)
        return ap

    def tiles(self, k, w):
        return [self.tile(w) for _ in range(k)]

    def push(self):
        self.marks.append(self.off)

    def pop(self):
        self.off = self.marks.pop()


def _finish_trunc(q, S, out, zerot, T):
    q.dma("gpsimd", [(out[tt * 128:(tt + 1) * 128, :], zerot[:, 0:1024])
                     for tt in range(T // 128)])
    S("gpsimd", lambda e: e.engine_nop())


def build_program(T=2048, TP=2048, n_ranks=NCORES, stop=99):
    nc = bass.Bass("TRN2", target_bir_lowering=False)
    st = ExitStack()
    q = SQ(nc, st)

    NCH = T // CH
    NCHP = TP // CH
    TALL = T * n_ranks
    NTILE_ALL = TALL // 128
    epc = E // n_ranks

    def din(name, shape, dt=F32):
        return nc.dram_tensor(name, shape, dt, kind="ExternalInput").ap()

    x_own = din("x_own", [T, D])
    x_prev = din("x_prev", [TP, D])
    flag = din("flag", [1, 1])
    c_row = din("c_row", [1, D])
    wmod = din("wmod", [D, 6 * D])
    ln1w = din("ln1w", [1, D]); ln1b = din("ln1b", [1, D])
    ln2w = din("ln2w", [1, D]); ln2b = din("ln2b", [1, D])
    normw = din("normw", [1, DI])
    mlpb1 = din("mlpb1", [1, MLP_H]); mlpb2 = din("mlpb2", [1, D])
    gateb = din("gateb", [1, E])
    a_log = din("a_log", [1, NH]); d_vec = din("d_vec", [1, NH])
    dtbias = din("dtbias", [1, NH])
    convw = din("convw", [CONVD, DCONV]); convb = din("convb", [CONVD, 1])
    w_ip = din("w_ip", [D, DIP])
    w_op = din("w_op", [DI, D])
    w_m1 = din("w_m1", [D, MLP_H])
    w_m2 = din("w_m2", [MLP_H, D])
    w_gt = din("w_gt", [D, E])
    w_s1 = din("w_s1", [D, SH_I]); w_s3 = din("w_s3", [D, SH_I])
    w_s2 = din("w_s2", [SH_I, D])
    w_e1 = din("w_e1", [epc, D, MOE_I])
    w_e3 = din("w_e3", [epc, D, MOE_I])
    w_e2 = din("w_e2", [epc, MOE_I, D])
    rank_i = din("rank_i", [1, 1], I32)

    out = nc.dram_tensor("out", [T, D], F32, kind="ExternalOutput").ap()

    def dint(name, shape, dt=F32):
        return nc.dram_tensor(name, shape, dt).ap()

    x_hbm = dint("x_hbm", [T, D])
    ag_xf_in = dint("ag_xf_in", [T, D])
    xf_all = dint("xf_all", [TALL + 128, D])
    ag_pay_in = dint("ag_pay_in", [T, 16])
    pay_all = dint("pay_all", [TALL, 16])
    tblh = dint("tblh", [E * TSTRIDE, 1])
    slotwf = dint("slotwf", [TALL, 16])
    cnts_h = dint("cnts_h", [1, E])
    tbl_own = dint("tbl_own", [E, 512])
    ag_yb_in = dint("ag_yb_in", [epc * CAP, D])
    yb_all = dint("yb_all", [E * CAP, D])
    z_hbm = dint("z_hbm", [T, D])

    POOLW = 53144
    poolt = st.enter_context(nc.sbuf_tensor("pool", [128, POOLW], F32))
    pool = Pool(poolt, POOLW)
    pooli = st.enter_context(nc.sbuf_tensor("pooli", [128, 64], I32))

    ones_row = pool.tile(128)
    ones_col = pool.tile(1)
    iot128 = pool.tile(128)
    iotc = pool.tile(1)
    ident = pool.tile(128)
    sltm = pool.tile(128)
    maskbig = pool.tile(128)
    modb = pool.tiles(6, D)
    lncol = pool.tiles(4, 8)
    b2b = pool.tile(D)
    gbB = pool.tile(E)
    nrmc = pool.tile(16)
    b1c = pool.tile(32)
    a128 = pool.tile(NH)
    d128 = pool.tile(NH)
    acol = pool.tile(1)
    nacol = pool.tile(1)
    dtbc = pool.tile(1)
    flagc = pool.tile(1)
    cw = pool.tiles(NXBC, DCONV)
    cb = pool.tiles(NXBC, 1)
    c_fm = pool.tile(8)
    state = pool.tile(NH * HD)
    tails = pool.tile(NXBC * 4)
    smallA = pool.tile(96)
    smallB = pool.tile(16)
    smallC = pool.tile(512)
    etimes = pool.tile(1)
    constc = pool.tile(4)
    baser = pool.tile(E)
    basec = pool.tile(1)

    persist_end = pool.off

    pool.push()
    srow = pool.tile(6 * D)
    zerot = pool.tile(2056)
    pool.pop()

    pool.push()
    xchunk = pool.tiles(4, D)
    htok = pool.tiles(4, D)
    x1tok = pool.tiles(4, D)
    hfm_big = pool.tile(8 * CH)
    hfm = [hfm_big[:, i * CH:(i + 1) * CH] for i in range(8)]
    zx = pool.tiles(NZX, CH)
    cvs = pool.tile(CH)
    ccopy = pool.tile(SC)
    wbufA = pool.tile(8 * 128)
    wbufB = pool.tile(8 * 128)
    wbufC = pool.tile(D)
    wbufD = pool.tile(D)
    xs_tok = hfm_big[:, 0:DI]
    y_tok = hfm_big[:, DI:2 * DI]
    bct = pool.tile(128)
    sdt = pool.tile(SC)
    sdtT = pool.tile(96)
    row0 = pool.tile(512)
    wfm = pool.tile(SC)
    dkc = pool.tile(1)
    dkrow = pool.tile(NH)
    asig = pool.tile(SC)
    aST = pool.tile(NH)
    easT = pool.tile(NH)
    dkerb = pool.tile(NH)
    scoresT = pool.tile(128)
    lpre = pool.tiles(4, 128)
    mtb = pool.tiles(4, 128)
    dtt = pool.tile(CH)
    onescN = pool.tile(SC)
    payb = pool.tiles(T // 128, 16)
    scorekeep = pool.tile(E)
    attn_end = pool.off
    assert attn_end < POOLW, attn_end

    print('pool hi', pool.hi, 'off', pool.off)
    pb = [st.enter_context(nc.psum_tensor(f"pb{i}", [128, 512], F32))
          for i in range(8)]

    def S(eng, *fns):
        q.step(eng, list(fns))

    def SL(eng, lag, *fns):
        q.step(eng, list(fns), lag=lag)

    def MM(psum, lhsT, rhs, start, stop):
        return lambda e, o=psum, l=lhsT, r=rhs, a=start, b=stop: \
            e.matmul(o, l, r, start=a, stop=b)

    def TR(psum, src, npart):
        return lambda e, o=psum, i=src, n=npart: \
            e.transpose(o, i, ident[0:n, 0:n])

    def CP(dst, src):
        return lambda e, d=dst, s=src: e.tensor_copy(d, s)

    def TT(dst, a, b, op):
        return lambda e, d=dst, a=a, b=b, o=op: e.tensor_tensor(d, a, b, o)

    def TS(dst, a, s1, s2, op0, op1=ALU.bypass):
        return lambda e, d=dst, a=a, s1=s1, s2=s2, o0=op0, o1=op1: \
            e.tensor_scalar(d, a, s1, s2, o0, o1)

    def STT(dst, in0, sc, in1, op0, op1):
        return lambda e, d=dst, i0=in0, s=sc, i1=in1, o0=op0, o1=op1: \
            e.scalar_tensor_tensor(d, i0, s, i1, o0, o1)

    def ACT(dst, src, f, bias=0.0, scale=1.0):
        return lambda e, d=dst, s=src, f=f, b=bias, sc=scale: \
            e.activation(d, s, f, bias=b, scale=sc)

    def RED(dst, src, op):
        return lambda e, d=dst, s=src, o=op: \
            e.tensor_reduce(d, s, mybir.AxisListType.X, o)

    def MS(dst, v):
        return lambda e, d=dst, v=v: e.memset(d, v)

    def bcast(psum, row, nparts):
        w = row.shape[-1]
        return MM(psum[0:nparts, 0:w], ones_row[0:1, 0:nparts], row,
                  True, True)

    q.dma("sync", [
        (c_fm, c_row.rearrange("o (q f) -> (o f) q", f=128)),
        (smallA[0:1, 0:NH], a_log[:, :]),
        (smallA[0:1, NH:2 * NH], d_vec[:, :]),
        (smallA[0:1, 2 * NH:3 * NH], dtbias[:, :]),
        (flagc[0:1, :], flag[:, :]),
        (nrmc, normw.rearrange("o (q f) -> (o f) q", f=128)),
        (b1c, mlpb1.rearrange("o (q f) -> (o f) q", f=128)),
        (smallC[0:1, 0:E], gateb[:, :]),
        (lncol[0], ln1w.rearrange("o (q f) -> (o f) q", f=128)),
        (lncol[1], ln1b.rearrange("o (q f) -> (o f) q", f=128)),
        (lncol[2], ln2w.rearrange("o (q f) -> (o f) q", f=128)),
        (lncol[3], ln2b.rearrange("o (q f) -> (o f) q", f=128)),
        (srow[0:1, 4 * D:5 * D], mlpb2[:, :]),
    ] + [(cw[i], convw[i * 128:(i + 1) * 128, :]) for i in range(NXBC)]
      + [(cb[i], convb[i * 128:(i + 1) * 128, :]) for i in range(NXBC)])

    S("vector", MS(ident, 1.0))
    S("vector", lambda e: e.tensor_tensor_scan(iot128, ident, ident, 0.0,
                                               ALU.mult, ALU.add))
    S("vector", TS(iot128, iot128, 1.0, None, ALU.subtract))
    S("tensor", TR(pb[0][0:128, 0:1], iot128[0:1, 0:128], 1))
    S("vector", CP(iotc, pb[0][0:128, 0:1]))
    S("vector",
      MS(ones_row[0:1, :], 1.0), MS(ones_col, 1.0), MS(onescN[0:NH, :], 1.0),
      MS(zerot, 0.0), MS(state[0:64, :], 0.0), MS(tails, 0.0),
      MS(baser[0:1, :], 0.0), MS(basec[0:64, :], 0.0),
      TS(ident, iot128, iotc, None, ALU.is_equal),
      TS(sltm, iot128, iotc, None, ALU.is_gt),
      TS(maskbig, iot128, iotc, None, ALU.is_ge),
      TS(etimes[0:64, :], iotc[0:64, :], float(TSTRIDE), None, ALU.mult),
      MS(constc[:, 0:1], float(EPS)), MS(constc[:, 1:2], 1.0),
      MS(constc[:, 2:3], -1.0), MS(constc[:, 3:4], 1.0 / DI))
    S("vector", TS(maskbig, maskbig, 1.0, 1e30, ALU.subtract, ALU.mult))
    S("scalar", ACT(smallA[0:1, 0:NH], smallA[0:1, 0:NH], AF.Exp))
    S("scalar", ACT(smallA[0:1, 0:NH], smallA[0:1, 0:NH], AF.Copy, scale=constc[0:1, 2:3]),
      ACT(c_fm, c_fm, AF.Silu))
    ztb = tblh.rearrange("(a b) o -> a (b o)", b=8224)
    q.dma("gpsimd", [(ztb[i * 128:(i + 1) * 128, j * 1028:(j + 1) * 1028],
                      zerot[:, 0:1028]) for i in range(1) for j in range(8)])
    q.dma("gpsimd", [(xf_all[TALL:TALL + 128, :], zerot[:, 0:1024])])

    bc_jobs = [(b2b, srow[0:1, 4 * D:5 * D]), (gbB, smallC[0:1, 0:E]),
               (a128, smallA[0:1, 0:NH]), (d128, smallA[0:1, NH:2 * NH])]
    for i, (dst, row) in enumerate(bc_jobs):
        w = row.shape[-1]
        for si in range((w + 511) // 512):
            lo, hi = si * 512, min(w, (si + 1) * 512)
            S("tensor", bcast(pb[0], row[:, lo:hi], 128))
            S("vector", CP(dst[:, lo:hi], pb[0][0:128, 0:hi - lo]))
    S("tensor", TR(pb[1][0:NH, 0:1], smallA[0:1, 0:NH], 1))
    S("vector", CP(acol[0:NH, :], pb[1][0:NH, 0:1]))
    S("vector", TS(nacol[0:NH, :], acol[0:NH, :], -1.0, None, ALU.mult))
    S("tensor", TR(pb[1][0:NH, 0:1], smallA[0:1, 2 * NH:3 * NH], 1))
    S("vector", CP(dtbc[0:NH, :], pb[1][0:NH, 0:1]))
    S("tensor", bcast(pb[1], flagc[0:1, 0:1], 128))
    S("vector", CP(flagc, pb[1][0:128, 0:1]))

    if stop < 1:
        _finish_trunc(q, S, out, zerot, T)
        q.emit(); st.close(); return nc
    NW = 6 * D // 512
    for nwi in range(NW):
        for kc in range(8):
            wb = wbufC if kc % 2 == 0 else wbufD
            q.dma("sync", [(wb[:, 0:512],
                            wmod[kc * 128:(kc + 1) * 128,
                                 nwi * 512:(nwi + 1) * 512])])
            S("tensor", MM(pb[2][0:1, 0:512], c_fm[:, kc:kc + 1],
                           wb[:, 0:512], kc == 0, kc == 7))
        S("vector", CP(srow[0:1, nwi * 512:(nwi + 1) * 512],
                       pb[2][0:1, 0:512]))
    for gi in (2, 5):
        sl = srow[0:1, gi * D:(gi + 1) * D]
        s0 = zerot[0:1, 0:D]
        s1 = zerot[0:1, D:2 * D]
        S("vector", TS(s0, sl, 0.0, 2.0, ALU.is_ge, ALU.mult),
          TS(s1, sl, -1.0, None, ALU.mult))
        S("vector", TS(s0, s0, 1.0, None, ALU.subtract),
          TT(s1, s1, sl, ALU.max))
        S("vector", TS(s1, s1, 0.25, None, ALU.max))
        S("vector", TT(sl, s0, s1, ALU.mult))
    S("vector", TS(srow[0:1, D:2 * D], srow[0:1, D:2 * D], 1.0, None, ALU.add),
      TS(srow[0:1, 4 * D:5 * D], srow[0:1, 4 * D:5 * D], 1.0, None, ALU.add))
    for i in range(6):
        row = srow[0:1, i * D:(i + 1) * D]
        for si in range(2):
            lo, hi = si * 512, (si + 1) * 512
            S("tensor", bcast(pb[0], row[:, lo:hi], 128))
            S("vector", CP(modb[i][:, lo:hi], pb[0][0:128, 0:512]))

    def ln_tiles(src_tiles, dst_tiles, mul_tile, add_tile, nt):
        for tt in range(nt):
            S("vector", lambda e, s=src_tiles[tt]: e.bn_stats(
                smallB[0:128, 0:6], s[:, 0:512]),
              lambda e, s=src_tiles[tt]: e.bn_stats(
                smallB[0:128, 6:12], s[:, 512:1024]))
            S("vector", lambda e: e.bn_aggr(smallB[0:128, 8:10] if False else
                                            smallB[0:128, 12:14],
                                            smallB[0:128, 0:12]))
            S("scalar", ACT(smallB[0:128, 10:11], smallB[0:128, 13:14],
                            AF.Sqrt, bias=constc[:, 0:1]))
            S("vector", lambda e: e.reciprocal(smallB[0:128, 10:11],
                                               smallB[0:128, 10:11]))
            S("vector",
              TS(dst_tiles[tt], src_tiles[tt], smallB[0:128, 12:13],
                 smallB[0:128, 10:11], ALU.subtract, ALU.mult))
            if mul_tile is not None:
                S("vector", TT(dst_tiles[tt], dst_tiles[tt], mul_tile,
                               ALU.mult))
                S("vector", TT(dst_tiles[tt], dst_tiles[tt], add_tile,
                               ALU.add))

    def to_fm(src_tiles, dst_fm_tiles, nf, nt, wcol=None, bcol=None):
        jobs = []
        for fc in range(nf):
            for tt in range(nt):
                jobs.append((fc, tt))
        for j0 in range(0, len(jobs), 4):
            grp = jobs[j0:j0 + 4]
            S("tensor", *[TR(pb[4 + i][0:128, 0:128],
                             src_tiles[tt][:, fc * 128:(fc + 1) * 128], 128)
                          for i, (fc, tt) in enumerate(grp)])
            if wcol is None:
                S("vector", *[CP(dst_fm_tiles[fc][:, tt * 128:(tt + 1) * 128],
                                 pb[4 + i][0:128, 0:128])
                              for i, (fc, tt) in enumerate(grp)])
            else:
                S("vector", *[TS(dst_fm_tiles[fc][:, tt * 128:(tt + 1) * 128],
                                 pb[4 + i][0:128, 0:128],
                                 wcol[:, fc:fc + 1], bcol[:, fc:fc + 1],
                                 ALU.mult, ALU.add)
                              for i, (fc, tt) in enumerate(grp)])

    def patternA(wdram, m0t, m1t, out_cb, Tn, kparts=8, mtot=None):
        if mtot is None:
            mtot = m1t * 128
        mcs = list(range(m0t, m1t))
        wid = {mc: min(128, mtot - mc * 128) for mc in mcs}
        wsrc = wdram.rearrange("(kc p) m -> p kc m", p=128)
        steps_done = 0
        buf_free_at = {0: None, 1: None}
        for bi, mc in enumerate(mcs):
            bsel = bi % 2
            wb = wbufA if bsel == 0 else wbufB
            mw = wid[mc]
            fa = buf_free_at[bsel]
            q.dma("sync", [(
                wb[:, 0:kparts * mw].rearrange("p (kc m) -> p kc m", kc=kparts),
                wsrc[:, 0:kparts, mc * 128:mc * 128 + mw])],
                lag=None if fa is None else
                {"tensor": steps_done - fa, "vector": 99, "scalar": 99,
                 "gpsimd": 99})
            buf_free_at[bsel] = steps_done + 1
            bank = pb[mc % 4]
            q.step("tensor",
                   [MM(bank[0:mw, 0:Tn],
                       wb[:, kc * mw:kc * mw + mw],
                       hfm[kc][:, 0:Tn], kc == 0, kc == kparts - 1)
                    for kc in range(kparts)],
                   lag=None if mc == mcs[0] else {"vector": 3, "scalar": 3})
            steps_done += 1
            out_cb(mc, bank[0:mw, 0:Tn])

    def conv_chunk(Tn):
        xbc = [cvs] + [zx[15 + i] for i in range(1, NXBC)]
        S("vector", *[CP(smallC[0:128, i * 3:(i + 1) * 3],
                         zx[16 + i][:, Tn - 3:Tn]) for i in range(NXBC)])
        for i in range(NXBC):
            srct = zx[16 + i]
            dstt = xbc[i]
            ops1 = [TS(dstt[:, 0:Tn], srct[:, 0:Tn], cw[i][:, 3:4], None,
                       ALU.mult)]
            S("vector", *ops1)
            for k in range(3):
                S("vector", STT(dstt[:, 3 - k:Tn], srct[:, 0:Tn - 3 + k],
                                cw[i][:, k:k + 1], dstt[:, 3 - k:Tn],
                                ALU.mult, ALU.add))
                tops = []
                for t in range(0, 3 - k):
                    tops.append(STT(dstt[:, t:t + 1],
                                    tails[:, i * 4 + t + k:i * 4 + t + k + 1],
                                    cw[i][:, k:k + 1], dstt[:, t:t + 1],
                                    ALU.mult, ALU.add))
                if tops:
                    S("vector", *tops)
        S("vector", *[CP(tails[:, i * 4:i * 4 + 3],
                         smallC[0:128, i * 3:(i + 1) * 3])
                      for i in range(NXBC)])
        S("scalar", *[ACT(xbc[i][:, 0:Tn], xbc[i][:, 0:Tn], AF.Silu,
                          bias=cb[i][:, :]) for i in range(NXBC)])
        return xbc

    def dt_chunk(Tn):
        dts = smallC
        S("scalar", ACT(dtt[0:NH, 0:Tn], zx[33][0:NH, 0:Tn], AF.Abs,
                        bias=dtbc[0:NH, :]))
        S("scalar", ACT(dts[0:NH, 0:Tn], dtt[0:NH, 0:Tn], AF.Exp, scale=constc[0:NH, 2:3]))
        S("scalar", ACT(dts[0:NH, 0:Tn], dts[0:NH, 0:Tn], AF.Ln, bias=constc[0:NH, 1:2]))
        S("scalar", ACT(dtt[0:NH, 0:Tn], zx[33][0:NH, 0:Tn], AF.Relu,
                        bias=dtbc[0:NH, :]))
        S("vector", TT(dtt[0:NH, 0:Tn], dtt[0:NH, 0:Tn], dts[0:NH, 0:Tn],
                       ALU.add))

    def ssd_subchunk(xbc, o, partial):
        bc_tile = xbc[16]
        q.dma("gpsimd", [(ccopy[0:64, 0:SC], bc_tile[64:128, o:o + SC])])
        S("vector",
          lambda e: e.tensor_tensor_scan(sdt[0:NH, 0:SC], onescN[0:NH, 0:SC],
                                         dtt[0:NH, o:o + SC], 0.0,
                                         ALU.mult, ALU.add))
        S("vector", TS(asig[0:NH, 0:SC], sdt[0:NH, 0:SC], acol[0:NH, :],
                       None, ALU.mult),
          TS(wfm[0:NH, 0:SC], sdt[0:NH, 0:SC],
             sdt[0:NH, SC - 1:SC], nacol[0:NH, :], ALU.subtract, ALU.mult))
        S("scalar", ACT(wfm[0:NH, 0:SC], wfm[0:NH, 0:SC], AF.Exp))
        S("vector", TT(wfm[0:NH, 0:SC], wfm[0:NH, 0:SC],
                       dtt[0:NH, o:o + SC], ALU.mult),
          TS(dkc[0:NH, :], sdt[0:NH, SC - 1:SC], acol[0:NH, :], None,
             ALU.mult))
        S("scalar", ACT(dkc[0:NH, :], dkc[0:NH, :], AF.Exp))
        S("tensor", TR(pb[4][0:SC, 0:NH], sdt[0:NH, 0:SC], NH),
          TR(pb[5][0:SC, 0:NH], dtt[0:NH, o:o + SC], NH),
          TR(pb[6][0:SC, 0:NH], wfm[0:NH, 0:SC], NH),
          TR(pb[7][0:1, 0:NH], dkc[0:NH, :], NH))
        S("vector", CP(sdtT[:, 0:NH], pb[4][0:SC, 0:NH]),
          CP(sdtT[:, NH:2 * NH], pb[5][0:SC, 0:NH]),
          CP(sdtT[:, 2 * NH:3 * NH], pb[6][0:SC, 0:NH]),
          CP(dkrow[0:1, 0:NH], pb[7][0:1, 0:NH]))
        S("vector", TT(aST[:, 0:NH], sdtT[:, 0:NH], a128, ALU.mult))
        S("scalar", ACT(easT[:, 0:NH], aST[:, 0:NH], AF.Exp))
        S("tensor", bcast(pb[6], dkrow[0:1, 0:NH], 64))
        S("vector", CP(dkerb[0:64, 0:NH], pb[6][0:64, 0:NH]))
        S("tensor", TR(pb[7][0:128, 0:128], bc_tile[:, o:o + SC], 128))
        S("vector", CP(bct[:, 0:128], pb[7][0:128, 0:128]))
        for f0 in range(0, 16, 4):
            S("tensor", *[TR(pb[4 + i][0:128, 0:128],
                             xbc[f0 + i][:, o:o + SC], 128) for i in range(4)])
            S("vector", *[CP(xs_tok[:, (f0 + i) * 128:(f0 + i + 1) * 128],
                             pb[4 + i][0:128, 0:128]) for i in range(4)])
        if not partial:
            S("tensor", MM(pb[6][0:128, 0:128], bc_tile[0:64, o:o + SC],
                           ccopy[0:64, 0:SC], True, True))
            S("vector", CP(scoresT[:, 0:128], pb[6][0:128, 0:128]))

        for hg in range(NH // 4):
            hs = hg * 4
            if not partial:
                S("tensor", *[TR(pb[4 + j][0:1, 0:128],
                                 aST[:, hs + j:hs + j + 1], 128)
                              for j in range(4)])
                S("vector", *[CP(row0[0:1, j * 128:(j + 1) * 128],
                                 pb[4 + j][0:1, 0:128]) for j in range(4)])
                S("tensor", *[MM(pb[0][0:128, j * 128:(j + 1) * 128],
                                 ones_row[0:1, 0:128],
                                 row0[0:1, j * 128:(j + 1) * 128],
                                 j == 0, j == 3) for j in range(4)])
                S("vector", *[TS(lpre[j], pb[0][0:128, j * 128:(j + 1) * 128],
                                 aST[:, hs + j:hs + j + 1], None,
                                 ALU.subtract) for j in range(4)])
                S("vector", *[TT(lpre[j], lpre[j], maskbig, ALU.min)
                              for j in range(4)])
                S("scalar", *[ACT(lpre[j], lpre[j], AF.Exp) for j in range(4)])
                S("vector", *[STT(mtb[j], lpre[j],
                                  sdtT[:, NH + hs + j:NH + hs + j + 1],
                                  scoresT, ALU.mult, ALU.mult)
                              for j in range(4)])
                S("tensor",
                  *[MM(pb[2][0:128, j * 64:(j + 1) * 64], mtb[j],
                       xs_tok[:, (hs + j) * 64:(hs + j + 1) * 64],
                       j == 0, j == 3) for j in range(4)],
                  *[MM(pb[3][0:128, j * 64:(j + 1) * 64],
                       ccopy[0:64, 0:SC],
                       state[0:64, (hs + j) * 64:(hs + j + 1) * 64],
                       j == 0, j == 3) for j in range(4)])
                S("vector", *[CP(y_tok[:, (hs + j) * 64:(hs + j + 1) * 64],
                                 pb[2][0:128, j * 64:(j + 1) * 64])
                              for j in range(4)])
                S("vector", *[STT(y_tok[:, (hs + j) * 64:(hs + j + 1) * 64],
                                  pb[3][0:128, j * 64:(j + 1) * 64],
                                  easT[:, hs + j:hs + j + 1],
                                  y_tok[:, (hs + j) * 64:(hs + j + 1) * 64],
                                  ALU.mult, ALU.add) for j in range(4)])
                S("vector", *[STT(y_tok[:, (hs + j) * 64:(hs + j + 1) * 64],
                                  xs_tok[:, (hs + j) * 64:(hs + j + 1) * 64],
                                  d128[:, hs + j:hs + j + 1],
                                  y_tok[:, (hs + j) * 64:(hs + j + 1) * 64],
                                  ALU.mult, ALU.add) for j in range(4)])
            S("vector", *[TS(xs_tok[:, (hs + j) * 64:(hs + j + 1) * 64],
                             xs_tok[:, (hs + j) * 64:(hs + j + 1) * 64],
                             sdtT[:, 2 * NH + hs + j:2 * NH + hs + j + 1],
                             None, ALU.mult)
                          for j in range(4)])
            S("tensor", *[MM(pb[1][0:64, j * 64:(j + 1) * 64], bct[:, 0:64],
                             xs_tok[:, (hs + j) * 64:(hs + j + 1) * 64],
                             j == 0, j == 3) for j in range(4)])
            S("vector", *[STT(state[0:64, (hs + j) * 64:(hs + j + 1) * 64],
                              state[0:64, (hs + j) * 64:(hs + j + 1) * 64],
                              dkerb[0:64, hs + j:hs + j + 1],
                              pb[1][0:64, j * 64:(j + 1) * 64],
                              ALU.mult, ALU.add) for j in range(4)])

    def attn_chunk(c, partial, xsrc):
        nt = CH // 128
        q.dma("sync", [(xchunk[tt], xsrc[c * CH + tt * 128:
                                         c * CH + (tt + 1) * 128, :])
                       for tt in range(nt)])
        ln_tiles(xchunk, htok, modb[1], modb[0], nt)
        ln_tiles(htok, x1tok, None, None, nt)
        to_fm(x1tok, hfm, 8, nt, lncol[0], lncol[1])
        m0 = 16 if partial else 0
        patternA(w_ip, m0, NZX,
                 lambda mc, ps: S("vector", CP(
                     zx[mc][0:(32 if mc == NZX - 1 else 128), 0:CH], ps)),
                 CH, mtot=DIP)
        xbc = conv_chunk(CH)
        dt_chunk(CH)
        for sci in range(CH // SC):
            ssd_subchunk(xbc, sci * SC, partial)
            if not partial:
                post_ssd_subchunk(c, sci, xbc)
        if not partial:
            post_chunk(c, nt)

    def post_ssd_subchunk(c, sci, xbc):
        o = sci * SC
        S("scalar", *[ACT(zx[fc][:, o:o + SC], zx[fc][:, o:o + SC], AF.Silu)
                      for fc in range(16)])
        for f0 in range(0, 16, 4):
            S("tensor", *[TR(pb[4 + i][0:128, 0:128],
                             y_tok[:, (f0 + i) * 128:(f0 + i + 1) * 128], 128)
                          for i in range(4)])
            S("vector", *[TT(zx[f0 + i][:, o:o + SC],
                             pb[4 + i][0:128, 0:128],
                             zx[f0 + i][:, o:o + SC], ALU.mult)
                          for i in range(4)])
        sqops = []
        mmops = []
        for fc in range(16):
            sqops.append(TT(lpre[fc % 4], zx[fc][:, o:o + SC],
                            zx[fc][:, o:o + SC], ALU.mult))
            mmops.append(MM(pb[6][0:1, 0:SC], ones_col[0:128, :],
                            lpre[fc % 4], fc == 0, fc == 15))
        for f0 in range(0, 16, 4):
            S("vector", *sqops[f0:f0 + 4])
            S("tensor", *mmops[f0:f0 + 4])
        S("scalar", ACT(smallC[0:1, 0:SC], pb[6][0:1, 0:SC], AF.Sqrt,
                        bias=constc[0:1, 0:1], scale=constc[0:1, 3:4]))
        S("vector", lambda e: e.reciprocal(smallC[0:1, 0:SC],
                                           smallC[0:1, 0:SC]))
        S("tensor", bcast(pb[7], smallC[0:1, 0:SC], 128))
        S("vector", *[STT(zx[fc][:, o:o + SC], pb[7][0:128, 0:SC],
                          nrmc[:, fc:fc + 1], zx[fc][:, o:o + SC],
                          ALU.mult, ALU.mult) for fc in range(16)])

    def post_chunk(c, nt):
        for kc in range(16):
            wb = wbufC if kc % 2 == 0 else wbufD
            q.dma("sync", [(wb[:, 0:D], w_op[kc * 128:(kc + 1) * 128, :])],
                  lag=None if kc <= 1 else {"tensor": 1, "vector": 99, "scalar": 99, "gpsimd": 99})
            q.step("tensor", [MM(pb[tt * 2 + si][0:128, 0:512],
                                 zx[kc][:, tt * 128:(tt + 1) * 128],
                                 wb[:, si * 512:(si + 1) * 512],
                                 kc == 0, kc == 15)
                              for tt in range(nt) for si in range(2)],
                   lag=None if kc == 0 else {"vector": 99, "scalar": 99})
        S("vector", *[TT(x1tok[tt][:, si * 512:(si + 1) * 512],
                         pb[tt * 2 + si][0:128, 0:512],
                         htok[tt][:, si * 512:(si + 1) * 512], ALU.add)
                      for tt in range(nt) for si in range(2)])
        ln_tiles(x1tok, htok, None, None, nt)
        to_fm(htok, hfm, 8, nt, lncol[2], lncol[3])
        patternA(w_m1, 0, 32,
                 lambda mc, ps: S("scalar",
                                  ACT(zx[mc][:, 0:CH], ps, AF.Gelu,
                                      bias=b1c[:, mc:mc + 1])), CH)
        for kc in range(32):
            wb = wbufC if kc % 2 == 0 else wbufD
            q.dma("sync", [(wb[:, 0:D], w_m2[kc * 128:(kc + 1) * 128, :])],
                  lag=None if kc <= 1 else {"tensor": 1, "vector": 99, "scalar": 99, "gpsimd": 99})
            q.step("tensor", [MM(pb[tt * 2 + si][0:128, 0:512],
                                 zx[kc][:, tt * 128:(tt + 1) * 128],
                                 wb[:, si * 512:(si + 1) * 512],
                                 kc == 0, kc == 31)
                              for tt in range(nt) for si in range(2)],
                   lag=None if kc == 0 else {"vector": 99, "scalar": 99})
        S("vector", *[TT(htok[tt][:, si * 512:(si + 1) * 512],
                         pb[tt * 2 + si][0:128, 0:512],
                         b2b[:, si * 512:(si + 1) * 512], ALU.add)
                      for tt in range(nt) for si in range(2)])
        S("vector", *[TT(htok[tt], htok[tt], x1tok[tt], ALU.add)
                      for tt in range(nt)])
        S("vector", *[TT(htok[tt], htok[tt], modb[2], ALU.mult)
                      for tt in range(nt)])
        S("vector", *[TT(xchunk[tt], xchunk[tt], htok[tt], ALU.add)
                      for tt in range(nt)])
        q.dma("gpsimd", [(x_hbm[c * CH + tt * 128:c * CH + (tt + 1) * 128, :],
                          xchunk[tt]) for tt in range(nt)])
        ln_tiles(xchunk, htok, modb[4], modb[3], nt)
        q.dma("gpsimd", [(ag_xf_in[c * CH + tt * 128:c * CH + (tt + 1) * 128, :],
                          htok[tt]) for tt in range(nt)])
        to_fm(htok, hfm, 8, nt)
        gate_topk(c, nt)

    def gate_topk(c, nt):
        for tt in range(nt):
            wb = wbufC
            if tt == 0:
                q.dma("sync", [(wbufC[:, kc * 64:(kc + 1) * 64],
                                w_gt[kc * 128:(kc + 1) * 128, :])
                               for kc in range(8)])
            S("tensor", *[MM(pb[4][0:128, 0:E],
                             hfm[kc][:, tt * 128:(tt + 1) * 128],
                             wbufC[:, kc * 64:(kc + 1) * 64],
                             kc == 0, kc == 7) for kc in range(8)])
            S("vector", RED(smallB[0:128, 0:1], pb[4][0:128, 0:E], ALU.max))
            S("vector", TS(smallB[0:128, 1:2], smallB[0:128, 0:1], -1.0,
                           None, ALU.mult))
            S("scalar", lambda e: e.activation(lpre[0][:, 0:E], pb[4][0:128, 0:E],
                                               AF.Exp, bias=smallB[0:128, 1:2]))
            S("vector", RED(smallB[0:128, 2:3], lpre[0][:, 0:E], ALU.add))
            S("vector", lambda e: e.reciprocal(smallB[0:128, 3:4],
                                               smallB[0:128, 2:3]))
            S("vector", TS(scorekeep[:, 0:E], lpre[0][:, 0:E],
                           smallB[0:128, 3:4], None, ALU.mult))
            S("vector", TT(lpre[1][:, 0:E], scorekeep[:, 0:E], gbB, ALU.add))
            pt = payb[c * nt + tt]
            S("vector", MS(pt, 0.0))
            for k in range(TOPK):
                S("vector", RED(smallB[0:128, 4:5], lpre[1][:, 0:E], ALU.max))
                S("vector", TS(lpre[2][:, 0:E], lpre[1][:, 0:E],
                               smallB[0:128, 4:5], None, ALU.is_equal))
                S("vector", TT(lpre[3][:, 0:E], lpre[2][:, 0:E], iot128[:, 0:64],
                               ALU.mult),
                  TS(mtb[0][:, 0:E], lpre[2][:, 0:E], 1.0, -64.0,
                     ALU.subtract, ALU.mult))
                S("vector", TT(lpre[3][:, 0:E], lpre[3][:, 0:E],
                               mtb[0][:, 0:E], ALU.add))
                S("vector", RED(pt[:, k:k + 1], lpre[3][:, 0:E], ALU.min))
                S("vector", TS(lpre[2][:, 0:E], iot128[:, 0:64], pt[:, k:k + 1], None,
                               ALU.is_equal))
                S("vector", TT(lpre[3][:, 0:E], lpre[2][:, 0:E],
                               scorekeep[:, 0:E], ALU.mult))
                S("vector", RED(pt[:, 8 + k:9 + k], lpre[3][:, 0:E], ALU.add))
                S("vector", STT(lpre[1][:, 0:E], lpre[2][:, 0:E], -1e30,
                                lpre[1][:, 0:E], ALU.mult, ALU.add))
        q.dma("gpsimd", [(ag_pay_in[c * CH + tt * 128:c * CH + (tt + 1) * 128, :],
                          payb[c * nt + tt]) for tt in range(nt)])

    if stop < 2:
        _finish_trunc(q, S, out, zerot, T)
        q.emit(); st.close(); return nc
    for c in range(NCHP):
        attn_chunk(c, True, x_prev)
    S("vector", TS(state[0:64, :], state[0:64, :], flagc[0:64, :], None,
                   ALU.mult),
      TS(tails, tails, flagc, None, ALU.mult))
    if stop < 3:
        _finish_trunc(q, S, out, zerot, T)
        q.emit(); st.close(); return nc
    for c in range(NCH):
        attn_chunk(c, False, x_own)

    if stop < 4:
        _finish_trunc(q, S, out, zerot, T)
        q.emit(); st.close(); return nc
    q.collective("AllGather", ALU.bypass, [list(range(n_ranks))],
                 [ag_pay_in.opt()], [pay_all.opt()])
    cc_pay = q.ccval
    q.collective("AllGather", ALU.bypass, [list(range(n_ranks))],
                 [ag_xf_in.opt()], [xf_all[0:TALL, :].opt()])
    cc_xf = q.ccval

    pool.pop()
    pool.push()
    ptile = pool.tile(16)
    otile = pool.tile(E)
    cex = pool.tile(E)
    ptab = pool.tile(E)
    msk = [pool.tile(E) for _ in range(TOPK)]
    colk = pool.tile(16)
    swf = pool.tile(16)
    ctile = pool.tile(E)
    tmpe = [pool.tile(E) for _ in range(2)]
    ofs = pool.tile(1)
    ofs_i = pooli[:, 0:1]

    if stop < 5:
        _finish_trunc(q, S, out, zerot[0:128, 0:1024] if False else zerot, T)
        q.emit(); st.close(); return nc
    q.wait_cc("sync", cc_pay)
    q.wait_cc("gpsimd", cc_pay)
    S("vector", MS(swf, 0.0))
    for ti in range(NTILE_ALL):
        q.dma("sync", [(ptile, pay_all[ti * 128:(ti + 1) * 128, :])])
        S("vector", *[TS(msk[k], iot128[:, 0:64], ptile[:, k:k + 1], None, ALU.is_equal)
                      for k in range(TOPK)],
          TS(colk[:, 15:16], iotc, float(ti * 128), None, ALU.add))
        S("vector", TT(otile, msk[0], msk[1], ALU.add),
          TT(tmpe[0], msk[2], msk[3], ALU.add),
          TT(tmpe[1], msk[4], msk[5], ALU.add))
        S("vector", TT(otile, otile, tmpe[0], ALU.add))
        S("vector", TT(otile, otile, tmpe[1], ALU.add))
        S("vector", TT(ofs[0:64, :], basec[0:64, :], etimes[0:64, :], ALU.add))
        S("vector", CP(ofs_i[0:64, :], ofs[0:64, :]))
        S("tensor", MM(pb[0][0:128, 0:E], sltm, otile, True, False),
          MM(pb[0][0:128, 0:E], ones_row[0:1, 0:128].rearrange("a b -> a b"),
             baser[0:1, 0:E], False, True))
        S("vector", CP(ptab, pb[0][0:128, 0:E]))
        S("tensor", MM(pb[0][0:128, 0:E], sltm, otile, True, True))
        S("vector", CP(cex, pb[0][0:128, 0:E]))
        S("tensor", MM(pb[1][0:1, 0:E], ones_col[0:128, :], otile,
                       True, True),
          MM(pb[2][0:64, 0:1], otile[:, 0:64], ones_col[0:128, :],
             True, True))
        for k in range(TOPK):
            S("vector", TT(tmpe[0], msk[k], cex, ALU.mult),
              TT(tmpe[1], msk[k], ptab, ALU.mult))
            S("vector", RED(colk[:, k:k + 1], tmpe[0], ALU.add),
              RED(colk[:, 8 + k:9 + k], tmpe[1], ALU.add))
        for k in range(TOPK):
            S("vector", TS(tmpe[0], iot128[:, 0:64], colk[:, k:k + 1], None,
                           ALU.is_equal))
            S("vector", TS(tmpe[0], tmpe[0], colk[:, 15:16], None, ALU.mult))
            S("tensor", MM(pb[3][0:64, 0:E], msk[k], tmpe[0],
                           k == 0, k == TOPK - 1))
        S("vector", CP(ctile[0:64, :], pb[3][0:64, 0:E]))
        q.indirect([(tblh[:, :], IndirectOffsetOnAxis(ofs_i[0:64, :], 0),
                     ctile[0:64, :], None)])
        S("vector", TT(baser[0:1, 0:E], baser[0:1, 0:E], pb[1][0:1, 0:E],
                       ALU.add),
          TT(basec[0:64, :], basec[0:64, :], pb[2][0:64, 0:1], ALU.add))
        S("vector", *[TS(swf[:, 8 + k:9 + k], colk[:, 8 + k:9 + k], 511.0,
                         None, ALU.min) for k in range(TOPK)])
        S("vector", *[STT(swf[:, k:k + 1], ptile[:, k:k + 1], 512.0,
                          swf[:, 8 + k:9 + k], ALU.mult, ALU.add)
                      for k in range(TOPK)])
        S("vector", *[TS(tmpe[0][:, k:k + 1], colk[:, 8 + k:9 + k], 512.0,
                         None, ALU.is_lt) for k in range(TOPK)])
        S("vector", *[TT(swf[:, 8 + k:9 + k], ptile[:, 8 + k:9 + k],
                         tmpe[0][:, k:k + 1], ALU.mult) for k in range(TOPK)])
        q.dma("gpsimd", [(slotwf[ti * 128:(ti + 1) * 128, :], swf)])
    q.dma("gpsimd", [(cnts_h[:, :], baser[0:1, 0:E])])

    if stop < 6:
        _finish_trunc(q, S, out, zerot, T)
        q.emit(); st.close(); return nc
    pool.pop()
    pool.push()
    xbt = pool.tiles(4, D)
    xbfm = pool.tiles(8, 512)
    he = pool.tiles(11, 512)
    ybt = pool.tiles(4, D)
    ewA = pool.tile(8 * 128)
    ewB = pool.tile(8 * 128)
    ewC = pool.tile(D)
    ewD = pool.tile(D)
    idxv = pool.tile(8)
    cnt8 = pool.tile(E)
    vld = pool.tile(4)
    swft = pool.tile(16)
    rankc = pool.tile(1)
    ofs = pool.tile(1)
    tbl_own_sb = smallC
    xmt = pool.tile(D)
    zt = pool.tile(D)
    gath = pool.tile(D)
    acc = pool.tile(D)
    moe_end = pool.off
    idx_i4 = pooli[:, 4:8]
    slot_i = pooli[:, 8:9]

    rank_s = pooli[:, 12:13]
    q.dma("gpsimd", [(rank_s[0:1, :], rank_i[:, :])])
    S("vector", CP(smallB[0:1, 14:15], rank_s[0:1, :]))
    S("tensor", bcast(pb[0], smallB[0:1, 14:15], 128))
    S("vector", CP(rankc, pb[0][0:128, 0:1]))
    S("vector", TS(ofs[0:epc, :], rankc[0:epc, :], float(epc), None,
                   ALU.mult))
    S("vector", TT(ofs[0:epc, :], ofs[0:epc, :], iotc[0:epc, :], ALU.add))
    S("vector", CP(pooli[0:epc, 13:14], ofs[0:epc, :]))
    cnts_flat = cnts_h.rearrange("o e -> (o e)")[:, None]
    q.indirect([(smallC[0:epc, 0:1], None, cnts_flat,
                 IndirectOffsetOnAxis(pooli[0:epc, 13:14], 0))])
    S("tensor", TR(pb[0][0:1, 0:epc], smallC[0:epc, 0:1], epc))
    S("vector", CP(smallC[0:1, 4:4 + epc], pb[0][0:1, 0:epc]))
    S("tensor", bcast(pb[0], smallC[0:1, 4:4 + epc], 128))
    S("vector", CP(cnt8[:, 0:epc], pb[0][0:128, 0:epc]))
    S("vector", TS(ofs[0:epc, :], rankc[0:epc, :], float(epc), None,
                   ALU.mult))
    S("vector", TT(ofs[0:epc, :], ofs[0:epc, :], iotc[0:epc, :], ALU.add))
    S("vector", TS(ofs[0:epc, :], ofs[0:epc, :], float(TSTRIDE), None,
                   ALU.mult))
    S("vector", CP(pooli[0:epc, 14:15], ofs[0:epc, :]))
    q.indirect([(tbl_own_sb[0:epc, 0:512], None, tblh[:, :],
                 IndirectOffsetOnAxis(pooli[0:epc, 14:15], 0))])
    q.dma("gpsimd", [(tbl_own[0:epc, :], tbl_own_sb[0:epc, 0:512])])
    for j in range(epc):
        q.dma("gpsimd", [(idxv[:, (j % 2) * 4:(j % 2) * 4 + 4],
                          tbl_own[j, 0:512].rearrange("(f q) -> q f", q=128))])
        jj = j % 2
        S("vector", *[TS(vld[:, col:col + 1],
                         iotc, float(col * 128), None, ALU.add)
                      for col in range(4)])
        S("vector", *[TS(vld[:, col:col + 1], vld[:, col:col + 1],
                         cnt8[:, j:j + 1], None, ALU.is_lt)
                      for col in range(4)])
        S("vector", TS(idxv[:, jj * 4:jj * 4 + 4],
                       idxv[:, jj * 4:jj * 4 + 4], float(TALL), None,
                       ALU.subtract))
        S("vector", TT(idxv[:, jj * 4:jj * 4 + 4],
                       idxv[:, jj * 4:jj * 4 + 4], vld, ALU.mult))
        S("vector", TS(idxv[:, jj * 4:jj * 4 + 4],
                       idxv[:, jj * 4:jj * 4 + 4], float(TALL), None,
                       ALU.add))
        S("vector", CP(idx_i4, idxv[:, jj * 4:jj * 4 + 4]))
        if j == 0:
            q.wait_cc("gpsimd", cc_xf)
        q.indirect([(xbt[col], None, xf_all[:, :],
                     IndirectOffsetOnAxis(idx_i4[:, col:col + 1], 0))
                    for col in range(4)])
        for f0 in range(0, 8, 2):
            S("tensor", *[TR(pb[4 * i + tt][0:128, 0:128],
                             xbt[tt][:, (f0 + i) * 128:(f0 + i + 1) * 128], 128)
                          for i in range(2) for tt in range(4)])
            S("vector", *[CP(xbfm[f0 + i][:, tt * 128:(tt + 1) * 128],
                             pb[4 * i + tt][0:128, 0:128])
                          for i in range(2) for tt in range(4)])
        mws = [128, 128, 128, 128, 128, 64]
        def eA(wdram, j, cb_):
            wsrc = wdram.rearrange("e (kc p) m -> e p kc m", p=128)
            for bi in range(6):
                wb = ewA if bi % 2 == 0 else ewB
                mw = mws[bi]
                q.dma("sync", [(
                    wb[:, 0:8 * mw].rearrange("p (kc m) -> p kc m", kc=8),
                    wsrc[j, :, 0:8, bi * 128:bi * 128 + mw])],
                    lag=None if bi <= 1 else {"tensor": 1, "vector": 99,
                                              "scalar": 99, "gpsimd": 99})
                bank = pb[bi % 4]
                q.step("tensor",
                       [MM(bank[0:mw, 0:512],
                           wb[:, kc * mw:kc * mw + mw],
                           xbfm[kc][:, 0:512], kc == 0, kc == 7)
                        for kc in range(8)],
                       lag=None if bi == 0 else {"vector": 3, "scalar": 3})
                cb_(bi, bank[0:mw, 0:512], mw)
        eA(w_e1, j, lambda ic, ps, mwid:
           S("scalar", ACT(he[ic][0:mwid, :], ps, AF.Silu)))
        eA(w_e3, j, lambda ic, ps, mwid:
           S("vector", TT(he[ic][0:mwid, :], he[ic][0:mwid, :], ps,
                          ALU.mult)))
        for kc2 in range(6):
            mwid = mws[kc2]
            wb = ewC if kc2 % 2 == 0 else ewD
            q.dma("sync", [(wb[0:mwid, 0:D],
                            w_e2[j, kc2 * 128:kc2 * 128 + mwid, :])],
                  lag=None if kc2 <= 1 else {"tensor": 1, "vector": 99, "scalar": 99, "gpsimd": 99})
            q.step("tensor", [MM(pb[sl * 2 + si][0:128, 0:512],
                                 he[kc2][0:mwid, sl * 128:(sl + 1) * 128],
                                 wb[0:mwid, si * 512:(si + 1) * 512],
                                 kc2 == 0, kc2 == 5)
                              for sl in range(4) for si in range(2)],
                   lag=None if kc2 == 0 else {"vector": 99, "scalar": 99})
        S("vector", *[CP(ybt[sl][:, si * 512:(si + 1) * 512],
                         pb[sl * 2 + si][0:128, 0:512])
                      for sl in range(4) for si in range(2)])
        q.dma("gpsimd", [(ag_yb_in[j * CAP + sl * 128:j * CAP + (sl + 1) * 128,
                                   :], ybt[sl]) for sl in range(4)])

    q.collective("AllGather", ALU.bypass, [list(range(n_ranks))],
                 [ag_yb_in.opt()], [yb_all.opt()])
    cc_yb = q.ccval

    if stop < 7:
        _finish_trunc(q, S, out, zerot, T)
        q.emit(); st.close(); return nc
    for c in range(NCH):
        nt = CH // 128
        q.dma("sync", [(xbt[tt], ag_xf_in[c * CH + tt * 128:
                                          c * CH + (tt + 1) * 128, :])
                       for tt in range(nt)])
        for f0 in range(0, 8, 2):
            S("tensor", *[TR(pb[4 * i + tt][0:128, 0:128],
                             xbt[tt][:, (f0 + i) * 128:(f0 + i + 1) * 128], 128)
                          for i in range(2) for tt in range(4)])
            S("vector", *[CP(xbfm[f0 + i][:, tt * 128:(tt + 1) * 128],
                             pb[4 * i + tt][0:128, 0:128])
                          for i in range(2) for tt in range(4)])
        def sA(wdram, cb_):
            wsrc = wdram.rearrange("(kc p) m -> p kc m", p=128)
            for bi in range(11):
                wb = ewA if bi % 2 == 0 else ewB
                q.dma("sync", [(
                    wb[:, 0:8 * 128].rearrange("p (kc m) -> p kc m", kc=8),
                    wsrc[:, 0:8, bi * 128:(bi + 1) * 128])],
                    lag=None if bi <= 1 else {"tensor": 1, "vector": 99,
                                              "scalar": 99, "gpsimd": 99})
                bank = pb[bi % 4]
                q.step("tensor",
                       [MM(bank[0:128, 0:512],
                           wb[:, kc * 128:(kc + 1) * 128],
                           xbfm[kc][:, 0:512], kc == 0, kc == 7)
                        for kc in range(8)],
                       lag=None if bi == 0 else {"vector": 3, "scalar": 3})
                cb_(bi, bank[0:128, 0:512])
        sA(w_s1, lambda ic, ps: S("scalar", ACT(he[ic], ps, AF.Silu)))
        sA(w_s3, lambda ic, ps: S("vector", TT(he[ic], he[ic], ps, ALU.mult)))
        for kc2 in range(11):
            wb = ewC if kc2 % 2 == 0 else ewD
            q.dma("sync", [(wb[:, 0:D], w_s2[kc2 * 128:(kc2 + 1) * 128, :])],
                  lag=None if kc2 <= 1 else {"tensor": 1, "vector": 99, "scalar": 99, "gpsimd": 99})
            q.step("tensor", [MM(pb[sl * 2 + si][0:128, 0:512],
                                 he[kc2][:, sl * 128:(sl + 1) * 128],
                                 wb[:, si * 512:(si + 1) * 512],
                                 kc2 == 0, kc2 == 10)
                              for sl in range(4) for si in range(2)],
                   lag=None if kc2 == 0 else {"vector": 99, "scalar": 99})
        S("vector", *[CP(ybt[sl][:, si * 512:(si + 1) * 512],
                         pb[sl * 2 + si][0:128, 0:512])
                      for sl in range(4) for si in range(2)])
        q.dma("gpsimd", [(z_hbm[c * CH + sl * 128:c * CH + (sl + 1) * 128, :],
                          ybt[sl]) for sl in range(4)])

    if stop < 8:
        _finish_trunc(q, S, out, zerot, T)
        q.emit(); st.close(); return nc
    q.wait_cc("gpsimd", cc_yb)
    for tt in range(T // 128):
        q.dma("sync", [(xmt, x_hbm[tt * 128:(tt + 1) * 128, :]),
                       (zt, z_hbm[tt * 128:(tt + 1) * 128, :])])
        S("vector", TS(ofs, rankc, float(T), None, ALU.mult))
        S("vector", TT(ofs, ofs, iotc, ALU.add))
        S("vector", TS(ofs, ofs, 1.0, float(tt * 128), ALU.mult, ALU.add))
        S("vector", CP(pooli[:, 15:16], ofs))
        q.indirect([(swft, None, slotwf[:, :],
                     IndirectOffsetOnAxis(pooli[:, 15:16], 0))])
        for k in range(TOPK):
            S("vector", CP(slot_i, swft[:, k:k + 1]))
            q.indirect([(gath, None, yb_all[:, :],
                         IndirectOffsetOnAxis(slot_i, 0))])
            if k == 0:
                S("vector", TS(acc, gath, swft[:, 8:9], None, ALU.mult))
            else:
                S("vector", STT(acc, gath, swft[:, 8 + k:9 + k], acc,
                                ALU.mult, ALU.add))
        S("vector", TT(acc, acc, zt, ALU.add))
        S("vector", TT(acc, acc, modb[5], ALU.mult))
        S("vector", TT(acc, acc, xmt, ALU.add))
        q.dma("gpsimd", [(out[tt * 128:(tt + 1) * 128, :], acc)])
    S("gpsimd", lambda e: e.engine_nop())

    q.emit()
    st.close()
    return nc


def prep_inmaps(inputs, T, n_ranks):
    def t(a):
        return np.ascontiguousarray(np.transpose(np.asarray(a, np.float32)))

    x = np.ascontiguousarray(inputs["L0_attn_x"], dtype=np.float32)
    epc = E // n_ranks
    shared = {
        "wmod": np.ascontiguousarray(inputs["L0_wmod"], np.float32),
        "ln1w": inputs["L0_attn_ln1_w"].reshape(1, D).astype(np.float32),
        "ln1b": inputs["L0_attn_ln1_b"].reshape(1, D).astype(np.float32),
        "ln2w": inputs["L0_attn_ln2_w"].reshape(1, D).astype(np.float32),
        "ln2b": inputs["L0_attn_ln2_b"].reshape(1, D).astype(np.float32),
        "normw": inputs["L0_attn_norm_w"].reshape(1, DI).astype(np.float32),
        "mlpb1": inputs["L0_attn_mlp_b1"].reshape(1, MLP_H).astype(np.float32),
        "mlpb2": inputs["L0_attn_mlp_b2"].reshape(1, D).astype(np.float32),
        "gateb": inputs["L0_ffn_gate_b"].reshape(1, E).astype(np.float32),
        "a_log": inputs["L0_attn_A_log"].reshape(1, NH).astype(np.float32),
        "d_vec": inputs["L0_attn_D"].reshape(1, NH).astype(np.float32),
        "dtbias": inputs["L0_attn_dt_bias"].reshape(1, NH).astype(np.float32),
        "convw": np.ascontiguousarray(inputs["L0_attn_conv_w"], np.float32),
        "convb": inputs["L0_attn_conv_b"].reshape(CONVD, 1).astype(np.float32),
        "w_ip": t(inputs["L0_attn_in_proj_w"]),
        "w_op": t(inputs["L0_attn_out_proj_w"]),
        "w_m1": t(inputs["L0_attn_mlp_w1"]),
        "w_m2": t(inputs["L0_attn_mlp_w2"]),
        "w_gt": t(inputs["L0_ffn_gate_w"]),
        "w_s1": t(inputs["L0_ffn_ws1"]),
        "w_s3": t(inputs["L0_ffn_ws3"]),
        "w_s2": t(inputs["L0_ffn_ws2"]),
    }
    we1 = np.ascontiguousarray(np.transpose(
        np.asarray(inputs["L0_ffn_w1"], dtype=np.float32), (0, 2, 1)))
    we3 = np.ascontiguousarray(np.transpose(
        np.asarray(inputs["L0_ffn_w3"], dtype=np.float32), (0, 2, 1)))
    we2 = np.ascontiguousarray(np.transpose(
        np.asarray(inputs["L0_ffn_w2"], dtype=np.float32), (0, 2, 1)))
    c_all = np.asarray(inputs["L0_c"], dtype=np.float32)

    in_maps = []
    zeros_prev = np.zeros((T, D), np.float32)
    for r in range(n_ranks):
        b, h = r // 2, r % 2
        m = dict(shared)
        m["c_row"] = np.ascontiguousarray(c_all[b:b + 1])
        m["x_own"] = np.ascontiguousarray(x[b, h * T:(h + 1) * T])
        m["x_prev"] = (np.ascontiguousarray(x[b, 0:T]) if h == 1
                       else zeros_prev)
        m["flag"] = np.array([[1.0 if h == 1 else 0.0]], np.float32)
        m["rank_i"] = np.array([[r]], np.int32)
        m["w_e1"] = np.ascontiguousarray(we1[r * epc:(r + 1) * epc])
        m["w_e3"] = np.ascontiguousarray(we3[r * epc:(r + 1) * epc])
        m["w_e2"] = np.ascontiguousarray(we2[r * epc:(r + 1) * epc])
        in_maps.append(m)
    return in_maps


def kernel(**inputs):
    T = 2048
    key = (T,)
    if key not in _CACHE:
        _CACHE[key] = build_program(T=T, TP=T, n_ranks=NCORES)
    nc = _CACHE[key]
    in_maps = prep_inmaps(inputs, T, NCORES)
    res = run_bass_kernel_spmd(nc, in_maps, core_ids=list(range(NCORES)))
    outs = [np.asarray(res.results[r]["out"]) for r in range(NCORES)]
    x = inputs["L0_attn_x"]
    B, L, _ = x.shape
    full = np.empty((B, L, D), np.float32)
    for r in range(NCORES):
        b, h = r // 2, r % 2
        full[b, h * T:(h + 1) * T] = outs[r]
    return full


# version 11
# speedup vs baseline: 1.1880x; 1.0018x over previous
import numpy as np
from contextlib import ExitStack

import concourse.bass as bass
import concourse.mybir as mybir
from concourse.bass import IndirectOffsetOnAxis
from concourse.bass_utils import run_bass_kernel_spmd

F32 = mybir.dt.float32
I32 = mybir.dt.int32
AF = mybir.ActivationFunctionType
ALU = mybir.AluOpType

D = 1024
DI = 2048
DSTATE = 64
NH = 32
HD = 64
DCONV = 4
CONVD = DI + 2 * DSTATE
DIP = 2 * DI + 2 * DSTATE + NH
NZX = (DIP + 127) // 128
NXBC = CONVD // 128
EPS = 1e-5
E = 64
CAP = 512
TOPK = 6
MOE_I = 704
SH_I = 1408
MLP_H = 4096
NCORES = 8
EPC = E // NCORES
TSTRIDE = 16448
SC = 128
CH = 512
BIGNEG = -1e30

_CACHE = {}


class SQ:
    ENGS = ("sync", "scalar", "vector", "tensor", "gpsimd")

    def __init__(self, nc, stack):
        self.nc = nc
        self.stack = stack
        self.ops = {e: [] for e in self.ENGS}
        self.csem = {e: self._mk(f"c_{e}") for e in self.ENGS}
        self.dsem = {e: self._mk(f"d_{e}") for e in self.ENGS}
        self.cval = {e: 0 for e in self.ENGS}
        self.dval = {e: 0 for e in self.ENGS}
        self.seen_c = {e: {f: 0 for f in self.ENGS} for e in self.ENGS}
        self.seen_d = {e: {f: 0 for f in self.ENGS} for e in self.ENGS}
        self.ccsem = self._mk("cc")
        self.ccval = 0
        self.seen_cc = {e: 0 for e in self.ENGS}

    def _mk(self, name):
        return self.stack.enter_context(self.nc.semaphore(name))

    def _barrier(self, eng, lag=None, cap=None):
        for f in self.ENGS:
            if f == eng:
                continue
            v = self.cval[f]
            if lag and f in lag:
                v = max(0, v - lag[f])
            if cap and f in cap:
                v = min(v, cap[f])
            if v > self.seen_c[eng][f]:
                self.ops[eng].append(
                    lambda e, h=self.csem[f], v=v: e.wait_ge(h, v))
                self.seen_c[eng][f] = v
            dv = self.dval[f]
            if dv > self.seen_d[eng][f]:
                self.ops[eng].append(
                    lambda e, h=self.dsem[f], v=dv: e.wait_ge(h, v))
                self.seen_d[eng][f] = dv
        dv = self.dval[eng]
        if dv > self.seen_d[eng][eng]:
            self.ops[eng].append(
                lambda e, h=self.dsem[eng], v=dv: e.wait_ge(h, v))
            self.seen_d[eng][eng] = dv

    def step(self, eng, fns, lag=None, cap=None):
        if not isinstance(fns, (list, tuple)):
            fns = [fns]
        self._barrier(eng, lag, cap)
        if eng in ("scalar", "vector", "gpsimd"):
            self.ops[eng].append(lambda e: e.drain(fusable=True))
        for fn in fns[:-1]:
            self.ops[eng].append(lambda e, fn=fn: fn(e))
        self.cval[eng] += 1
        self.ops[eng].append(
            lambda e, fn=fns[-1], h=self.csem[eng]:
            fn(e).then_inc(h, 1))

    def dma(self, eng, pairs, lag=None):
        self._barrier(eng, lag)
        for dst, src in pairs:
            self.dval[eng] += 16
            self.ops[eng].append(
                lambda e, d=dst, s=src, h=self.dsem[eng]:
                e.dma_start(out=d, in_=s, allow_slow_non_contiguous=True).then_inc(h, 16))

    def indirect(self, pairs):
        self._barrier("gpsimd")
        for o, oo, i, io in pairs:
            self.dval["gpsimd"] += 16
            self.ops["gpsimd"].append(
                lambda e, o=o, oo=oo, i=i, io=io, h=self.dsem["gpsimd"]:
                e.indirect_dma_start(out=o, out_offset=oo, in_=i,
                                     in_offset=io).then_inc(h, 16))

    def collective(self, kind, op, groups, ins, outs):
        self._barrier("gpsimd")
        self.ccval += 1
        self.ops["gpsimd"].append(
            lambda e, k=kind, o=op, g=groups, i=ins, u=outs, h=self.ccsem:
            e.collective_compute(k, o, replica_groups=g, ins=i,
                                 outs=u).then_inc(h, 1))

    def wait_cc(self, eng, upto=None):
        v = self.ccval if upto is None else upto
        if v > self.seen_cc[eng]:
            self.ops[eng].append(lambda e, h=self.ccsem, v=v: e.wait_ge(h, v))
            self.seen_cc[eng] = v

    def raw(self, eng, fn):
        self.ops[eng].append(fn)

    def emit(self):
        ops = self.ops
        with self.nc.Block() as block:
            @block.sync
            def _(e):
                for f in ops["sync"]:
                    f(e)

            @block.scalar
            def _(e):
                for f in ops["scalar"]:
                    f(e)

            @block.vector
            def _(e):
                for f in ops["vector"]:
                    f(e)

            @block.tensor
            def _(e):
                for f in ops["tensor"]:
                    f(e)

            @block.gpsimd
            def _(e):
                for f in ops["gpsimd"]:
                    f(e)


class Pool:
    def __init__(self, t, n):
        self.t = t
        self.n = n
        self.off = 0
        self.hi = 0
        self.marks = []

    def tile(self, w):
        assert self.off + w <= self.n, f"pool overflow {self.off}+{w}"
        ap = self.t[:, self.off:self.off + w]
        self.off += w
        self.hi = max(self.hi, self.off)
        return ap

    def tiles(self, k, w):
        return [self.tile(w) for _ in range(k)]

    def push(self):
        self.marks.append(self.off)

    def pop(self):
        self.off = self.marks.pop()


def _finish_trunc(q, S, out, zerot, T):
    q.dma("gpsimd", [(out[tt * 128:(tt + 1) * 128, :], zerot[:, 0:1024])
                     for tt in range(T // 128)])
    S("gpsimd", lambda e: e.engine_nop())


def build_program(T=2048, TP=2048, n_ranks=NCORES, stop=99):
    nc = bass.Bass("TRN2", target_bir_lowering=False)
    st = ExitStack()
    q = SQ(nc, st)

    NCH = T // CH
    NCHP = TP // CH
    TALL = T * n_ranks
    NTILE_ALL = TALL // 128
    epc = E // n_ranks

    def din(name, shape, dt=F32):
        return nc.dram_tensor(name, shape, dt, kind="ExternalInput").ap()

    x_own = din("x_own", [T, D])
    x_prev = din("x_prev", [TP, D])
    flag = din("flag", [1, 1])
    c_row = din("c_row", [1, D])
    wmod = din("wmod", [D, 6 * D])
    ln1w = din("ln1w", [1, D]); ln1b = din("ln1b", [1, D])
    ln2w = din("ln2w", [1, D]); ln2b = din("ln2b", [1, D])
    normw = din("normw", [1, DI])
    mlpb1 = din("mlpb1", [1, MLP_H]); mlpb2 = din("mlpb2", [1, D])
    gateb = din("gateb", [1, E])
    a_log = din("a_log", [1, NH]); d_vec = din("d_vec", [1, NH])
    dtbias = din("dtbias", [1, NH])
    convw = din("convw", [CONVD, DCONV]); convb = din("convb", [CONVD, 1])
    w_ip = din("w_ip", [D, DIP])
    w_op = din("w_op", [DI, D])
    w_m1 = din("w_m1", [D, MLP_H])
    w_m2 = din("w_m2", [MLP_H, D])
    w_gt = din("w_gt", [D, E])
    w_s1 = din("w_s1", [D, SH_I]); w_s3 = din("w_s3", [D, SH_I])
    w_s2 = din("w_s2", [SH_I, D])
    w_e1 = din("w_e1", [epc, D, MOE_I])
    w_e3 = din("w_e3", [epc, D, MOE_I])
    w_e2 = din("w_e2", [epc, MOE_I, D])
    rank_i = din("rank_i", [1, 1], I32)

    out = nc.dram_tensor("out", [T, D], F32, kind="ExternalOutput").ap()

    def dint(name, shape, dt=F32):
        return nc.dram_tensor(name, shape, dt).ap()

    x_hbm = dint("x_hbm", [T, D])
    ag_xf_in = dint("ag_xf_in", [T, D])
    xf_all = dint("xf_all", [TALL + 128, D])
    ag_pay_in = dint("ag_pay_in", [T, 16])
    pay_all = dint("pay_all", [TALL, 16])
    tblh = dint("tblh", [E * TSTRIDE, 1])
    slotwf = dint("slotwf", [TALL, 16])
    cnts_h = dint("cnts_h", [1, E])
    tbl_own = dint("tbl_own", [E, 512])
    ag_yb_in = dint("ag_yb_in", [epc * CAP, D])
    yb_all = dint("yb_all", [E * CAP, D])
    z_hbm = dint("z_hbm", [T, D])

    POOLW = 53144
    poolt = st.enter_context(nc.sbuf_tensor("pool", [128, POOLW], F32))
    pool = Pool(poolt, POOLW)
    pooli = st.enter_context(nc.sbuf_tensor("pooli", [128, 64], I32))

    ones_row = pool.tile(128)
    ones_col = pool.tile(1)
    iot128 = pool.tile(128)
    iotc = pool.tile(1)
    ident = pool.tile(128)
    sltm = pool.tile(128)
    maskbig = pool.tile(128)
    modb = pool.tiles(6, D)
    lncol = pool.tiles(4, 8)
    b2b = pool.tile(D)
    gbB = pool.tile(E)
    nrmc = pool.tile(16)
    b1c = pool.tile(32)
    a128 = pool.tile(NH)
    d128 = pool.tile(NH)
    acol = pool.tile(1)
    nacol = pool.tile(1)
    dtbc = pool.tile(1)
    flagc = pool.tile(1)
    cw = pool.tiles(NXBC, DCONV)
    cb = pool.tiles(NXBC, 1)
    c_fm = pool.tile(8)
    state = pool.tile(NH * HD)
    tails = pool.tile(NXBC * 4)
    smallA = pool.tile(96)
    smallB = pool.tile(16)
    smallC = pool.tile(512)
    etimes = pool.tile(1)
    constc = pool.tile(4)
    baser = pool.tile(E)
    basec = pool.tile(1)

    persist_end = pool.off

    pool.push()
    srow = pool.tile(6 * D)
    zerot = pool.tile(2056)
    pool.pop()

    pool.push()
    xchunk = pool.tiles(4, D)
    htok = pool.tiles(4, D)
    x1tok = pool.tiles(4, D)
    hfm_big = pool.tile(8 * CH)
    hfm = [hfm_big[:, i * CH:(i + 1) * CH] for i in range(8)]
    zx = pool.tiles(NZX, CH)
    cvs = pool.tile(CH)
    ccopy = pool.tile(SC)
    wbufA = pool.tile(8 * 128)
    wbufB = pool.tile(8 * 128)
    wbufC = pool.tile(D)
    wbufD = pool.tile(D)
    xs_tok = hfm_big[:, 0:DI]
    y_tok = hfm_big[:, DI:2 * DI]
    bct = pool.tile(128)
    sdt = pool.tile(SC)
    sdtT = pool.tile(96)
    row0 = pool.tile(512)
    wfm = pool.tile(SC)
    dkc = pool.tile(1)
    dkrow = pool.tile(NH)
    asig = pool.tile(SC)
    aST = pool.tile(NH)
    easT = pool.tile(NH)
    dkerb = pool.tile(NH)
    scoresT = pool.tile(128)
    lpre = pool.tiles(4, 128)
    mtb = pool.tiles(4, 128)
    dtt = pool.tile(CH)
    onescN = pool.tile(SC)
    payb = pool.tiles(T // 128, 16)
    scorekeep = pool.tile(E)
    attn_end = pool.off
    assert attn_end < POOLW, attn_end

    print('pool hi', pool.hi, 'off', pool.off)
    pb = [st.enter_context(nc.psum_tensor(f"pb{i}", [128, 512], F32))
          for i in range(8)]

    def S(eng, *fns):
        q.step(eng, list(fns))

    def SL(eng, lag, *fns):
        q.step(eng, list(fns), lag=lag)

    def MM(psum, lhsT, rhs, start, stop):
        return lambda e, o=psum, l=lhsT, r=rhs, a=start, b=stop: \
            e.matmul(o, l, r, start=a, stop=b)

    def TR(psum, src, npart):
        return lambda e, o=psum, i=src, n=npart: \
            e.transpose(o, i, ident[0:n, 0:n])

    def CP(dst, src):
        return lambda e, d=dst, s=src: e.tensor_copy(d, s)

    def TT(dst, a, b, op):
        return lambda e, d=dst, a=a, b=b, o=op: e.tensor_tensor(d, a, b, o)

    def TS(dst, a, s1, s2, op0, op1=ALU.bypass):
        return lambda e, d=dst, a=a, s1=s1, s2=s2, o0=op0, o1=op1: \
            e.tensor_scalar(d, a, s1, s2, o0, o1)

    def STT(dst, in0, sc, in1, op0, op1):
        return lambda e, d=dst, i0=in0, s=sc, i1=in1, o0=op0, o1=op1: \
            e.scalar_tensor_tensor(d, i0, s, i1, o0, o1)

    def ACT(dst, src, f, bias=0.0, scale=1.0):
        return lambda e, d=dst, s=src, f=f, b=bias, sc=scale: \
            e.activation(d, s, f, bias=b, scale=sc)

    def RED(dst, src, op):
        return lambda e, d=dst, s=src, o=op: \
            e.tensor_reduce(d, s, mybir.AxisListType.X, o)

    def MS(dst, v):
        return lambda e, d=dst, v=v: e.memset(d, v)

    def bcast(psum, row, nparts):
        w = row.shape[-1]
        return MM(psum[0:nparts, 0:w], ones_row[0:1, 0:nparts], row,
                  True, True)

    q.dma("sync", [
        (c_fm, c_row.rearrange("o (q f) -> (o f) q", f=128)),
        (smallA[0:1, 0:NH], a_log[:, :]),
        (smallA[0:1, NH:2 * NH], d_vec[:, :]),
        (smallA[0:1, 2 * NH:3 * NH], dtbias[:, :]),
        (flagc[0:1, :], flag[:, :]),
        (nrmc, normw.rearrange("o (q f) -> (o f) q", f=128)),
        (b1c, mlpb1.rearrange("o (q f) -> (o f) q", f=128)),
        (smallC[0:1, 0:E], gateb[:, :]),
        (lncol[0], ln1w.rearrange("o (q f) -> (o f) q", f=128)),
        (lncol[1], ln1b.rearrange("o (q f) -> (o f) q", f=128)),
        (lncol[2], ln2w.rearrange("o (q f) -> (o f) q", f=128)),
        (lncol[3], ln2b.rearrange("o (q f) -> (o f) q", f=128)),
        (srow[0:1, 4 * D:5 * D], mlpb2[:, :]),
    ] + [(cw[i], convw[i * 128:(i + 1) * 128, :]) for i in range(NXBC)]
      + [(cb[i], convb[i * 128:(i + 1) * 128, :]) for i in range(NXBC)])

    S("vector", MS(ident, 1.0))
    S("vector", lambda e: e.tensor_tensor_scan(iot128, ident, ident, 0.0,
                                               ALU.mult, ALU.add))
    S("vector", TS(iot128, iot128, 1.0, None, ALU.subtract))
    S("tensor", TR(pb[0][0:128, 0:1], iot128[0:1, 0:128], 1))
    S("vector", CP(iotc, pb[0][0:128, 0:1]))
    S("vector",
      MS(ones_row[0:1, :], 1.0), MS(ones_col, 1.0), MS(onescN[0:NH, :], 1.0),
      MS(zerot, 0.0), MS(state[0:64, :], 0.0), MS(tails, 0.0),
      MS(baser[0:1, :], 0.0), MS(basec[0:64, :], 0.0),
      TS(ident, iot128, iotc, None, ALU.is_equal),
      TS(sltm, iot128, iotc, None, ALU.is_gt),
      TS(maskbig, iot128, iotc, None, ALU.is_ge),
      TS(etimes[0:64, :], iotc[0:64, :], float(TSTRIDE), None, ALU.mult),
      MS(constc[:, 0:1], float(EPS)), MS(constc[:, 1:2], 1.0),
      MS(constc[:, 2:3], -1.0), MS(constc[:, 3:4], 1.0 / DI))
    S("vector", TS(maskbig, maskbig, 1.0, 1e30, ALU.subtract, ALU.mult))
    S("scalar", ACT(smallA[0:1, 0:NH], smallA[0:1, 0:NH], AF.Exp))
    S("scalar", ACT(smallA[0:1, 0:NH], smallA[0:1, 0:NH], AF.Copy, scale=constc[0:1, 2:3]),
      ACT(c_fm, c_fm, AF.Silu))
    ztb = tblh.rearrange("(a b) o -> a (b o)", b=8224)
    q.dma("gpsimd", [(ztb[i * 128:(i + 1) * 128, j * 1028:(j + 1) * 1028],
                      zerot[:, 0:1028]) for i in range(1) for j in range(8)])
    q.dma("gpsimd", [(xf_all[TALL:TALL + 128, :], zerot[:, 0:1024])])

    bc_jobs = [(b2b, srow[0:1, 4 * D:5 * D]), (gbB, smallC[0:1, 0:E]),
               (a128, smallA[0:1, 0:NH]), (d128, smallA[0:1, NH:2 * NH])]
    for i, (dst, row) in enumerate(bc_jobs):
        w = row.shape[-1]
        for si in range((w + 511) // 512):
            lo, hi = si * 512, min(w, (si + 1) * 512)
            S("tensor", bcast(pb[0], row[:, lo:hi], 128))
            S("vector", CP(dst[:, lo:hi], pb[0][0:128, 0:hi - lo]))
    S("tensor", TR(pb[1][0:NH, 0:1], smallA[0:1, 0:NH], 1))
    S("vector", CP(acol[0:NH, :], pb[1][0:NH, 0:1]))
    S("vector", TS(nacol[0:NH, :], acol[0:NH, :], -1.0, None, ALU.mult))
    S("tensor", TR(pb[1][0:NH, 0:1], smallA[0:1, 2 * NH:3 * NH], 1))
    S("vector", CP(dtbc[0:NH, :], pb[1][0:NH, 0:1]))
    S("tensor", bcast(pb[1], flagc[0:1, 0:1], 128))
    S("vector", CP(flagc, pb[1][0:128, 0:1]))

    if stop < 1:
        _finish_trunc(q, S, out, zerot, T)
        q.emit(); st.close(); return nc
    NW = 6 * D // 512
    for nwi in range(NW):
        for kc in range(8):
            wb = wbufC if kc % 2 == 0 else wbufD
            q.dma("sync", [(wb[:, 0:512],
                            wmod[kc * 128:(kc + 1) * 128,
                                 nwi * 512:(nwi + 1) * 512])])
            S("tensor", MM(pb[2][0:1, 0:512], c_fm[:, kc:kc + 1],
                           wb[:, 0:512], kc == 0, kc == 7))
        S("vector", CP(srow[0:1, nwi * 512:(nwi + 1) * 512],
                       pb[2][0:1, 0:512]))
    for gi in (2, 5):
        sl = srow[0:1, gi * D:(gi + 1) * D]
        s0 = zerot[0:1, 0:D]
        s1 = zerot[0:1, D:2 * D]
        S("vector", TS(s0, sl, 0.0, 2.0, ALU.is_ge, ALU.mult),
          TS(s1, sl, -1.0, None, ALU.mult))
        S("vector", TS(s0, s0, 1.0, None, ALU.subtract),
          TT(s1, s1, sl, ALU.max))
        S("vector", TS(s1, s1, 0.25, None, ALU.max))
        S("vector", TT(sl, s0, s1, ALU.mult))
    S("vector", TS(srow[0:1, D:2 * D], srow[0:1, D:2 * D], 1.0, None, ALU.add),
      TS(srow[0:1, 4 * D:5 * D], srow[0:1, 4 * D:5 * D], 1.0, None, ALU.add))
    for i in range(6):
        row = srow[0:1, i * D:(i + 1) * D]
        for si in range(2):
            lo, hi = si * 512, (si + 1) * 512
            S("tensor", bcast(pb[0], row[:, lo:hi], 128))
            S("vector", CP(modb[i][:, lo:hi], pb[0][0:128, 0:512]))

    def ln_tiles(src_tiles, dst_tiles, mul_tile, add_tile, nt):
        for tt in range(nt):
            S("vector", lambda e, s=src_tiles[tt]: e.bn_stats(
                smallB[0:128, 0:6], s[:, 0:512]),
              lambda e, s=src_tiles[tt]: e.bn_stats(
                smallB[0:128, 6:12], s[:, 512:1024]))
            S("vector", lambda e: e.bn_aggr(smallB[0:128, 8:10] if False else
                                            smallB[0:128, 12:14],
                                            smallB[0:128, 0:12]))
            S("scalar", ACT(smallB[0:128, 10:11], smallB[0:128, 13:14],
                            AF.Sqrt, bias=constc[:, 0:1]))
            S("vector", lambda e: e.reciprocal(smallB[0:128, 10:11],
                                               smallB[0:128, 10:11]))
            S("vector",
              TS(dst_tiles[tt], src_tiles[tt], smallB[0:128, 12:13],
                 smallB[0:128, 10:11], ALU.subtract, ALU.mult))
            if mul_tile is not None:
                S("vector", TT(dst_tiles[tt], dst_tiles[tt], mul_tile,
                               ALU.mult))
                S("vector", TT(dst_tiles[tt], dst_tiles[tt], add_tile,
                               ALU.add))

    def to_fm(src_tiles, dst_fm_tiles, nf, nt, wcol=None, bcol=None):
        jobs = []
        for fc in range(nf):
            for tt in range(nt):
                jobs.append((fc, tt))
        for j0 in range(0, len(jobs), 4):
            grp = jobs[j0:j0 + 4]
            S("tensor", *[TR(pb[4 + i][0:128, 0:128],
                             src_tiles[tt][:, fc * 128:(fc + 1) * 128], 128)
                          for i, (fc, tt) in enumerate(grp)])
            if wcol is None:
                S("vector", *[CP(dst_fm_tiles[fc][:, tt * 128:(tt + 1) * 128],
                                 pb[4 + i][0:128, 0:128])
                              for i, (fc, tt) in enumerate(grp)])
            else:
                S("vector", *[TS(dst_fm_tiles[fc][:, tt * 128:(tt + 1) * 128],
                                 pb[4 + i][0:128, 0:128],
                                 wcol[:, fc:fc + 1], bcol[:, fc:fc + 1],
                                 ALU.mult, ALU.add)
                              for i, (fc, tt) in enumerate(grp)])

    def patternA(wdram, m0t, m1t, out_cb, Tn, kparts=8, mtot=None,
                 floors=None):
        if mtot is None:
            mtot = m1t * 128
        mcs = list(range(m0t, m1t))
        wid = {mc: min(128, mtot - mc * 128) for mc in mcs}
        wsrc = wdram.rearrange("(kc p) m -> p kc m", p=128)
        steps_done = 0
        buf_free_at = {0: None, 1: None}
        if floors is None:
            fl = {"vector": q.cval["vector"], "scalar": q.cval["scalar"],
                  "gpsimd": q.cval["gpsimd"]}
        else:
            fl = dict(floors)
        evac_pos = []
        for bi, mc in enumerate(mcs):
            bsel = bi % 2
            wb = wbufA if bsel == 0 else wbufB
            mw = wid[mc]
            fa = buf_free_at[bsel]
            q.dma("sync", [(
                wb[:, 0:kparts * mw].rearrange("p (kc m) -> p kc m", kc=kparts),
                wsrc[:, 0:kparts, mc * 128:mc * 128 + mw])],
                lag=None if fa is None else
                {"tensor": steps_done - fa, "vector": 99, "scalar": 99,
                 "gpsimd": 99})
            buf_free_at[bsel] = steps_done + 1
            bank = pb[mc % 4]
            if bi < 4:
                cap = dict(fl)
            else:
                ep = evac_pos[bi - 4]
                cap = {k: max(fl.get(k, 0), ep.get(k, 0))
                       for k in ("vector", "scalar", "gpsimd")}
            q.step("tensor",
                   [MM(bank[0:mw, 0:Tn],
                       wb[:, kc * mw:kc * mw + mw],
                       hfm[kc][:, 0:Tn], kc == 0, kc == kparts - 1)
                    for kc in range(kparts)], cap=cap)
            steps_done += 1
            out_cb(mc, bank[0:mw, 0:Tn])
            evac_pos.append({"vector": q.cval["vector"],
                             "scalar": q.cval["scalar"],
                             "gpsimd": q.cval["gpsimd"]})

    def conv_chunk(Tn):
        xbc = [cvs] + [zx[15 + i] for i in range(1, NXBC)]
        S("vector", *[CP(smallC[0:128, i * 3:(i + 1) * 3],
                         zx[16 + i][:, Tn - 3:Tn]) for i in range(NXBC)])
        for i in range(NXBC):
            srct = zx[16 + i]
            dstt = xbc[i]
            ops1 = [TS(dstt[:, 0:Tn], srct[:, 0:Tn], cw[i][:, 3:4], None,
                       ALU.mult)]
            S("vector", *ops1)
            for k in range(3):
                S("vector", STT(dstt[:, 3 - k:Tn], srct[:, 0:Tn - 3 + k],
                                cw[i][:, k:k + 1], dstt[:, 3 - k:Tn],
                                ALU.mult, ALU.add))
                tops = []
                for t in range(0, 3 - k):
                    tops.append(STT(dstt[:, t:t + 1],
                                    tails[:, i * 4 + t + k:i * 4 + t + k + 1],
                                    cw[i][:, k:k + 1], dstt[:, t:t + 1],
                                    ALU.mult, ALU.add))
                if tops:
                    S("vector", *tops)
        S("vector", *[CP(tails[:, i * 4:i * 4 + 3],
                         smallC[0:128, i * 3:(i + 1) * 3])
                      for i in range(NXBC)])
        S("scalar", *[ACT(xbc[i][:, 0:Tn], xbc[i][:, 0:Tn], AF.Silu,
                          bias=cb[i][:, :]) for i in range(NXBC)])
        return xbc

    def dt_chunk(Tn):
        dts = smallC
        S("scalar", ACT(dtt[0:NH, 0:Tn], zx[33][0:NH, 0:Tn], AF.Abs,
                        bias=dtbc[0:NH, :]))
        S("scalar", ACT(dts[0:NH, 0:Tn], dtt[0:NH, 0:Tn], AF.Exp, scale=constc[0:NH, 2:3]))
        S("scalar", ACT(dts[0:NH, 0:Tn], dts[0:NH, 0:Tn], AF.Ln, bias=constc[0:NH, 1:2]))
        S("scalar", ACT(dtt[0:NH, 0:Tn], zx[33][0:NH, 0:Tn], AF.Relu,
                        bias=dtbc[0:NH, :]))
        S("vector", TT(dtt[0:NH, 0:Tn], dtt[0:NH, 0:Tn], dts[0:NH, 0:Tn],
                       ALU.add))

    def ssd_subchunk(xbc, o, partial):
        bc_tile = xbc[16]
        q.dma("gpsimd", [(ccopy[0:64, 0:SC], bc_tile[64:128, o:o + SC])])
        S("vector",
          lambda e: e.tensor_tensor_scan(sdt[0:NH, 0:SC], onescN[0:NH, 0:SC],
                                         dtt[0:NH, o:o + SC], 0.0,
                                         ALU.mult, ALU.add))
        S("vector", TS(asig[0:NH, 0:SC], sdt[0:NH, 0:SC], acol[0:NH, :],
                       None, ALU.mult),
          TS(wfm[0:NH, 0:SC], sdt[0:NH, 0:SC],
             sdt[0:NH, SC - 1:SC], nacol[0:NH, :], ALU.subtract, ALU.mult))
        S("scalar", ACT(wfm[0:NH, 0:SC], wfm[0:NH, 0:SC], AF.Exp))
        S("vector", TT(wfm[0:NH, 0:SC], wfm[0:NH, 0:SC],
                       dtt[0:NH, o:o + SC], ALU.mult),
          TS(dkc[0:NH, :], sdt[0:NH, SC - 1:SC], acol[0:NH, :], None,
             ALU.mult))
        S("scalar", ACT(dkc[0:NH, :], dkc[0:NH, :], AF.Exp))
        S("tensor", TR(pb[4][0:SC, 0:NH], sdt[0:NH, 0:SC], NH),
          TR(pb[5][0:SC, 0:NH], dtt[0:NH, o:o + SC], NH),
          TR(pb[6][0:SC, 0:NH], wfm[0:NH, 0:SC], NH),
          TR(pb[7][0:1, 0:NH], dkc[0:NH, :], NH))
        S("vector", CP(sdtT[:, 0:NH], pb[4][0:SC, 0:NH]),
          CP(sdtT[:, NH:2 * NH], pb[5][0:SC, 0:NH]),
          CP(sdtT[:, 2 * NH:3 * NH], pb[6][0:SC, 0:NH]),
          CP(dkrow[0:1, 0:NH], pb[7][0:1, 0:NH]))
        S("vector", TT(aST[:, 0:NH], sdtT[:, 0:NH], a128, ALU.mult))
        S("scalar", ACT(easT[:, 0:NH], aST[:, 0:NH], AF.Exp))
        S("tensor", bcast(pb[6], dkrow[0:1, 0:NH], 64))
        S("vector", CP(dkerb[0:64, 0:NH], pb[6][0:64, 0:NH]))
        S("tensor", TR(pb[7][0:128, 0:128], bc_tile[:, o:o + SC], 128))
        S("vector", CP(bct[:, 0:128], pb[7][0:128, 0:128]))
        for f0 in range(0, 16, 4):
            S("tensor", *[TR(pb[4 + i][0:128, 0:128],
                             xbc[f0 + i][:, o:o + SC], 128) for i in range(4)])
            S("vector", *[CP(xs_tok[:, (f0 + i) * 128:(f0 + i + 1) * 128],
                             pb[4 + i][0:128, 0:128]) for i in range(4)])
        if not partial:
            S("tensor", MM(pb[6][0:128, 0:128], bc_tile[0:64, o:o + SC],
                           ccopy[0:64, 0:SC], True, True))
            S("vector", CP(scoresT[:, 0:128], pb[6][0:128, 0:128]))

        for hg in range(NH // 4):
            hs = hg * 4
            if not partial:
                S("tensor", *[TR(pb[4 + j][0:1, 0:128],
                                 aST[:, hs + j:hs + j + 1], 128)
                              for j in range(4)])
                S("vector", *[CP(row0[0:1, j * 128:(j + 1) * 128],
                                 pb[4 + j][0:1, 0:128]) for j in range(4)])
                S("tensor", *[MM(pb[0][0:128, j * 128:(j + 1) * 128],
                                 ones_row[0:1, 0:128],
                                 row0[0:1, j * 128:(j + 1) * 128],
                                 j == 0, j == 3) for j in range(4)])
                S("vector", *[TS(lpre[j], pb[0][0:128, j * 128:(j + 1) * 128],
                                 aST[:, hs + j:hs + j + 1], None,
                                 ALU.subtract) for j in range(4)])
                S("vector", *[TT(lpre[j], lpre[j], maskbig, ALU.min)
                              for j in range(4)])
                S("scalar", *[ACT(lpre[j], lpre[j], AF.Exp) for j in range(4)])
                S("vector", *[STT(mtb[j], lpre[j],
                                  sdtT[:, NH + hs + j:NH + hs + j + 1],
                                  scoresT, ALU.mult, ALU.mult)
                              for j in range(4)])
                S("tensor",
                  *[MM(pb[2][0:128, j * 64:(j + 1) * 64], mtb[j],
                       xs_tok[:, (hs + j) * 64:(hs + j + 1) * 64],
                       j == 0, j == 3) for j in range(4)],
                  *[MM(pb[3][0:128, j * 64:(j + 1) * 64],
                       ccopy[0:64, 0:SC],
                       state[0:64, (hs + j) * 64:(hs + j + 1) * 64],
                       j == 0, j == 3) for j in range(4)])
                S("vector", *[CP(y_tok[:, (hs + j) * 64:(hs + j + 1) * 64],
                                 pb[2][0:128, j * 64:(j + 1) * 64])
                              for j in range(4)])
                S("vector", *[STT(y_tok[:, (hs + j) * 64:(hs + j + 1) * 64],
                                  pb[3][0:128, j * 64:(j + 1) * 64],
                                  easT[:, hs + j:hs + j + 1],
                                  y_tok[:, (hs + j) * 64:(hs + j + 1) * 64],
                                  ALU.mult, ALU.add) for j in range(4)])
                S("vector", *[STT(y_tok[:, (hs + j) * 64:(hs + j + 1) * 64],
                                  xs_tok[:, (hs + j) * 64:(hs + j + 1) * 64],
                                  d128[:, hs + j:hs + j + 1],
                                  y_tok[:, (hs + j) * 64:(hs + j + 1) * 64],
                                  ALU.mult, ALU.add) for j in range(4)])
            S("vector", *[TS(xs_tok[:, (hs + j) * 64:(hs + j + 1) * 64],
                             xs_tok[:, (hs + j) * 64:(hs + j + 1) * 64],
                             sdtT[:, 2 * NH + hs + j:2 * NH + hs + j + 1],
                             None, ALU.mult)
                          for j in range(4)])
            S("tensor", *[MM(pb[1][0:64, j * 64:(j + 1) * 64], bct[:, 0:64],
                             xs_tok[:, (hs + j) * 64:(hs + j + 1) * 64],
                             j == 0, j == 3) for j in range(4)])
            S("vector", *[STT(state[0:64, (hs + j) * 64:(hs + j + 1) * 64],
                              state[0:64, (hs + j) * 64:(hs + j + 1) * 64],
                              dkerb[0:64, hs + j:hs + j + 1],
                              pb[1][0:64, j * 64:(j + 1) * 64],
                              ALU.mult, ALU.add) for j in range(4)])

    def attn_chunk(c, partial, xsrc):
        nt = CH // 128
        q.dma("sync", [(xchunk[tt], xsrc[c * CH + tt * 128:
                                         c * CH + (tt + 1) * 128, :])
                       for tt in range(nt)])
        ln_tiles(xchunk, htok, modb[1], modb[0], nt)
        ln_tiles(htok, x1tok, None, None, nt)
        to_fm(x1tok, hfm, 8, nt, lncol[0], lncol[1])
        def ip_cb(mc, ps):
            S("vector", CP(zx[mc][0:(32 if mc == NZX - 1 else 128), 0:CH], ps))
        patternA(w_ip, 16, NZX, ip_cb, CH, mtot=DIP)
        post_xbc = {"vector": q.cval["vector"], "scalar": q.cval["scalar"],
                    "gpsimd": q.cval["gpsimd"]}
        xbc = conv_chunk(CH)
        dt_chunk(CH)
        if not partial:
            patternA(w_ip, 0, 16, ip_cb, CH, mtot=DIP, floors=post_xbc)
        for sci in range(CH // SC):
            ssd_subchunk(xbc, sci * SC, partial)
            if not partial:
                post_ssd_subchunk(c, sci, xbc)
        if not partial:
            post_chunk(c, nt)

    def post_ssd_subchunk(c, sci, xbc):
        o = sci * SC
        S("scalar", *[ACT(zx[fc][:, o:o + SC], zx[fc][:, o:o + SC], AF.Silu)
                      for fc in range(16)])
        for f0 in range(0, 16, 4):
            S("tensor", *[TR(pb[4 + i][0:128, 0:128],
                             y_tok[:, (f0 + i) * 128:(f0 + i + 1) * 128], 128)
                          for i in range(4)])
            S("vector", *[TT(zx[f0 + i][:, o:o + SC],
                             pb[4 + i][0:128, 0:128],
                             zx[f0 + i][:, o:o + SC], ALU.mult)
                          for i in range(4)])
        sqops = []
        mmops = []
        for fc in range(16):
            sqops.append(TT(lpre[fc % 4], zx[fc][:, o:o + SC],
                            zx[fc][:, o:o + SC], ALU.mult))
            mmops.append(MM(pb[6][0:1, 0:SC], ones_col[0:128, :],
                            lpre[fc % 4], fc == 0, fc == 15))
        for f0 in range(0, 16, 4):
            S("vector", *sqops[f0:f0 + 4])
            S("tensor", *mmops[f0:f0 + 4])
        S("scalar", ACT(smallC[0:1, 0:SC], pb[6][0:1, 0:SC], AF.Sqrt,
                        bias=constc[0:1, 0:1], scale=constc[0:1, 3:4]))
        S("vector", lambda e: e.reciprocal(smallC[0:1, 0:SC],
                                           smallC[0:1, 0:SC]))
        S("tensor", bcast(pb[7], smallC[0:1, 0:SC], 128))
        S("vector", *[STT(zx[fc][:, o:o + SC], pb[7][0:128, 0:SC],
                          nrmc[:, fc:fc + 1], zx[fc][:, o:o + SC],
                          ALU.mult, ALU.mult) for fc in range(16)])

    def post_chunk(c, nt):
        for kc in range(16):
            wb = wbufC if kc % 2 == 0 else wbufD
            q.dma("sync", [(wb[:, 0:D], w_op[kc * 128:(kc + 1) * 128, :])],
                  lag=None if kc <= 1 else {"tensor": 1, "vector": 99, "scalar": 99, "gpsimd": 99})
            q.step("tensor", [MM(pb[tt * 2 + si][0:128, 0:512],
                                 zx[kc][:, tt * 128:(tt + 1) * 128],
                                 wb[:, si * 512:(si + 1) * 512],
                                 kc == 0, kc == 15)
                              for tt in range(nt) for si in range(2)],
                   lag=None if kc == 0 else {"vector": 99, "scalar": 99})
        S("vector", *[TT(x1tok[tt][:, si * 512:(si + 1) * 512],
                         pb[tt * 2 + si][0:128, 0:512],
                         htok[tt][:, si * 512:(si + 1) * 512], ALU.add)
                      for tt in range(nt) for si in range(2)])
        ln_tiles(x1tok, htok, None, None, nt)
        to_fm(htok, hfm, 8, nt, lncol[2], lncol[3])
        patternA(w_m1, 0, 32,
                 lambda mc, ps: S("scalar",
                                  ACT(zx[mc][:, 0:CH], ps, AF.Gelu,
                                      bias=b1c[:, mc:mc + 1])), CH)
        for kc in range(32):
            wb = wbufC if kc % 2 == 0 else wbufD
            q.dma("sync", [(wb[:, 0:D], w_m2[kc * 128:(kc + 1) * 128, :])],
                  lag=None if kc <= 1 else {"tensor": 1, "vector": 99, "scalar": 99, "gpsimd": 99})
            q.step("tensor", [MM(pb[tt * 2 + si][0:128, 0:512],
                                 zx[kc][:, tt * 128:(tt + 1) * 128],
                                 wb[:, si * 512:(si + 1) * 512],
                                 kc == 0, kc == 31)
                              for tt in range(nt) for si in range(2)],
                   lag=None if kc == 0 else {"vector": 99, "scalar": 99})
        S("vector", *[TT(htok[tt][:, si * 512:(si + 1) * 512],
                         pb[tt * 2 + si][0:128, 0:512],
                         b2b[:, si * 512:(si + 1) * 512], ALU.add)
                      for tt in range(nt) for si in range(2)])
        S("vector", *[TT(htok[tt], htok[tt], x1tok[tt], ALU.add)
                      for tt in range(nt)])
        S("vector", *[TT(htok[tt], htok[tt], modb[2], ALU.mult)
                      for tt in range(nt)])
        S("vector", *[TT(xchunk[tt], xchunk[tt], htok[tt], ALU.add)
                      for tt in range(nt)])
        q.dma("gpsimd", [(x_hbm[c * CH + tt * 128:c * CH + (tt + 1) * 128, :],
                          xchunk[tt]) for tt in range(nt)])
        ln_tiles(xchunk, htok, modb[4], modb[3], nt)
        q.dma("gpsimd", [(ag_xf_in[c * CH + tt * 128:c * CH + (tt + 1) * 128, :],
                          htok[tt]) for tt in range(nt)])
        to_fm(htok, hfm, 8, nt)
        gate_topk(c, nt)

    def gate_topk(c, nt):
        for tt in range(nt):
            wb = wbufC
            if tt == 0:
                q.dma("sync", [(wbufC[:, kc * 64:(kc + 1) * 64],
                                w_gt[kc * 128:(kc + 1) * 128, :])
                               for kc in range(8)])
            S("tensor", *[MM(pb[4][0:128, 0:E],
                             hfm[kc][:, tt * 128:(tt + 1) * 128],
                             wbufC[:, kc * 64:(kc + 1) * 64],
                             kc == 0, kc == 7) for kc in range(8)])
            S("vector", RED(smallB[0:128, 0:1], pb[4][0:128, 0:E], ALU.max))
            S("vector", TS(smallB[0:128, 1:2], smallB[0:128, 0:1], -1.0,
                           None, ALU.mult))
            S("scalar", lambda e: e.activation(lpre[0][:, 0:E], pb[4][0:128, 0:E],
                                               AF.Exp, bias=smallB[0:128, 1:2]))
            S("vector", RED(smallB[0:128, 2:3], lpre[0][:, 0:E], ALU.add))
            S("vector", lambda e: e.reciprocal(smallB[0:128, 3:4],
                                               smallB[0:128, 2:3]))
            S("vector", TS(scorekeep[:, 0:E], lpre[0][:, 0:E],
                           smallB[0:128, 3:4], None, ALU.mult))
            S("vector", TT(lpre[1][:, 0:E], scorekeep[:, 0:E], gbB, ALU.add))
            pt = payb[c * nt + tt]
            S("vector", MS(pt, 0.0))
            for k in range(TOPK):
                S("vector", RED(smallB[0:128, 4:5], lpre[1][:, 0:E], ALU.max))
                S("vector", TS(lpre[2][:, 0:E], lpre[1][:, 0:E],
                               smallB[0:128, 4:5], None, ALU.is_equal))
                S("vector", TT(lpre[3][:, 0:E], lpre[2][:, 0:E], iot128[:, 0:64],
                               ALU.mult),
                  TS(mtb[0][:, 0:E], lpre[2][:, 0:E], 1.0, -64.0,
                     ALU.subtract, ALU.mult))
                S("vector", TT(lpre[3][:, 0:E], lpre[3][:, 0:E],
                               mtb[0][:, 0:E], ALU.add))
                S("vector", RED(pt[:, k:k + 1], lpre[3][:, 0:E], ALU.min))
                S("vector", TS(lpre[2][:, 0:E], iot128[:, 0:64], pt[:, k:k + 1], None,
                               ALU.is_equal))
                S("vector", TT(lpre[3][:, 0:E], lpre[2][:, 0:E],
                               scorekeep[:, 0:E], ALU.mult))
                S("vector", RED(pt[:, 8 + k:9 + k], lpre[3][:, 0:E], ALU.add))
                S("vector", STT(lpre[1][:, 0:E], lpre[2][:, 0:E], -1e30,
                                lpre[1][:, 0:E], ALU.mult, ALU.add))
        q.dma("gpsimd", [(ag_pay_in[c * CH + tt * 128:c * CH + (tt + 1) * 128, :],
                          payb[c * nt + tt]) for tt in range(nt)])

    if stop < 2:
        _finish_trunc(q, S, out, zerot, T)
        q.emit(); st.close(); return nc
    for c in range(NCHP):
        attn_chunk(c, True, x_prev)
    S("vector", TS(state[0:64, :], state[0:64, :], flagc[0:64, :], None,
                   ALU.mult),
      TS(tails, tails, flagc, None, ALU.mult))
    if stop < 3:
        _finish_trunc(q, S, out, zerot, T)
        q.emit(); st.close(); return nc
    for c in range(NCH):
        attn_chunk(c, False, x_own)

    if stop < 4:
        _finish_trunc(q, S, out, zerot, T)
        q.emit(); st.close(); return nc
    q.collective("AllGather", ALU.bypass, [list(range(n_ranks))],
                 [ag_pay_in.opt()], [pay_all.opt()])
    cc_pay = q.ccval
    q.collective("AllGather", ALU.bypass, [list(range(n_ranks))],
                 [ag_xf_in.opt()], [xf_all[0:TALL, :].opt()])
    cc_xf = q.ccval

    pool.pop()
    pool.push()
    ptile = pool.tile(16)
    otile = pool.tile(E)
    cex = pool.tile(E)
    ptab = pool.tile(E)
    msk = [pool.tile(E) for _ in range(TOPK)]
    colk = pool.tile(16)
    swf = pool.tile(16)
    ctile = pool.tile(E)
    tmpe = [pool.tile(E) for _ in range(2)]
    ofs = pool.tile(1)
    ofs_i = pooli[:, 0:1]

    if stop < 5:
        _finish_trunc(q, S, out, zerot[0:128, 0:1024] if False else zerot, T)
        q.emit(); st.close(); return nc
    q.wait_cc("sync", cc_pay)
    q.wait_cc("gpsimd", cc_pay)
    S("vector", MS(swf, 0.0))
    for ti in range(NTILE_ALL):
        q.dma("sync", [(ptile, pay_all[ti * 128:(ti + 1) * 128, :])])
        S("vector", *[TS(msk[k], iot128[:, 0:64], ptile[:, k:k + 1], None, ALU.is_equal)
                      for k in range(TOPK)],
          TS(colk[:, 15:16], iotc, float(ti * 128), None, ALU.add))
        S("vector", TT(otile, msk[0], msk[1], ALU.add),
          TT(tmpe[0], msk[2], msk[3], ALU.add),
          TT(tmpe[1], msk[4], msk[5], ALU.add))
        S("vector", TT(otile, otile, tmpe[0], ALU.add))
        S("vector", TT(otile, otile, tmpe[1], ALU.add))
        S("vector", TT(ofs[0:64, :], basec[0:64, :], etimes[0:64, :], ALU.add))
        S("vector", CP(ofs_i[0:64, :], ofs[0:64, :]))
        S("tensor", MM(pb[0][0:128, 0:E], sltm, otile, True, False),
          MM(pb[0][0:128, 0:E], ones_row[0:1, 0:128].rearrange("a b -> a b"),
             baser[0:1, 0:E], False, True))
        S("vector", CP(ptab, pb[0][0:128, 0:E]))
        S("tensor", MM(pb[0][0:128, 0:E], sltm, otile, True, True))
        S("vector", CP(cex, pb[0][0:128, 0:E]))
        S("tensor", MM(pb[1][0:1, 0:E], ones_col[0:128, :], otile,
                       True, True),
          MM(pb[2][0:64, 0:1], otile[:, 0:64], ones_col[0:128, :],
             True, True))
        for k in range(TOPK):
            S("vector", TT(tmpe[0], msk[k], cex, ALU.mult),
              TT(tmpe[1], msk[k], ptab, ALU.mult))
            S("vector", RED(colk[:, k:k + 1], tmpe[0], ALU.add),
              RED(colk[:, 8 + k:9 + k], tmpe[1], ALU.add))
        for k in range(TOPK):
            S("vector", TS(tmpe[0], iot128[:, 0:64], colk[:, k:k + 1], None,
                           ALU.is_equal))
            S("vector", TS(tmpe[0], tmpe[0], colk[:, 15:16], None, ALU.mult))
            S("tensor", MM(pb[3][0:64, 0:E], msk[k], tmpe[0],
                           k == 0, k == TOPK - 1))
        S("vector", CP(ctile[0:64, :], pb[3][0:64, 0:E]))
        q.indirect([(tblh[:, :], IndirectOffsetOnAxis(ofs_i[0:64, :], 0),
                     ctile[0:64, :], None)])
        S("vector", TT(baser[0:1, 0:E], baser[0:1, 0:E], pb[1][0:1, 0:E],
                       ALU.add),
          TT(basec[0:64, :], basec[0:64, :], pb[2][0:64, 0:1], ALU.add))
        S("vector", *[TS(swf[:, 8 + k:9 + k], colk[:, 8 + k:9 + k], 511.0,
                         None, ALU.min) for k in range(TOPK)])
        S("vector", *[STT(swf[:, k:k + 1], ptile[:, k:k + 1], 512.0,
                          swf[:, 8 + k:9 + k], ALU.mult, ALU.add)
                      for k in range(TOPK)])
        S("vector", *[TS(tmpe[0][:, k:k + 1], colk[:, 8 + k:9 + k], 512.0,
                         None, ALU.is_lt) for k in range(TOPK)])
        S("vector", *[TT(swf[:, 8 + k:9 + k], ptile[:, 8 + k:9 + k],
                         tmpe[0][:, k:k + 1], ALU.mult) for k in range(TOPK)])
        q.dma("gpsimd", [(slotwf[ti * 128:(ti + 1) * 128, :], swf)])
    q.dma("gpsimd", [(cnts_h[:, :], baser[0:1, 0:E])])

    if stop < 6:
        _finish_trunc(q, S, out, zerot, T)
        q.emit(); st.close(); return nc
    pool.pop()
    pool.push()
    xbt = pool.tiles(4, D)
    xbfm = pool.tiles(8, 512)
    he = pool.tiles(11, 512)
    ybt = pool.tiles(4, D)
    ewA = pool.tile(8 * 128)
    ewB = pool.tile(8 * 128)
    ewC = pool.tile(D)
    ewD = pool.tile(D)
    idxv = pool.tile(8)
    cnt8 = pool.tile(E)
    vld = pool.tile(4)
    swft = pool.tile(16)
    rankc = pool.tile(1)
    ofs = pool.tile(1)
    tbl_own_sb = smallC
    xmt = pool.tile(D)
    zt = pool.tile(D)
    gath = pool.tile(D)
    acc = pool.tile(D)
    moe_end = pool.off
    idx_i4 = pooli[:, 4:8]
    slot_i = pooli[:, 8:9]

    rank_s = pooli[:, 12:13]
    q.dma("gpsimd", [(rank_s[0:1, :], rank_i[:, :])])
    S("vector", CP(smallB[0:1, 14:15], rank_s[0:1, :]))
    S("tensor", bcast(pb[0], smallB[0:1, 14:15], 128))
    S("vector", CP(rankc, pb[0][0:128, 0:1]))
    S("vector", TS(ofs[0:epc, :], rankc[0:epc, :], float(epc), None,
                   ALU.mult))
    S("vector", TT(ofs[0:epc, :], ofs[0:epc, :], iotc[0:epc, :], ALU.add))
    S("vector", CP(pooli[0:epc, 13:14], ofs[0:epc, :]))
    cnts_flat = cnts_h.rearrange("o e -> (o e)")[:, None]
    q.indirect([(smallC[0:epc, 0:1], None, cnts_flat,
                 IndirectOffsetOnAxis(pooli[0:epc, 13:14], 0))])
    S("tensor", TR(pb[0][0:1, 0:epc], smallC[0:epc, 0:1], epc))
    S("vector", CP(smallC[0:1, 4:4 + epc], pb[0][0:1, 0:epc]))
    S("tensor", bcast(pb[0], smallC[0:1, 4:4 + epc], 128))
    S("vector", CP(cnt8[:, 0:epc], pb[0][0:128, 0:epc]))
    S("vector", TS(ofs[0:epc, :], rankc[0:epc, :], float(epc), None,
                   ALU.mult))
    S("vector", TT(ofs[0:epc, :], ofs[0:epc, :], iotc[0:epc, :], ALU.add))
    S("vector", TS(ofs[0:epc, :], ofs[0:epc, :], float(TSTRIDE), None,
                   ALU.mult))
    S("vector", CP(pooli[0:epc, 14:15], ofs[0:epc, :]))
    q.indirect([(tbl_own_sb[0:epc, 0:512], None, tblh[:, :],
                 IndirectOffsetOnAxis(pooli[0:epc, 14:15], 0))])
    q.dma("gpsimd", [(tbl_own[0:epc, :], tbl_own_sb[0:epc, 0:512])])
    for j in range(epc):
        q.dma("gpsimd", [(idxv[:, (j % 2) * 4:(j % 2) * 4 + 4],
                          tbl_own[j, 0:512].rearrange("(f q) -> q f", q=128))])
        jj = j % 2
        S("vector", *[TS(vld[:, col:col + 1],
                         iotc, float(col * 128), None, ALU.add)
                      for col in range(4)])
        S("vector", *[TS(vld[:, col:col + 1], vld[:, col:col + 1],
                         cnt8[:, j:j + 1], None, ALU.is_lt)
                      for col in range(4)])
        S("vector", TS(idxv[:, jj * 4:jj * 4 + 4],
                       idxv[:, jj * 4:jj * 4 + 4], float(TALL), None,
                       ALU.subtract))
        S("vector", TT(idxv[:, jj * 4:jj * 4 + 4],
                       idxv[:, jj * 4:jj * 4 + 4], vld, ALU.mult))
        S("vector", TS(idxv[:, jj * 4:jj * 4 + 4],
                       idxv[:, jj * 4:jj * 4 + 4], float(TALL), None,
                       ALU.add))
        S("vector", CP(idx_i4, idxv[:, jj * 4:jj * 4 + 4]))
        if j == 0:
            q.wait_cc("gpsimd", cc_xf)
        q.indirect([(xbt[col], None, xf_all[:, :],
                     IndirectOffsetOnAxis(idx_i4[:, col:col + 1], 0))
                    for col in range(4)])
        for f0 in range(0, 8, 2):
            S("tensor", *[TR(pb[4 * i + tt][0:128, 0:128],
                             xbt[tt][:, (f0 + i) * 128:(f0 + i + 1) * 128], 128)
                          for i in range(2) for tt in range(4)])
            S("vector", *[CP(xbfm[f0 + i][:, tt * 128:(tt + 1) * 128],
                             pb[4 * i + tt][0:128, 0:128])
                          for i in range(2) for tt in range(4)])
        mws = [128, 128, 128, 128, 128, 64]
        def eA(wdram, j, cb_):
            wsrc = wdram.rearrange("e (kc p) m -> e p kc m", p=128)
            for bi in range(6):
                wb = ewA if bi % 2 == 0 else ewB
                mw = mws[bi]
                q.dma("sync", [(
                    wb[:, 0:8 * mw].rearrange("p (kc m) -> p kc m", kc=8),
                    wsrc[j, :, 0:8, bi * 128:bi * 128 + mw])],
                    lag=None if bi <= 1 else {"tensor": 1, "vector": 99,
                                              "scalar": 99, "gpsimd": 99})
                bank = pb[bi % 4]
                q.step("tensor",
                       [MM(bank[0:mw, 0:512],
                           wb[:, kc * mw:kc * mw + mw],
                           xbfm[kc][:, 0:512], kc == 0, kc == 7)
                        for kc in range(8)],
                       lag=None if bi == 0 else {"vector": 3, "scalar": 3})
                cb_(bi, bank[0:mw, 0:512], mw)
        eA(w_e1, j, lambda ic, ps, mwid:
           S("scalar", ACT(he[ic][0:mwid, :], ps, AF.Silu)))
        eA(w_e3, j, lambda ic, ps, mwid:
           S("vector", TT(he[ic][0:mwid, :], he[ic][0:mwid, :], ps,
                          ALU.mult)))
        for kc2 in range(6):
            mwid = mws[kc2]
            wb = ewC if kc2 % 2 == 0 else ewD
            q.dma("sync", [(wb[0:mwid, 0:D],
                            w_e2[j, kc2 * 128:kc2 * 128 + mwid, :])],
                  lag=None if kc2 <= 1 else {"tensor": 1, "vector": 99, "scalar": 99, "gpsimd": 99})
            q.step("tensor", [MM(pb[sl * 2 + si][0:128, 0:512],
                                 he[kc2][0:mwid, sl * 128:(sl + 1) * 128],
                                 wb[0:mwid, si * 512:(si + 1) * 512],
                                 kc2 == 0, kc2 == 5)
                              for sl in range(4) for si in range(2)],
                   lag=None if kc2 == 0 else {"vector": 99, "scalar": 99})
        S("vector", *[CP(ybt[sl][:, si * 512:(si + 1) * 512],
                         pb[sl * 2 + si][0:128, 0:512])
                      for sl in range(4) for si in range(2)])
        q.dma("gpsimd", [(ag_yb_in[j * CAP + sl * 128:j * CAP + (sl + 1) * 128,
                                   :], ybt[sl]) for sl in range(4)])

    q.collective("AllGather", ALU.bypass, [list(range(n_ranks))],
                 [ag_yb_in.opt()], [yb_all.opt()])
    cc_yb = q.ccval

    if stop < 7:
        _finish_trunc(q, S, out, zerot, T)
        q.emit(); st.close(); return nc
    for c in range(NCH):
        nt = CH // 128
        q.dma("sync", [(xbt[tt], ag_xf_in[c * CH + tt * 128:
                                          c * CH + (tt + 1) * 128, :])
                       for tt in range(nt)])
        for f0 in range(0, 8, 2):
            S("tensor", *[TR(pb[4 * i + tt][0:128, 0:128],
                             xbt[tt][:, (f0 + i) * 128:(f0 + i + 1) * 128], 128)
                          for i in range(2) for tt in range(4)])
            S("vector", *[CP(xbfm[f0 + i][:, tt * 128:(tt + 1) * 128],
                             pb[4 * i + tt][0:128, 0:128])
                          for i in range(2) for tt in range(4)])
        def sA(wdram, cb_):
            wsrc = wdram.rearrange("(kc p) m -> p kc m", p=128)
            for bi in range(11):
                wb = ewA if bi % 2 == 0 else ewB
                q.dma("sync", [(
                    wb[:, 0:8 * 128].rearrange("p (kc m) -> p kc m", kc=8),
                    wsrc[:, 0:8, bi * 128:(bi + 1) * 128])],
                    lag=None if bi <= 1 else {"tensor": 1, "vector": 99,
                                              "scalar": 99, "gpsimd": 99})
                bank = pb[bi % 4]
                q.step("tensor",
                       [MM(bank[0:128, 0:512],
                           wb[:, kc * 128:(kc + 1) * 128],
                           xbfm[kc][:, 0:512], kc == 0, kc == 7)
                        for kc in range(8)],
                       lag=None if bi == 0 else {"vector": 3, "scalar": 3})
                cb_(bi, bank[0:128, 0:512])
        sA(w_s1, lambda ic, ps: S("scalar", ACT(he[ic], ps, AF.Silu)))
        sA(w_s3, lambda ic, ps: S("vector", TT(he[ic], he[ic], ps, ALU.mult)))
        for kc2 in range(11):
            wb = ewC if kc2 % 2 == 0 else ewD
            q.dma("sync", [(wb[:, 0:D], w_s2[kc2 * 128:(kc2 + 1) * 128, :])],
                  lag=None if kc2 <= 1 else {"tensor": 1, "vector": 99, "scalar": 99, "gpsimd": 99})
            q.step("tensor", [MM(pb[sl * 2 + si][0:128, 0:512],
                                 he[kc2][:, sl * 128:(sl + 1) * 128],
                                 wb[:, si * 512:(si + 1) * 512],
                                 kc2 == 0, kc2 == 10)
                              for sl in range(4) for si in range(2)],
                   lag=None if kc2 == 0 else {"vector": 99, "scalar": 99})
        S("vector", *[CP(ybt[sl][:, si * 512:(si + 1) * 512],
                         pb[sl * 2 + si][0:128, 0:512])
                      for sl in range(4) for si in range(2)])
        q.dma("gpsimd", [(z_hbm[c * CH + sl * 128:c * CH + (sl + 1) * 128, :],
                          ybt[sl]) for sl in range(4)])

    if stop < 8:
        _finish_trunc(q, S, out, zerot, T)
        q.emit(); st.close(); return nc
    q.wait_cc("gpsimd", cc_yb)
    for tt in range(T // 128):
        q.dma("sync", [(xmt, x_hbm[tt * 128:(tt + 1) * 128, :]),
                       (zt, z_hbm[tt * 128:(tt + 1) * 128, :])])
        S("vector", TS(ofs, rankc, float(T), None, ALU.mult))
        S("vector", TT(ofs, ofs, iotc, ALU.add))
        S("vector", TS(ofs, ofs, 1.0, float(tt * 128), ALU.mult, ALU.add))
        S("vector", CP(pooli[:, 15:16], ofs))
        q.indirect([(swft, None, slotwf[:, :],
                     IndirectOffsetOnAxis(pooli[:, 15:16], 0))])
        for k in range(TOPK):
            S("vector", CP(slot_i, swft[:, k:k + 1]))
            q.indirect([(gath, None, yb_all[:, :],
                         IndirectOffsetOnAxis(slot_i, 0))])
            if k == 0:
                S("vector", TS(acc, gath, swft[:, 8:9], None, ALU.mult))
            else:
                S("vector", STT(acc, gath, swft[:, 8 + k:9 + k], acc,
                                ALU.mult, ALU.add))
        S("vector", TT(acc, acc, zt, ALU.add))
        S("vector", TT(acc, acc, modb[5], ALU.mult))
        S("vector", TT(acc, acc, xmt, ALU.add))
        q.dma("gpsimd", [(out[tt * 128:(tt + 1) * 128, :], acc)])
    S("gpsimd", lambda e: e.engine_nop())

    q.emit()
    st.close()
    return nc


def prep_inmaps(inputs, T, n_ranks):
    def t(a):
        return np.ascontiguousarray(np.transpose(np.asarray(a, np.float32)))

    x = np.ascontiguousarray(inputs["L0_attn_x"], dtype=np.float32)
    epc = E // n_ranks
    shared = {
        "wmod": np.ascontiguousarray(inputs["L0_wmod"], np.float32),
        "ln1w": inputs["L0_attn_ln1_w"].reshape(1, D).astype(np.float32),
        "ln1b": inputs["L0_attn_ln1_b"].reshape(1, D).astype(np.float32),
        "ln2w": inputs["L0_attn_ln2_w"].reshape(1, D).astype(np.float32),
        "ln2b": inputs["L0_attn_ln2_b"].reshape(1, D).astype(np.float32),
        "normw": inputs["L0_attn_norm_w"].reshape(1, DI).astype(np.float32),
        "mlpb1": inputs["L0_attn_mlp_b1"].reshape(1, MLP_H).astype(np.float32),
        "mlpb2": inputs["L0_attn_mlp_b2"].reshape(1, D).astype(np.float32),
        "gateb": inputs["L0_ffn_gate_b"].reshape(1, E).astype(np.float32),
        "a_log": inputs["L0_attn_A_log"].reshape(1, NH).astype(np.float32),
        "d_vec": inputs["L0_attn_D"].reshape(1, NH).astype(np.float32),
        "dtbias": inputs["L0_attn_dt_bias"].reshape(1, NH).astype(np.float32),
        "convw": np.ascontiguousarray(inputs["L0_attn_conv_w"], np.float32),
        "convb": inputs["L0_attn_conv_b"].reshape(CONVD, 1).astype(np.float32),
        "w_ip": t(inputs["L0_attn_in_proj_w"]),
        "w_op": t(inputs["L0_attn_out_proj_w"]),
        "w_m1": t(inputs["L0_attn_mlp_w1"]),
        "w_m2": t(inputs["L0_attn_mlp_w2"]),
        "w_gt": t(inputs["L0_ffn_gate_w"]),
        "w_s1": t(inputs["L0_ffn_ws1"]),
        "w_s3": t(inputs["L0_ffn_ws3"]),
        "w_s2": t(inputs["L0_ffn_ws2"]),
    }
    we1 = np.ascontiguousarray(np.transpose(
        np.asarray(inputs["L0_ffn_w1"], dtype=np.float32), (0, 2, 1)))
    we3 = np.ascontiguousarray(np.transpose(
        np.asarray(inputs["L0_ffn_w3"], dtype=np.float32), (0, 2, 1)))
    we2 = np.ascontiguousarray(np.transpose(
        np.asarray(inputs["L0_ffn_w2"], dtype=np.float32), (0, 2, 1)))
    c_all = np.asarray(inputs["L0_c"], dtype=np.float32)

    in_maps = []
    zeros_prev = np.zeros((T, D), np.float32)
    for r in range(n_ranks):
        b, h = r // 2, r % 2
        m = dict(shared)
        m["c_row"] = np.ascontiguousarray(c_all[b:b + 1])
        m["x_own"] = np.ascontiguousarray(x[b, h * T:(h + 1) * T])
        m["x_prev"] = (np.ascontiguousarray(x[b, 0:T]) if h == 1
                       else zeros_prev)
        m["flag"] = np.array([[1.0 if h == 1 else 0.0]], np.float32)
        m["rank_i"] = np.array([[r]], np.int32)
        m["w_e1"] = np.ascontiguousarray(we1[r * epc:(r + 1) * epc])
        m["w_e3"] = np.ascontiguousarray(we3[r * epc:(r + 1) * epc])
        m["w_e2"] = np.ascontiguousarray(we2[r * epc:(r + 1) * epc])
        in_maps.append(m)
    return in_maps


def kernel(**inputs):
    T = 2048
    key = (T,)
    if key not in _CACHE:
        _CACHE[key] = build_program(T=T, TP=T, n_ranks=NCORES)
    nc = _CACHE[key]
    in_maps = prep_inmaps(inputs, T, NCORES)
    res = run_bass_kernel_spmd(nc, in_maps, core_ids=list(range(NCORES)))
    outs = [np.asarray(res.results[r]["out"]) for r in range(NCORES)]
    x = inputs["L0_attn_x"]
    B, L, _ = x.shape
    full = np.empty((B, L, D), np.float32)
    for r in range(NCORES):
        b, h = r // 2, r % 2
        full[b, h * T:(h + 1) * T] = outs[r]
    return full


# version 12
# speedup vs baseline: 1.2601x; 1.0606x over previous
import numpy as np
from contextlib import ExitStack

import concourse.bass as bass
import concourse.mybir as mybir
from concourse.bass import IndirectOffsetOnAxis
from concourse.bass_utils import run_bass_kernel_spmd

F32 = mybir.dt.float32
I32 = mybir.dt.int32
AF = mybir.ActivationFunctionType
ALU = mybir.AluOpType

D = 1024
DI = 2048
DSTATE = 64
NH = 32
HD = 64
DCONV = 4
CONVD = DI + 2 * DSTATE
DIP = 2 * DI + 2 * DSTATE + NH
NZX = (DIP + 127) // 128
NXBC = CONVD // 128
EPS = 1e-5
E = 64
CAP = 512
TOPK = 6
MOE_I = 704
SH_I = 1408
MLP_H = 4096
NCORES = 8
EPC = E // NCORES
TSTRIDE = 16448
SC = 128
CH = 512
BIGNEG = -1e30

_CACHE = {}


class SQ:
    ENGS = ("sync", "scalar", "vector", "tensor", "gpsimd")

    def __init__(self, nc, stack):
        self.nc = nc
        self.stack = stack
        self.ops = {e: [] for e in self.ENGS}
        self.csem = {e: self._mk(f"c_{e}") for e in self.ENGS}
        self.dsem = {e: self._mk(f"d_{e}") for e in self.ENGS}
        self.cval = {e: 0 for e in self.ENGS}
        self.dval = {e: 0 for e in self.ENGS}
        self.seen_c = {e: {f: 0 for f in self.ENGS} for e in self.ENGS}
        self.seen_d = {e: {f: 0 for f in self.ENGS} for e in self.ENGS}
        self.ccsem = self._mk("cc")
        self.ccval = 0
        self.seen_cc = {e: 0 for e in self.ENGS}

    def _mk(self, name):
        return self.stack.enter_context(self.nc.semaphore(name))

    def _barrier(self, eng, lag=None, cap=None):
        for f in self.ENGS:
            if f == eng:
                continue
            v = self.cval[f]
            if lag and f in lag:
                v = max(0, v - lag[f])
            if cap and f in cap:
                v = min(v, cap[f])
            if v > self.seen_c[eng][f]:
                self.ops[eng].append(
                    lambda e, h=self.csem[f], v=v: e.wait_ge(h, v))
                self.seen_c[eng][f] = v
            dv = self.dval[f]
            if dv > self.seen_d[eng][f]:
                self.ops[eng].append(
                    lambda e, h=self.dsem[f], v=dv: e.wait_ge(h, v))
                self.seen_d[eng][f] = dv
        dv = self.dval[eng]
        if dv > self.seen_d[eng][eng]:
            self.ops[eng].append(
                lambda e, h=self.dsem[eng], v=dv: e.wait_ge(h, v))
            self.seen_d[eng][eng] = dv

    def step(self, eng, fns, lag=None, cap=None):
        if not isinstance(fns, (list, tuple)):
            fns = [fns]
        self._barrier(eng, lag, cap)
        if eng in ("scalar", "vector", "gpsimd"):
            self.ops[eng].append(lambda e: e.drain(fusable=True))
        for fn in fns[:-1]:
            self.ops[eng].append(lambda e, fn=fn: fn(e))
        self.cval[eng] += 1
        self.ops[eng].append(
            lambda e, fn=fns[-1], h=self.csem[eng]:
            fn(e).then_inc(h, 1))

    def dma(self, eng, pairs, lag=None):
        self._barrier(eng, lag)
        for dst, src in pairs:
            self.dval[eng] += 16
            self.ops[eng].append(
                lambda e, d=dst, s=src, h=self.dsem[eng]:
                e.dma_start(out=d, in_=s, allow_slow_non_contiguous=True).then_inc(h, 16))

    def indirect(self, pairs):
        self._barrier("gpsimd")
        for o, oo, i, io in pairs:
            self.dval["gpsimd"] += 16
            self.ops["gpsimd"].append(
                lambda e, o=o, oo=oo, i=i, io=io, h=self.dsem["gpsimd"]:
                e.indirect_dma_start(out=o, out_offset=oo, in_=i,
                                     in_offset=io).then_inc(h, 16))

    def collective(self, kind, op, groups, ins, outs):
        self._barrier("gpsimd")
        self.ccval += 1
        self.ops["gpsimd"].append(
            lambda e, k=kind, o=op, g=groups, i=ins, u=outs, h=self.ccsem:
            e.collective_compute(k, o, replica_groups=g, ins=i,
                                 outs=u).then_inc(h, 1))

    def wait_cc(self, eng, upto=None):
        v = self.ccval if upto is None else upto
        if v > self.seen_cc[eng]:
            self.ops[eng].append(lambda e, h=self.ccsem, v=v: e.wait_ge(h, v))
            self.seen_cc[eng] = v

    def raw(self, eng, fn):
        self.ops[eng].append(fn)

    def emit(self):
        ops = self.ops
        with self.nc.Block() as block:
            @block.sync
            def _(e):
                for f in ops["sync"]:
                    f(e)

            @block.scalar
            def _(e):
                for f in ops["scalar"]:
                    f(e)

            @block.vector
            def _(e):
                for f in ops["vector"]:
                    f(e)

            @block.tensor
            def _(e):
                for f in ops["tensor"]:
                    f(e)

            @block.gpsimd
            def _(e):
                for f in ops["gpsimd"]:
                    f(e)


class Pool:
    def __init__(self, t, n):
        self.t = t
        self.n = n
        self.off = 0
        self.hi = 0
        self.marks = []

    def tile(self, w):
        assert self.off + w <= self.n, f"pool overflow {self.off}+{w}"
        ap = self.t[:, self.off:self.off + w]
        self.off += w
        self.hi = max(self.hi, self.off)
        return ap

    def tiles(self, k, w):
        return [self.tile(w) for _ in range(k)]

    def push(self):
        self.marks.append(self.off)

    def pop(self):
        self.off = self.marks.pop()


def _finish_trunc(q, S, out, zerot, T):
    q.dma("gpsimd", [(out[tt * 128:(tt + 1) * 128, :], zerot[:, 0:1024])
                     for tt in range(T // 128)])
    S("gpsimd", lambda e: e.engine_nop())


def build_program(T=2048, TP=2048, n_ranks=NCORES, stop=99):
    nc = bass.Bass("TRN2", target_bir_lowering=False)
    st = ExitStack()
    q = SQ(nc, st)

    NCH = T // CH
    NCHP = TP // CH
    TALL = T * n_ranks
    NTILE_ALL = TALL // 128
    epc = E // n_ranks

    def din(name, shape, dt=F32):
        return nc.dram_tensor(name, shape, dt, kind="ExternalInput").ap()

    x_own = din("x_own", [T, D])
    x_prev = din("x_prev", [TP, D])
    flag = din("flag", [1, 1])
    c_row = din("c_row", [1, D])
    wmod = din("wmod", [D, 6 * D])
    ln1w = din("ln1w", [1, D]); ln1b = din("ln1b", [1, D])
    ln2w = din("ln2w", [1, D]); ln2b = din("ln2b", [1, D])
    normw = din("normw", [1, DI])
    mlpb1 = din("mlpb1", [1, MLP_H]); mlpb2 = din("mlpb2", [1, D])
    gateb = din("gateb", [1, E])
    a_log = din("a_log", [1, NH]); d_vec = din("d_vec", [1, NH])
    dtbias = din("dtbias", [1, NH])
    convw = din("convw", [CONVD, DCONV]); convb = din("convb", [CONVD, 1])
    w_ip = din("w_ip", [D, DIP])
    w_op = din("w_op", [DI, D])
    w_m1 = din("w_m1", [D, MLP_H])
    w_m2 = din("w_m2", [MLP_H, D])
    w_gt = din("w_gt", [D, E])
    w_s1 = din("w_s1", [D, SH_I]); w_s3 = din("w_s3", [D, SH_I])
    w_s2 = din("w_s2", [SH_I, D])
    w_e1 = din("w_e1", [epc, D, MOE_I])
    w_e3 = din("w_e3", [epc, D, MOE_I])
    w_e2 = din("w_e2", [epc, MOE_I, D])
    rank_i = din("rank_i", [1, 1], I32)

    out = nc.dram_tensor("out", [T, D], F32, kind="ExternalOutput").ap()

    def dint(name, shape, dt=F32):
        return nc.dram_tensor(name, shape, dt).ap()

    x_hbm = dint("x_hbm", [T, D])
    ag_xf_in = dint("ag_xf_in", [T, D])
    xf_all = dint("xf_all", [TALL + 128, D])
    ag_pay_in = dint("ag_pay_in", [T, 16])
    pay_all = dint("pay_all", [TALL, 16])
    tblh = dint("tblh", [E * TSTRIDE, 1])
    slotwf = dint("slotwf", [TALL, 16])
    cnts_h = dint("cnts_h", [1, E])
    tbl_own = dint("tbl_own", [E, 512])
    ag_yb_in = dint("ag_yb_in", [epc * CAP, D])
    yb_all = dint("yb_all", [E * CAP, D])
    z_hbm = dint("z_hbm", [T, D])

    POOLW = 53144
    poolt = st.enter_context(nc.sbuf_tensor("pool", [128, POOLW], F32))
    pool = Pool(poolt, POOLW)
    pooli = st.enter_context(nc.sbuf_tensor("pooli", [128, 64], I32))

    ones_row = pool.tile(128)
    ones_col = pool.tile(1)
    iot128 = pool.tile(128)
    iotc = pool.tile(1)
    ident = pool.tile(128)
    sltm = pool.tile(128)
    maskbig = pool.tile(128)
    modb = pool.tiles(6, D)
    lncol = pool.tiles(4, 8)
    b2b = pool.tile(D)
    gbB = pool.tile(E)
    nrmc = pool.tile(16)
    b1c = pool.tile(32)
    a128 = pool.tile(NH)
    d128 = pool.tile(NH)
    acol = pool.tile(1)
    nacol = pool.tile(1)
    dtbc = pool.tile(1)
    flagc = pool.tile(1)
    cw = pool.tiles(NXBC, DCONV)
    cb = pool.tiles(NXBC, 1)
    c_fm = pool.tile(8)
    state = pool.tile(NH * HD)
    tails = pool.tile(NXBC * 4)
    smallA = pool.tile(96)
    smallB = pool.tile(16)
    lnst = pool.tile(64)
    smallC = pool.tile(512)
    etimes = pool.tile(1)
    constc = pool.tile(4)
    baser = pool.tile(E)
    basec = pool.tile(1)

    persist_end = pool.off

    pool.push()
    srow = pool.tile(6 * D)
    zerot = pool.tile(2056)
    pool.pop()

    pool.push()
    xchunk = pool.tiles(4, D)
    htok = pool.tiles(4, D)
    x1tok = pool.tiles(4, D)
    hfm_big = pool.tile(8 * CH)
    hfm = [hfm_big[:, i * CH:(i + 1) * CH] for i in range(8)]
    zx = pool.tiles(NZX, CH)
    cvs = pool.tile(CH)
    ccopy = pool.tile(SC)
    wbufA = pool.tile(8 * 128)
    wbufB = pool.tile(8 * 128)
    wbufC = pool.tile(D)
    wbufD = pool.tile(D)
    xs_tok = hfm_big[:, 0:DI]
    y_tok = hfm_big[:, DI:2 * DI]
    bct = pool.tile(128)
    sdt = pool.tile(SC)
    sdtT = pool.tile(96)
    row0 = pool.tile(512)
    wfm = pool.tile(SC)
    dkc = pool.tile(1)
    dkrow = pool.tile(NH)
    asig = pool.tile(SC)
    aST = pool.tile(NH)
    easT = pool.tile(NH)
    dkerb = pool.tile(NH)
    scoresT = pool.tile(128)
    lpre = pool.tiles(4, 128)
    mtb = pool.tiles(4, 128)
    dtt = pool.tile(CH)
    onescN = pool.tile(SC)
    payb = pool.tiles(T // 128, 16)
    scorekeep = pool.tile(E)
    attn_end = pool.off
    assert attn_end < POOLW, attn_end

    print('pool hi', pool.hi, 'off', pool.off)
    pb = [st.enter_context(nc.psum_tensor(f"pb{i}", [128, 512], F32))
          for i in range(8)]

    def S(eng, *fns):
        q.step(eng, list(fns))

    def SL(eng, lag, *fns):
        q.step(eng, list(fns), lag=lag)

    def MM(psum, lhsT, rhs, start, stop):
        return lambda e, o=psum, l=lhsT, r=rhs, a=start, b=stop: \
            e.matmul(o, l, r, start=a, stop=b)

    def TR(psum, src, npart):
        return lambda e, o=psum, i=src, n=npart: \
            e.transpose(o, i, ident[0:n, 0:n])

    def CP(dst, src):
        return lambda e, d=dst, s=src: e.tensor_copy(d, s)

    def TT(dst, a, b, op):
        return lambda e, d=dst, a=a, b=b, o=op: e.tensor_tensor(d, a, b, o)

    def TS(dst, a, s1, s2, op0, op1=ALU.bypass):
        return lambda e, d=dst, a=a, s1=s1, s2=s2, o0=op0, o1=op1: \
            e.tensor_scalar(d, a, s1, s2, o0, o1)

    def STT(dst, in0, sc, in1, op0, op1):
        return lambda e, d=dst, i0=in0, s=sc, i1=in1, o0=op0, o1=op1: \
            e.scalar_tensor_tensor(d, i0, s, i1, o0, o1)

    def ACT(dst, src, f, bias=0.0, scale=1.0):
        return lambda e, d=dst, s=src, f=f, b=bias, sc=scale: \
            e.activation(d, s, f, bias=b, scale=sc)

    def RED(dst, src, op):
        return lambda e, d=dst, s=src, o=op: \
            e.tensor_reduce(d, s, mybir.AxisListType.X, o)

    def MS(dst, v):
        return lambda e, d=dst, v=v: e.memset(d, v)

    def bcast(psum, row, nparts):
        w = row.shape[-1]
        return MM(psum[0:nparts, 0:w], ones_row[0:1, 0:nparts], row,
                  True, True)

    q.dma("sync", [
        (c_fm, c_row.rearrange("o (q f) -> (o f) q", f=128)),
        (smallA[0:1, 0:NH], a_log[:, :]),
        (smallA[0:1, NH:2 * NH], d_vec[:, :]),
        (smallA[0:1, 2 * NH:3 * NH], dtbias[:, :]),
        (flagc[0:1, :], flag[:, :]),
        (nrmc, normw.rearrange("o (q f) -> (o f) q", f=128)),
        (b1c, mlpb1.rearrange("o (q f) -> (o f) q", f=128)),
        (smallC[0:1, 0:E], gateb[:, :]),
        (lncol[0], ln1w.rearrange("o (q f) -> (o f) q", f=128)),
        (lncol[1], ln1b.rearrange("o (q f) -> (o f) q", f=128)),
        (lncol[2], ln2w.rearrange("o (q f) -> (o f) q", f=128)),
        (lncol[3], ln2b.rearrange("o (q f) -> (o f) q", f=128)),
        (srow[0:1, 4 * D:5 * D], mlpb2[:, :]),
    ] + [(cw[i], convw[i * 128:(i + 1) * 128, :]) for i in range(NXBC)]
      + [(cb[i], convb[i * 128:(i + 1) * 128, :]) for i in range(NXBC)])

    S("vector", MS(ident, 1.0))
    S("vector", lambda e: e.tensor_tensor_scan(iot128, ident, ident, 0.0,
                                               ALU.mult, ALU.add))
    S("vector", TS(iot128, iot128, 1.0, None, ALU.subtract))
    S("tensor", TR(pb[0][0:128, 0:1], iot128[0:1, 0:128], 1))
    S("vector", CP(iotc, pb[0][0:128, 0:1]))
    S("vector",
      MS(ones_row[0:1, :], 1.0), MS(ones_col, 1.0), MS(onescN[0:NH, :], 1.0),
      MS(zerot, 0.0), MS(state[0:64, :], 0.0), MS(tails, 0.0),
      MS(baser[0:1, :], 0.0), MS(basec[0:64, :], 0.0),
      TS(ident, iot128, iotc, None, ALU.is_equal),
      TS(sltm, iot128, iotc, None, ALU.is_gt),
      TS(maskbig, iot128, iotc, None, ALU.is_ge),
      TS(etimes[0:64, :], iotc[0:64, :], float(TSTRIDE), None, ALU.mult),
      MS(constc[:, 0:1], float(EPS)), MS(constc[:, 1:2], 1.0),
      MS(constc[:, 2:3], -1.0), MS(constc[:, 3:4], 1.0 / DI))
    S("vector", TS(maskbig, maskbig, 1.0, 1e30, ALU.subtract, ALU.mult))
    S("scalar", ACT(smallA[0:1, 0:NH], smallA[0:1, 0:NH], AF.Exp))
    S("scalar", ACT(smallA[0:1, 0:NH], smallA[0:1, 0:NH], AF.Copy, scale=constc[0:1, 2:3]),
      ACT(c_fm, c_fm, AF.Silu))
    ztb = tblh.rearrange("(a b) o -> a (b o)", b=8224)
    q.dma("gpsimd", [(ztb[i * 128:(i + 1) * 128, j * 1028:(j + 1) * 1028],
                      zerot[:, 0:1028]) for i in range(1) for j in range(8)])
    q.dma("gpsimd", [(xf_all[TALL:TALL + 128, :], zerot[:, 0:1024])])

    bc_jobs = [(b2b, srow[0:1, 4 * D:5 * D]), (gbB, smallC[0:1, 0:E]),
               (a128, smallA[0:1, 0:NH]), (d128, smallA[0:1, NH:2 * NH])]
    for i, (dst, row) in enumerate(bc_jobs):
        w = row.shape[-1]
        for si in range((w + 511) // 512):
            lo, hi = si * 512, min(w, (si + 1) * 512)
            S("tensor", bcast(pb[0], row[:, lo:hi], 128))
            S("vector", CP(dst[:, lo:hi], pb[0][0:128, 0:hi - lo]))
    S("tensor", TR(pb[1][0:NH, 0:1], smallA[0:1, 0:NH], 1))
    S("vector", CP(acol[0:NH, :], pb[1][0:NH, 0:1]))
    S("vector", TS(nacol[0:NH, :], acol[0:NH, :], -1.0, None, ALU.mult))
    S("tensor", TR(pb[1][0:NH, 0:1], smallA[0:1, 2 * NH:3 * NH], 1))
    S("vector", CP(dtbc[0:NH, :], pb[1][0:NH, 0:1]))
    S("tensor", bcast(pb[1], flagc[0:1, 0:1], 128))
    S("vector", CP(flagc, pb[1][0:128, 0:1]))

    if stop < 1:
        _finish_trunc(q, S, out, zerot, T)
        q.emit(); st.close(); return nc
    NW = 6 * D // 512
    for nwi in range(NW):
        for kc in range(8):
            wb = wbufC if kc % 2 == 0 else wbufD
            q.dma("sync", [(wb[:, 0:512],
                            wmod[kc * 128:(kc + 1) * 128,
                                 nwi * 512:(nwi + 1) * 512])])
            S("tensor", MM(pb[2][0:1, 0:512], c_fm[:, kc:kc + 1],
                           wb[:, 0:512], kc == 0, kc == 7))
        S("vector", CP(srow[0:1, nwi * 512:(nwi + 1) * 512],
                       pb[2][0:1, 0:512]))
    for gi in (2, 5):
        sl = srow[0:1, gi * D:(gi + 1) * D]
        s0 = zerot[0:1, 0:D]
        s1 = zerot[0:1, D:2 * D]
        S("vector", TS(s0, sl, 0.0, 2.0, ALU.is_ge, ALU.mult),
          TS(s1, sl, -1.0, None, ALU.mult))
        S("vector", TS(s0, s0, 1.0, None, ALU.subtract),
          TT(s1, s1, sl, ALU.max))
        S("vector", TS(s1, s1, 0.25, None, ALU.max))
        S("vector", TT(sl, s0, s1, ALU.mult))
    S("vector", TS(srow[0:1, D:2 * D], srow[0:1, D:2 * D], 1.0, None, ALU.add),
      TS(srow[0:1, 4 * D:5 * D], srow[0:1, 4 * D:5 * D], 1.0, None, ALU.add))
    for i in range(6):
        row = srow[0:1, i * D:(i + 1) * D]
        for si in range(2):
            lo, hi = si * 512, (si + 1) * 512
            S("tensor", bcast(pb[0], row[:, lo:hi], 128))
            S("vector", CP(modb[i][:, lo:hi], pb[0][0:128, 0:512]))

    def ln_tiles(src_tiles, dst_tiles, mul_tile, add_tile, nt):
        S("vector", *[f for tt in range(nt) for f in (
            (lambda e, s=src_tiles[tt], tt=tt: e.bn_stats(
                lnst[0:128, tt * 16:tt * 16 + 6], s[:, 0:512])),
            (lambda e, s=src_tiles[tt], tt=tt: e.bn_stats(
                lnst[0:128, tt * 16 + 6:tt * 16 + 12], s[:, 512:1024])))])
        S("vector", *[(lambda e, tt=tt: e.bn_aggr(
            lnst[0:128, tt * 16 + 12:tt * 16 + 14],
            lnst[0:128, tt * 16:tt * 16 + 12])) for tt in range(nt)])
        S("scalar", *[ACT(lnst[0:128, tt * 16 + 14:tt * 16 + 15],
                          lnst[0:128, tt * 16 + 13:tt * 16 + 14],
                          AF.Sqrt, bias=constc[:, 0:1]) for tt in range(nt)])
        S("vector", *[(lambda e, tt=tt: e.reciprocal(
            lnst[0:128, tt * 16 + 14:tt * 16 + 15],
            lnst[0:128, tt * 16 + 14:tt * 16 + 15])) for tt in range(nt)])
        S("vector", *[TS(dst_tiles[tt], src_tiles[tt],
                         lnst[0:128, tt * 16 + 12:tt * 16 + 13],
                         lnst[0:128, tt * 16 + 14:tt * 16 + 15],
                         ALU.subtract, ALU.mult) for tt in range(nt)])
        if mul_tile is not None:
            S("vector", *[TT(dst_tiles[tt], dst_tiles[tt], mul_tile, ALU.mult)
                          for tt in range(nt)])
            S("vector", *[TT(dst_tiles[tt], dst_tiles[tt], add_tile, ALU.add)
                          for tt in range(nt)])

    def to_fm(src_tiles, dst_fm_tiles, nf, nt, wcol=None, bcol=None):
        jobs = []
        for fc in range(nf):
            for tt in range(nt):
                jobs.append((fc, tt))
        for j0 in range(0, len(jobs), 4):
            grp = jobs[j0:j0 + 4]
            S("tensor", *[TR(pb[4 + i][0:128, 0:128],
                             src_tiles[tt][:, fc * 128:(fc + 1) * 128], 128)
                          for i, (fc, tt) in enumerate(grp)])
            if wcol is None:
                S("vector", *[CP(dst_fm_tiles[fc][:, tt * 128:(tt + 1) * 128],
                                 pb[4 + i][0:128, 0:128])
                              for i, (fc, tt) in enumerate(grp)])
            else:
                S("vector", *[TS(dst_fm_tiles[fc][:, tt * 128:(tt + 1) * 128],
                                 pb[4 + i][0:128, 0:128],
                                 wcol[:, fc:fc + 1], bcol[:, fc:fc + 1],
                                 ALU.mult, ALU.add)
                              for i, (fc, tt) in enumerate(grp)])

    def patternA(wdram, m0t, m1t, out_cb, Tn, kparts=8, mtot=None,
                 floors=None):
        if mtot is None:
            mtot = m1t * 128
        mcs = list(range(m0t, m1t))
        wid = {mc: min(128, mtot - mc * 128) for mc in mcs}
        wsrc = wdram.rearrange("(kc p) m -> p kc m", p=128)
        steps_done = 0
        buf_free_at = {0: None, 1: None}
        if floors is None:
            fl = {"vector": q.cval["vector"], "scalar": q.cval["scalar"],
                  "gpsimd": q.cval["gpsimd"]}
        else:
            fl = dict(floors)
        evac_pos = []
        for bi, mc in enumerate(mcs):
            bsel = bi % 2
            wb = wbufA if bsel == 0 else wbufB
            mw = wid[mc]
            fa = buf_free_at[bsel]
            q.dma("sync", [(
                wb[:, 0:kparts * mw].rearrange("p (kc m) -> p kc m", kc=kparts),
                wsrc[:, 0:kparts, mc * 128:mc * 128 + mw])],
                lag=None if fa is None else
                {"tensor": steps_done - fa, "vector": 99, "scalar": 99,
                 "gpsimd": 99})
            buf_free_at[bsel] = steps_done + 1
            bank = pb[mc % 4]
            if bi < 4:
                cap = dict(fl)
            else:
                ep = evac_pos[bi - 4]
                cap = {k: max(fl.get(k, 0), ep.get(k, 0))
                       for k in ("vector", "scalar", "gpsimd")}
            q.step("tensor",
                   [MM(bank[0:mw, 0:Tn],
                       wb[:, kc * mw:kc * mw + mw],
                       hfm[kc][:, 0:Tn], kc == 0, kc == kparts - 1)
                    for kc in range(kparts)], cap=cap)
            steps_done += 1
            out_cb(mc, bank[0:mw, 0:Tn])
            evac_pos.append({"vector": q.cval["vector"],
                             "scalar": q.cval["scalar"],
                             "gpsimd": q.cval["gpsimd"]})

    def conv_chunk(Tn):
        xbc = [cvs] + [zx[15 + i] for i in range(1, NXBC)]
        S("vector", *[CP(smallC[0:128, i * 3:(i + 1) * 3],
                         zx[16 + i][:, Tn - 3:Tn]) for i in range(NXBC)])
        for i in range(NXBC):
            srct = zx[16 + i]
            dstt = xbc[i]
            ops1 = [TS(dstt[:, 0:Tn], srct[:, 0:Tn], cw[i][:, 3:4], None,
                       ALU.mult)]
            S("vector", *ops1)
            for k in range(3):
                S("vector", STT(dstt[:, 3 - k:Tn], srct[:, 0:Tn - 3 + k],
                                cw[i][:, k:k + 1], dstt[:, 3 - k:Tn],
                                ALU.mult, ALU.add))
                tops = []
                for t in range(0, 3 - k):
                    tops.append(STT(dstt[:, t:t + 1],
                                    tails[:, i * 4 + t + k:i * 4 + t + k + 1],
                                    cw[i][:, k:k + 1], dstt[:, t:t + 1],
                                    ALU.mult, ALU.add))
                if tops:
                    S("vector", *tops)
        S("vector", *[CP(tails[:, i * 4:i * 4 + 3],
                         smallC[0:128, i * 3:(i + 1) * 3])
                      for i in range(NXBC)])
        S("scalar", *[ACT(xbc[i][:, 0:Tn], xbc[i][:, 0:Tn], AF.Silu,
                          bias=cb[i][:, :]) for i in range(NXBC)])
        return xbc

    def dt_chunk(Tn):
        dts = smallC
        S("scalar", ACT(dtt[0:NH, 0:Tn], zx[33][0:NH, 0:Tn], AF.Abs,
                        bias=dtbc[0:NH, :]))
        S("scalar", ACT(dts[0:NH, 0:Tn], dtt[0:NH, 0:Tn], AF.Exp, scale=constc[0:NH, 2:3]))
        S("scalar", ACT(dts[0:NH, 0:Tn], dts[0:NH, 0:Tn], AF.Ln, bias=constc[0:NH, 1:2]))
        S("scalar", ACT(dtt[0:NH, 0:Tn], zx[33][0:NH, 0:Tn], AF.Relu,
                        bias=dtbc[0:NH, :]))
        S("vector", TT(dtt[0:NH, 0:Tn], dtt[0:NH, 0:Tn], dts[0:NH, 0:Tn],
                       ALU.add))

    def ssd_subchunk(xbc, o, partial):
        bc_tile = xbc[16]
        q.dma("gpsimd", [(ccopy[0:64, 0:SC], bc_tile[64:128, o:o + SC])])
        S("vector",
          lambda e: e.tensor_tensor_scan(sdt[0:NH, 0:SC], onescN[0:NH, 0:SC],
                                         dtt[0:NH, o:o + SC], 0.0,
                                         ALU.mult, ALU.add))
        S("vector", TS(asig[0:NH, 0:SC], sdt[0:NH, 0:SC], acol[0:NH, :],
                       None, ALU.mult),
          TS(wfm[0:NH, 0:SC], sdt[0:NH, 0:SC],
             sdt[0:NH, SC - 1:SC], nacol[0:NH, :], ALU.subtract, ALU.mult))
        S("scalar", ACT(wfm[0:NH, 0:SC], wfm[0:NH, 0:SC], AF.Exp))
        S("vector", TT(wfm[0:NH, 0:SC], wfm[0:NH, 0:SC],
                       dtt[0:NH, o:o + SC], ALU.mult),
          TS(dkc[0:NH, :], sdt[0:NH, SC - 1:SC], acol[0:NH, :], None,
             ALU.mult))
        S("scalar", ACT(dkc[0:NH, :], dkc[0:NH, :], AF.Exp))
        S("tensor", TR(pb[4][0:SC, 0:NH], sdt[0:NH, 0:SC], NH),
          TR(pb[5][0:SC, 0:NH], dtt[0:NH, o:o + SC], NH),
          TR(pb[6][0:SC, 0:NH], wfm[0:NH, 0:SC], NH),
          TR(pb[7][0:1, 0:NH], dkc[0:NH, :], NH))
        S("vector", CP(sdtT[:, 0:NH], pb[4][0:SC, 0:NH]),
          CP(sdtT[:, NH:2 * NH], pb[5][0:SC, 0:NH]),
          CP(sdtT[:, 2 * NH:3 * NH], pb[6][0:SC, 0:NH]),
          CP(dkrow[0:1, 0:NH], pb[7][0:1, 0:NH]))
        S("vector", TT(aST[:, 0:NH], sdtT[:, 0:NH], a128, ALU.mult))
        S("scalar", ACT(easT[:, 0:NH], aST[:, 0:NH], AF.Exp))
        S("tensor", bcast(pb[6], dkrow[0:1, 0:NH], 64))
        S("vector", CP(dkerb[0:64, 0:NH], pb[6][0:64, 0:NH]))
        S("tensor", TR(pb[7][0:128, 0:128], bc_tile[:, o:o + SC], 128))
        S("vector", CP(bct[:, 0:128], pb[7][0:128, 0:128]))
        for f0 in range(0, 16, 4):
            S("tensor", *[TR(pb[4 + i][0:128, 0:128],
                             xbc[f0 + i][:, o:o + SC], 128) for i in range(4)])
            S("vector", *[CP(xs_tok[:, (f0 + i) * 128:(f0 + i + 1) * 128],
                             pb[4 + i][0:128, 0:128]) for i in range(4)])
        if not partial:
            S("tensor", MM(pb[6][0:128, 0:128], bc_tile[0:64, o:o + SC],
                           ccopy[0:64, 0:SC], True, True))
            S("vector", CP(scoresT[:, 0:128], pb[6][0:128, 0:128]))

        for hg in range(NH // 4):
            hs = hg * 4
            if not partial:
                S("tensor", *[TR(pb[4 + j][0:1, 0:128],
                                 aST[:, hs + j:hs + j + 1], 128)
                              for j in range(4)])
                S("vector", *[CP(row0[0:1, j * 128:(j + 1) * 128],
                                 pb[4 + j][0:1, 0:128]) for j in range(4)])
                S("tensor", *[MM(pb[0][0:128, j * 128:(j + 1) * 128],
                                 ones_row[0:1, 0:128],
                                 row0[0:1, j * 128:(j + 1) * 128],
                                 j == 0, j == 3) for j in range(4)])
                S("vector", *[TS(lpre[j], pb[0][0:128, j * 128:(j + 1) * 128],
                                 aST[:, hs + j:hs + j + 1], None,
                                 ALU.subtract) for j in range(4)])
                S("vector", *[TT(lpre[j], lpre[j], maskbig, ALU.min)
                              for j in range(4)])
                S("scalar", *[ACT(lpre[j], lpre[j], AF.Exp) for j in range(4)])
                S("vector", *[STT(mtb[j], lpre[j],
                                  sdtT[:, NH + hs + j:NH + hs + j + 1],
                                  scoresT, ALU.mult, ALU.mult)
                              for j in range(4)])
                S("tensor",
                  *[MM(pb[2][0:128, j * 64:(j + 1) * 64], mtb[j],
                       xs_tok[:, (hs + j) * 64:(hs + j + 1) * 64],
                       j == 0, j == 3) for j in range(4)],
                  *[MM(pb[3][0:128, j * 64:(j + 1) * 64],
                       ccopy[0:64, 0:SC],
                       state[0:64, (hs + j) * 64:(hs + j + 1) * 64],
                       j == 0, j == 3) for j in range(4)])
                S("vector", *[CP(y_tok[:, (hs + j) * 64:(hs + j + 1) * 64],
                                 pb[2][0:128, j * 64:(j + 1) * 64])
                              for j in range(4)])
                S("vector", *[STT(y_tok[:, (hs + j) * 64:(hs + j + 1) * 64],
                                  pb[3][0:128, j * 64:(j + 1) * 64],
                                  easT[:, hs + j:hs + j + 1],
                                  y_tok[:, (hs + j) * 64:(hs + j + 1) * 64],
                                  ALU.mult, ALU.add) for j in range(4)])
                S("vector", *[STT(y_tok[:, (hs + j) * 64:(hs + j + 1) * 64],
                                  xs_tok[:, (hs + j) * 64:(hs + j + 1) * 64],
                                  d128[:, hs + j:hs + j + 1],
                                  y_tok[:, (hs + j) * 64:(hs + j + 1) * 64],
                                  ALU.mult, ALU.add) for j in range(4)])
            S("vector", *[TS(xs_tok[:, (hs + j) * 64:(hs + j + 1) * 64],
                             xs_tok[:, (hs + j) * 64:(hs + j + 1) * 64],
                             sdtT[:, 2 * NH + hs + j:2 * NH + hs + j + 1],
                             None, ALU.mult)
                          for j in range(4)])
            S("tensor", *[MM(pb[1][0:64, j * 64:(j + 1) * 64], bct[:, 0:64],
                             xs_tok[:, (hs + j) * 64:(hs + j + 1) * 64],
                             j == 0, j == 3) for j in range(4)])
            S("vector", *[STT(state[0:64, (hs + j) * 64:(hs + j + 1) * 64],
                              state[0:64, (hs + j) * 64:(hs + j + 1) * 64],
                              dkerb[0:64, hs + j:hs + j + 1],
                              pb[1][0:64, j * 64:(j + 1) * 64],
                              ALU.mult, ALU.add) for j in range(4)])

    def attn_chunk(c, partial, xsrc):
        nt = CH // 128
        q.dma("sync", [(xchunk[tt], xsrc[c * CH + tt * 128:
                                         c * CH + (tt + 1) * 128, :])
                       for tt in range(nt)])
        ln_tiles(xchunk, htok, modb[1], modb[0], nt)
        ln_tiles(htok, x1tok, None, None, nt)
        to_fm(x1tok, hfm, 8, nt, lncol[0], lncol[1])
        def ip_cb(mc, ps):
            S("vector", CP(zx[mc][0:(32 if mc == NZX - 1 else 128), 0:CH], ps))
        patternA(w_ip, 16, NZX, ip_cb, CH, mtot=DIP)
        post_xbc = {"vector": q.cval["vector"], "scalar": q.cval["scalar"],
                    "gpsimd": q.cval["gpsimd"]}
        xbc = conv_chunk(CH)
        dt_chunk(CH)
        if not partial:
            patternA(w_ip, 0, 16, ip_cb, CH, mtot=DIP, floors=post_xbc)
        for sci in range(CH // SC):
            ssd_subchunk(xbc, sci * SC, partial)
            if not partial:
                post_ssd_subchunk(c, sci, xbc)
        if not partial:
            post_chunk(c, nt)

    def post_ssd_subchunk(c, sci, xbc):
        o = sci * SC
        S("scalar", *[ACT(zx[fc][:, o:o + SC], zx[fc][:, o:o + SC], AF.Silu)
                      for fc in range(16)])
        for f0 in range(0, 16, 4):
            S("tensor", *[TR(pb[4 + i][0:128, 0:128],
                             y_tok[:, (f0 + i) * 128:(f0 + i + 1) * 128], 128)
                          for i in range(4)])
            S("vector", *[TT(zx[f0 + i][:, o:o + SC],
                             pb[4 + i][0:128, 0:128],
                             zx[f0 + i][:, o:o + SC], ALU.mult)
                          for i in range(4)])
        sqops = []
        mmops = []
        for fc in range(16):
            sqops.append(TT(lpre[fc % 4], zx[fc][:, o:o + SC],
                            zx[fc][:, o:o + SC], ALU.mult))
            mmops.append(MM(pb[6][0:1, 0:SC], ones_col[0:128, :],
                            lpre[fc % 4], fc == 0, fc == 15))
        for f0 in range(0, 16, 4):
            S("vector", *sqops[f0:f0 + 4])
            S("tensor", *mmops[f0:f0 + 4])
        S("scalar", ACT(smallC[0:1, 0:SC], pb[6][0:1, 0:SC], AF.Sqrt,
                        bias=constc[0:1, 0:1], scale=constc[0:1, 3:4]))
        S("vector", lambda e: e.reciprocal(smallC[0:1, 0:SC],
                                           smallC[0:1, 0:SC]))
        S("tensor", bcast(pb[7], smallC[0:1, 0:SC], 128))
        S("vector", *[STT(zx[fc][:, o:o + SC], pb[7][0:128, 0:SC],
                          nrmc[:, fc:fc + 1], zx[fc][:, o:o + SC],
                          ALU.mult, ALU.mult) for fc in range(16)])

    def post_chunk(c, nt):
        for kc in range(16):
            wb = wbufC if kc % 2 == 0 else wbufD
            q.dma("sync", [(wb[:, 0:D], w_op[kc * 128:(kc + 1) * 128, :])],
                  lag=None if kc <= 1 else {"tensor": 1, "vector": 99, "scalar": 99, "gpsimd": 99})
            q.step("tensor", [MM(pb[tt * 2 + si][0:128, 0:512],
                                 zx[kc][:, tt * 128:(tt + 1) * 128],
                                 wb[:, si * 512:(si + 1) * 512],
                                 kc == 0, kc == 15)
                              for tt in range(nt) for si in range(2)],
                   lag=None if kc == 0 else {"vector": 99, "scalar": 99})
        S("vector", *[TT(x1tok[tt][:, si * 512:(si + 1) * 512],
                         pb[tt * 2 + si][0:128, 0:512],
                         htok[tt][:, si * 512:(si + 1) * 512], ALU.add)
                      for tt in range(nt) for si in range(2)])
        ln_tiles(x1tok, htok, None, None, nt)
        to_fm(htok, hfm, 8, nt, lncol[2], lncol[3])
        patternA(w_m1, 0, 32,
                 lambda mc, ps: S("scalar",
                                  ACT(zx[mc][:, 0:CH], ps, AF.Gelu,
                                      bias=b1c[:, mc:mc + 1])), CH)
        for kc in range(32):
            wb = wbufC if kc % 2 == 0 else wbufD
            q.dma("sync", [(wb[:, 0:D], w_m2[kc * 128:(kc + 1) * 128, :])],
                  lag=None if kc <= 1 else {"tensor": 1, "vector": 99, "scalar": 99, "gpsimd": 99})
            q.step("tensor", [MM(pb[tt * 2 + si][0:128, 0:512],
                                 zx[kc][:, tt * 128:(tt + 1) * 128],
                                 wb[:, si * 512:(si + 1) * 512],
                                 kc == 0, kc == 31)
                              for tt in range(nt) for si in range(2)],
                   lag=None if kc == 0 else {"vector": 99, "scalar": 99})
        S("vector", *[TT(htok[tt][:, si * 512:(si + 1) * 512],
                         pb[tt * 2 + si][0:128, 0:512],
                         b2b[:, si * 512:(si + 1) * 512], ALU.add)
                      for tt in range(nt) for si in range(2)])
        S("vector", *[TT(htok[tt], htok[tt], x1tok[tt], ALU.add)
                      for tt in range(nt)])
        S("vector", *[TT(htok[tt], htok[tt], modb[2], ALU.mult)
                      for tt in range(nt)])
        S("vector", *[TT(xchunk[tt], xchunk[tt], htok[tt], ALU.add)
                      for tt in range(nt)])
        q.dma("gpsimd", [(x_hbm[c * CH + tt * 128:c * CH + (tt + 1) * 128, :],
                          xchunk[tt]) for tt in range(nt)])
        ln_tiles(xchunk, htok, modb[4], modb[3], nt)
        q.dma("gpsimd", [(ag_xf_in[c * CH + tt * 128:c * CH + (tt + 1) * 128, :],
                          htok[tt]) for tt in range(nt)])
        to_fm(htok, hfm, 8, nt)
        gate_topk(c, nt)

    def gate_topk(c, nt):
        for tt in range(nt):
            wb = wbufC
            if tt == 0:
                q.dma("sync", [(wbufC[:, kc * 64:(kc + 1) * 64],
                                w_gt[kc * 128:(kc + 1) * 128, :])
                               for kc in range(8)])
            S("tensor", *[MM(pb[4][0:128, 0:E],
                             hfm[kc][:, tt * 128:(tt + 1) * 128],
                             wbufC[:, kc * 64:(kc + 1) * 64],
                             kc == 0, kc == 7) for kc in range(8)])
            S("vector", RED(smallB[0:128, 0:1], pb[4][0:128, 0:E], ALU.max))
            S("vector", TS(smallB[0:128, 1:2], smallB[0:128, 0:1], -1.0,
                           None, ALU.mult))
            S("scalar", lambda e: e.activation(lpre[0][:, 0:E], pb[4][0:128, 0:E],
                                               AF.Exp, bias=smallB[0:128, 1:2]))
            S("vector", RED(smallB[0:128, 2:3], lpre[0][:, 0:E], ALU.add))
            S("vector", lambda e: e.reciprocal(smallB[0:128, 3:4],
                                               smallB[0:128, 2:3]))
            S("vector", TS(scorekeep[:, 0:E], lpre[0][:, 0:E],
                           smallB[0:128, 3:4], None, ALU.mult))
            S("vector", TT(lpre[1][:, 0:E], scorekeep[:, 0:E], gbB, ALU.add))
            pt = payb[c * nt + tt]
            S("vector", MS(pt, 0.0))
            for k in range(TOPK):
                S("vector", RED(smallB[0:128, 4:5], lpre[1][:, 0:E], ALU.max))
                S("vector", TS(lpre[2][:, 0:E], lpre[1][:, 0:E],
                               smallB[0:128, 4:5], None, ALU.is_equal))
                S("vector", TT(lpre[3][:, 0:E], lpre[2][:, 0:E], iot128[:, 0:64],
                               ALU.mult),
                  TS(mtb[0][:, 0:E], lpre[2][:, 0:E], 1.0, -64.0,
                     ALU.subtract, ALU.mult))
                S("vector", TT(lpre[3][:, 0:E], lpre[3][:, 0:E],
                               mtb[0][:, 0:E], ALU.add))
                S("vector", RED(pt[:, k:k + 1], lpre[3][:, 0:E], ALU.min))
                S("vector", TS(lpre[2][:, 0:E], iot128[:, 0:64], pt[:, k:k + 1], None,
                               ALU.is_equal))
                S("vector", TT(lpre[3][:, 0:E], lpre[2][:, 0:E],
                               scorekeep[:, 0:E], ALU.mult))
                S("vector", RED(pt[:, 8 + k:9 + k], lpre[3][:, 0:E], ALU.add))
                S("vector", STT(lpre[1][:, 0:E], lpre[2][:, 0:E], -1e30,
                                lpre[1][:, 0:E], ALU.mult, ALU.add))
        q.dma("gpsimd", [(ag_pay_in[c * CH + tt * 128:c * CH + (tt + 1) * 128, :],
                          payb[c * nt + tt]) for tt in range(nt)])

    if stop < 2:
        _finish_trunc(q, S, out, zerot, T)
        q.emit(); st.close(); return nc
    for c in range(NCHP):
        attn_chunk(c, True, x_prev)
    S("vector", TS(state[0:64, :], state[0:64, :], flagc[0:64, :], None,
                   ALU.mult),
      TS(tails, tails, flagc, None, ALU.mult))
    if stop < 3:
        _finish_trunc(q, S, out, zerot, T)
        q.emit(); st.close(); return nc
    for c in range(NCH):
        attn_chunk(c, False, x_own)

    if stop < 4:
        _finish_trunc(q, S, out, zerot, T)
        q.emit(); st.close(); return nc
    q.collective("AllGather", ALU.bypass, [list(range(n_ranks))],
                 [ag_pay_in.opt()], [pay_all.opt()])
    cc_pay = q.ccval
    q.collective("AllGather", ALU.bypass, [list(range(n_ranks))],
                 [ag_xf_in.opt()], [xf_all[0:TALL, :].opt()])
    cc_xf = q.ccval

    pool.pop()
    pool.push()
    ptile = pool.tile(16)
    otile = pool.tile(E)
    cex = pool.tile(E)
    ptab = pool.tile(E)
    mskb = pool.tile(TOPK * E)
    bkb = pool.tile(TOPK * E)
    tmpk = pool.tile(2 * TOPK * E)
    colk = pool.tile(16)
    swf = pool.tile(16)
    ctile = pool.tile(E)
    vald = pool.tile(8)
    ofs = pool.tile(1)
    ofs_i = pooli[:, 0:1]

    if stop < 5:
        _finish_trunc(q, S, out, zerot[0:128, 0:1024] if False else zerot, T)
        q.emit(); st.close(); return nc
    q.wait_cc("sync", cc_pay)
    q.wait_cc("gpsimd", cc_pay)
    S("vector", MS(swf, 0.0))
    for ti in range(NTILE_ALL):
        q.dma("sync", [(ptile, pay_all[ti * 128:(ti + 1) * 128, :])])
        S("vector",
          *[TS(mskb[:, k * 64:(k + 1) * 64], iot128[:, 0:64],
               ptile[:, k:k + 1], None, ALU.is_equal) for k in range(TOPK)],
          TS(colk[:, 15:16], iotc, float(ti * 128), None, ALU.add),
          TT(ofs[0:64, :], basec[0:64, :], etimes[0:64, :], ALU.add))
        S("vector",
          lambda e: e.tensor_reduce(otile,
                                    mskb.rearrange("p (k e) -> p e k", k=TOPK),
                                    mybir.AxisListType.X, ALU.add),
          CP(ofs_i[0:64, :], ofs[0:64, :]))
        S("tensor",
          MM(pb[0][0:128, 0:E], sltm, otile, True, True),
          MM(pb[5][0:128, 0:E], sltm, otile, True, False),
          MM(pb[5][0:128, 0:E], ones_row[0:1, 0:128], baser[0:1, 0:E],
             False, True),
          MM(pb[1][0:1, 0:E], ones_col[0:128, :], otile, True, True),
          MM(pb[2][0:64, 0:1], otile[:, 0:64], ones_col[0:128, :],
             True, True))
        S("vector", CP(cex, pb[0][0:128, 0:E]), CP(ptab, pb[5][0:128, 0:E]))
        S("vector",
          *[TT(tmpk[:, k * 64:(k + 1) * 64], mskb[:, k * 64:(k + 1) * 64],
               cex, ALU.mult) for k in range(TOPK)],
          *[TT(tmpk[:, (6 + k) * 64:(7 + k) * 64],
               mskb[:, k * 64:(k + 1) * 64], ptab, ALU.mult)
            for k in range(TOPK)])
        S("vector",
          *[RED(colk[:, k:k + 1], tmpk[:, k * 64:(k + 1) * 64], ALU.add)
            for k in range(TOPK)],
          *[RED(colk[:, 8 + k:9 + k], tmpk[:, (6 + k) * 64:(7 + k) * 64],
                ALU.add) for k in range(TOPK)])
        S("vector",
          *[TS(bkb[:, k * 64:(k + 1) * 64], iot128[:, 0:64],
               colk[:, k:k + 1], colk[:, 15:16], ALU.is_equal, ALU.mult)
            for k in range(TOPK)])
        S("tensor",
          *[MM(pb[3][0:64, 0:E], mskb[:, k * 64:(k + 1) * 64],
               bkb[:, k * 64:(k + 1) * 64], k == 0, k == TOPK - 1)
            for k in range(TOPK)])
        S("vector",
          CP(ctile[0:64, :], pb[3][0:64, 0:E]),
          TT(baser[0:1, 0:E], baser[0:1, 0:E], pb[1][0:1, 0:E], ALU.add),
          TT(basec[0:64, :], basec[0:64, :], pb[2][0:64, 0:1], ALU.add),
          *[TS(swf[:, 8 + k:9 + k], colk[:, 8 + k:9 + k], 511.0, None,
               ALU.min) for k in range(TOPK)],
          *[TS(tmpk[:, k:k + 1] if False else vald[:, k:k + 1],
               colk[:, 8 + k:9 + k], 512.0, None, ALU.is_lt)
            for k in range(TOPK)])
        S("vector",
          *[STT(swf[:, k:k + 1], ptile[:, k:k + 1], 512.0,
                swf[:, 8 + k:9 + k], ALU.mult, ALU.add) for k in range(TOPK)])
        S("vector",
          *[TT(swf[:, 8 + k:9 + k], ptile[:, 8 + k:9 + k],
               vald[:, k:k + 1], ALU.mult) for k in range(TOPK)])
        q.indirect([(tblh[:, :], IndirectOffsetOnAxis(ofs_i[0:64, :], 0),
                     ctile[0:64, :], None)])
        q.dma("gpsimd", [(slotwf[ti * 128:(ti + 1) * 128, :], swf)])
    q.dma("gpsimd", [(cnts_h[:, :], baser[0:1, 0:E])])

    if stop < 6:
        _finish_trunc(q, S, out, zerot, T)
        q.emit(); st.close(); return nc
    pool.pop()
    pool.push()
    xbt = pool.tiles(4, D)
    xbfm = pool.tiles(8, 512)
    he = pool.tiles(11, 512)
    ybt = pool.tiles(4, D)
    ewA = pool.tile(8 * 128)
    ewB = pool.tile(8 * 128)
    ewC = pool.tile(D)
    ewD = pool.tile(D)
    idxv = pool.tile(8)
    cnt8 = pool.tile(E)
    vld = pool.tile(4)
    swft = pool.tile(16)
    rankc = pool.tile(1)
    ofs = pool.tile(1)
    tbl_own_sb = smallC
    xmt = pool.tile(D)
    zt = pool.tile(D)
    gath = pool.tile(D)
    acc = pool.tile(D)
    moe_end = pool.off
    idx_i4 = pooli[:, 4:8]
    slot_i = pooli[:, 8:9]

    rank_s = pooli[:, 12:13]
    q.dma("gpsimd", [(rank_s[0:1, :], rank_i[:, :])])
    S("vector", CP(smallB[0:1, 14:15], rank_s[0:1, :]))
    S("tensor", bcast(pb[0], smallB[0:1, 14:15], 128))
    S("vector", CP(rankc, pb[0][0:128, 0:1]))
    S("vector", TS(ofs[0:epc, :], rankc[0:epc, :], float(epc), None,
                   ALU.mult))
    S("vector", TT(ofs[0:epc, :], ofs[0:epc, :], iotc[0:epc, :], ALU.add))
    S("vector", CP(pooli[0:epc, 13:14], ofs[0:epc, :]))
    cnts_flat = cnts_h.rearrange("o e -> (o e)")[:, None]
    q.indirect([(smallC[0:epc, 0:1], None, cnts_flat,
                 IndirectOffsetOnAxis(pooli[0:epc, 13:14], 0))])
    S("tensor", TR(pb[0][0:1, 0:epc], smallC[0:epc, 0:1], epc))
    S("vector", CP(smallC[0:1, 4:4 + epc], pb[0][0:1, 0:epc]))
    S("tensor", bcast(pb[0], smallC[0:1, 4:4 + epc], 128))
    S("vector", CP(cnt8[:, 0:epc], pb[0][0:128, 0:epc]))
    S("vector", TS(ofs[0:epc, :], rankc[0:epc, :], float(epc), None,
                   ALU.mult))
    S("vector", TT(ofs[0:epc, :], ofs[0:epc, :], iotc[0:epc, :], ALU.add))
    S("vector", TS(ofs[0:epc, :], ofs[0:epc, :], float(TSTRIDE), None,
                   ALU.mult))
    S("vector", CP(pooli[0:epc, 14:15], ofs[0:epc, :]))
    q.indirect([(tbl_own_sb[0:epc, 0:512], None, tblh[:, :],
                 IndirectOffsetOnAxis(pooli[0:epc, 14:15], 0))])
    q.dma("gpsimd", [(tbl_own[0:epc, :], tbl_own_sb[0:epc, 0:512])])
    for j in range(epc):
        q.dma("gpsimd", [(idxv[:, (j % 2) * 4:(j % 2) * 4 + 4],
                          tbl_own[j, 0:512].rearrange("(f q) -> q f", q=128))])
        jj = j % 2
        S("vector", *[TS(vld[:, col:col + 1],
                         iotc, float(col * 128), None, ALU.add)
                      for col in range(4)])
        S("vector", *[TS(vld[:, col:col + 1], vld[:, col:col + 1],
                         cnt8[:, j:j + 1], None, ALU.is_lt)
                      for col in range(4)])
        S("vector", TS(idxv[:, jj * 4:jj * 4 + 4],
                       idxv[:, jj * 4:jj * 4 + 4], float(TALL), None,
                       ALU.subtract))
        S("vector", TT(idxv[:, jj * 4:jj * 4 + 4],
                       idxv[:, jj * 4:jj * 4 + 4], vld, ALU.mult))
        S("vector", TS(idxv[:, jj * 4:jj * 4 + 4],
                       idxv[:, jj * 4:jj * 4 + 4], float(TALL), None,
                       ALU.add))
        S("vector", CP(idx_i4, idxv[:, jj * 4:jj * 4 + 4]))
        if j == 0:
            q.wait_cc("gpsimd", cc_xf)
        q.indirect([(xbt[col], None, xf_all[:, :],
                     IndirectOffsetOnAxis(idx_i4[:, col:col + 1], 0))
                    for col in range(4)])
        for f0 in range(0, 8, 2):
            S("tensor", *[TR(pb[4 * i + tt][0:128, 0:128],
                             xbt[tt][:, (f0 + i) * 128:(f0 + i + 1) * 128], 128)
                          for i in range(2) for tt in range(4)])
            S("vector", *[CP(xbfm[f0 + i][:, tt * 128:(tt + 1) * 128],
                             pb[4 * i + tt][0:128, 0:128])
                          for i in range(2) for tt in range(4)])
        mws = [128, 128, 128, 128, 128, 64]
        def eA(wdram, j, cb_):
            wsrc = wdram.rearrange("e (kc p) m -> e p kc m", p=128)
            for bi in range(6):
                wb = ewA if bi % 2 == 0 else ewB
                mw = mws[bi]
                q.dma("sync", [(
                    wb[:, 0:8 * mw].rearrange("p (kc m) -> p kc m", kc=8),
                    wsrc[j, :, 0:8, bi * 128:bi * 128 + mw])],
                    lag=None if bi <= 1 else {"tensor": 1, "vector": 99,
                                              "scalar": 99, "gpsimd": 99})
                bank = pb[bi % 4]
                q.step("tensor",
                       [MM(bank[0:mw, 0:512],
                           wb[:, kc * mw:kc * mw + mw],
                           xbfm[kc][:, 0:512], kc == 0, kc == 7)
                        for kc in range(8)],
                       lag=None if bi == 0 else {"vector": 3, "scalar": 3})
                cb_(bi, bank[0:mw, 0:512], mw)
        eA(w_e1, j, lambda ic, ps, mwid:
           S("scalar", ACT(he[ic][0:mwid, :], ps, AF.Silu)))
        eA(w_e3, j, lambda ic, ps, mwid:
           S("vector", TT(he[ic][0:mwid, :], he[ic][0:mwid, :], ps,
                          ALU.mult)))
        for kc2 in range(6):
            mwid = mws[kc2]
            wb = ewC if kc2 % 2 == 0 else ewD
            q.dma("sync", [(wb[0:mwid, 0:D],
                            w_e2[j, kc2 * 128:kc2 * 128 + mwid, :])],
                  lag=None if kc2 <= 1 else {"tensor": 1, "vector": 99, "scalar": 99, "gpsimd": 99})
            q.step("tensor", [MM(pb[sl * 2 + si][0:128, 0:512],
                                 he[kc2][0:mwid, sl * 128:(sl + 1) * 128],
                                 wb[0:mwid, si * 512:(si + 1) * 512],
                                 kc2 == 0, kc2 == 5)
                              for sl in range(4) for si in range(2)],
                   lag=None if kc2 == 0 else {"vector": 99, "scalar": 99})
        S("vector", *[CP(ybt[sl][:, si * 512:(si + 1) * 512],
                         pb[sl * 2 + si][0:128, 0:512])
                      for sl in range(4) for si in range(2)])
        q.dma("gpsimd", [(ag_yb_in[j * CAP + sl * 128:j * CAP + (sl + 1) * 128,
                                   :], ybt[sl]) for sl in range(4)])

    q.collective("AllGather", ALU.bypass, [list(range(n_ranks))],
                 [ag_yb_in.opt()], [yb_all.opt()])
    cc_yb = q.ccval

    if stop < 7:
        _finish_trunc(q, S, out, zerot, T)
        q.emit(); st.close(); return nc
    for c in range(NCH):
        nt = CH // 128
        q.dma("sync", [(xbt[tt], ag_xf_in[c * CH + tt * 128:
                                          c * CH + (tt + 1) * 128, :])
                       for tt in range(nt)])
        for f0 in range(0, 8, 2):
            S("tensor", *[TR(pb[4 * i + tt][0:128, 0:128],
                             xbt[tt][:, (f0 + i) * 128:(f0 + i + 1) * 128], 128)
                          for i in range(2) for tt in range(4)])
            S("vector", *[CP(xbfm[f0 + i][:, tt * 128:(tt + 1) * 128],
                             pb[4 * i + tt][0:128, 0:128])
                          for i in range(2) for tt in range(4)])
        def sA(wdram, cb_):
            wsrc = wdram.rearrange("(kc p) m -> p kc m", p=128)
            for bi in range(11):
                wb = ewA if bi % 2 == 0 else ewB
                q.dma("sync", [(
                    wb[:, 0:8 * 128].rearrange("p (kc m) -> p kc m", kc=8),
                    wsrc[:, 0:8, bi * 128:(bi + 1) * 128])],
                    lag=None if bi <= 1 else {"tensor": 1, "vector": 99,
                                              "scalar": 99, "gpsimd": 99})
                bank = pb[bi % 4]
                q.step("tensor",
                       [MM(bank[0:128, 0:512],
                           wb[:, kc * 128:(kc + 1) * 128],
                           xbfm[kc][:, 0:512], kc == 0, kc == 7)
                        for kc in range(8)],
                       lag=None if bi == 0 else {"vector": 3, "scalar": 3})
                cb_(bi, bank[0:128, 0:512])
        sA(w_s1, lambda ic, ps: S("scalar", ACT(he[ic], ps, AF.Silu)))
        sA(w_s3, lambda ic, ps: S("vector", TT(he[ic], he[ic], ps, ALU.mult)))
        for kc2 in range(11):
            wb = ewC if kc2 % 2 == 0 else ewD
            q.dma("sync", [(wb[:, 0:D], w_s2[kc2 * 128:(kc2 + 1) * 128, :])],
                  lag=None if kc2 <= 1 else {"tensor": 1, "vector": 99, "scalar": 99, "gpsimd": 99})
            q.step("tensor", [MM(pb[sl * 2 + si][0:128, 0:512],
                                 he[kc2][:, sl * 128:(sl + 1) * 128],
                                 wb[:, si * 512:(si + 1) * 512],
                                 kc2 == 0, kc2 == 10)
                              for sl in range(4) for si in range(2)],
                   lag=None if kc2 == 0 else {"vector": 99, "scalar": 99})
        S("vector", *[CP(ybt[sl][:, si * 512:(si + 1) * 512],
                         pb[sl * 2 + si][0:128, 0:512])
                      for sl in range(4) for si in range(2)])
        q.dma("gpsimd", [(z_hbm[c * CH + sl * 128:c * CH + (sl + 1) * 128, :],
                          ybt[sl]) for sl in range(4)])

    if stop < 8:
        _finish_trunc(q, S, out, zerot, T)
        q.emit(); st.close(); return nc
    q.wait_cc("gpsimd", cc_yb)
    for tt in range(T // 128):
        q.dma("sync", [(xmt, x_hbm[tt * 128:(tt + 1) * 128, :]),
                       (zt, z_hbm[tt * 128:(tt + 1) * 128, :])])
        S("vector", TS(ofs, rankc, float(T), None, ALU.mult))
        S("vector", TT(ofs, ofs, iotc, ALU.add))
        S("vector", TS(ofs, ofs, 1.0, float(tt * 128), ALU.mult, ALU.add))
        S("vector", CP(pooli[:, 15:16], ofs))
        q.indirect([(swft, None, slotwf[:, :],
                     IndirectOffsetOnAxis(pooli[:, 15:16], 0))])
        for k in range(TOPK):
            S("vector", CP(slot_i, swft[:, k:k + 1]))
            q.indirect([(gath, None, yb_all[:, :],
                         IndirectOffsetOnAxis(slot_i, 0))])
            if k == 0:
                S("vector", TS(acc, gath, swft[:, 8:9], None, ALU.mult))
            else:
                S("vector", STT(acc, gath, swft[:, 8 + k:9 + k], acc,
                                ALU.mult, ALU.add))
        S("vector", TT(acc, acc, zt, ALU.add))
        S("vector", TT(acc, acc, modb[5], ALU.mult))
        S("vector", TT(acc, acc, xmt, ALU.add))
        q.dma("gpsimd", [(out[tt * 128:(tt + 1) * 128, :], acc)])
    S("gpsimd", lambda e: e.engine_nop())

    q.emit()
    st.close()
    return nc


def prep_inmaps(inputs, T, n_ranks):
    def t(a):
        return np.ascontiguousarray(np.transpose(np.asarray(a, np.float32)))

    x = np.ascontiguousarray(inputs["L0_attn_x"], dtype=np.float32)
    epc = E // n_ranks
    shared = {
        "wmod": np.ascontiguousarray(inputs["L0_wmod"], np.float32),
        "ln1w": inputs["L0_attn_ln1_w"].reshape(1, D).astype(np.float32),
        "ln1b": inputs["L0_attn_ln1_b"].reshape(1, D).astype(np.float32),
        "ln2w": inputs["L0_attn_ln2_w"].reshape(1, D).astype(np.float32),
        "ln2b": inputs["L0_attn_ln2_b"].reshape(1, D).astype(np.float32),
        "normw": inputs["L0_attn_norm_w"].reshape(1, DI).astype(np.float32),
        "mlpb1": inputs["L0_attn_mlp_b1"].reshape(1, MLP_H).astype(np.float32),
        "mlpb2": inputs["L0_attn_mlp_b2"].reshape(1, D).astype(np.float32),
        "gateb": inputs["L0_ffn_gate_b"].reshape(1, E).astype(np.float32),
        "a_log": inputs["L0_attn_A_log"].reshape(1, NH).astype(np.float32),
        "d_vec": inputs["L0_attn_D"].reshape(1, NH).astype(np.float32),
        "dtbias": inputs["L0_attn_dt_bias"].reshape(1, NH).astype(np.float32),
        "convw": np.ascontiguousarray(inputs["L0_attn_conv_w"], np.float32),
        "convb": inputs["L0_attn_conv_b"].reshape(CONVD, 1).astype(np.float32),
        "w_ip": t(inputs["L0_attn_in_proj_w"]),
        "w_op": t(inputs["L0_attn_out_proj_w"]),
        "w_m1": t(inputs["L0_attn_mlp_w1"]),
        "w_m2": t(inputs["L0_attn_mlp_w2"]),
        "w_gt": t(inputs["L0_ffn_gate_w"]),
        "w_s1": t(inputs["L0_ffn_ws1"]),
        "w_s3": t(inputs["L0_ffn_ws3"]),
        "w_s2": t(inputs["L0_ffn_ws2"]),
    }
    we1 = np.ascontiguousarray(np.transpose(
        np.asarray(inputs["L0_ffn_w1"], dtype=np.float32), (0, 2, 1)))
    we3 = np.ascontiguousarray(np.transpose(
        np.asarray(inputs["L0_ffn_w3"], dtype=np.float32), (0, 2, 1)))
    we2 = np.ascontiguousarray(np.transpose(
        np.asarray(inputs["L0_ffn_w2"], dtype=np.float32), (0, 2, 1)))
    c_all = np.asarray(inputs["L0_c"], dtype=np.float32)

    in_maps = []
    zeros_prev = np.zeros((T, D), np.float32)
    for r in range(n_ranks):
        b, h = r // 2, r % 2
        m = dict(shared)
        m["c_row"] = np.ascontiguousarray(c_all[b:b + 1])
        m["x_own"] = np.ascontiguousarray(x[b, h * T:(h + 1) * T])
        m["x_prev"] = (np.ascontiguousarray(x[b, 0:T]) if h == 1
                       else zeros_prev)
        m["flag"] = np.array([[1.0 if h == 1 else 0.0]], np.float32)
        m["rank_i"] = np.array([[r]], np.int32)
        m["w_e1"] = np.ascontiguousarray(we1[r * epc:(r + 1) * epc])
        m["w_e3"] = np.ascontiguousarray(we3[r * epc:(r + 1) * epc])
        m["w_e2"] = np.ascontiguousarray(we2[r * epc:(r + 1) * epc])
        in_maps.append(m)
    return in_maps


def kernel(**inputs):
    T = 2048
    key = (T,)
    if key not in _CACHE:
        _CACHE[key] = build_program(T=T, TP=T, n_ranks=NCORES)
    nc = _CACHE[key]
    in_maps = prep_inmaps(inputs, T, NCORES)
    res = run_bass_kernel_spmd(nc, in_maps, core_ids=list(range(NCORES)))
    outs = [np.asarray(res.results[r]["out"]) for r in range(NCORES)]
    x = inputs["L0_attn_x"]
    B, L, _ = x.shape
    full = np.empty((B, L, D), np.float32)
    for r in range(NCORES):
        b, h = r // 2, r % 2
        full[b, h * T:(h + 1) * T] = outs[r]
    return full


# version 14
# speedup vs baseline: 1.2756x; 1.0123x over previous
import numpy as np
from contextlib import ExitStack

import concourse.bass as bass
import concourse.mybir as mybir
from concourse.bass import IndirectOffsetOnAxis
from concourse.bass_utils import run_bass_kernel_spmd

F32 = mybir.dt.float32
I32 = mybir.dt.int32
AF = mybir.ActivationFunctionType
ALU = mybir.AluOpType

D = 1024
DI = 2048
DSTATE = 64
NH = 32
HD = 64
DCONV = 4
CONVD = DI + 2 * DSTATE
DIP = 2 * DI + 2 * DSTATE + NH
NZX = (DIP + 127) // 128
NXBC = CONVD // 128
EPS = 1e-5
E = 64
CAP = 512
TOPK = 6
MOE_I = 704
SH_I = 1408
MLP_H = 4096
NCORES = 8
EPC = E // NCORES
TSTRIDE = 16448
SC = 128
CH = 512
BIGNEG = -1e30

_CACHE = {}


class SQ:
    ENGS = ("sync", "scalar", "vector", "tensor", "gpsimd")

    def __init__(self, nc, stack):
        self.nc = nc
        self.stack = stack
        self.ops = {e: [] for e in self.ENGS}
        self.csem = {e: self._mk(f"c_{e}") for e in self.ENGS}
        self.dsem = {e: self._mk(f"d_{e}") for e in self.ENGS}
        self.cval = {e: 0 for e in self.ENGS}
        self.dval = {e: 0 for e in self.ENGS}
        self.seen_c = {e: {f: 0 for f in self.ENGS} for e in self.ENGS}
        self.seen_d = {e: {f: 0 for f in self.ENGS} for e in self.ENGS}
        self.ccsem = self._mk("cc")
        self.ccval = 0
        self.seen_cc = {e: 0 for e in self.ENGS}

    def _mk(self, name):
        return self.stack.enter_context(self.nc.semaphore(name))

    def _barrier(self, eng, lag=None, cap=None):
        for f in self.ENGS:
            if f == eng:
                continue
            v = self.cval[f]
            if lag and f in lag:
                v = max(0, v - lag[f])
            if cap and f in cap:
                v = min(v, cap[f])
            if v > self.seen_c[eng][f]:
                self.ops[eng].append(
                    lambda e, h=self.csem[f], v=v: e.wait_ge(h, v))
                self.seen_c[eng][f] = v
            dv = self.dval[f]
            if dv > self.seen_d[eng][f]:
                self.ops[eng].append(
                    lambda e, h=self.dsem[f], v=dv: e.wait_ge(h, v))
                self.seen_d[eng][f] = dv
        dv = self.dval[eng]
        if dv > self.seen_d[eng][eng]:
            self.ops[eng].append(
                lambda e, h=self.dsem[eng], v=dv: e.wait_ge(h, v))
            self.seen_d[eng][eng] = dv

    def step(self, eng, fns, lag=None, cap=None):
        if not isinstance(fns, (list, tuple)):
            fns = [fns]
        self._barrier(eng, lag, cap)
        if eng in ("scalar", "vector", "gpsimd"):
            self.ops[eng].append(lambda e: e.drain(fusable=True))
        for fn in fns[:-1]:
            self.ops[eng].append(lambda e, fn=fn: fn(e))
        self.cval[eng] += 1
        self.ops[eng].append(
            lambda e, fn=fns[-1], h=self.csem[eng]:
            fn(e).then_inc(h, 1))

    def dma(self, eng, pairs, lag=None):
        self._barrier(eng, lag)
        for dst, src in pairs:
            self.dval[eng] += 16
            self.ops[eng].append(
                lambda e, d=dst, s=src, h=self.dsem[eng]:
                e.dma_start(out=d, in_=s, allow_slow_non_contiguous=True).then_inc(h, 16))

    def indirect(self, pairs):
        self._barrier("gpsimd")
        for o, oo, i, io in pairs:
            self.dval["gpsimd"] += 16
            self.ops["gpsimd"].append(
                lambda e, o=o, oo=oo, i=i, io=io, h=self.dsem["gpsimd"]:
                e.indirect_dma_start(out=o, out_offset=oo, in_=i,
                                     in_offset=io).then_inc(h, 16))

    def collective(self, kind, op, groups, ins, outs):
        self._barrier("gpsimd")
        self.ccval += 1
        self.ops["gpsimd"].append(
            lambda e, k=kind, o=op, g=groups, i=ins, u=outs, h=self.ccsem:
            e.collective_compute(k, o, replica_groups=g, ins=i,
                                 outs=u).then_inc(h, 1))

    def wait_cc(self, eng, upto=None):
        v = self.ccval if upto is None else upto
        if v > self.seen_cc[eng]:
            self.ops[eng].append(lambda e, h=self.ccsem, v=v: e.wait_ge(h, v))
            self.seen_cc[eng] = v

    def raw(self, eng, fn):
        self.ops[eng].append(fn)

    def emit(self):
        ops = self.ops
        with self.nc.Block() as block:
            @block.sync
            def _(e):
                for f in ops["sync"]:
                    f(e)

            @block.scalar
            def _(e):
                for f in ops["scalar"]:
                    f(e)

            @block.vector
            def _(e):
                for f in ops["vector"]:
                    f(e)

            @block.tensor
            def _(e):
                for f in ops["tensor"]:
                    f(e)

            @block.gpsimd
            def _(e):
                for f in ops["gpsimd"]:
                    f(e)


class Pool:
    def __init__(self, t, n):
        self.t = t
        self.n = n
        self.off = 0
        self.hi = 0
        self.marks = []

    def tile(self, w):
        assert self.off + w <= self.n, f"pool overflow {self.off}+{w}"
        ap = self.t[:, self.off:self.off + w]
        self.off += w
        self.hi = max(self.hi, self.off)
        return ap

    def tiles(self, k, w):
        return [self.tile(w) for _ in range(k)]

    def push(self):
        self.marks.append(self.off)

    def pop(self):
        self.off = self.marks.pop()


def _finish_trunc(q, S, out, zerot, T):
    q.dma("gpsimd", [(out[tt * 128:(tt + 1) * 128, :], zerot[:, 0:1024])
                     for tt in range(T // 128)])
    S("gpsimd", lambda e: e.engine_nop())


def build_program(T=2048, TP=2048, n_ranks=NCORES, stop=99):
    nc = bass.Bass("TRN2", target_bir_lowering=False)
    st = ExitStack()
    q = SQ(nc, st)

    NCH = T // CH
    NCHP = TP // CH
    TALL = T * n_ranks
    NTILE_ALL = TALL // 128
    epc = E // n_ranks

    def din(name, shape, dt=F32):
        return nc.dram_tensor(name, shape, dt, kind="ExternalInput").ap()

    x_own = din("x_own", [T, D])
    x_prev = din("x_prev", [TP, D])
    flag = din("flag", [1, 1])
    c_row = din("c_row", [1, D])
    wmod = din("wmod", [D, 6 * D])
    ln1w = din("ln1w", [1, D]); ln1b = din("ln1b", [1, D])
    ln2w = din("ln2w", [1, D]); ln2b = din("ln2b", [1, D])
    normw = din("normw", [1, DI])
    mlpb1 = din("mlpb1", [1, MLP_H]); mlpb2 = din("mlpb2", [1, D])
    gateb = din("gateb", [1, E])
    a_log = din("a_log", [1, NH]); d_vec = din("d_vec", [1, NH])
    dtbias = din("dtbias", [1, NH])
    convw = din("convw", [CONVD, DCONV]); convb = din("convb", [CONVD, 1])
    w_ip = din("w_ip", [D, DIP])
    w_op = din("w_op", [DI, D])
    w_m1 = din("w_m1", [D, MLP_H])
    w_m2 = din("w_m2", [MLP_H, D])
    w_gt = din("w_gt", [D, E])
    w_s1 = din("w_s1", [D, SH_I]); w_s3 = din("w_s3", [D, SH_I])
    w_s2 = din("w_s2", [SH_I, D])
    w_e1 = din("w_e1", [epc, D, MOE_I])
    w_e3 = din("w_e3", [epc, D, MOE_I])
    w_e2 = din("w_e2", [epc, MOE_I, D])
    rank_i = din("rank_i", [1, 1], I32)

    out = nc.dram_tensor("out", [T, D], F32, kind="ExternalOutput").ap()

    def dint(name, shape, dt=F32):
        return nc.dram_tensor(name, shape, dt).ap()

    x_hbm = dint("x_hbm", [T, D])
    ag_xf_in = dint("ag_xf_in", [T, D])
    xf_all = dint("xf_all", [TALL + 128, D])
    ag_pay_in = dint("ag_pay_in", [T, 16])
    pay_all = dint("pay_all", [TALL, 16])
    tblh = dint("tblh", [E * TSTRIDE, 1])
    slotwf = dint("slotwf", [TALL, 16])
    cnts_h = dint("cnts_h", [1, E])
    tbl_own = dint("tbl_own", [E, 512])
    ag_yb_in = dint("ag_yb_in", [epc * CAP, D])
    yb_all = dint("yb_all", [E * CAP, D])
    z_hbm = dint("z_hbm", [T, D])

    POOLW = 53144
    poolt = st.enter_context(nc.sbuf_tensor("pool", [128, POOLW], F32))
    pool = Pool(poolt, POOLW)
    pooli = st.enter_context(nc.sbuf_tensor("pooli", [128, 64], I32))

    ones_row = pool.tile(128)
    ones_col = pool.tile(1)
    iot128 = pool.tile(128)
    iotc = pool.tile(1)
    ident = pool.tile(128)
    sltm = pool.tile(128)
    maskbig = pool.tile(128)
    modb = pool.tiles(6, D)
    lncol = pool.tiles(4, 8)
    b2b = pool.tile(D)
    gbB = pool.tile(E)
    nrmc = pool.tile(16)
    b1c = pool.tile(32)
    a128 = pool.tile(NH)
    d128 = pool.tile(NH)
    acol = pool.tile(1)
    nacol = pool.tile(1)
    dtbc = pool.tile(1)
    flagc = pool.tile(1)
    cw = pool.tiles(NXBC, DCONV)
    cb = pool.tiles(NXBC, 1)
    c_fm = pool.tile(8)
    state = pool.tile(NH * HD)
    tails = pool.tile(NXBC * 4)
    smallA = pool.tile(96)
    smallB = pool.tile(16)
    lnst = pool.tile(64)
    smallC = pool.tile(512)
    etimes = pool.tile(1)
    constc = pool.tile(4)
    baser = pool.tile(E)
    basec = pool.tile(1)

    persist_end = pool.off

    pool.push()
    srow = pool.tile(6 * D)
    zerot = pool.tile(2056)
    pool.pop()

    pool.push()
    xchunk = pool.tiles(4, D)
    htok = pool.tiles(4, D)
    x1tok = pool.tiles(4, D)
    hfm_big = pool.tile(8 * CH)
    hfm = [hfm_big[:, i * CH:(i + 1) * CH] for i in range(8)]
    zx = pool.tiles(NZX, CH)
    cvs = pool.tile(CH)
    ccopy = pool.tile(SC)
    wbufA = pool.tile(8 * 128)
    wbufB = pool.tile(8 * 128)
    wbufC = pool.tile(D)
    wbufD = pool.tile(D)
    xs_tok = hfm_big[:, 0:DI]
    y_tok = hfm_big[:, DI:2 * DI]
    bct = pool.tile(128)
    sdt = pool.tile(SC)
    sdtT = pool.tile(96)
    row0 = pool.tile(512)
    wfm = pool.tile(SC)
    dkc = pool.tile(1)
    dkrow = pool.tile(NH)
    asig = pool.tile(SC)
    aST = pool.tile(NH)
    easT = pool.tile(NH)
    dkerb = pool.tile(NH)
    scoresT = pool.tile(128)
    lpre = pool.tiles(4, 128)
    mtb = pool.tiles(4, 128)
    dtt = pool.tile(CH)
    onescN = pool.tile(SC)
    payb = pool.tiles(4, 16)
    scorekeep = pool.tile(E)
    attn_end = pool.off
    assert attn_end < POOLW, attn_end

    print('pool hi', pool.hi, 'off', pool.off)
    pb = [st.enter_context(nc.psum_tensor(f"pb{i}", [128, 512], F32))
          for i in range(8)]

    def S(eng, *fns):
        q.step(eng, list(fns))

    def SL(eng, lag, *fns):
        q.step(eng, list(fns), lag=lag)

    def MM(psum, lhsT, rhs, start, stop):
        return lambda e, o=psum, l=lhsT, r=rhs, a=start, b=stop: \
            e.matmul(o, l, r, start=a, stop=b)

    def TR(psum, src, npart):
        return lambda e, o=psum, i=src, n=npart: \
            e.transpose(o, i, ident[0:n, 0:n])

    def CP(dst, src):
        return lambda e, d=dst, s=src: e.tensor_copy(d, s)

    def TT(dst, a, b, op):
        return lambda e, d=dst, a=a, b=b, o=op: e.tensor_tensor(d, a, b, o)

    def TS(dst, a, s1, s2, op0, op1=ALU.bypass):
        return lambda e, d=dst, a=a, s1=s1, s2=s2, o0=op0, o1=op1: \
            e.tensor_scalar(d, a, s1, s2, o0, o1)

    def STT(dst, in0, sc, in1, op0, op1):
        return lambda e, d=dst, i0=in0, s=sc, i1=in1, o0=op0, o1=op1: \
            e.scalar_tensor_tensor(d, i0, s, i1, o0, o1)

    def ACT(dst, src, f, bias=0.0, scale=1.0):
        return lambda e, d=dst, s=src, f=f, b=bias, sc=scale: \
            e.activation(d, s, f, bias=b, scale=sc)

    def RED(dst, src, op):
        return lambda e, d=dst, s=src, o=op: \
            e.tensor_reduce(d, s, mybir.AxisListType.X, o)

    def MS(dst, v):
        return lambda e, d=dst, v=v: e.memset(d, v)

    def bcast(psum, row, nparts):
        w = row.shape[-1]
        return MM(psum[0:nparts, 0:w], ones_row[0:1, 0:nparts], row,
                  True, True)

    q.dma("sync", [
        (c_fm, c_row.rearrange("o (q f) -> (o f) q", f=128)),
        (smallA[0:1, 0:NH], a_log[:, :]),
        (smallA[0:1, NH:2 * NH], d_vec[:, :]),
        (smallA[0:1, 2 * NH:3 * NH], dtbias[:, :]),
        (flagc[0:1, :], flag[:, :]),
        (nrmc, normw.rearrange("o (q f) -> (o f) q", f=128)),
        (b1c, mlpb1.rearrange("o (q f) -> (o f) q", f=128)),
        (smallC[0:1, 0:E], gateb[:, :]),
        (lncol[0], ln1w.rearrange("o (q f) -> (o f) q", f=128)),
        (lncol[1], ln1b.rearrange("o (q f) -> (o f) q", f=128)),
        (lncol[2], ln2w.rearrange("o (q f) -> (o f) q", f=128)),
        (lncol[3], ln2b.rearrange("o (q f) -> (o f) q", f=128)),
        (srow[0:1, 4 * D:5 * D], mlpb2[:, :]),
    ] + [(cw[i], convw[i * 128:(i + 1) * 128, :]) for i in range(NXBC)]
      + [(cb[i], convb[i * 128:(i + 1) * 128, :]) for i in range(NXBC)])

    S("vector", MS(ident, 1.0))
    S("vector", lambda e: e.tensor_tensor_scan(iot128, ident, ident, 0.0,
                                               ALU.mult, ALU.add))
    S("vector", TS(iot128, iot128, 1.0, None, ALU.subtract))
    S("tensor", TR(pb[0][0:128, 0:1], iot128[0:1, 0:128], 1))
    S("vector", CP(iotc, pb[0][0:128, 0:1]))
    S("vector",
      MS(ones_row[0:1, :], 1.0), MS(ones_col, 1.0), MS(onescN[0:NH, :], 1.0),
      MS(zerot, 0.0), MS(state[0:64, :], 0.0), MS(tails, 0.0),
      MS(baser[0:1, :], 0.0), MS(basec[0:64, :], 0.0),
      TS(ident, iot128, iotc, None, ALU.is_equal),
      TS(sltm, iot128, iotc, None, ALU.is_gt),
      TS(maskbig, iot128, iotc, None, ALU.is_ge),
      TS(etimes[0:64, :], iotc[0:64, :], float(TSTRIDE), None, ALU.mult),
      MS(constc[:, 0:1], float(EPS)), MS(constc[:, 1:2], 1.0),
      MS(constc[:, 2:3], -1.0), MS(constc[:, 3:4], 1.0 / DI))
    S("vector", TS(maskbig, maskbig, 1.0, 1e30, ALU.subtract, ALU.mult))
    S("scalar", ACT(smallA[0:1, 0:NH], smallA[0:1, 0:NH], AF.Exp))
    S("scalar", ACT(smallA[0:1, 0:NH], smallA[0:1, 0:NH], AF.Copy, scale=constc[0:1, 2:3]),
      ACT(c_fm, c_fm, AF.Silu))
    ztb = tblh.rearrange("(a b) o -> a (b o)", b=8224)
    q.dma("gpsimd", [(ztb[i * 128:(i + 1) * 128, j * 1028:(j + 1) * 1028],
                      zerot[:, 0:1028]) for i in range(1) for j in range(8)])
    q.dma("gpsimd", [(xf_all[TALL:TALL + 128, :], zerot[:, 0:1024])])

    bc_jobs = [(b2b, srow[0:1, 4 * D:5 * D]), (gbB, smallC[0:1, 0:E]),
               (a128, smallA[0:1, 0:NH]), (d128, smallA[0:1, NH:2 * NH])]
    for i, (dst, row) in enumerate(bc_jobs):
        w = row.shape[-1]
        for si in range((w + 511) // 512):
            lo, hi = si * 512, min(w, (si + 1) * 512)
            S("tensor", bcast(pb[0], row[:, lo:hi], 128))
            S("vector", CP(dst[:, lo:hi], pb[0][0:128, 0:hi - lo]))
    S("tensor", TR(pb[1][0:NH, 0:1], smallA[0:1, 0:NH], 1))
    S("vector", CP(acol[0:NH, :], pb[1][0:NH, 0:1]))
    S("vector", TS(nacol[0:NH, :], acol[0:NH, :], -1.0, None, ALU.mult))
    S("tensor", TR(pb[1][0:NH, 0:1], smallA[0:1, 2 * NH:3 * NH], 1))
    S("vector", CP(dtbc[0:NH, :], pb[1][0:NH, 0:1]))
    S("tensor", bcast(pb[1], flagc[0:1, 0:1], 128))
    S("vector", CP(flagc, pb[1][0:128, 0:1]))

    if stop < 1:
        _finish_trunc(q, S, out, zerot, T)
        q.emit(); st.close(); return nc
    NW = 6 * D // 512
    for nwi in range(NW):
        for kc in range(8):
            wb = wbufC if kc % 2 == 0 else wbufD
            q.dma("sync", [(wb[:, 0:512],
                            wmod[kc * 128:(kc + 1) * 128,
                                 nwi * 512:(nwi + 1) * 512])])
            S("tensor", MM(pb[2][0:1, 0:512], c_fm[:, kc:kc + 1],
                           wb[:, 0:512], kc == 0, kc == 7))
        S("vector", CP(srow[0:1, nwi * 512:(nwi + 1) * 512],
                       pb[2][0:1, 0:512]))
    for gi in (2, 5):
        sl = srow[0:1, gi * D:(gi + 1) * D]
        s0 = zerot[0:1, 0:D]
        s1 = zerot[0:1, D:2 * D]
        S("vector", TS(s0, sl, 0.0, 2.0, ALU.is_ge, ALU.mult),
          TS(s1, sl, -1.0, None, ALU.mult))
        S("vector", TS(s0, s0, 1.0, None, ALU.subtract),
          TT(s1, s1, sl, ALU.max))
        S("vector", TS(s1, s1, 0.25, None, ALU.max))
        S("vector", TT(sl, s0, s1, ALU.mult))
    S("vector", TS(srow[0:1, D:2 * D], srow[0:1, D:2 * D], 1.0, None, ALU.add),
      TS(srow[0:1, 4 * D:5 * D], srow[0:1, 4 * D:5 * D], 1.0, None, ALU.add))
    for i in range(6):
        row = srow[0:1, i * D:(i + 1) * D]
        for si in range(2):
            lo, hi = si * 512, (si + 1) * 512
            S("tensor", bcast(pb[0], row[:, lo:hi], 128))
            S("vector", CP(modb[i][:, lo:hi], pb[0][0:128, 0:512]))

    def ln_tiles(src_tiles, dst_tiles, mul_tile, add_tile, nt):
        S("vector", *[f for tt in range(nt) for f in (
            (lambda e, s=src_tiles[tt], tt=tt: e.bn_stats(
                lnst[0:128, tt * 16:tt * 16 + 6], s[:, 0:512])),
            (lambda e, s=src_tiles[tt], tt=tt: e.bn_stats(
                lnst[0:128, tt * 16 + 6:tt * 16 + 12], s[:, 512:1024])))])
        S("vector", *[(lambda e, tt=tt: e.bn_aggr(
            lnst[0:128, tt * 16 + 12:tt * 16 + 14],
            lnst[0:128, tt * 16:tt * 16 + 12])) for tt in range(nt)])
        S("scalar", *[ACT(lnst[0:128, tt * 16 + 14:tt * 16 + 15],
                          lnst[0:128, tt * 16 + 13:tt * 16 + 14],
                          AF.Sqrt, bias=constc[:, 0:1]) for tt in range(nt)])
        S("vector", *[(lambda e, tt=tt: e.reciprocal(
            lnst[0:128, tt * 16 + 14:tt * 16 + 15],
            lnst[0:128, tt * 16 + 14:tt * 16 + 15])) for tt in range(nt)])
        S("vector", *[TS(dst_tiles[tt], src_tiles[tt],
                         lnst[0:128, tt * 16 + 12:tt * 16 + 13],
                         lnst[0:128, tt * 16 + 14:tt * 16 + 15],
                         ALU.subtract, ALU.mult) for tt in range(nt)])
        if mul_tile is not None:
            S("vector", *[TT(dst_tiles[tt], dst_tiles[tt], mul_tile, ALU.mult)
                          for tt in range(nt)])
            S("vector", *[TT(dst_tiles[tt], dst_tiles[tt], add_tile, ALU.add)
                          for tt in range(nt)])

    def to_fm(src_tiles, dst_fm_tiles, nf, nt, wcol=None, bcol=None):
        jobs = []
        for fc in range(nf):
            for tt in range(nt):
                jobs.append((fc, tt))
        for j0 in range(0, len(jobs), 4):
            grp = jobs[j0:j0 + 4]
            S("tensor", *[TR(pb[4 + i][0:128, 0:128],
                             src_tiles[tt][:, fc * 128:(fc + 1) * 128], 128)
                          for i, (fc, tt) in enumerate(grp)])
            if wcol is None:
                S("vector", *[CP(dst_fm_tiles[fc][:, tt * 128:(tt + 1) * 128],
                                 pb[4 + i][0:128, 0:128])
                              for i, (fc, tt) in enumerate(grp)])
            else:
                S("vector", *[TS(dst_fm_tiles[fc][:, tt * 128:(tt + 1) * 128],
                                 pb[4 + i][0:128, 0:128],
                                 wcol[:, fc:fc + 1], bcol[:, fc:fc + 1],
                                 ALU.mult, ALU.add)
                              for i, (fc, tt) in enumerate(grp)])

    def patternA(wdram, m0t, m1t, out_cb, Tn, kparts=8, mtot=None,
                 floors=None):
        if mtot is None:
            mtot = m1t * 128
        mcs = list(range(m0t, m1t))
        wid = {mc: min(128, mtot - mc * 128) for mc in mcs}
        wsrc = wdram.rearrange("(kc p) m -> p kc m", p=128)
        steps_done = 0
        buf_free_at = {0: None, 1: None}
        if floors is None:
            fl = {"vector": q.cval["vector"], "scalar": q.cval["scalar"],
                  "gpsimd": q.cval["gpsimd"]}
        else:
            fl = dict(floors)
        evac_pos = []
        for bi, mc in enumerate(mcs):
            bsel = bi % 2
            wb = wbufA if bsel == 0 else wbufB
            mw = wid[mc]
            fa = buf_free_at[bsel]
            q.dma("sync", [(
                wb[:, 0:kparts * mw].rearrange("p (kc m) -> p kc m", kc=kparts),
                wsrc[:, 0:kparts, mc * 128:mc * 128 + mw])],
                lag=None if fa is None else
                {"tensor": steps_done - fa, "vector": 99, "scalar": 99,
                 "gpsimd": 99})
            buf_free_at[bsel] = steps_done + 1
            bank = pb[mc % 4]
            if bi < 4:
                cap = dict(fl)
            else:
                ep = evac_pos[bi - 4]
                cap = {k: max(fl.get(k, 0), ep.get(k, 0))
                       for k in ("vector", "scalar", "gpsimd")}
            q.step("tensor",
                   [MM(bank[0:mw, 0:Tn],
                       wb[:, kc * mw:kc * mw + mw],
                       hfm[kc][:, 0:Tn], kc == 0, kc == kparts - 1)
                    for kc in range(kparts)], cap=cap)
            steps_done += 1
            out_cb(mc, bank[0:mw, 0:Tn])
            evac_pos.append({"vector": q.cval["vector"],
                             "scalar": q.cval["scalar"],
                             "gpsimd": q.cval["gpsimd"]})

    def conv_chunk(Tn):
        xbc = [cvs] + [zx[15 + i] for i in range(1, NXBC)]
        S("vector", *[CP(smallC[0:128, i * 3:(i + 1) * 3],
                         zx[16 + i][:, Tn - 3:Tn]) for i in range(NXBC)])
        for i in range(NXBC):
            srct = zx[16 + i]
            dstt = xbc[i]
            ops1 = [TS(dstt[:, 0:Tn], srct[:, 0:Tn], cw[i][:, 3:4], None,
                       ALU.mult)]
            S("vector", *ops1)
            for k in range(3):
                S("vector", STT(dstt[:, 3 - k:Tn], srct[:, 0:Tn - 3 + k],
                                cw[i][:, k:k + 1], dstt[:, 3 - k:Tn],
                                ALU.mult, ALU.add))
                tops = []
                for t in range(0, 3 - k):
                    tops.append(STT(dstt[:, t:t + 1],
                                    tails[:, i * 4 + t + k:i * 4 + t + k + 1],
                                    cw[i][:, k:k + 1], dstt[:, t:t + 1],
                                    ALU.mult, ALU.add))
                if tops:
                    S("vector", *tops)
        S("vector", *[CP(tails[:, i * 4:i * 4 + 3],
                         smallC[0:128, i * 3:(i + 1) * 3])
                      for i in range(NXBC)])
        S("scalar", *[ACT(xbc[i][:, 0:Tn], xbc[i][:, 0:Tn], AF.Silu,
                          bias=cb[i][:, :]) for i in range(NXBC)])
        return xbc

    def dt_chunk(Tn):
        dts = smallC
        S("scalar", ACT(dtt[0:NH, 0:Tn], zx[33][0:NH, 0:Tn], AF.Abs,
                        bias=dtbc[0:NH, :]))
        S("scalar", ACT(dts[0:NH, 0:Tn], dtt[0:NH, 0:Tn], AF.Exp, scale=constc[0:NH, 2:3]))
        S("scalar", ACT(dts[0:NH, 0:Tn], dts[0:NH, 0:Tn], AF.Ln, bias=constc[0:NH, 1:2]))
        S("scalar", ACT(dtt[0:NH, 0:Tn], zx[33][0:NH, 0:Tn], AF.Relu,
                        bias=dtbc[0:NH, :]))
        S("vector", TT(dtt[0:NH, 0:Tn], dtt[0:NH, 0:Tn], dts[0:NH, 0:Tn],
                       ALU.add))

    def ssd_subchunk(xbc, o, partial):
        bc_tile = xbc[16]
        q.dma("gpsimd", [(ccopy[0:64, 0:SC], bc_tile[64:128, o:o + SC])])
        S("vector",
          lambda e: e.tensor_tensor_scan(sdt[0:NH, 0:SC], onescN[0:NH, 0:SC],
                                         dtt[0:NH, o:o + SC], 0.0,
                                         ALU.mult, ALU.add))
        S("vector", TS(asig[0:NH, 0:SC], sdt[0:NH, 0:SC], acol[0:NH, :],
                       None, ALU.mult),
          TS(wfm[0:NH, 0:SC], sdt[0:NH, 0:SC],
             sdt[0:NH, SC - 1:SC], nacol[0:NH, :], ALU.subtract, ALU.mult))
        S("scalar", ACT(wfm[0:NH, 0:SC], wfm[0:NH, 0:SC], AF.Exp))
        S("vector", TT(wfm[0:NH, 0:SC], wfm[0:NH, 0:SC],
                       dtt[0:NH, o:o + SC], ALU.mult),
          TS(dkc[0:NH, :], sdt[0:NH, SC - 1:SC], acol[0:NH, :], None,
             ALU.mult))
        S("scalar", ACT(dkc[0:NH, :], dkc[0:NH, :], AF.Exp))
        S("tensor", TR(pb[4][0:SC, 0:NH], sdt[0:NH, 0:SC], NH),
          TR(pb[5][0:SC, 0:NH], dtt[0:NH, o:o + SC], NH),
          TR(pb[6][0:SC, 0:NH], wfm[0:NH, 0:SC], NH),
          TR(pb[7][0:1, 0:NH], dkc[0:NH, :], NH))
        S("vector", CP(sdtT[:, 0:NH], pb[4][0:SC, 0:NH]),
          CP(sdtT[:, NH:2 * NH], pb[5][0:SC, 0:NH]),
          CP(sdtT[:, 2 * NH:3 * NH], pb[6][0:SC, 0:NH]),
          CP(dkrow[0:1, 0:NH], pb[7][0:1, 0:NH]))
        S("vector", TT(aST[:, 0:NH], sdtT[:, 0:NH], a128, ALU.mult))
        S("scalar", ACT(easT[:, 0:NH], aST[:, 0:NH], AF.Exp))
        S("tensor", bcast(pb[6], dkrow[0:1, 0:NH], 64))
        S("vector", CP(dkerb[0:64, 0:NH], pb[6][0:64, 0:NH]))
        S("tensor", TR(pb[7][0:128, 0:128], bc_tile[:, o:o + SC], 128))
        S("vector", CP(bct[:, 0:128], pb[7][0:128, 0:128]))
        for f0 in range(0, 16, 4):
            S("tensor", *[TR(pb[4 + i][0:128, 0:128],
                             xbc[f0 + i][:, o:o + SC], 128) for i in range(4)])
            S("vector", *[CP(xs_tok[:, (f0 + i) * 128:(f0 + i + 1) * 128],
                             pb[4 + i][0:128, 0:128]) for i in range(4)])
        if not partial:
            S("tensor", MM(pb[6][0:128, 0:128], bc_tile[0:64, o:o + SC],
                           ccopy[0:64, 0:SC], True, True))
            S("vector", CP(scoresT[:, 0:128], pb[6][0:128, 0:128]))

        for hg in range(NH // 4):
            hs = hg * 4
            if not partial:
                S("tensor", *[TR(pb[4 + j][0:1, 0:128],
                                 aST[:, hs + j:hs + j + 1], 128)
                              for j in range(4)])
                S("vector", *[CP(row0[0:1, j * 128:(j + 1) * 128],
                                 pb[4 + j][0:1, 0:128]) for j in range(4)])
                S("tensor", *[MM(pb[0][0:128, j * 128:(j + 1) * 128],
                                 ones_row[0:1, 0:128],
                                 row0[0:1, j * 128:(j + 1) * 128],
                                 j == 0, j == 3) for j in range(4)])
                S("vector", *[TS(lpre[j], pb[0][0:128, j * 128:(j + 1) * 128],
                                 aST[:, hs + j:hs + j + 1], None,
                                 ALU.subtract) for j in range(4)])
                S("vector", *[TT(lpre[j], lpre[j], maskbig, ALU.min)
                              for j in range(4)])
                S("scalar", *[ACT(lpre[j], lpre[j], AF.Exp) for j in range(4)])
                S("vector", *[STT(mtb[j], lpre[j],
                                  sdtT[:, NH + hs + j:NH + hs + j + 1],
                                  scoresT, ALU.mult, ALU.mult)
                              for j in range(4)])
                S("tensor",
                  *[MM(pb[2][0:128, j * 64:(j + 1) * 64], mtb[j],
                       xs_tok[:, (hs + j) * 64:(hs + j + 1) * 64],
                       j == 0, j == 3) for j in range(4)],
                  *[MM(pb[3][0:128, j * 64:(j + 1) * 64],
                       ccopy[0:64, 0:SC],
                       state[0:64, (hs + j) * 64:(hs + j + 1) * 64],
                       j == 0, j == 3) for j in range(4)])
                S("vector", *[CP(y_tok[:, (hs + j) * 64:(hs + j + 1) * 64],
                                 pb[2][0:128, j * 64:(j + 1) * 64])
                              for j in range(4)])
                S("vector", *[STT(y_tok[:, (hs + j) * 64:(hs + j + 1) * 64],
                                  pb[3][0:128, j * 64:(j + 1) * 64],
                                  easT[:, hs + j:hs + j + 1],
                                  y_tok[:, (hs + j) * 64:(hs + j + 1) * 64],
                                  ALU.mult, ALU.add) for j in range(4)])
                S("vector", *[STT(y_tok[:, (hs + j) * 64:(hs + j + 1) * 64],
                                  xs_tok[:, (hs + j) * 64:(hs + j + 1) * 64],
                                  d128[:, hs + j:hs + j + 1],
                                  y_tok[:, (hs + j) * 64:(hs + j + 1) * 64],
                                  ALU.mult, ALU.add) for j in range(4)])
            S("vector", *[TS(xs_tok[:, (hs + j) * 64:(hs + j + 1) * 64],
                             xs_tok[:, (hs + j) * 64:(hs + j + 1) * 64],
                             sdtT[:, 2 * NH + hs + j:2 * NH + hs + j + 1],
                             None, ALU.mult)
                          for j in range(4)])
            S("tensor", *[MM(pb[1][0:64, j * 64:(j + 1) * 64], bct[:, 0:64],
                             xs_tok[:, (hs + j) * 64:(hs + j + 1) * 64],
                             j == 0, j == 3) for j in range(4)])
            S("vector", *[STT(state[0:64, (hs + j) * 64:(hs + j + 1) * 64],
                              state[0:64, (hs + j) * 64:(hs + j + 1) * 64],
                              dkerb[0:64, hs + j:hs + j + 1],
                              pb[1][0:64, j * 64:(j + 1) * 64],
                              ALU.mult, ALU.add) for j in range(4)])

    def attn_chunk(c, partial, xsrc):
        nt = CH // 128
        q.dma("sync", [(xchunk[tt], xsrc[c * CH + tt * 128:
                                         c * CH + (tt + 1) * 128, :])
                       for tt in range(nt)])
        ln_tiles(xchunk, htok, modb[1], modb[0], nt)
        ln_tiles(htok, x1tok, None, None, nt)
        to_fm(x1tok, hfm, 8, nt, lncol[0], lncol[1])
        def ip_cb(mc, ps):
            S("vector", CP(zx[mc][0:(32 if mc == NZX - 1 else 128), 0:CH], ps))
        patternA(w_ip, 16, NZX, ip_cb, CH, mtot=DIP)
        post_xbc = {"vector": q.cval["vector"], "scalar": q.cval["scalar"],
                    "gpsimd": q.cval["gpsimd"]}
        xbc = conv_chunk(CH)
        dt_chunk(CH)
        if not partial:
            patternA(w_ip, 0, 16, ip_cb, CH, mtot=DIP, floors=post_xbc)
        for sci in range(CH // SC):
            ssd_subchunk(xbc, sci * SC, partial)
            if not partial:
                post_ssd_subchunk(c, sci, xbc)
        if not partial:
            post_chunk(c, nt)

    def post_ssd_subchunk(c, sci, xbc):
        o = sci * SC
        S("scalar", *[ACT(zx[fc][:, o:o + SC], zx[fc][:, o:o + SC], AF.Silu)
                      for fc in range(16)])
        for f0 in range(0, 16, 4):
            S("tensor", *[TR(pb[4 + i][0:128, 0:128],
                             y_tok[:, (f0 + i) * 128:(f0 + i + 1) * 128], 128)
                          for i in range(4)])
            S("vector", *[TT(zx[f0 + i][:, o:o + SC],
                             pb[4 + i][0:128, 0:128],
                             zx[f0 + i][:, o:o + SC], ALU.mult)
                          for i in range(4)])
        sqops = []
        mmops = []
        for fc in range(16):
            sqops.append(TT(lpre[fc % 4], zx[fc][:, o:o + SC],
                            zx[fc][:, o:o + SC], ALU.mult))
            mmops.append(MM(pb[6][0:1, 0:SC], ones_col[0:128, :],
                            lpre[fc % 4], fc == 0, fc == 15))
        for f0 in range(0, 16, 4):
            S("vector", *sqops[f0:f0 + 4])
            S("tensor", *mmops[f0:f0 + 4])
        S("scalar", ACT(smallC[0:1, 0:SC], pb[6][0:1, 0:SC], AF.Sqrt,
                        bias=constc[0:1, 0:1], scale=constc[0:1, 3:4]))
        S("vector", lambda e: e.reciprocal(smallC[0:1, 0:SC],
                                           smallC[0:1, 0:SC]))
        S("tensor", bcast(pb[7], smallC[0:1, 0:SC], 128))
        S("vector", *[STT(zx[fc][:, o:o + SC], pb[7][0:128, 0:SC],
                          nrmc[:, fc:fc + 1], zx[fc][:, o:o + SC],
                          ALU.mult, ALU.mult) for fc in range(16)])

    def post_chunk(c, nt):
        for kc in range(16):
            wb = wbufC if kc % 2 == 0 else wbufD
            q.dma("sync", [(wb[:, 0:D], w_op[kc * 128:(kc + 1) * 128, :])],
                  lag=None if kc <= 1 else {"tensor": 1, "vector": 99, "scalar": 99, "gpsimd": 99})
            q.step("tensor", [MM(pb[tt * 2 + si][0:128, 0:512],
                                 zx[kc][:, tt * 128:(tt + 1) * 128],
                                 wb[:, si * 512:(si + 1) * 512],
                                 kc == 0, kc == 15)
                              for tt in range(nt) for si in range(2)],
                   lag=None if kc == 0 else {"vector": 99, "scalar": 99})
        S("vector", *[TT(x1tok[tt][:, si * 512:(si + 1) * 512],
                         pb[tt * 2 + si][0:128, 0:512],
                         htok[tt][:, si * 512:(si + 1) * 512], ALU.add)
                      for tt in range(nt) for si in range(2)])
        ln_tiles(x1tok, htok, None, None, nt)
        to_fm(htok, hfm, 8, nt, lncol[2], lncol[3])
        patternA(w_m1, 0, 32,
                 lambda mc, ps: S("scalar",
                                  ACT(zx[mc][:, 0:CH], ps, AF.Gelu,
                                      bias=b1c[:, mc:mc + 1])), CH)
        for kc in range(32):
            wb = wbufC if kc % 2 == 0 else wbufD
            q.dma("sync", [(wb[:, 0:D], w_m2[kc * 128:(kc + 1) * 128, :])],
                  lag=None if kc <= 1 else {"tensor": 1, "vector": 99, "scalar": 99, "gpsimd": 99})
            q.step("tensor", [MM(pb[tt * 2 + si][0:128, 0:512],
                                 zx[kc][:, tt * 128:(tt + 1) * 128],
                                 wb[:, si * 512:(si + 1) * 512],
                                 kc == 0, kc == 31)
                              for tt in range(nt) for si in range(2)],
                   lag=None if kc == 0 else {"vector": 99, "scalar": 99})
        S("vector", *[TT(htok[tt][:, si * 512:(si + 1) * 512],
                         pb[tt * 2 + si][0:128, 0:512],
                         b2b[:, si * 512:(si + 1) * 512], ALU.add)
                      for tt in range(nt) for si in range(2)])
        S("vector", *[TT(htok[tt], htok[tt], x1tok[tt], ALU.add)
                      for tt in range(nt)])
        S("vector", *[TT(htok[tt], htok[tt], modb[2], ALU.mult)
                      for tt in range(nt)])
        S("vector", *[TT(xchunk[tt], xchunk[tt], htok[tt], ALU.add)
                      for tt in range(nt)])
        q.dma("gpsimd", [(x_hbm[c * CH + tt * 128:c * CH + (tt + 1) * 128, :],
                          xchunk[tt]) for tt in range(nt)])
        ln_tiles(xchunk, htok, modb[4], modb[3], nt)
        q.dma("gpsimd", [(ag_xf_in[c * CH + tt * 128:c * CH + (tt + 1) * 128, :],
                          htok[tt]) for tt in range(nt)])
        to_fm(htok, hfm, 8, nt)
        gate_topk(c, nt)

    def gate_topk(c, nt):
        sco = [y_tok[:, tt * 64:(tt + 1) * 64] for tt in range(nt)]
        cur = [y_tok[:, 256 + tt * 64:256 + (tt + 1) * 64] for tt in range(nt)]
        eq = [y_tok[:, 512 + tt * 64:512 + (tt + 1) * 64] for tt in range(nt)]
        sel = [y_tok[:, 768 + tt * 64:768 + (tt + 1) * 64] for tt in range(nt)]
        tmp = [y_tok[:, 1024 + tt * 64:1024 + (tt + 1) * 64]
               for tt in range(nt)]
        st_ = lambda tt, k: lnst[0:128, tt * 16 + k:tt * 16 + k + 1]
        pts = [payb[tt] for tt in range(nt)]
        q.dma("sync", [(wbufC[:, kc * 64:(kc + 1) * 64],
                        w_gt[kc * 128:(kc + 1) * 128, :])
                       for kc in range(8)])
        S("tensor", *[MM(pb[4 + tt][0:128, 0:E],
                         hfm[kc][:, tt * 128:(tt + 1) * 128],
                         wbufC[:, kc * 64:(kc + 1) * 64],
                         kc == 0, kc == 7)
                      for tt in range(nt) for kc in range(8)])
        S("vector", *[RED(st_(tt, 0), pb[4 + tt][0:128, 0:E], ALU.max)
                      for tt in range(nt)])
        S("vector", *[TS(st_(tt, 1), st_(tt, 0), -1.0, None, ALU.mult)
                      for tt in range(nt)])
        S("scalar", *[(lambda e, tt=tt: e.activation(
            sco[tt], pb[4 + tt][0:128, 0:E], AF.Exp, bias=st_(tt, 1)))
            for tt in range(nt)])
        S("vector", *[RED(st_(tt, 2), sco[tt], ALU.add) for tt in range(nt)])
        S("vector", *[(lambda e, tt=tt: e.reciprocal(st_(tt, 3), st_(tt, 2)))
                      for tt in range(nt)])
        S("vector", *[TS(sco[tt], sco[tt], st_(tt, 3), None, ALU.mult)
                      for tt in range(nt)])
        S("vector", *[TT(cur[tt], sco[tt], gbB, ALU.add) for tt in range(nt)],
          *[MS(pts[tt], 0.0) for tt in range(nt)])
        for k in range(TOPK):
            S("vector", *[RED(st_(tt, 4), cur[tt], ALU.max)
                          for tt in range(nt)])
            S("vector", *[TS(eq[tt], cur[tt], st_(tt, 4), None, ALU.is_equal)
                          for tt in range(nt)])
            S("vector", *[TT(sel[tt], eq[tt], iot128[:, 0:64], ALU.mult)
                          for tt in range(nt)],
              *[TS(tmp[tt], eq[tt], 1.0, -64.0, ALU.subtract, ALU.mult)
                for tt in range(nt)])
            S("vector", *[TT(sel[tt], sel[tt], tmp[tt], ALU.add)
                          for tt in range(nt)])
            S("vector", *[RED(pts[tt][:, k:k + 1], sel[tt], ALU.min)
                          for tt in range(nt)])
            S("vector", *[TS(eq[tt], iot128[:, 0:64], pts[tt][:, k:k + 1],
                             None, ALU.is_equal) for tt in range(nt)])
            S("vector", *[TT(sel[tt], eq[tt], sco[tt], ALU.mult)
                          for tt in range(nt)])
            S("vector", *[RED(pts[tt][:, 8 + k:9 + k], sel[tt], ALU.add)
                          for tt in range(nt)],
              *[STT(cur[tt], eq[tt], -1e30, cur[tt], ALU.mult, ALU.add)
                for tt in range(nt)])
        q.dma("gpsimd",
              [(ag_pay_in[c * CH + tt * 128:c * CH + (tt + 1) * 128, :],
                pts[tt]) for tt in range(nt)])

    if stop < 2:
        _finish_trunc(q, S, out, zerot, T)
        q.emit(); st.close(); return nc
    for c in range(NCHP):
        attn_chunk(c, True, x_prev)
    S("vector", TS(state[0:64, :], state[0:64, :], flagc[0:64, :], None,
                   ALU.mult),
      TS(tails, tails, flagc, None, ALU.mult))
    if stop < 3:
        _finish_trunc(q, S, out, zerot, T)
        q.emit(); st.close(); return nc
    for c in range(NCH):
        attn_chunk(c, False, x_own)

    if stop < 4:
        _finish_trunc(q, S, out, zerot, T)
        q.emit(); st.close(); return nc
    q.collective("AllGather", ALU.bypass, [list(range(n_ranks))],
                 [ag_pay_in.opt()], [pay_all.opt()])
    cc_pay = q.ccval
    q.collective("AllGather", ALU.bypass, [list(range(n_ranks))],
                 [ag_xf_in.opt()], [xf_all[0:TALL, :].opt()])
    cc_xf = q.ccval

    pool.pop()
    pool.push()
    ptile = pool.tile(16)
    otile = pool.tile(E)
    cex = pool.tile(E)
    ptab = pool.tile(E)
    mskb = pool.tile(TOPK * E)
    bkb = pool.tile(TOPK * E)
    tmpk = pool.tile(2 * TOPK * E)
    colk = pool.tile(16)
    swf = pool.tile(16)
    ctile = pool.tile(E)
    vald = pool.tile(8)
    ofs = pool.tile(1)
    ofs_i = pooli[:, 0:1]

    if stop < 5:
        _finish_trunc(q, S, out, zerot[0:128, 0:1024] if False else zerot, T)
        q.emit(); st.close(); return nc
    q.wait_cc("sync", cc_pay)
    q.wait_cc("gpsimd", cc_pay)
    S("vector", MS(swf, 0.0))
    for ti in range(NTILE_ALL):
        q.dma("sync", [(ptile, pay_all[ti * 128:(ti + 1) * 128, :])])
        S("vector",
          *[TS(mskb[:, k * 64:(k + 1) * 64], iot128[:, 0:64],
               ptile[:, k:k + 1], None, ALU.is_equal) for k in range(TOPK)],
          TS(colk[:, 15:16], iotc, float(ti * 128), None, ALU.add),
          TT(ofs[0:64, :], basec[0:64, :], etimes[0:64, :], ALU.add))
        S("vector",
          lambda e: e.tensor_reduce(otile,
                                    mskb.rearrange("p (k e) -> p e k", k=TOPK),
                                    mybir.AxisListType.X, ALU.add),
          CP(ofs_i[0:64, :], ofs[0:64, :]))
        S("tensor",
          MM(pb[0][0:128, 0:E], sltm, otile, True, True),
          MM(pb[5][0:128, 0:E], sltm, otile, True, False),
          MM(pb[5][0:128, 0:E], ones_row[0:1, 0:128], baser[0:1, 0:E],
             False, True),
          MM(pb[1][0:1, 0:E], ones_col[0:128, :], otile, True, True),
          MM(pb[2][0:64, 0:1], otile[:, 0:64], ones_col[0:128, :],
             True, True))
        S("vector", CP(cex, pb[0][0:128, 0:E]), CP(ptab, pb[5][0:128, 0:E]))
        S("vector",
          *[TT(tmpk[:, k * 64:(k + 1) * 64], mskb[:, k * 64:(k + 1) * 64],
               cex, ALU.mult) for k in range(TOPK)],
          *[TT(tmpk[:, (6 + k) * 64:(7 + k) * 64],
               mskb[:, k * 64:(k + 1) * 64], ptab, ALU.mult)
            for k in range(TOPK)])
        S("vector",
          *[RED(colk[:, k:k + 1], tmpk[:, k * 64:(k + 1) * 64], ALU.add)
            for k in range(TOPK)],
          *[RED(colk[:, 8 + k:9 + k], tmpk[:, (6 + k) * 64:(7 + k) * 64],
                ALU.add) for k in range(TOPK)])
        S("vector",
          *[TS(bkb[:, k * 64:(k + 1) * 64], iot128[:, 0:64],
               colk[:, k:k + 1], colk[:, 15:16], ALU.is_equal, ALU.mult)
            for k in range(TOPK)])
        S("tensor",
          *[MM(pb[3][0:64, 0:E], mskb[:, k * 64:(k + 1) * 64],
               bkb[:, k * 64:(k + 1) * 64], k == 0, k == TOPK - 1)
            for k in range(TOPK)])
        S("vector",
          CP(ctile[0:64, :], pb[3][0:64, 0:E]),
          TT(baser[0:1, 0:E], baser[0:1, 0:E], pb[1][0:1, 0:E], ALU.add),
          TT(basec[0:64, :], basec[0:64, :], pb[2][0:64, 0:1], ALU.add),
          *[TS(swf[:, 8 + k:9 + k], colk[:, 8 + k:9 + k], 511.0, None,
               ALU.min) for k in range(TOPK)],
          *[TS(tmpk[:, k:k + 1] if False else vald[:, k:k + 1],
               colk[:, 8 + k:9 + k], 512.0, None, ALU.is_lt)
            for k in range(TOPK)])
        S("vector",
          *[STT(swf[:, k:k + 1], ptile[:, k:k + 1], 512.0,
                swf[:, 8 + k:9 + k], ALU.mult, ALU.add) for k in range(TOPK)])
        S("vector",
          *[TT(swf[:, 8 + k:9 + k], ptile[:, 8 + k:9 + k],
               vald[:, k:k + 1], ALU.mult) for k in range(TOPK)])
        q.indirect([(tblh[:, :], IndirectOffsetOnAxis(ofs_i[0:64, :], 0),
                     ctile[0:64, :], None)])
        q.dma("gpsimd", [(slotwf[ti * 128:(ti + 1) * 128, :], swf)])
    q.dma("gpsimd", [(cnts_h[:, :], baser[0:1, 0:E])])

    if stop < 6:
        _finish_trunc(q, S, out, zerot, T)
        q.emit(); st.close(); return nc
    pool.pop()
    pool.push()
    xbt = pool.tiles(4, D)
    xbfm = pool.tiles(8, 512)
    he = pool.tiles(11, 512)
    ybt = pool.tiles(4, D)
    ewA = pool.tile(8 * 128)
    ewB = pool.tile(8 * 128)
    ewC = pool.tile(D)
    ewD = pool.tile(D)
    idxv = pool.tile(8)
    cnt8 = pool.tile(E)
    vld = pool.tile(4)
    swft = pool.tile(16)
    rankc = pool.tile(1)
    ofs = pool.tile(1)
    tbl_own_sb = smallC
    xmt = pool.tile(D)
    zt = pool.tile(D)
    gath = pool.tile(D)
    acc = pool.tile(D)
    moe_end = pool.off
    idx_i4 = pooli[:, 4:8]
    slot_i = pooli[:, 8:9]

    rank_s = pooli[:, 12:13]
    q.dma("gpsimd", [(rank_s[0:1, :], rank_i[:, :])])
    S("vector", CP(smallB[0:1, 14:15], rank_s[0:1, :]))
    S("tensor", bcast(pb[0], smallB[0:1, 14:15], 128))
    S("vector", CP(rankc, pb[0][0:128, 0:1]))
    S("vector", TS(ofs[0:epc, :], rankc[0:epc, :], float(epc), None,
                   ALU.mult))
    S("vector", TT(ofs[0:epc, :], ofs[0:epc, :], iotc[0:epc, :], ALU.add))
    S("vector", CP(pooli[0:epc, 13:14], ofs[0:epc, :]))
    cnts_flat = cnts_h.rearrange("o e -> (o e)")[:, None]
    q.indirect([(smallC[0:epc, 0:1], None, cnts_flat,
                 IndirectOffsetOnAxis(pooli[0:epc, 13:14], 0))])
    S("tensor", TR(pb[0][0:1, 0:epc], smallC[0:epc, 0:1], epc))
    S("vector", CP(smallC[0:1, 4:4 + epc], pb[0][0:1, 0:epc]))
    S("tensor", bcast(pb[0], smallC[0:1, 4:4 + epc], 128))
    S("vector", CP(cnt8[:, 0:epc], pb[0][0:128, 0:epc]))
    S("vector", TS(ofs[0:epc, :], rankc[0:epc, :], float(epc), None,
                   ALU.mult))
    S("vector", TT(ofs[0:epc, :], ofs[0:epc, :], iotc[0:epc, :], ALU.add))
    S("vector", TS(ofs[0:epc, :], ofs[0:epc, :], float(TSTRIDE), None,
                   ALU.mult))
    S("vector", CP(pooli[0:epc, 14:15], ofs[0:epc, :]))
    q.indirect([(tbl_own_sb[0:epc, 0:512], None, tblh[:, :],
                 IndirectOffsetOnAxis(pooli[0:epc, 14:15], 0))])
    q.dma("gpsimd", [(tbl_own[0:epc, :], tbl_own_sb[0:epc, 0:512])])
    for j in range(epc):
        q.dma("gpsimd", [(idxv[:, (j % 2) * 4:(j % 2) * 4 + 4],
                          tbl_own[j, 0:512].rearrange("(f q) -> q f", q=128))])
        jj = j % 2
        S("vector", *[TS(vld[:, col:col + 1],
                         iotc, float(col * 128), None, ALU.add)
                      for col in range(4)])
        S("vector", *[TS(vld[:, col:col + 1], vld[:, col:col + 1],
                         cnt8[:, j:j + 1], None, ALU.is_lt)
                      for col in range(4)])
        S("vector", TS(idxv[:, jj * 4:jj * 4 + 4],
                       idxv[:, jj * 4:jj * 4 + 4], float(TALL), None,
                       ALU.subtract))
        S("vector", TT(idxv[:, jj * 4:jj * 4 + 4],
                       idxv[:, jj * 4:jj * 4 + 4], vld, ALU.mult))
        S("vector", TS(idxv[:, jj * 4:jj * 4 + 4],
                       idxv[:, jj * 4:jj * 4 + 4], float(TALL), None,
                       ALU.add))
        S("vector", CP(idx_i4, idxv[:, jj * 4:jj * 4 + 4]))
        if j == 0:
            q.wait_cc("gpsimd", cc_xf)
        q.indirect([(xbt[col], None, xf_all[:, :],
                     IndirectOffsetOnAxis(idx_i4[:, col:col + 1], 0))
                    for col in range(4)])
        for f0 in range(0, 8, 2):
            S("tensor", *[TR(pb[4 * i + tt][0:128, 0:128],
                             xbt[tt][:, (f0 + i) * 128:(f0 + i + 1) * 128], 128)
                          for i in range(2) for tt in range(4)])
            S("vector", *[CP(xbfm[f0 + i][:, tt * 128:(tt + 1) * 128],
                             pb[4 * i + tt][0:128, 0:128])
                          for i in range(2) for tt in range(4)])
        mws = [128, 128, 128, 128, 128, 64]
        def eA(wdram, j, cb_):
            wsrc = wdram.rearrange("e (kc p) m -> e p kc m", p=128)
            for bi in range(6):
                wb = ewA if bi % 2 == 0 else ewB
                mw = mws[bi]
                q.dma("sync", [(
                    wb[:, 0:8 * mw].rearrange("p (kc m) -> p kc m", kc=8),
                    wsrc[j, :, 0:8, bi * 128:bi * 128 + mw])],
                    lag=None if bi <= 1 else {"tensor": 1, "vector": 99,
                                              "scalar": 99, "gpsimd": 99})
                bank = pb[bi % 4]
                q.step("tensor",
                       [MM(bank[0:mw, 0:512],
                           wb[:, kc * mw:kc * mw + mw],
                           xbfm[kc][:, 0:512], kc == 0, kc == 7)
                        for kc in range(8)],
                       lag=None if bi == 0 else {"vector": 3, "scalar": 3})
                cb_(bi, bank[0:mw, 0:512], mw)
        eA(w_e1, j, lambda ic, ps, mwid:
           S("scalar", ACT(he[ic][0:mwid, :], ps, AF.Silu)))
        eA(w_e3, j, lambda ic, ps, mwid:
           S("vector", TT(he[ic][0:mwid, :], he[ic][0:mwid, :], ps,
                          ALU.mult)))
        for kc2 in range(6):
            mwid = mws[kc2]
            wb = ewC if kc2 % 2 == 0 else ewD
            q.dma("sync", [(wb[0:mwid, 0:D],
                            w_e2[j, kc2 * 128:kc2 * 128 + mwid, :])],
                  lag=None if kc2 <= 1 else {"tensor": 1, "vector": 99, "scalar": 99, "gpsimd": 99})
            q.step("tensor", [MM(pb[sl * 2 + si][0:128, 0:512],
                                 he[kc2][0:mwid, sl * 128:(sl + 1) * 128],
                                 wb[0:mwid, si * 512:(si + 1) * 512],
                                 kc2 == 0, kc2 == 5)
                              for sl in range(4) for si in range(2)],
                   lag=None if kc2 == 0 else {"vector": 99, "scalar": 99})
        S("vector", *[CP(ybt[sl][:, si * 512:(si + 1) * 512],
                         pb[sl * 2 + si][0:128, 0:512])
                      for sl in range(4) for si in range(2)])
        q.dma("gpsimd", [(ag_yb_in[j * CAP + sl * 128:j * CAP + (sl + 1) * 128,
                                   :], ybt[sl]) for sl in range(4)])

    q.collective("AllGather", ALU.bypass, [list(range(n_ranks))],
                 [ag_yb_in.opt()], [yb_all.opt()])
    cc_yb = q.ccval

    if stop < 7:
        _finish_trunc(q, S, out, zerot, T)
        q.emit(); st.close(); return nc
    for c in range(NCH):
        nt = CH // 128
        q.dma("sync", [(xbt[tt], ag_xf_in[c * CH + tt * 128:
                                          c * CH + (tt + 1) * 128, :])
                       for tt in range(nt)])
        for f0 in range(0, 8, 2):
            S("tensor", *[TR(pb[4 * i + tt][0:128, 0:128],
                             xbt[tt][:, (f0 + i) * 128:(f0 + i + 1) * 128], 128)
                          for i in range(2) for tt in range(4)])
            S("vector", *[CP(xbfm[f0 + i][:, tt * 128:(tt + 1) * 128],
                             pb[4 * i + tt][0:128, 0:128])
                          for i in range(2) for tt in range(4)])
        def sA(wdram, cb_):
            wsrc = wdram.rearrange("(kc p) m -> p kc m", p=128)
            for bi in range(11):
                wb = ewA if bi % 2 == 0 else ewB
                q.dma("sync", [(
                    wb[:, 0:8 * 128].rearrange("p (kc m) -> p kc m", kc=8),
                    wsrc[:, 0:8, bi * 128:(bi + 1) * 128])],
                    lag=None if bi <= 1 else {"tensor": 1, "vector": 99,
                                              "scalar": 99, "gpsimd": 99})
                bank = pb[bi % 4]
                q.step("tensor",
                       [MM(bank[0:128, 0:512],
                           wb[:, kc * 128:(kc + 1) * 128],
                           xbfm[kc][:, 0:512], kc == 0, kc == 7)
                        for kc in range(8)],
                       lag=None if bi == 0 else {"vector": 3, "scalar": 3})
                cb_(bi, bank[0:128, 0:512])
        sA(w_s1, lambda ic, ps: S("scalar", ACT(he[ic], ps, AF.Silu)))
        sA(w_s3, lambda ic, ps: S("vector", TT(he[ic], he[ic], ps, ALU.mult)))
        for kc2 in range(11):
            wb = ewC if kc2 % 2 == 0 else ewD
            q.dma("sync", [(wb[:, 0:D], w_s2[kc2 * 128:(kc2 + 1) * 128, :])],
                  lag=None if kc2 <= 1 else {"tensor": 1, "vector": 99, "scalar": 99, "gpsimd": 99})
            q.step("tensor", [MM(pb[sl * 2 + si][0:128, 0:512],
                                 he[kc2][:, sl * 128:(sl + 1) * 128],
                                 wb[:, si * 512:(si + 1) * 512],
                                 kc2 == 0, kc2 == 10)
                              for sl in range(4) for si in range(2)],
                   lag=None if kc2 == 0 else {"vector": 99, "scalar": 99})
        S("vector", *[CP(ybt[sl][:, si * 512:(si + 1) * 512],
                         pb[sl * 2 + si][0:128, 0:512])
                      for sl in range(4) for si in range(2)])
        q.dma("gpsimd", [(z_hbm[c * CH + sl * 128:c * CH + (sl + 1) * 128, :],
                          ybt[sl]) for sl in range(4)])

    if stop < 8:
        _finish_trunc(q, S, out, zerot, T)
        q.emit(); st.close(); return nc
    q.wait_cc("gpsimd", cc_yb)
    for tt in range(T // 128):
        q.dma("sync", [(xmt, x_hbm[tt * 128:(tt + 1) * 128, :]),
                       (zt, z_hbm[tt * 128:(tt + 1) * 128, :])])
        S("vector", TS(ofs, rankc, float(T), None, ALU.mult))
        S("vector", TT(ofs, ofs, iotc, ALU.add))
        S("vector", TS(ofs, ofs, 1.0, float(tt * 128), ALU.mult, ALU.add))
        S("vector", CP(pooli[:, 15:16], ofs))
        q.indirect([(swft, None, slotwf[:, :],
                     IndirectOffsetOnAxis(pooli[:, 15:16], 0))])
        for k in range(TOPK):
            S("vector", CP(slot_i, swft[:, k:k + 1]))
            q.indirect([(gath, None, yb_all[:, :],
                         IndirectOffsetOnAxis(slot_i, 0))])
            if k == 0:
                S("vector", TS(acc, gath, swft[:, 8:9], None, ALU.mult))
            else:
                S("vector", STT(acc, gath, swft[:, 8 + k:9 + k], acc,
                                ALU.mult, ALU.add))
        S("vector", TT(acc, acc, zt, ALU.add))
        S("vector", TT(acc, acc, modb[5], ALU.mult))
        S("vector", TT(acc, acc, xmt, ALU.add))
        q.dma("gpsimd", [(out[tt * 128:(tt + 1) * 128, :], acc)])
    S("gpsimd", lambda e: e.engine_nop())

    q.emit()
    st.close()
    return nc


def prep_inmaps(inputs, T, n_ranks):
    def t(a):
        return np.ascontiguousarray(np.transpose(np.asarray(a, np.float32)))

    x = np.ascontiguousarray(inputs["L0_attn_x"], dtype=np.float32)
    epc = E // n_ranks
    shared = {
        "wmod": np.ascontiguousarray(inputs["L0_wmod"], np.float32),
        "ln1w": inputs["L0_attn_ln1_w"].reshape(1, D).astype(np.float32),
        "ln1b": inputs["L0_attn_ln1_b"].reshape(1, D).astype(np.float32),
        "ln2w": inputs["L0_attn_ln2_w"].reshape(1, D).astype(np.float32),
        "ln2b": inputs["L0_attn_ln2_b"].reshape(1, D).astype(np.float32),
        "normw": inputs["L0_attn_norm_w"].reshape(1, DI).astype(np.float32),
        "mlpb1": inputs["L0_attn_mlp_b1"].reshape(1, MLP_H).astype(np.float32),
        "mlpb2": inputs["L0_attn_mlp_b2"].reshape(1, D).astype(np.float32),
        "gateb": inputs["L0_ffn_gate_b"].reshape(1, E).astype(np.float32),
        "a_log": inputs["L0_attn_A_log"].reshape(1, NH).astype(np.float32),
        "d_vec": inputs["L0_attn_D"].reshape(1, NH).astype(np.float32),
        "dtbias": inputs["L0_attn_dt_bias"].reshape(1, NH).astype(np.float32),
        "convw": np.ascontiguousarray(inputs["L0_attn_conv_w"], np.float32),
        "convb": inputs["L0_attn_conv_b"].reshape(CONVD, 1).astype(np.float32),
        "w_ip": t(inputs["L0_attn_in_proj_w"]),
        "w_op": t(inputs["L0_attn_out_proj_w"]),
        "w_m1": t(inputs["L0_attn_mlp_w1"]),
        "w_m2": t(inputs["L0_attn_mlp_w2"]),
        "w_gt": t(inputs["L0_ffn_gate_w"]),
        "w_s1": t(inputs["L0_ffn_ws1"]),
        "w_s3": t(inputs["L0_ffn_ws3"]),
        "w_s2": t(inputs["L0_ffn_ws2"]),
    }
    we1 = np.ascontiguousarray(np.transpose(
        np.asarray(inputs["L0_ffn_w1"], dtype=np.float32), (0, 2, 1)))
    we3 = np.ascontiguousarray(np.transpose(
        np.asarray(inputs["L0_ffn_w3"], dtype=np.float32), (0, 2, 1)))
    we2 = np.ascontiguousarray(np.transpose(
        np.asarray(inputs["L0_ffn_w2"], dtype=np.float32), (0, 2, 1)))
    c_all = np.asarray(inputs["L0_c"], dtype=np.float32)

    in_maps = []
    zeros_prev = np.zeros((T, D), np.float32)
    for r in range(n_ranks):
        b, h = r // 2, r % 2
        m = dict(shared)
        m["c_row"] = np.ascontiguousarray(c_all[b:b + 1])
        m["x_own"] = np.ascontiguousarray(x[b, h * T:(h + 1) * T])
        m["x_prev"] = (np.ascontiguousarray(x[b, 0:T]) if h == 1
                       else zeros_prev)
        m["flag"] = np.array([[1.0 if h == 1 else 0.0]], np.float32)
        m["rank_i"] = np.array([[r]], np.int32)
        m["w_e1"] = np.ascontiguousarray(we1[r * epc:(r + 1) * epc])
        m["w_e3"] = np.ascontiguousarray(we3[r * epc:(r + 1) * epc])
        m["w_e2"] = np.ascontiguousarray(we2[r * epc:(r + 1) * epc])
        in_maps.append(m)
    return in_maps


def kernel(**inputs):
    T = 2048
    key = (T,)
    if key not in _CACHE:
        _CACHE[key] = build_program(T=T, TP=T, n_ranks=NCORES)
    nc = _CACHE[key]
    in_maps = prep_inmaps(inputs, T, NCORES)
    res = run_bass_kernel_spmd(nc, in_maps, core_ids=list(range(NCORES)))
    outs = [np.asarray(res.results[r]["out"]) for r in range(NCORES)]
    x = inputs["L0_attn_x"]
    B, L, _ = x.shape
    full = np.empty((B, L, D), np.float32)
    for r in range(NCORES):
        b, h = r // 2, r % 2
        full[b, h * T:(h + 1) * T] = outs[r]
    return full


# version 15
# speedup vs baseline: 1.2823x; 1.0053x over previous
import numpy as np
from contextlib import ExitStack

import concourse.bass as bass
import concourse.mybir as mybir
from concourse.bass import IndirectOffsetOnAxis
from concourse.bass_utils import run_bass_kernel_spmd

F32 = mybir.dt.float32
I32 = mybir.dt.int32
AF = mybir.ActivationFunctionType
ALU = mybir.AluOpType

D = 1024
DI = 2048
DSTATE = 64
NH = 32
HD = 64
DCONV = 4
CONVD = DI + 2 * DSTATE
DIP = 2 * DI + 2 * DSTATE + NH
NZX = (DIP + 127) // 128
NXBC = CONVD // 128
EPS = 1e-5
E = 64
CAP = 512
TOPK = 6
MOE_I = 704
SH_I = 1408
MLP_H = 4096
NCORES = 8
EPC = E // NCORES
TSTRIDE = 16448
SC = 128
CH = 512
BIGNEG = -1e30

_CACHE = {}


class SQ:
    ENGS = ("sync", "scalar", "vector", "tensor", "gpsimd")

    def __init__(self, nc, stack):
        self.nc = nc
        self.stack = stack
        self.ops = {e: [] for e in self.ENGS}
        self.csem = {e: self._mk(f"c_{e}") for e in self.ENGS}
        self.dsem = {e: self._mk(f"d_{e}") for e in self.ENGS}
        self.cval = {e: 0 for e in self.ENGS}
        self.dval = {e: 0 for e in self.ENGS}
        self.seen_c = {e: {f: 0 for f in self.ENGS} for e in self.ENGS}
        self.seen_d = {e: {f: 0 for f in self.ENGS} for e in self.ENGS}
        self.ccsem = self._mk("cc")
        self.ccval = 0
        self.seen_cc = {e: 0 for e in self.ENGS}

    def _mk(self, name):
        return self.stack.enter_context(self.nc.semaphore(name))

    def _barrier(self, eng, lag=None, cap=None):
        for f in self.ENGS:
            if f == eng:
                continue
            v = self.cval[f]
            if lag and f in lag:
                v = max(0, v - lag[f])
            if cap and f in cap:
                v = min(v, cap[f])
            if v > self.seen_c[eng][f]:
                self.ops[eng].append(
                    lambda e, h=self.csem[f], v=v: e.wait_ge(h, v))
                self.seen_c[eng][f] = v
            dv = self.dval[f]
            if dv > self.seen_d[eng][f]:
                self.ops[eng].append(
                    lambda e, h=self.dsem[f], v=dv: e.wait_ge(h, v))
                self.seen_d[eng][f] = dv
        dv = self.dval[eng]
        if dv > self.seen_d[eng][eng]:
            self.ops[eng].append(
                lambda e, h=self.dsem[eng], v=dv: e.wait_ge(h, v))
            self.seen_d[eng][eng] = dv

    def step(self, eng, fns, lag=None, cap=None):
        if not isinstance(fns, (list, tuple)):
            fns = [fns]
        self._barrier(eng, lag, cap)
        if eng in ("scalar", "vector", "gpsimd"):
            self.ops[eng].append(lambda e: e.drain(fusable=True))
        for fn in fns[:-1]:
            self.ops[eng].append(lambda e, fn=fn: fn(e))
        self.cval[eng] += 1
        self.ops[eng].append(
            lambda e, fn=fns[-1], h=self.csem[eng]:
            fn(e).then_inc(h, 1))

    def dma(self, eng, pairs, lag=None):
        self._barrier(eng, lag)
        for dst, src in pairs:
            self.dval[eng] += 16
            self.ops[eng].append(
                lambda e, d=dst, s=src, h=self.dsem[eng]:
                e.dma_start(out=d, in_=s, allow_slow_non_contiguous=True).then_inc(h, 16))

    def indirect(self, pairs):
        self._barrier("gpsimd")
        for o, oo, i, io in pairs:
            self.dval["gpsimd"] += 16
            self.ops["gpsimd"].append(
                lambda e, o=o, oo=oo, i=i, io=io, h=self.dsem["gpsimd"]:
                e.indirect_dma_start(out=o, out_offset=oo, in_=i,
                                     in_offset=io).then_inc(h, 16))

    def collective(self, kind, op, groups, ins, outs):
        self._barrier("gpsimd")
        self.ccval += 1
        self.ops["gpsimd"].append(
            lambda e, k=kind, o=op, g=groups, i=ins, u=outs, h=self.ccsem:
            e.collective_compute(k, o, replica_groups=g, ins=i,
                                 outs=u).then_inc(h, 1))

    def wait_cc(self, eng, upto=None):
        v = self.ccval if upto is None else upto
        if v > self.seen_cc[eng]:
            self.ops[eng].append(lambda e, h=self.ccsem, v=v: e.wait_ge(h, v))
            self.seen_cc[eng] = v

    def raw(self, eng, fn):
        self.ops[eng].append(fn)

    def emit(self):
        ops = self.ops
        with self.nc.Block() as block:
            @block.sync
            def _(e):
                for f in ops["sync"]:
                    f(e)

            @block.scalar
            def _(e):
                for f in ops["scalar"]:
                    f(e)

            @block.vector
            def _(e):
                for f in ops["vector"]:
                    f(e)

            @block.tensor
            def _(e):
                for f in ops["tensor"]:
                    f(e)

            @block.gpsimd
            def _(e):
                for f in ops["gpsimd"]:
                    f(e)


class Pool:
    def __init__(self, t, n):
        self.t = t
        self.n = n
        self.off = 0
        self.hi = 0
        self.marks = []

    def tile(self, w):
        assert self.off + w <= self.n, f"pool overflow {self.off}+{w}"
        ap = self.t[:, self.off:self.off + w]
        self.off += w
        self.hi = max(self.hi, self.off)
        return ap

    def tiles(self, k, w):
        return [self.tile(w) for _ in range(k)]

    def push(self):
        self.marks.append(self.off)

    def pop(self):
        self.off = self.marks.pop()


def _finish_trunc(q, S, out, zerot, T):
    q.dma("gpsimd", [(out[tt * 128:(tt + 1) * 128, :], zerot[:, 0:1024])
                     for tt in range(T // 128)])
    S("gpsimd", lambda e: e.engine_nop())


def build_program(T=2048, TP=2048, n_ranks=NCORES, stop=99):
    nc = bass.Bass("TRN2", target_bir_lowering=False)
    st = ExitStack()
    q = SQ(nc, st)

    NCH = T // CH
    NCHP = TP // CH
    TALL = T * n_ranks
    NTILE_ALL = TALL // 128
    epc = E // n_ranks

    def din(name, shape, dt=F32):
        return nc.dram_tensor(name, shape, dt, kind="ExternalInput").ap()

    x_own = din("x_own", [T, D])
    x_prev = din("x_prev", [TP, D])
    flag = din("flag", [1, 1])
    c_row = din("c_row", [1, D])
    wmod = din("wmod", [D, 6 * D])
    ln1w = din("ln1w", [1, D]); ln1b = din("ln1b", [1, D])
    ln2w = din("ln2w", [1, D]); ln2b = din("ln2b", [1, D])
    normw = din("normw", [1, DI])
    mlpb1 = din("mlpb1", [1, MLP_H]); mlpb2 = din("mlpb2", [1, D])
    gateb = din("gateb", [1, E])
    a_log = din("a_log", [1, NH]); d_vec = din("d_vec", [1, NH])
    dtbias = din("dtbias", [1, NH])
    convw = din("convw", [CONVD, DCONV]); convb = din("convb", [CONVD, 1])
    w_ip = din("w_ip", [D, DIP])
    w_op = din("w_op", [DI, D])
    w_m1 = din("w_m1", [D, MLP_H])
    w_m2 = din("w_m2", [MLP_H, D])
    w_gt = din("w_gt", [D, E])
    w_s1 = din("w_s1", [D, SH_I]); w_s3 = din("w_s3", [D, SH_I])
    w_s2 = din("w_s2", [SH_I, D])
    w_e1 = din("w_e1", [epc, D, MOE_I])
    w_e3 = din("w_e3", [epc, D, MOE_I])
    w_e2 = din("w_e2", [epc, MOE_I, D])
    rank_i = din("rank_i", [1, 1], I32)

    out = nc.dram_tensor("out", [T, D], F32, kind="ExternalOutput").ap()

    def dint(name, shape, dt=F32):
        return nc.dram_tensor(name, shape, dt).ap()

    x_hbm = dint("x_hbm", [T, D])
    ag_xf_in = dint("ag_xf_in", [T, D])
    xf_all = dint("xf_all", [TALL + 128, D])
    ag_pay_in = dint("ag_pay_in", [T, 16])
    pay_all = dint("pay_all", [TALL, 16])
    tblh = dint("tblh", [E * TSTRIDE, 1])
    slotwf = dint("slotwf", [TALL, 16])
    cnts_h = dint("cnts_h", [1, E])
    tbl_own = dint("tbl_own", [E, 512])
    ag_yb_in = dint("ag_yb_in", [epc * CAP, D])
    yb_all = dint("yb_all", [E * CAP, D])
    z_hbm = dint("z_hbm", [T, D])

    POOLW = 53144
    poolt = st.enter_context(nc.sbuf_tensor("pool", [128, POOLW], F32))
    pool = Pool(poolt, POOLW)
    pooli = st.enter_context(nc.sbuf_tensor("pooli", [128, 64], I32))

    ones_row = pool.tile(128)
    ones_col = pool.tile(1)
    iot128 = pool.tile(128)
    iotc = pool.tile(1)
    ident = pool.tile(128)
    sltm = pool.tile(128)
    mask01 = pool.tile(128)
    modb = pool.tiles(6, D)
    lncol = pool.tiles(4, 8)
    b2b = pool.tile(D)
    gbB = pool.tile(E)
    nrmc = pool.tile(16)
    b1c = pool.tile(32)
    a128 = pool.tile(NH)
    d128 = pool.tile(NH)
    acol = pool.tile(1)
    nacol = pool.tile(1)
    dtbc = pool.tile(1)
    flagc = pool.tile(1)
    cw = pool.tiles(NXBC, DCONV)
    cb = pool.tiles(NXBC, 1)
    c_fm = pool.tile(8)
    state = pool.tile(NH * HD)
    tails = pool.tile(NXBC * 4)
    smallA = pool.tile(96)
    smallB = pool.tile(16)
    lnst = pool.tile(64)
    smallC = pool.tile(512)
    etimes = pool.tile(1)
    constc = pool.tile(4)
    baser = pool.tile(E)
    basec = pool.tile(1)

    persist_end = pool.off

    pool.push()
    srow = pool.tile(6 * D)
    zerot = pool.tile(2056)
    pool.pop()

    pool.push()
    xchunk = pool.tiles(4, D)
    htok = pool.tiles(4, D)
    x1tok = pool.tiles(4, D)
    hfm_big = pool.tile(8 * CH)
    hfm = [hfm_big[:, i * CH:(i + 1) * CH] for i in range(8)]
    zx = pool.tiles(NZX, CH)
    cvs = pool.tile(CH)
    ccopy = pool.tile(SC)
    wbufA = pool.tile(8 * 128)
    wbufB = pool.tile(8 * 128)
    wbufC = pool.tile(D)
    wbufD = pool.tile(D)
    xs_tok = hfm_big[:, 0:DI]
    y_tok = hfm_big[:, DI:2 * DI]
    bct = pool.tile(128)
    sdt = pool.tile(SC)
    sdtT = pool.tile(96)
    row0 = pool.tile(512)
    wfm = pool.tile(SC)
    dkc = pool.tile(1)
    dkrow = pool.tile(NH)
    asig = pool.tile(SC)
    aST = pool.tile(NH)
    easT = pool.tile(NH)
    dkerb = pool.tile(NH)
    scoresT = pool.tile(128)
    lpre = pool.tiles(4, 128)
    mtb = pool.tiles(4, 128)
    dtt = pool.tile(CH)
    onescN = pool.tile(SC)
    payb = pool.tiles(4, 16)
    scorekeep = pool.tile(E)
    attn_end = pool.off
    assert attn_end < POOLW, attn_end

    print('pool hi', pool.hi, 'off', pool.off)
    pb = [st.enter_context(nc.psum_tensor(f"pb{i}", [128, 512], F32))
          for i in range(8)]

    def S(eng, *fns):
        q.step(eng, list(fns))

    def SL(eng, lag, *fns):
        q.step(eng, list(fns), lag=lag)

    def MM(psum, lhsT, rhs, start, stop):
        return lambda e, o=psum, l=lhsT, r=rhs, a=start, b=stop: \
            e.matmul(o, l, r, start=a, stop=b)

    def TR(psum, src, npart):
        return lambda e, o=psum, i=src, n=npart: \
            e.transpose(o, i, ident[0:n, 0:n])

    def CP(dst, src):
        return lambda e, d=dst, s=src: e.tensor_copy(d, s)

    def TT(dst, a, b, op):
        return lambda e, d=dst, a=a, b=b, o=op: e.tensor_tensor(d, a, b, o)

    def TS(dst, a, s1, s2, op0, op1=ALU.bypass):
        return lambda e, d=dst, a=a, s1=s1, s2=s2, o0=op0, o1=op1: \
            e.tensor_scalar(d, a, s1, s2, o0, o1)

    def STT(dst, in0, sc, in1, op0, op1):
        return lambda e, d=dst, i0=in0, s=sc, i1=in1, o0=op0, o1=op1: \
            e.scalar_tensor_tensor(d, i0, s, i1, o0, o1)

    def ACT(dst, src, f, bias=0.0, scale=1.0):
        return lambda e, d=dst, s=src, f=f, b=bias, sc=scale: \
            e.activation(d, s, f, bias=b, scale=sc)

    def RED(dst, src, op):
        return lambda e, d=dst, s=src, o=op: \
            e.tensor_reduce(d, s, mybir.AxisListType.X, o)

    def MS(dst, v):
        return lambda e, d=dst, v=v: e.memset(d, v)

    def bcast(psum, row, nparts):
        w = row.shape[-1]
        return MM(psum[0:nparts, 0:w], ones_row[0:1, 0:nparts], row,
                  True, True)

    q.dma("sync", [
        (c_fm, c_row.rearrange("o (q f) -> (o f) q", f=128)),
        (smallA[0:1, 0:NH], a_log[:, :]),
        (smallA[0:1, NH:2 * NH], d_vec[:, :]),
        (smallA[0:1, 2 * NH:3 * NH], dtbias[:, :]),
        (flagc[0:1, :], flag[:, :]),
        (nrmc, normw.rearrange("o (q f) -> (o f) q", f=128)),
        (b1c, mlpb1.rearrange("o (q f) -> (o f) q", f=128)),
        (smallC[0:1, 0:E], gateb[:, :]),
        (lncol[0], ln1w.rearrange("o (q f) -> (o f) q", f=128)),
        (lncol[1], ln1b.rearrange("o (q f) -> (o f) q", f=128)),
        (lncol[2], ln2w.rearrange("o (q f) -> (o f) q", f=128)),
        (lncol[3], ln2b.rearrange("o (q f) -> (o f) q", f=128)),
        (srow[0:1, 4 * D:5 * D], mlpb2[:, :]),
    ] + [(cw[i], convw[i * 128:(i + 1) * 128, :]) for i in range(NXBC)]
      + [(cb[i], convb[i * 128:(i + 1) * 128, :]) for i in range(NXBC)])

    S("vector", MS(ident, 1.0))
    S("vector", lambda e: e.tensor_tensor_scan(iot128, ident, ident, 0.0,
                                               ALU.mult, ALU.add))
    S("vector", TS(iot128, iot128, 1.0, None, ALU.subtract))
    S("tensor", TR(pb[0][0:128, 0:1], iot128[0:1, 0:128], 1))
    S("vector", CP(iotc, pb[0][0:128, 0:1]))
    S("vector",
      MS(ones_row[0:1, :], 1.0), MS(ones_col, 1.0), MS(onescN[0:NH, :], 1.0),
      MS(zerot, 0.0), MS(state[0:64, :], 0.0), MS(tails, 0.0),
      MS(baser[0:1, :], 0.0), MS(basec[0:64, :], 0.0),
      TS(ident, iot128, iotc, None, ALU.is_equal),
      TS(sltm, iot128, iotc, None, ALU.is_gt),
      TS(mask01, iot128, iotc, None, ALU.is_ge),
      TS(etimes[0:64, :], iotc[0:64, :], float(TSTRIDE), None, ALU.mult),
      MS(constc[:, 0:1], float(EPS)), MS(constc[:, 1:2], 1.0),
      MS(constc[:, 2:3], -1.0), MS(constc[:, 3:4], 1.0 / DI))

    S("scalar", ACT(smallA[0:1, 0:NH], smallA[0:1, 0:NH], AF.Exp))
    S("scalar", ACT(smallA[0:1, 0:NH], smallA[0:1, 0:NH], AF.Copy, scale=constc[0:1, 2:3]),
      ACT(c_fm, c_fm, AF.Silu))
    ztb = tblh.rearrange("(a b) o -> a (b o)", b=8224)
    q.dma("gpsimd", [(ztb[i * 128:(i + 1) * 128, j * 1028:(j + 1) * 1028],
                      zerot[:, 0:1028]) for i in range(1) for j in range(8)])
    q.dma("gpsimd", [(xf_all[TALL:TALL + 128, :], zerot[:, 0:1024])])

    bc_jobs = [(b2b, srow[0:1, 4 * D:5 * D]), (gbB, smallC[0:1, 0:E]),
               (a128, smallA[0:1, 0:NH]), (d128, smallA[0:1, NH:2 * NH])]
    for i, (dst, row) in enumerate(bc_jobs):
        w = row.shape[-1]
        for si in range((w + 511) // 512):
            lo, hi = si * 512, min(w, (si + 1) * 512)
            S("tensor", bcast(pb[0], row[:, lo:hi], 128))
            S("vector", CP(dst[:, lo:hi], pb[0][0:128, 0:hi - lo]))
    S("tensor", TR(pb[1][0:NH, 0:1], smallA[0:1, 0:NH], 1))
    S("vector", CP(acol[0:NH, :], pb[1][0:NH, 0:1]))
    S("vector", TS(nacol[0:NH, :], acol[0:NH, :], -1.0, None, ALU.mult))
    S("tensor", TR(pb[1][0:NH, 0:1], smallA[0:1, 2 * NH:3 * NH], 1))
    S("vector", CP(dtbc[0:NH, :], pb[1][0:NH, 0:1]))
    S("tensor", bcast(pb[1], flagc[0:1, 0:1], 128))
    S("vector", CP(flagc, pb[1][0:128, 0:1]))

    if stop < 1:
        _finish_trunc(q, S, out, zerot, T)
        q.emit(); st.close(); return nc
    NW = 6 * D // 512
    for nwi in range(NW):
        for kc in range(8):
            wb = wbufC if kc % 2 == 0 else wbufD
            q.dma("sync", [(wb[:, 0:512],
                            wmod[kc * 128:(kc + 1) * 128,
                                 nwi * 512:(nwi + 1) * 512])])
            S("tensor", MM(pb[2][0:1, 0:512], c_fm[:, kc:kc + 1],
                           wb[:, 0:512], kc == 0, kc == 7))
        S("vector", CP(srow[0:1, nwi * 512:(nwi + 1) * 512],
                       pb[2][0:1, 0:512]))
    for gi in (2, 5):
        sl = srow[0:1, gi * D:(gi + 1) * D]
        s0 = zerot[0:1, 0:D]
        s1 = zerot[0:1, D:2 * D]
        S("vector", TS(s0, sl, 0.0, 2.0, ALU.is_ge, ALU.mult),
          TS(s1, sl, -1.0, None, ALU.mult))
        S("vector", TS(s0, s0, 1.0, None, ALU.subtract),
          TT(s1, s1, sl, ALU.max))
        S("vector", TS(s1, s1, 0.25, None, ALU.max))
        S("vector", TT(sl, s0, s1, ALU.mult))
    S("vector", TS(srow[0:1, D:2 * D], srow[0:1, D:2 * D], 1.0, None, ALU.add),
      TS(srow[0:1, 4 * D:5 * D], srow[0:1, 4 * D:5 * D], 1.0, None, ALU.add))
    for i in range(6):
        row = srow[0:1, i * D:(i + 1) * D]
        for si in range(2):
            lo, hi = si * 512, (si + 1) * 512
            S("tensor", bcast(pb[0], row[:, lo:hi], 128))
            S("vector", CP(modb[i][:, lo:hi], pb[0][0:128, 0:512]))

    def ln_tiles(src_tiles, dst_tiles, mul_tile, add_tile, nt):
        S("vector", *[f for tt in range(nt) for f in (
            (lambda e, s=src_tiles[tt], tt=tt: e.bn_stats(
                lnst[0:128, tt * 16:tt * 16 + 6], s[:, 0:512])),
            (lambda e, s=src_tiles[tt], tt=tt: e.bn_stats(
                lnst[0:128, tt * 16 + 6:tt * 16 + 12], s[:, 512:1024])))])
        S("vector", *[(lambda e, tt=tt: e.bn_aggr(
            lnst[0:128, tt * 16 + 12:tt * 16 + 14],
            lnst[0:128, tt * 16:tt * 16 + 12])) for tt in range(nt)])
        S("scalar", *[ACT(lnst[0:128, tt * 16 + 14:tt * 16 + 15],
                          lnst[0:128, tt * 16 + 13:tt * 16 + 14],
                          AF.Sqrt, bias=constc[:, 0:1]) for tt in range(nt)])
        S("vector", *[(lambda e, tt=tt: e.reciprocal(
            lnst[0:128, tt * 16 + 14:tt * 16 + 15],
            lnst[0:128, tt * 16 + 14:tt * 16 + 15])) for tt in range(nt)])
        S("vector", *[TS(dst_tiles[tt], src_tiles[tt],
                         lnst[0:128, tt * 16 + 12:tt * 16 + 13],
                         lnst[0:128, tt * 16 + 14:tt * 16 + 15],
                         ALU.subtract, ALU.mult) for tt in range(nt)])
        if mul_tile is not None:
            S("vector", *[TT(dst_tiles[tt], dst_tiles[tt], mul_tile, ALU.mult)
                          for tt in range(nt)])
            S("vector", *[TT(dst_tiles[tt], dst_tiles[tt], add_tile, ALU.add)
                          for tt in range(nt)])

    def to_fm(src_tiles, dst_fm_tiles, nf, nt, wcol=None, bcol=None):
        jobs = []
        for fc in range(nf):
            for tt in range(nt):
                jobs.append((fc, tt))
        for j0 in range(0, len(jobs), 4):
            grp = jobs[j0:j0 + 4]
            S("tensor", *[TR(pb[4 + i][0:128, 0:128],
                             src_tiles[tt][:, fc * 128:(fc + 1) * 128], 128)
                          for i, (fc, tt) in enumerate(grp)])
            if wcol is None:
                S("vector", *[CP(dst_fm_tiles[fc][:, tt * 128:(tt + 1) * 128],
                                 pb[4 + i][0:128, 0:128])
                              for i, (fc, tt) in enumerate(grp)])
            else:
                S("vector", *[TS(dst_fm_tiles[fc][:, tt * 128:(tt + 1) * 128],
                                 pb[4 + i][0:128, 0:128],
                                 wcol[:, fc:fc + 1], bcol[:, fc:fc + 1],
                                 ALU.mult, ALU.add)
                              for i, (fc, tt) in enumerate(grp)])

    def patternA(wdram, m0t, m1t, out_cb, Tn, kparts=8, mtot=None,
                 floors=None):
        if mtot is None:
            mtot = m1t * 128
        mcs = list(range(m0t, m1t))
        wid = {mc: min(128, mtot - mc * 128) for mc in mcs}
        wsrc = wdram.rearrange("(kc p) m -> p kc m", p=128)
        steps_done = 0
        buf_free_at = {0: None, 1: None}
        if floors is None:
            fl = {"vector": q.cval["vector"], "scalar": q.cval["scalar"],
                  "gpsimd": q.cval["gpsimd"]}
        else:
            fl = dict(floors)
        evac_pos = []
        for bi, mc in enumerate(mcs):
            bsel = bi % 2
            wb = wbufA if bsel == 0 else wbufB
            mw = wid[mc]
            fa = buf_free_at[bsel]
            q.dma("sync", [(
                wb[:, 0:kparts * mw].rearrange("p (kc m) -> p kc m", kc=kparts),
                wsrc[:, 0:kparts, mc * 128:mc * 128 + mw])],
                lag=None if fa is None else
                {"tensor": steps_done - fa, "vector": 99, "scalar": 99,
                 "gpsimd": 99})
            buf_free_at[bsel] = steps_done + 1
            bank = pb[mc % 4]
            if bi < 4:
                cap = dict(fl)
            else:
                ep = evac_pos[bi - 4]
                cap = {k: max(fl.get(k, 0), ep.get(k, 0))
                       for k in ("vector", "scalar", "gpsimd")}
            q.step("tensor",
                   [MM(bank[0:mw, 0:Tn],
                       wb[:, kc * mw:kc * mw + mw],
                       hfm[kc][:, 0:Tn], kc == 0, kc == kparts - 1)
                    for kc in range(kparts)], cap=cap)
            steps_done += 1
            out_cb(mc, bank[0:mw, 0:Tn])
            evac_pos.append({"vector": q.cval["vector"],
                             "scalar": q.cval["scalar"],
                             "gpsimd": q.cval["gpsimd"]})

    def conv_chunk(Tn):
        xbc = [cvs] + [zx[15 + i] for i in range(1, NXBC)]
        S("vector", *[CP(smallC[0:128, i * 3:(i + 1) * 3],
                         zx[16 + i][:, Tn - 3:Tn]) for i in range(NXBC)])
        for i in range(NXBC):
            srct = zx[16 + i]
            dstt = xbc[i]
            ops1 = [TS(dstt[:, 0:Tn], srct[:, 0:Tn], cw[i][:, 3:4], None,
                       ALU.mult)]
            S("vector", *ops1)
            for k in range(3):
                S("vector", STT(dstt[:, 3 - k:Tn], srct[:, 0:Tn - 3 + k],
                                cw[i][:, k:k + 1], dstt[:, 3 - k:Tn],
                                ALU.mult, ALU.add))
                tops = []
                for t in range(0, 3 - k):
                    tops.append(STT(dstt[:, t:t + 1],
                                    tails[:, i * 4 + t + k:i * 4 + t + k + 1],
                                    cw[i][:, k:k + 1], dstt[:, t:t + 1],
                                    ALU.mult, ALU.add))
                if tops:
                    S("vector", *tops)
        S("vector", *[CP(tails[:, i * 4:i * 4 + 3],
                         smallC[0:128, i * 3:(i + 1) * 3])
                      for i in range(NXBC)])
        S("scalar", *[ACT(xbc[i][:, 0:Tn], xbc[i][:, 0:Tn], AF.Silu,
                          bias=cb[i][:, :]) for i in range(NXBC)])
        return xbc

    def dt_chunk(Tn):
        dts = smallC
        S("scalar", ACT(dtt[0:NH, 0:Tn], zx[33][0:NH, 0:Tn], AF.Abs,
                        bias=dtbc[0:NH, :]))
        S("scalar", ACT(dts[0:NH, 0:Tn], dtt[0:NH, 0:Tn], AF.Exp, scale=constc[0:NH, 2:3]))
        S("scalar", ACT(dts[0:NH, 0:Tn], dts[0:NH, 0:Tn], AF.Ln, bias=constc[0:NH, 1:2]))
        S("scalar", ACT(dtt[0:NH, 0:Tn], zx[33][0:NH, 0:Tn], AF.Relu,
                        bias=dtbc[0:NH, :]))
        S("vector", TT(dtt[0:NH, 0:Tn], dtt[0:NH, 0:Tn], dts[0:NH, 0:Tn],
                       ALU.add))

    def ssd_subchunk(xbc, o, partial):
        bc_tile = xbc[16]
        q.dma("gpsimd", [(ccopy[0:64, 0:SC], bc_tile[64:128, o:o + SC])])
        S("vector",
          lambda e: e.tensor_tensor_scan(sdt[0:NH, 0:SC], onescN[0:NH, 0:SC],
                                         dtt[0:NH, o:o + SC], 0.0,
                                         ALU.mult, ALU.add))
        S("vector", TS(asig[0:NH, 0:SC], sdt[0:NH, 0:SC], acol[0:NH, :],
                       None, ALU.mult),
          TS(wfm[0:NH, 0:SC], sdt[0:NH, 0:SC],
             sdt[0:NH, SC - 1:SC], nacol[0:NH, :], ALU.subtract, ALU.mult))
        S("scalar", ACT(wfm[0:NH, 0:SC], wfm[0:NH, 0:SC], AF.Exp))
        S("vector", TT(wfm[0:NH, 0:SC], wfm[0:NH, 0:SC],
                       dtt[0:NH, o:o + SC], ALU.mult),
          TS(dkc[0:NH, :], sdt[0:NH, SC - 1:SC], acol[0:NH, :], None,
             ALU.mult))
        S("scalar", ACT(dkc[0:NH, :], dkc[0:NH, :], AF.Exp))
        S("tensor", TR(pb[4][0:SC, 0:NH], sdt[0:NH, 0:SC], NH),
          TR(pb[5][0:SC, 0:NH], dtt[0:NH, o:o + SC], NH),
          TR(pb[6][0:SC, 0:NH], wfm[0:NH, 0:SC], NH),
          TR(pb[7][0:1, 0:NH], dkc[0:NH, :], NH))
        S("vector", CP(sdtT[:, 0:NH], pb[4][0:SC, 0:NH]),
          CP(sdtT[:, NH:2 * NH], pb[5][0:SC, 0:NH]),
          CP(sdtT[:, 2 * NH:3 * NH], pb[6][0:SC, 0:NH]),
          CP(dkrow[0:1, 0:NH], pb[7][0:1, 0:NH]))
        S("vector", TT(aST[:, 0:NH], sdtT[:, 0:NH], a128, ALU.mult),
          TS(smallA[:, 0:NH], sdtT[:, 0:NH], -1.0, None, ALU.mult))
        S("scalar", ACT(easT[:, 0:NH], aST[:, 0:NH], AF.Exp))
        S("tensor", bcast(pb[6], dkrow[0:1, 0:NH], 64))
        S("vector", CP(dkerb[0:64, 0:NH], pb[6][0:64, 0:NH]))
        S("tensor", TR(pb[7][0:128, 0:128], bc_tile[:, o:o + SC], 128))
        S("vector", CP(bct[:, 0:128], pb[7][0:128, 0:128]))
        for f0 in range(0, 16, 4):
            S("tensor", *[TR(pb[4 + i][0:128, 0:128],
                             xbc[f0 + i][:, o:o + SC], 128) for i in range(4)])
            S("vector", *[CP(xs_tok[:, (f0 + i) * 128:(f0 + i + 1) * 128],
                             pb[4 + i][0:128, 0:128]) for i in range(4)])
        if not partial:
            S("tensor", MM(pb[6][0:128, 0:128], bc_tile[0:64, o:o + SC],
                           ccopy[0:64, 0:SC], True, True))
            S("vector", TT(scoresT[:, 0:128], pb[6][0:128, 0:128], mask01,
                           ALU.mult))

        for hg in range(NH // 4):
            hs = hg * 4
            if not partial:
                S("tensor", *[TR(pb[4 + j][0:1, 0:128],
                                 sdtT[:, hs + j:hs + j + 1], 128)
                              for j in range(4)])
                S("vector", *[CP(row0[0:1, j * 128:(j + 1) * 128],
                                 pb[4 + j][0:1, 0:128]) for j in range(4)])
                S("tensor", *[MM(pb[0][0:128, j * 128:(j + 1) * 128],
                                 ones_row[0:1, 0:128],
                                 row0[0:1, j * 128:(j + 1) * 128],
                                 j == 0, j == 3) for j in range(4)])
                S("scalar", *[ACT(lpre[j], pb[0][0:128, j * 128:(j + 1) * 128],
                                  AF.Relu, bias=smallA[:, hs + j:hs + j + 1])
                              for j in range(4)])
                S("scalar", *[(lambda e, j=j, h=hs + j: e.activation(
                    lpre[j], lpre[j], AF.Exp, scale=a128[:, h:h + 1]))
                    for j in range(4)])
                S("vector", *[STT(mtb[j], lpre[j],
                                  sdtT[:, NH + hs + j:NH + hs + j + 1],
                                  scoresT, ALU.mult, ALU.mult)
                              for j in range(4)])
                S("tensor",
                  *[MM(pb[2][0:128, j * 64:(j + 1) * 64], mtb[j],
                       xs_tok[:, (hs + j) * 64:(hs + j + 1) * 64],
                       j == 0, j == 3) for j in range(4)],
                  *[MM(pb[3][0:128, j * 64:(j + 1) * 64],
                       ccopy[0:64, 0:SC],
                       state[0:64, (hs + j) * 64:(hs + j + 1) * 64],
                       j == 0, j == 3) for j in range(4)])
                S("vector", *[CP(y_tok[:, (hs + j) * 64:(hs + j + 1) * 64],
                                 pb[2][0:128, j * 64:(j + 1) * 64])
                              for j in range(4)])
                S("vector", *[STT(y_tok[:, (hs + j) * 64:(hs + j + 1) * 64],
                                  pb[3][0:128, j * 64:(j + 1) * 64],
                                  easT[:, hs + j:hs + j + 1],
                                  y_tok[:, (hs + j) * 64:(hs + j + 1) * 64],
                                  ALU.mult, ALU.add) for j in range(4)])
                S("vector", *[STT(y_tok[:, (hs + j) * 64:(hs + j + 1) * 64],
                                  xs_tok[:, (hs + j) * 64:(hs + j + 1) * 64],
                                  d128[:, hs + j:hs + j + 1],
                                  y_tok[:, (hs + j) * 64:(hs + j + 1) * 64],
                                  ALU.mult, ALU.add) for j in range(4)])
            S("vector", *[TS(xs_tok[:, (hs + j) * 64:(hs + j + 1) * 64],
                             xs_tok[:, (hs + j) * 64:(hs + j + 1) * 64],
                             sdtT[:, 2 * NH + hs + j:2 * NH + hs + j + 1],
                             None, ALU.mult)
                          for j in range(4)])
            S("tensor", *[MM(pb[1][0:64, j * 64:(j + 1) * 64], bct[:, 0:64],
                             xs_tok[:, (hs + j) * 64:(hs + j + 1) * 64],
                             j == 0, j == 3) for j in range(4)])
            S("vector", *[STT(state[0:64, (hs + j) * 64:(hs + j + 1) * 64],
                              state[0:64, (hs + j) * 64:(hs + j + 1) * 64],
                              dkerb[0:64, hs + j:hs + j + 1],
                              pb[1][0:64, j * 64:(j + 1) * 64],
                              ALU.mult, ALU.add) for j in range(4)])

    def attn_chunk(c, partial, xsrc):
        nt = CH // 128
        q.dma("sync", [(xchunk[tt], xsrc[c * CH + tt * 128:
                                         c * CH + (tt + 1) * 128, :])
                       for tt in range(nt)])
        ln_tiles(xchunk, htok, modb[1], modb[0], nt)
        ln_tiles(htok, x1tok, None, None, nt)
        to_fm(x1tok, hfm, 8, nt, lncol[0], lncol[1])
        def ip_cb(mc, ps):
            S("vector", CP(zx[mc][0:(32 if mc == NZX - 1 else 128), 0:CH], ps))
        patternA(w_ip, 16, NZX, ip_cb, CH, mtot=DIP)
        post_xbc = {"vector": q.cval["vector"], "scalar": q.cval["scalar"],
                    "gpsimd": q.cval["gpsimd"]}
        xbc = conv_chunk(CH)
        dt_chunk(CH)
        if not partial:
            patternA(w_ip, 0, 16, ip_cb, CH, mtot=DIP, floors=post_xbc)
        for sci in range(CH // SC):
            ssd_subchunk(xbc, sci * SC, partial)
            if not partial:
                post_ssd_subchunk(c, sci, xbc)
        if not partial:
            post_chunk(c, nt)

    def post_ssd_subchunk(c, sci, xbc):
        o = sci * SC
        S("scalar", *[ACT(zx[fc][:, o:o + SC], zx[fc][:, o:o + SC], AF.Silu)
                      for fc in range(16)])
        for f0 in range(0, 16, 4):
            S("tensor", *[TR(pb[4 + i][0:128, 0:128],
                             y_tok[:, (f0 + i) * 128:(f0 + i + 1) * 128], 128)
                          for i in range(4)])
            S("vector", *[TT(zx[f0 + i][:, o:o + SC],
                             pb[4 + i][0:128, 0:128],
                             zx[f0 + i][:, o:o + SC], ALU.mult)
                          for i in range(4)])
        sqops = []
        mmops = []
        for fc in range(16):
            sqops.append(TT(lpre[fc % 4], zx[fc][:, o:o + SC],
                            zx[fc][:, o:o + SC], ALU.mult))
            mmops.append(MM(pb[6][0:1, 0:SC], ones_col[0:128, :],
                            lpre[fc % 4], fc == 0, fc == 15))
        for f0 in range(0, 16, 4):
            S("vector", *sqops[f0:f0 + 4])
            S("tensor", *mmops[f0:f0 + 4])
        S("scalar", ACT(smallC[0:1, 0:SC], pb[6][0:1, 0:SC], AF.Sqrt,
                        bias=constc[0:1, 0:1], scale=constc[0:1, 3:4]))
        S("vector", lambda e: e.reciprocal(smallC[0:1, 0:SC],
                                           smallC[0:1, 0:SC]))
        S("tensor", bcast(pb[7], smallC[0:1, 0:SC], 128))
        S("vector", *[STT(zx[fc][:, o:o + SC], pb[7][0:128, 0:SC],
                          nrmc[:, fc:fc + 1], zx[fc][:, o:o + SC],
                          ALU.mult, ALU.mult) for fc in range(16)])

    def post_chunk(c, nt):
        for kc in range(16):
            wb = wbufC if kc % 2 == 0 else wbufD
            q.dma("sync", [(wb[:, 0:D], w_op[kc * 128:(kc + 1) * 128, :])],
                  lag=None if kc <= 1 else {"tensor": 1, "vector": 99, "scalar": 99, "gpsimd": 99})
            q.step("tensor", [MM(pb[tt * 2 + si][0:128, 0:512],
                                 zx[kc][:, tt * 128:(tt + 1) * 128],
                                 wb[:, si * 512:(si + 1) * 512],
                                 kc == 0, kc == 15)
                              for tt in range(nt) for si in range(2)],
                   lag=None if kc == 0 else {"vector": 99, "scalar": 99})
        S("vector", *[TT(x1tok[tt][:, si * 512:(si + 1) * 512],
                         pb[tt * 2 + si][0:128, 0:512],
                         htok[tt][:, si * 512:(si + 1) * 512], ALU.add)
                      for tt in range(nt) for si in range(2)])
        ln_tiles(x1tok, htok, None, None, nt)
        to_fm(htok, hfm, 8, nt, lncol[2], lncol[3])
        patternA(w_m1, 0, 32,
                 lambda mc, ps: S("scalar",
                                  ACT(zx[mc][:, 0:CH], ps, AF.Gelu,
                                      bias=b1c[:, mc:mc + 1])), CH)
        for kc in range(32):
            wb = wbufC if kc % 2 == 0 else wbufD
            q.dma("sync", [(wb[:, 0:D], w_m2[kc * 128:(kc + 1) * 128, :])],
                  lag=None if kc <= 1 else {"tensor": 1, "vector": 99, "scalar": 99, "gpsimd": 99})
            q.step("tensor", [MM(pb[tt * 2 + si][0:128, 0:512],
                                 zx[kc][:, tt * 128:(tt + 1) * 128],
                                 wb[:, si * 512:(si + 1) * 512],
                                 kc == 0, kc == 31)
                              for tt in range(nt) for si in range(2)],
                   lag=None if kc == 0 else {"vector": 99, "scalar": 99})
        S("vector", *[TT(htok[tt][:, si * 512:(si + 1) * 512],
                         pb[tt * 2 + si][0:128, 0:512],
                         b2b[:, si * 512:(si + 1) * 512], ALU.add)
                      for tt in range(nt) for si in range(2)])
        S("vector", *[TT(htok[tt], htok[tt], x1tok[tt], ALU.add)
                      for tt in range(nt)])
        S("vector", *[TT(htok[tt], htok[tt], modb[2], ALU.mult)
                      for tt in range(nt)])
        S("vector", *[TT(xchunk[tt], xchunk[tt], htok[tt], ALU.add)
                      for tt in range(nt)])
        q.dma("gpsimd", [(x_hbm[c * CH + tt * 128:c * CH + (tt + 1) * 128, :],
                          xchunk[tt]) for tt in range(nt)])
        ln_tiles(xchunk, htok, modb[4], modb[3], nt)
        q.dma("gpsimd", [(ag_xf_in[c * CH + tt * 128:c * CH + (tt + 1) * 128, :],
                          htok[tt]) for tt in range(nt)])
        to_fm(htok, hfm, 8, nt)
        gate_topk(c, nt)

    def gate_topk(c, nt):
        sco = [y_tok[:, tt * 64:(tt + 1) * 64] for tt in range(nt)]
        cur = [y_tok[:, 256 + tt * 64:256 + (tt + 1) * 64] for tt in range(nt)]
        eq = [y_tok[:, 512 + tt * 64:512 + (tt + 1) * 64] for tt in range(nt)]
        sel = [y_tok[:, 768 + tt * 64:768 + (tt + 1) * 64] for tt in range(nt)]
        tmp = [y_tok[:, 1024 + tt * 64:1024 + (tt + 1) * 64]
               for tt in range(nt)]
        st_ = lambda tt, k: lnst[0:128, tt * 16 + k:tt * 16 + k + 1]
        pts = [payb[tt] for tt in range(nt)]
        q.dma("sync", [(wbufC[:, kc * 64:(kc + 1) * 64],
                        w_gt[kc * 128:(kc + 1) * 128, :])
                       for kc in range(8)])
        S("tensor", *[MM(pb[4 + tt][0:128, 0:E],
                         hfm[kc][:, tt * 128:(tt + 1) * 128],
                         wbufC[:, kc * 64:(kc + 1) * 64],
                         kc == 0, kc == 7)
                      for tt in range(nt) for kc in range(8)])
        S("vector", *[RED(st_(tt, 0), pb[4 + tt][0:128, 0:E], ALU.max)
                      for tt in range(nt)])
        S("vector", *[TS(st_(tt, 1), st_(tt, 0), -1.0, None, ALU.mult)
                      for tt in range(nt)])
        S("scalar", *[(lambda e, tt=tt: e.activation(
            sco[tt], pb[4 + tt][0:128, 0:E], AF.Exp, bias=st_(tt, 1)))
            for tt in range(nt)])
        S("vector", *[RED(st_(tt, 2), sco[tt], ALU.add) for tt in range(nt)])
        S("vector", *[(lambda e, tt=tt: e.reciprocal(st_(tt, 3), st_(tt, 2)))
                      for tt in range(nt)])
        S("vector", *[TS(sco[tt], sco[tt], st_(tt, 3), None, ALU.mult)
                      for tt in range(nt)])
        S("vector", *[TT(cur[tt], sco[tt], gbB, ALU.add) for tt in range(nt)],
          *[MS(pts[tt], 0.0) for tt in range(nt)])
        for k in range(TOPK):
            S("vector", *[RED(st_(tt, 4), cur[tt], ALU.max)
                          for tt in range(nt)])
            S("vector", *[TS(eq[tt], cur[tt], st_(tt, 4), None, ALU.is_equal)
                          for tt in range(nt)])
            S("vector", *[TT(sel[tt], eq[tt], iot128[:, 0:64], ALU.mult)
                          for tt in range(nt)],
              *[TS(tmp[tt], eq[tt], 1.0, -64.0, ALU.subtract, ALU.mult)
                for tt in range(nt)])
            S("vector", *[TT(sel[tt], sel[tt], tmp[tt], ALU.add)
                          for tt in range(nt)])
            S("vector", *[RED(pts[tt][:, k:k + 1], sel[tt], ALU.min)
                          for tt in range(nt)])
            S("vector", *[TS(eq[tt], iot128[:, 0:64], pts[tt][:, k:k + 1],
                             None, ALU.is_equal) for tt in range(nt)])
            S("vector", *[TT(sel[tt], eq[tt], sco[tt], ALU.mult)
                          for tt in range(nt)])
            S("vector", *[RED(pts[tt][:, 8 + k:9 + k], sel[tt], ALU.add)
                          for tt in range(nt)],
              *[STT(cur[tt], eq[tt], -1e30, cur[tt], ALU.mult, ALU.add)
                for tt in range(nt)])
        q.dma("gpsimd",
              [(ag_pay_in[c * CH + tt * 128:c * CH + (tt + 1) * 128, :],
                pts[tt]) for tt in range(nt)])

    if stop < 2:
        _finish_trunc(q, S, out, zerot, T)
        q.emit(); st.close(); return nc
    for c in range(NCHP):
        attn_chunk(c, True, x_prev)
    S("vector", TS(state[0:64, :], state[0:64, :], flagc[0:64, :], None,
                   ALU.mult),
      TS(tails, tails, flagc, None, ALU.mult))
    if stop < 3:
        _finish_trunc(q, S, out, zerot, T)
        q.emit(); st.close(); return nc
    for c in range(NCH):
        attn_chunk(c, False, x_own)

    if stop < 4:
        _finish_trunc(q, S, out, zerot, T)
        q.emit(); st.close(); return nc
    q.collective("AllGather", ALU.bypass, [list(range(n_ranks))],
                 [ag_pay_in.opt()], [pay_all.opt()])
    cc_pay = q.ccval
    q.collective("AllGather", ALU.bypass, [list(range(n_ranks))],
                 [ag_xf_in.opt()], [xf_all[0:TALL, :].opt()])
    cc_xf = q.ccval

    pool.pop()
    pool.push()
    ptile = pool.tile(16)
    otile = pool.tile(E)
    cex = pool.tile(E)
    ptab = pool.tile(E)
    mskb = pool.tile(TOPK * E)
    bkb = pool.tile(TOPK * E)
    tmpk = pool.tile(2 * TOPK * E)
    colk = pool.tile(16)
    swf = pool.tile(16)
    ctile = pool.tile(E)
    vald = pool.tile(8)
    ofs = pool.tile(1)
    ofs_i = pooli[:, 0:1]

    if stop < 5:
        _finish_trunc(q, S, out, zerot[0:128, 0:1024] if False else zerot, T)
        q.emit(); st.close(); return nc
    q.wait_cc("sync", cc_pay)
    q.wait_cc("gpsimd", cc_pay)
    S("vector", MS(swf, 0.0))
    for ti in range(NTILE_ALL):
        q.dma("sync", [(ptile, pay_all[ti * 128:(ti + 1) * 128, :])])
        S("vector",
          *[TS(mskb[:, k * 64:(k + 1) * 64], iot128[:, 0:64],
               ptile[:, k:k + 1], None, ALU.is_equal) for k in range(TOPK)],
          TS(colk[:, 15:16], iotc, float(ti * 128), None, ALU.add),
          TT(ofs[0:64, :], basec[0:64, :], etimes[0:64, :], ALU.add))
        S("vector",
          lambda e: e.tensor_reduce(otile,
                                    mskb.rearrange("p (k e) -> p e k", k=TOPK),
                                    mybir.AxisListType.X, ALU.add),
          CP(ofs_i[0:64, :], ofs[0:64, :]))
        S("tensor",
          MM(pb[0][0:128, 0:E], sltm, otile, True, True),
          MM(pb[5][0:128, 0:E], sltm, otile, True, False),
          MM(pb[5][0:128, 0:E], ones_row[0:1, 0:128], baser[0:1, 0:E],
             False, True),
          MM(pb[1][0:1, 0:E], ones_col[0:128, :], otile, True, True),
          MM(pb[2][0:64, 0:1], otile[:, 0:64], ones_col[0:128, :],
             True, True))
        S("vector", CP(cex, pb[0][0:128, 0:E]), CP(ptab, pb[5][0:128, 0:E]))
        S("vector",
          *[TT(tmpk[:, k * 64:(k + 1) * 64], mskb[:, k * 64:(k + 1) * 64],
               cex, ALU.mult) for k in range(TOPK)],
          *[TT(tmpk[:, (6 + k) * 64:(7 + k) * 64],
               mskb[:, k * 64:(k + 1) * 64], ptab, ALU.mult)
            for k in range(TOPK)])
        S("vector",
          *[RED(colk[:, k:k + 1], tmpk[:, k * 64:(k + 1) * 64], ALU.add)
            for k in range(TOPK)],
          *[RED(colk[:, 8 + k:9 + k], tmpk[:, (6 + k) * 64:(7 + k) * 64],
                ALU.add) for k in range(TOPK)])
        S("vector",
          *[TS(bkb[:, k * 64:(k + 1) * 64], iot128[:, 0:64],
               colk[:, k:k + 1], colk[:, 15:16], ALU.is_equal, ALU.mult)
            for k in range(TOPK)])
        S("tensor",
          *[MM(pb[3][0:64, 0:E], mskb[:, k * 64:(k + 1) * 64],
               bkb[:, k * 64:(k + 1) * 64], k == 0, k == TOPK - 1)
            for k in range(TOPK)])
        S("vector",
          CP(ctile[0:64, :], pb[3][0:64, 0:E]),
          TT(baser[0:1, 0:E], baser[0:1, 0:E], pb[1][0:1, 0:E], ALU.add),
          TT(basec[0:64, :], basec[0:64, :], pb[2][0:64, 0:1], ALU.add),
          *[TS(swf[:, 8 + k:9 + k], colk[:, 8 + k:9 + k], 511.0, None,
               ALU.min) for k in range(TOPK)],
          *[TS(tmpk[:, k:k + 1] if False else vald[:, k:k + 1],
               colk[:, 8 + k:9 + k], 512.0, None, ALU.is_lt)
            for k in range(TOPK)])
        S("vector",
          *[STT(swf[:, k:k + 1], ptile[:, k:k + 1], 512.0,
                swf[:, 8 + k:9 + k], ALU.mult, ALU.add) for k in range(TOPK)])
        S("vector",
          *[TT(swf[:, 8 + k:9 + k], ptile[:, 8 + k:9 + k],
               vald[:, k:k + 1], ALU.mult) for k in range(TOPK)])
        q.indirect([(tblh[:, :], IndirectOffsetOnAxis(ofs_i[0:64, :], 0),
                     ctile[0:64, :], None)])
        q.dma("gpsimd", [(slotwf[ti * 128:(ti + 1) * 128, :], swf)])
    q.dma("gpsimd", [(cnts_h[:, :], baser[0:1, 0:E])])

    if stop < 6:
        _finish_trunc(q, S, out, zerot, T)
        q.emit(); st.close(); return nc
    pool.pop()
    pool.push()
    xbt = pool.tiles(4, D)
    xbfm = pool.tiles(8, 512)
    he = pool.tiles(11, 512)
    ybt = pool.tiles(4, D)
    ewA = pool.tile(8 * 128)
    ewB = pool.tile(8 * 128)
    ewC = pool.tile(D)
    ewD = pool.tile(D)
    idxv = pool.tile(8)
    cnt8 = pool.tile(E)
    vld = pool.tile(4)
    swft = pool.tile(16)
    rankc = pool.tile(1)
    ofs = pool.tile(1)
    tbl_own_sb = smallC
    xmt = pool.tile(D)
    zt = pool.tile(D)
    gath = pool.tile(D)
    acc = pool.tile(D)
    moe_end = pool.off
    idx_i4 = pooli[:, 4:8]
    slot_i = pooli[:, 8:9]

    rank_s = pooli[:, 12:13]
    q.dma("gpsimd", [(rank_s[0:1, :], rank_i[:, :])])
    S("vector", CP(smallB[0:1, 14:15], rank_s[0:1, :]))
    S("tensor", bcast(pb[0], smallB[0:1, 14:15], 128))
    S("vector", CP(rankc, pb[0][0:128, 0:1]))
    S("vector", TS(ofs[0:epc, :], rankc[0:epc, :], float(epc), None,
                   ALU.mult))
    S("vector", TT(ofs[0:epc, :], ofs[0:epc, :], iotc[0:epc, :], ALU.add))
    S("vector", CP(pooli[0:epc, 13:14], ofs[0:epc, :]))
    cnts_flat = cnts_h.rearrange("o e -> (o e)")[:, None]
    q.indirect([(smallC[0:epc, 0:1], None, cnts_flat,
                 IndirectOffsetOnAxis(pooli[0:epc, 13:14], 0))])
    S("tensor", TR(pb[0][0:1, 0:epc], smallC[0:epc, 0:1], epc))
    S("vector", CP(smallC[0:1, 4:4 + epc], pb[0][0:1, 0:epc]))
    S("tensor", bcast(pb[0], smallC[0:1, 4:4 + epc], 128))
    S("vector", CP(cnt8[:, 0:epc], pb[0][0:128, 0:epc]))
    S("vector", TS(ofs[0:epc, :], rankc[0:epc, :], float(epc), None,
                   ALU.mult))
    S("vector", TT(ofs[0:epc, :], ofs[0:epc, :], iotc[0:epc, :], ALU.add))
    S("vector", TS(ofs[0:epc, :], ofs[0:epc, :], float(TSTRIDE), None,
                   ALU.mult))
    S("vector", CP(pooli[0:epc, 14:15], ofs[0:epc, :]))
    q.indirect([(tbl_own_sb[0:epc, 0:512], None, tblh[:, :],
                 IndirectOffsetOnAxis(pooli[0:epc, 14:15], 0))])
    q.dma("gpsimd", [(tbl_own[0:epc, :], tbl_own_sb[0:epc, 0:512])])
    for j in range(epc):
        q.dma("gpsimd", [(idxv[:, (j % 2) * 4:(j % 2) * 4 + 4],
                          tbl_own[j, 0:512].rearrange("(f q) -> q f", q=128))])
        jj = j % 2
        S("vector", *[TS(vld[:, col:col + 1],
                         iotc, float(col * 128), None, ALU.add)
                      for col in range(4)])
        S("vector", *[TS(vld[:, col:col + 1], vld[:, col:col + 1],
                         cnt8[:, j:j + 1], None, ALU.is_lt)
                      for col in range(4)])
        S("vector", TS(idxv[:, jj * 4:jj * 4 + 4],
                       idxv[:, jj * 4:jj * 4 + 4], float(TALL), None,
                       ALU.subtract))
        S("vector", TT(idxv[:, jj * 4:jj * 4 + 4],
                       idxv[:, jj * 4:jj * 4 + 4], vld, ALU.mult))
        S("vector", TS(idxv[:, jj * 4:jj * 4 + 4],
                       idxv[:, jj * 4:jj * 4 + 4], float(TALL), None,
                       ALU.add))
        S("vector", CP(idx_i4, idxv[:, jj * 4:jj * 4 + 4]))
        if j == 0:
            q.wait_cc("gpsimd", cc_xf)
        q.indirect([(xbt[col], None, xf_all[:, :],
                     IndirectOffsetOnAxis(idx_i4[:, col:col + 1], 0))
                    for col in range(4)])
        for f0 in range(0, 8, 2):
            S("tensor", *[TR(pb[4 * i + tt][0:128, 0:128],
                             xbt[tt][:, (f0 + i) * 128:(f0 + i + 1) * 128], 128)
                          for i in range(2) for tt in range(4)])
            S("vector", *[CP(xbfm[f0 + i][:, tt * 128:(tt + 1) * 128],
                             pb[4 * i + tt][0:128, 0:128])
                          for i in range(2) for tt in range(4)])
        mws = [128, 128, 128, 128, 128, 64]
        def eA(wdram, j, cb_):
            wsrc = wdram.rearrange("e (kc p) m -> e p kc m", p=128)
            for bi in range(6):
                wb = ewA if bi % 2 == 0 else ewB
                mw = mws[bi]
                q.dma("sync", [(
                    wb[:, 0:8 * mw].rearrange("p (kc m) -> p kc m", kc=8),
                    wsrc[j, :, 0:8, bi * 128:bi * 128 + mw])],
                    lag=None if bi <= 1 else {"tensor": 1, "vector": 99,
                                              "scalar": 99, "gpsimd": 99})
                bank = pb[bi % 4]
                q.step("tensor",
                       [MM(bank[0:mw, 0:512],
                           wb[:, kc * mw:kc * mw + mw],
                           xbfm[kc][:, 0:512], kc == 0, kc == 7)
                        for kc in range(8)],
                       lag=None if bi == 0 else {"vector": 3, "scalar": 3})
                cb_(bi, bank[0:mw, 0:512], mw)
        eA(w_e1, j, lambda ic, ps, mwid:
           S("scalar", ACT(he[ic][0:mwid, :], ps, AF.Silu)))
        eA(w_e3, j, lambda ic, ps, mwid:
           S("vector", TT(he[ic][0:mwid, :], he[ic][0:mwid, :], ps,
                          ALU.mult)))
        for kc2 in range(6):
            mwid = mws[kc2]
            wb = ewC if kc2 % 2 == 0 else ewD
            q.dma("sync", [(wb[0:mwid, 0:D],
                            w_e2[j, kc2 * 128:kc2 * 128 + mwid, :])],
                  lag=None if kc2 <= 1 else {"tensor": 1, "vector": 99, "scalar": 99, "gpsimd": 99})
            q.step("tensor", [MM(pb[sl * 2 + si][0:128, 0:512],
                                 he[kc2][0:mwid, sl * 128:(sl + 1) * 128],
                                 wb[0:mwid, si * 512:(si + 1) * 512],
                                 kc2 == 0, kc2 == 5)
                              for sl in range(4) for si in range(2)],
                   lag=None if kc2 == 0 else {"vector": 99, "scalar": 99})
        S("vector", *[CP(ybt[sl][:, si * 512:(si + 1) * 512],
                         pb[sl * 2 + si][0:128, 0:512])
                      for sl in range(4) for si in range(2)])
        q.dma("gpsimd", [(ag_yb_in[j * CAP + sl * 128:j * CAP + (sl + 1) * 128,
                                   :], ybt[sl]) for sl in range(4)])

    q.collective("AllGather", ALU.bypass, [list(range(n_ranks))],
                 [ag_yb_in.opt()], [yb_all.opt()])
    cc_yb = q.ccval

    if stop < 7:
        _finish_trunc(q, S, out, zerot, T)
        q.emit(); st.close(); return nc
    for c in range(NCH):
        nt = CH // 128
        q.dma("sync", [(xbt[tt], ag_xf_in[c * CH + tt * 128:
                                          c * CH + (tt + 1) * 128, :])
                       for tt in range(nt)])
        for f0 in range(0, 8, 2):
            S("tensor", *[TR(pb[4 * i + tt][0:128, 0:128],
                             xbt[tt][:, (f0 + i) * 128:(f0 + i + 1) * 128], 128)
                          for i in range(2) for tt in range(4)])
            S("vector", *[CP(xbfm[f0 + i][:, tt * 128:(tt + 1) * 128],
                             pb[4 * i + tt][0:128, 0:128])
                          for i in range(2) for tt in range(4)])
        def sA(wdram, cb_):
            wsrc = wdram.rearrange("(kc p) m -> p kc m", p=128)
            for bi in range(11):
                wb = ewA if bi % 2 == 0 else ewB
                q.dma("sync", [(
                    wb[:, 0:8 * 128].rearrange("p (kc m) -> p kc m", kc=8),
                    wsrc[:, 0:8, bi * 128:(bi + 1) * 128])],
                    lag=None if bi <= 1 else {"tensor": 1, "vector": 99,
                                              "scalar": 99, "gpsimd": 99})
                bank = pb[bi % 4]
                q.step("tensor",
                       [MM(bank[0:128, 0:512],
                           wb[:, kc * 128:(kc + 1) * 128],
                           xbfm[kc][:, 0:512], kc == 0, kc == 7)
                        for kc in range(8)],
                       lag=None if bi == 0 else {"vector": 3, "scalar": 3})
                cb_(bi, bank[0:128, 0:512])
        sA(w_s1, lambda ic, ps: S("scalar", ACT(he[ic], ps, AF.Silu)))
        sA(w_s3, lambda ic, ps: S("vector", TT(he[ic], he[ic], ps, ALU.mult)))
        for kc2 in range(11):
            wb = ewC if kc2 % 2 == 0 else ewD
            q.dma("sync", [(wb[:, 0:D], w_s2[kc2 * 128:(kc2 + 1) * 128, :])],
                  lag=None if kc2 <= 1 else {"tensor": 1, "vector": 99, "scalar": 99, "gpsimd": 99})
            q.step("tensor", [MM(pb[sl * 2 + si][0:128, 0:512],
                                 he[kc2][:, sl * 128:(sl + 1) * 128],
                                 wb[:, si * 512:(si + 1) * 512],
                                 kc2 == 0, kc2 == 10)
                              for sl in range(4) for si in range(2)],
                   lag=None if kc2 == 0 else {"vector": 99, "scalar": 99})
        S("vector", *[CP(ybt[sl][:, si * 512:(si + 1) * 512],
                         pb[sl * 2 + si][0:128, 0:512])
                      for sl in range(4) for si in range(2)])
        q.dma("gpsimd", [(z_hbm[c * CH + sl * 128:c * CH + (sl + 1) * 128, :],
                          ybt[sl]) for sl in range(4)])

    if stop < 8:
        _finish_trunc(q, S, out, zerot, T)
        q.emit(); st.close(); return nc
    q.wait_cc("gpsimd", cc_yb)
    for tt in range(T // 128):
        q.dma("sync", [(xmt, x_hbm[tt * 128:(tt + 1) * 128, :]),
                       (zt, z_hbm[tt * 128:(tt + 1) * 128, :])])
        S("vector", TS(ofs, rankc, float(T), None, ALU.mult))
        S("vector", TT(ofs, ofs, iotc, ALU.add))
        S("vector", TS(ofs, ofs, 1.0, float(tt * 128), ALU.mult, ALU.add))
        S("vector", CP(pooli[:, 15:16], ofs))
        q.indirect([(swft, None, slotwf[:, :],
                     IndirectOffsetOnAxis(pooli[:, 15:16], 0))])
        for k in range(TOPK):
            S("vector", CP(slot_i, swft[:, k:k + 1]))
            q.indirect([(gath, None, yb_all[:, :],
                         IndirectOffsetOnAxis(slot_i, 0))])
            if k == 0:
                S("vector", TS(acc, gath, swft[:, 8:9], None, ALU.mult))
            else:
                S("vector", STT(acc, gath, swft[:, 8 + k:9 + k], acc,
                                ALU.mult, ALU.add))
        S("vector", TT(acc, acc, zt, ALU.add))
        S("vector", TT(acc, acc, modb[5], ALU.mult))
        S("vector", TT(acc, acc, xmt, ALU.add))
        q.dma("gpsimd", [(out[tt * 128:(tt + 1) * 128, :], acc)])
    S("gpsimd", lambda e: e.engine_nop())

    q.emit()
    st.close()
    return nc


def prep_inmaps(inputs, T, n_ranks):
    def t(a):
        return np.ascontiguousarray(np.transpose(np.asarray(a, np.float32)))

    x = np.ascontiguousarray(inputs["L0_attn_x"], dtype=np.float32)
    epc = E // n_ranks
    shared = {
        "wmod": np.ascontiguousarray(inputs["L0_wmod"], np.float32),
        "ln1w": inputs["L0_attn_ln1_w"].reshape(1, D).astype(np.float32),
        "ln1b": inputs["L0_attn_ln1_b"].reshape(1, D).astype(np.float32),
        "ln2w": inputs["L0_attn_ln2_w"].reshape(1, D).astype(np.float32),
        "ln2b": inputs["L0_attn_ln2_b"].reshape(1, D).astype(np.float32),
        "normw": inputs["L0_attn_norm_w"].reshape(1, DI).astype(np.float32),
        "mlpb1": inputs["L0_attn_mlp_b1"].reshape(1, MLP_H).astype(np.float32),
        "mlpb2": inputs["L0_attn_mlp_b2"].reshape(1, D).astype(np.float32),
        "gateb": inputs["L0_ffn_gate_b"].reshape(1, E).astype(np.float32),
        "a_log": inputs["L0_attn_A_log"].reshape(1, NH).astype(np.float32),
        "d_vec": inputs["L0_attn_D"].reshape(1, NH).astype(np.float32),
        "dtbias": inputs["L0_attn_dt_bias"].reshape(1, NH).astype(np.float32),
        "convw": np.ascontiguousarray(inputs["L0_attn_conv_w"], np.float32),
        "convb": inputs["L0_attn_conv_b"].reshape(CONVD, 1).astype(np.float32),
        "w_ip": t(inputs["L0_attn_in_proj_w"]),
        "w_op": t(inputs["L0_attn_out_proj_w"]),
        "w_m1": t(inputs["L0_attn_mlp_w1"]),
        "w_m2": t(inputs["L0_attn_mlp_w2"]),
        "w_gt": t(inputs["L0_ffn_gate_w"]),
        "w_s1": t(inputs["L0_ffn_ws1"]),
        "w_s3": t(inputs["L0_ffn_ws3"]),
        "w_s2": t(inputs["L0_ffn_ws2"]),
    }
    we1 = np.ascontiguousarray(np.transpose(
        np.asarray(inputs["L0_ffn_w1"], dtype=np.float32), (0, 2, 1)))
    we3 = np.ascontiguousarray(np.transpose(
        np.asarray(inputs["L0_ffn_w3"], dtype=np.float32), (0, 2, 1)))
    we2 = np.ascontiguousarray(np.transpose(
        np.asarray(inputs["L0_ffn_w2"], dtype=np.float32), (0, 2, 1)))
    c_all = np.asarray(inputs["L0_c"], dtype=np.float32)

    in_maps = []
    zeros_prev = np.zeros((T, D), np.float32)
    for r in range(n_ranks):
        b, h = r // 2, r % 2
        m = dict(shared)
        m["c_row"] = np.ascontiguousarray(c_all[b:b + 1])
        m["x_own"] = np.ascontiguousarray(x[b, h * T:(h + 1) * T])
        m["x_prev"] = (np.ascontiguousarray(x[b, 0:T]) if h == 1
                       else zeros_prev)
        m["flag"] = np.array([[1.0 if h == 1 else 0.0]], np.float32)
        m["rank_i"] = np.array([[r]], np.int32)
        m["w_e1"] = np.ascontiguousarray(we1[r * epc:(r + 1) * epc])
        m["w_e3"] = np.ascontiguousarray(we3[r * epc:(r + 1) * epc])
        m["w_e2"] = np.ascontiguousarray(we2[r * epc:(r + 1) * epc])
        in_maps.append(m)
    return in_maps


def kernel(**inputs):
    T = 2048
    key = (T,)
    if key not in _CACHE:
        _CACHE[key] = build_program(T=T, TP=T, n_ranks=NCORES)
    nc = _CACHE[key]
    in_maps = prep_inmaps(inputs, T, NCORES)
    res = run_bass_kernel_spmd(nc, in_maps, core_ids=list(range(NCORES)))
    outs = [np.asarray(res.results[r]["out"]) for r in range(NCORES)]
    x = inputs["L0_attn_x"]
    B, L, _ = x.shape
    full = np.empty((B, L, D), np.float32)
    for r in range(NCORES):
        b, h = r // 2, r % 2
        full[b, h * T:(h + 1) * T] = outs[r]
    return full
